# version 02
# baseline (speedup 1.0000x reference)
import numpy as np
import ml_dtypes
import concourse.bass as bass
import concourse.mybir as mybir
from concourse.bass_utils import run_bass_kernel_spmd

F32 = mybir.dt.float32
BF16 = mybir.dt.bfloat16
I32 = mybir.dt.int32
AF = mybir.ActivationFunctionType
ALU = mybir.AluOpType
AX = mybir.AxisListType

BF = ml_dtypes.bfloat16

S = 2048
D = 1024
H = 16
HD = 64
NP = 8
C = 128
NCH = S // C

ENGS = ("sync", "tensor", "vector", "scalar", "gpsimd")


class Prog:

    def __init__(self, nc):
        self.nc = nc
        self.q = {e: [] for e in ENGS}
        self.cnt = {e: 0 for e in ENGS}
        self.last_tok = {e: (e, 0) for e in ENGS}
        self._dirty = {e: False for e in ENGS}
        self._last_dma = {e: False for e in ENGS}
        self.chans = {}
        self._fence_seen = {}

    SELF_SERIAL = ("vector", "scalar")

    def op(self, eng, fn, dma=False):
        import traceback
        src = traceback.extract_stack(limit=4)
        if eng in self.SELF_SERIAL:
            if self.cnt[eng] > 0:
                self.q[eng].append(["wait", eng, self.cnt[eng]])
            self.q[eng].append(["op", fn, 0, src, eng])
            self._dirty[eng] = True
            self._last_dma[eng] = dma
            self.tick(eng)
            return
        self.q[eng].append(["op", fn, 0, src, eng])
        self._dirty[eng] = True
        self._last_dma[eng] = dma

    def tick(self, eng):
        ent = self.q[eng][-1]
        assert ent[0] == "op"
        inc = 16 if self._last_dma[eng] else 1
        if ent[2] == 0:
            ent[2] = inc
            self.cnt[eng] += inc
        self._dirty[eng] = False
        self.last_tok[eng] = (eng, self.cnt[eng])
        return self.last_tok[eng]

    def dma(self, eng, out, in_, ch):
        if ch not in self.chans:
            self.chans[ch] = 0
        self.chans[ch] += 16
        tok = ("#" + ch, self.chans[ch])
        self.q[eng].append(["op", lambda g, o=out, i=in_:
                            g.dma_start(out=o, in_=i), 16, None, "#" + ch])
        return tok

    def wait(self, eng, *toks):
        for (e2, v) in toks:
            if v > 0:
                self.q[eng].append(["wait", e2, v])

    def wait_chan_totals(self, eng, chans):
        for ch in chans:
            if self.chans.get(ch, 0) > 0:
                self.q[eng].append(["wait", "#" + ch, self.chans[ch]])

    def fence(self, *engs):
        engs = engs or ENGS
        toks = []
        for e in engs:
            if self._dirty[e]:
                toks.append(self.tick(e))
            elif self.cnt[e] > 0:
                toks.append(self.last_tok[e])
        for ch, cnt in self.chans.items():
            if cnt > 0 and self._fence_seen.get(ch) != cnt:
                toks.append(("#" + ch, cnt))
                self._fence_seen[ch] = cnt
        for e in engs:
            self.wait(e, *[t for t in toks if t[0] != e])
        return toks

    def build(self):
        nc = self.nc
        sems = {}
        ctxs = []
        for e in ENGS:
            cm = nc.semaphore(f"sem_{e}")
            sems[e] = cm.__enter__()
            ctxs.append(cm)
        for ch in self.chans:
            cm = nc.semaphore(f"ch_{ch}")
            sems["#" + ch] = cm.__enter__()
            ctxs.append(cm)
        blkcm = nc.Block()
        blk = blkcm.__enter__()
        for ename in ENGS:
            entries = self.q[ename]

            def body(eng, entries=entries, ename=ename):
                for ent in entries:
                    if ent[0] == "wait":
                        eng.wait_ge(sems[ent[1]], ent[2])
                    else:
                        try:
                            inst = ent[1](eng)
                        except Exception:
                            if ent[3] is not None:
                                print("FAILED OP emitted at:")
                                for fr in ent[3]:
                                    print(f"  {fr.filename}:{fr.lineno} {fr.line}")
                            raise
                        if ent[2]:
                            inst.then_inc(sems[ent[4]], ent[2])

            getattr(blk, ename)(body)
        blkcm.__exit__(None, None, None)
        for cm in ctxs:
            cm.__exit__(None, None, None)



def MM(p, out, lhsT, rhs, start=True, stop=True):
    p.op("tensor", lambda t, o=out, l=lhsT, r=rhs, a=start, b=stop:
         t.matmul(o, l, r, start=a, stop=b))


def DMA(p, eng, out, in_, ch="ini"):
    return p.dma(eng, out, in_, ch)


def ACT(p, out, in_, func, bias=0.0, scale=1.0):
    p.op("scalar", lambda s, o=out, i=in_, f=func, b=bias, sc=scale:
         s.activation(out=o, in_=i, func=f, bias=b, scale=sc))


def TT(p, out, in0, in1, alu, eng="vector"):
    p.op(eng, lambda v, o=out, a=in0, b=in1, op=alu:
         v.tensor_tensor(out=o, in0=a, in1=b, op=op))


def TS(p, out, in0, s1, op0, s2=None, op1=None, eng="vector"):
    if s2 is None:
        p.op(eng, lambda v, o=out, a=in0, x=s1, op=op0:
             v.tensor_single_scalar(out=o, in_=a, scalar=x, op=op))
    else:
        p.op(eng, lambda v, o=out, a=in0, x=s1, y=s2, oa=op0, ob=op1:
             v.tensor_scalar(out=o, in0=a, scalar1=x, scalar2=y, op0=oa, op1=ob))


def STT(p, out, in0, scal, in1, op0, op1, eng="vector"):
    p.op(eng, lambda v, o=out, a=in0, s=scal, b=in1, oa=op0, ob=op1:
         v.scalar_tensor_tensor(out=o, in0=a, scalar=s, in1=b, op0=oa, op1=ob))


def COPY(p, out, in_, eng="vector"):
    p.op(eng, lambda v, o=out, i=in_: v.tensor_copy(out=o, in_=i))


def SCAN(p, out, d0, d1, init, op0, op1):
    p.op("vector", lambda v, o=out, a=d0, b=d1, i=init, oa=op0, ob=op1:
         v.tensor_tensor_scan(out=o, data0=a, data1=b, initial=i, op0=oa, op1=ob))


def REDUCE(p, out, in_, alu, eng="vector"):
    p.op(eng, lambda v, o=out, i=in_, op=alu:
         v.tensor_reduce(out=o, in_=i, axis=AX.X, op=op))


def MEMSET(p, ap, val, eng="vector"):
    p.op(eng, lambda v, a=ap, x=val: v.memset(a, x))


def _dump_stop(p, nc, tiles):
    p.fence()
    for nm, ap in tiles.items():
        d = nc.dram_tensor("dbg_" + nm, list(ap.shape), ap.dtype,
                           kind="ExternalOutput").ap()
        DMA(p, "sync", d, ap, ch="dbg")
    p.wait_chan_totals("gpsimd", ["dbg"])
    p.op("gpsimd", lambda g: g.engine_nop())
    p.tick("gpsimd")
    p.build()



def _host_consts_A():
    c = {}
    c["c_id16f"] = np.vstack([np.eye(16)] * 3).astype(np.float32)
    c["c_id64b"] = np.vstack([np.eye(64), np.eye(64)]).astype(BF)
    c["c_id128b"] = np.eye(128, dtype=BF)
    c["c_ones1f"] = np.ones((1, 128), np.float32)
    c["c_ones1b"] = np.ones((1, 128), BF)
    c["c_ones128b"] = np.ones((128, 1), BF)
    t = np.arange(128)
    ls = (t[None, :] < t[:, None]).astype(np.float32)
    us = (t[None, :] > t[:, None]).astype(np.float32)
    ui = (t[None, :] >= t[:, None]).astype(np.float32)
    c["c_maskLsNegB"] = np.tile(-ls, (1, 16)).astype(BF)
    c["c_maskUsNegB"] = np.tile(-us, (1, 16)).astype(BF)
    c["c_maskUiB"] = np.tile(ui, (1, 16)).astype(BF)
    sel = np.zeros((16, 8 * 128), np.float32)
    for hp in range(8):
        sel[2 * hp, hp * 128: hp * 128 + 64] = 1.0
        sel[2 * hp + 1, hp * 128 + 64: hp * 128 + 128] = 1.0
    c["c_selpairB"] = sel.astype(BF)
    bo = np.zeros((128, 2), np.float32)
    bo[0:64, 0] = 1.0
    bo[64:128, 1] = 1.0
    c["c_bo2"] = bo.astype(BF)
    c["c_bo2T"] = bo.T.astype(BF).copy()
    bd = np.zeros((16, 1024), np.float32)
    for g in range(16):
        bd[g, g * 64:(g + 1) * 64] = 1.0
    c["c_BD16f"] = bd
    c["c_ones16_64f"] = np.ones((16, 64), np.float32)
    return c


def _layout_A_weights(w):
    d = {}
    d["a_wq"] = np.ascontiguousarray(w["L0_attn_Wq"])
    d["a_wk"] = np.ascontiguousarray(w["L0_attn_Wk"])
    d["a_wv"] = np.ascontiguousarray(w["L0_attn_Wv"])
    d["a_wg"] = np.ascontiguousarray(w["L0_attn_Wg"])
    d["a_wo"] = np.ascontiguousarray(w["L0_attn_Wo"])
    d["a_wa"] = np.ascontiguousarray(w["L0_attn_Wa"])
    d["a_wb"] = np.ascontiguousarray(w["L0_attn_Wb"])
    for nm, key in (("a_bq", "L0_attn_bq"), ("a_bk", "L0_attn_bk"),
                    ("a_bv", "L0_attn_bv"), ("a_bg", "L0_attn_bg"),
                    ("a_bo", "L0_attn_bo"), ("a_gns", "L0_attn_gn_scale"),
                    ("a_gnb", "L0_attn_gn_bias")):
        d[nm] = np.ascontiguousarray(w[key].reshape(8, 128).T)
    d["a_ba"] = np.ascontiguousarray(w["L0_attn_ba"].reshape(16, 1))
    d["a_bb"] = np.ascontiguousarray(w["L0_attn_bb"].reshape(16, 1))
    for nm, key in (("a_qc", "L0_attn_q_conv"), ("a_kc", "L0_attn_k_conv"),
                    ("a_vc", "L0_attn_v_conv")):
        d[nm] = np.ascontiguousarray(
            w[key].reshape(8, 128, 4).transpose(1, 0, 2).reshape(128, 32))
    d["a_wgg"] = np.ascontiguousarray(w["L0_ffn_wg"])
    d["a_bgg"] = np.ascontiguousarray(w["L0_ffn_bg"].reshape(1, 16))
    return d




def build_A(stop=None, nch=NCH, nstage=99):
    nc = bass.Bass("TRN2", target_bir_lowering=False)
    p = Prog(nc)

    x0T = nc.dram_tensor("x0T", [D, S], F32, kind="ExternalInput").ap()
    x1T = nc.dram_tensor("x1T", [D, S], F32, kind="ExternalOutput").ap()
    maskedT = nc.dram_tensor("maskedT", [128, 256], F32, kind="ExternalOutput").ap()
    spill = nc.dram_tensor("spillA", [3, NCH, 128, NP * C], BF16, kind="Internal").ap()

    dram = {}
    for nm in ("a_wq", "a_wk", "a_wv", "a_wg", "a_wo"):
        dram[nm] = nc.dram_tensor(nm, [D, D], F32, kind="ExternalInput").ap()
    for nm in ("a_wa", "a_wb", "a_wgg"):
        dram[nm] = nc.dram_tensor(nm, [D, 16], F32, kind="ExternalInput").ap()
    small_params = (("a_bq", [128, 8]), ("a_bk", [128, 8]), ("a_bv", [128, 8]),
                    ("a_bg", [128, 8]), ("a_bo", [128, 8]), ("a_gns", [128, 8]),
                    ("a_gnb", [128, 8]), ("a_ba", [16, 1]), ("a_bb", [16, 1]),
                    ("a_qc", [128, 32]), ("a_kc", [128, 32]), ("a_vc", [128, 32]),
                    ("a_bgg", [1, 16]))
    for nm, shp in small_params:
        dram[nm] = nc.dram_tensor(nm, shp, F32, kind="ExternalInput").ap()
    cshapes = (("c_id16f", [48, 16], F32), ("c_id64b", [128, 64], BF16),
               ("c_id128b", [128, 128], BF16),
               ("c_ones1f", [1, 128], F32), ("c_ones1b", [1, 128], BF16),
               ("c_ones128b", [128, 1], BF16),
               ("c_maskLsNegB", [128, 2048], BF16), ("c_maskUsNegB", [128, 2048], BF16),
               ("c_maskUiB", [128, 2048], BF16), ("c_selpairB", [16, 1024], BF16),
               ("c_bo2", [128, 2], BF16), ("c_bo2T", [2, 128], BF16))
    for nm, shp, dt in cshapes:
        dram[nm] = nc.dram_tensor(nm, shp, dt, kind="ExternalInput").ap()

    A = nc.alloc_sbuf_tensor
    x0b = A("x0b", [128, 8 * 2048], BF16).ap()
    chain = A("chain", [128, 8 * 2048], BF16).ap()
    outs = A("outs", [128, 8 * 2048], BF16).ap()
    wbuf = A("wbuf", [128, 8 * 1024], BF16).ap()
    smal = A("smal", [128, 8 * 1024], BF16).ap()
    smal2 = A("smal2", [128, 4 * 1024], BF16).ap()
    fsc = A("fsc", [128, 10240], F32).ap()
    stf = A("stf", [128, 512], F32).ap()
    stb = A("stb", [128, 512], BF16).ap()
    wvt = A("wvt", [128, 1024], BF16).ap()
    gate = A("gate", [128, 256], F32).ap()
    gate2 = A("gate2", [128, 256], F32).ap()
    m8t = A("m8t", [128, 8], F32).ap()
    mk16 = A("mk16", [128, 16], F32).ap()
    betaBb = A("betaBb", [16, 2048], BF16).ap()
    normb = A("normb", [2, 2048], BF16).ap()
    cumTf = A("cumTf", [128, 16], F32).ap()
    gTnegf = A("gTnegf", [128, 16], F32).ap()
    gslb = A("gslb", [16, 128], BF16).ap()
    dslb = A("dslb", [16, 128], BF16).ap()
    gsl32 = A("gsl32", [16, 128], F32).ap()
    gccol = A("gccol", [128, 8], BF16).ap()
    gcb16 = A("gcb16", [16, 1], BF16).ap()
    statm = A("statm", [2, 8], F32).ap()
    statq = A("statq", [2, 8], F32).ap()
    statv = A("statv", [2, 8], F32).ap()
    statmb = A("statmb", [2, 8], BF16).ap()
    statrb = A("statrb", [2, 8], BF16).ap()
    gcol2 = A("gcol2", [128, 2], BF16).ap()
    murb = A("murb", [1, 2048], BF16).ap()
    rsrb = A("rsrb", [1, 2048], BF16).ap()
    sbc = {}
    for nm, shp, dt in cshapes:
        sbc[nm] = A(nm + "_s", shp, dt).ap()
    sbp = {}
    for nm, shp in small_params:
        sbp[nm] = A(nm + "_s", shp, F32).ap()
    wab = A("wab", [128, 8 * 16 * 3], BF16).ap()
    bggb = A("bggb", [1, 16], BF16).ap()

    f_tmpd = fsc[:, 0:2048]
    f_x0fA = fsc[:, 0:2048]
    f_x0fB = fsc[:, 2048:4096]
    f_p1a = fsc[0:16, 2048:4096]
    f_nrmA = fsc[32:34, 2048:4096]
    f_nrmB = fsc[64:66, 2048:4096]
    f_cum = fsc[0:16, 4096:6144]
    f_rev = fsc[32:48, 4096:6144]
    f_beta = fsc[64:80, 4096:6144]
    f_lnA = fsc[0:16, 6144:8192]
    f_murow = fsc[32:33, 6144:8192]
    f_cumflat = fsc[0:1, 6144:8192]
    f_rstdrow = fsc[64:65, 8192:10240]
    f_mu2row = fsc[32:33, 8192:10240]

    cs = [chain[:, i * 2048:(i + 1) * 2048] for i in range(8)]
    sY1, sZ1, sAt, s3, s4, s5, s6, sTmp = cs

    pairbuf = [smal[:, 0:2048], smal[:, 2048:4096]]
    sqbuf = [smal[:, 4096:5120], smal[:, 5120:6144]]
    convsc = [smal[:, 6144:6704], smal[:, 6704:7264]]
    qch = [smal[:, 0:1024], smal[:, 1024:2048]]
    kch = [smal[:, 2048:3072], smal[:, 3072:4096]]
    vch = [smal[:, 4096:5120], smal[:, 5120:6144]]
    kstar = smal[:, 6144:7168]
    qplus = smal[:, 7168:8192]
    muBCb = smal[:, 0:2048]
    rsBCb = smal[:, 2048:4096]
    sqgn = [smal[:, 4096:6144], smal[:, 6144:8192]]
    vTsb = smal2[:, 0:1024]
    vTsb = smal2[:, 3072:4096]
    kdec2dm = smal2[:, 1024:2048]
    kdec2tm = smal2[:, 2048:3072]
    gtmp = [smal2[:, 0:2048], smal2[:, 2048:4096]]

    psA = nc.alloc_psum_tensor("psA", [128, 2048], F32).ap()
    psB = nc.alloc_psum_tensor("psB", [128, 1024], F32).ap()
    psC = nc.alloc_psum_tensor("psC", [128, 512], F32).ap()
    psD = nc.alloc_psum_tensor("psD", [128, 512], F32).ap()

    for nm, shp, dt in cshapes:
        DMA(p, "sync", sbc[nm][:, :], dram[nm])
    for nm, shp in small_params:
        DMA(p, "sync", sbp[nm][:, :], dram[nm])
    DMA(p, "gpsimd", x0b.rearrange("p (c t) -> p c t", t=2048),
        x0T.rearrange("(c p) t -> p c t", p=128), ch="inig")
    for i, nm in enumerate(("a_wa", "a_wb", "a_wgg")):
        DMA(p, "gpsimd", wab.rearrange("p (j c e) -> p j c e", j=3, c=8)[:, i],
            dram[nm].rearrange("(c p) e -> p c e", p=128))
    COPY(p, bggb[:, :], sbp["a_bgg"][:, :])
    MEMSET(p, cbias[:, 0:1], 1e-5)
    MEMSET(p, cbias[:, 1:2], 1.0)
    p.fence()

    wa_v = wab.rearrange("p (j c e) -> p j c e", j=3, c=8)
    x0v = x0b.rearrange("p (c t) -> p c t", t=2048)

    def ab_logits(j):
        for cg in range(4):
            for k in range(8):
                MM(p, psA[0:16, cg * 512:(cg + 1) * 512], wa_v[:, j, k],
                   x0v[:, k, cg * 512:(cg + 1) * 512], start=(k == 0), stop=(k == 7))

    ab_logits(0)
    tA = p.tick("tensor")
    p.wait("scalar", tA)
    ACT(p, f_p1a, psA[0:16, :], AF.Abs, bias=sbp["a_ba"][:, 0:1])
    ACT(p, f_lnA, f_p1a, AF.Exp, scale=-1.0)
    ACT(p, f_p1a, f_lnA, AF.Ln, bias=cbias[0:16, 1:2])
    tS = p.tick("scalar")
    p.wait("vector", tS)
    TS(p, f_lnA, psA[0:16, :], sbp["a_ba"][:, 0:1], ALU.add, 0.0, ALU.min)
    TT(p, f_lnA, f_lnA, f_p1a, ALU.subtract)
    tV = p.tick("vector")
    p.wait("tensor", tV)
    ab_logits(1)
    tA = p.tick("tensor")
    p.wait("scalar", tA)
    ACT(p, f_beta, psA[0:16, :], AF.Sigmoid, bias=sbp["a_bb"][:, 0:1])
    tS = p.tick("scalar")
    p.wait("vector", tS)
    COPY(p, betaBb[:, :], f_beta)
    for c in range(NCH):
        sl = slice(c * C, (c + 1) * C)
        SCAN(p, f_cum[:, sl], f_lnA[:, sl], f_lnA[:, sl], 0.0, ALU.add, ALU.bypass)
        TS(p, f_rev[:, sl], f_cum[:, sl], f_cum[:, c * C + C - 1:c * C + C],
           ALU.subtract, -1.0, ALU.mult)
    p.fence()

    tens_cfg = (("a_wq", "a_bq", "a_qc", True, 0), ("a_wk", "a_bk", "a_kc", True, 1),
                ("a_wv", "a_bv", "a_vc", False, 2))
    wslot = [wbuf[:, i * 1024:(i + 1) * 1024] for i in range(4)]
    wslot_tok = [None] * 4
    pair_tok = [None, None]
    psA_half = [psA[:, 0:1024], psA[:, 1024:2048]]
    psA_tok = [None, None]
    si = 0
    for (wnm, bnm, cnm, do_norm, jidx) in tens_cfg:
        wv_ = dram[wnm].rearrange("(c p) m -> p c m", p=128)
        for hp in range(8):
            par = si % 2
            ws = wslot[si % 8]
            if wslot_tok[si % 8] is not None:
                p.wait("gpsimd", wslot_tok[si % 8])
            DMA(p, "gpsimd", ws.rearrange("p (c m) -> p c m", m=128),
                wv_[:, :, hp * 128:(hp + 1) * 128])
            tL = p.tick("gpsimd")
            p.wait("tensor", tL)
            if psA_tok[par] is not None:
                p.wait("tensor", psA_tok[par])
            if pair_tok[par] is not None:
                p.wait("tensor", pair_tok[par])
            wsv = ws.rearrange("p (c m) -> p c m", m=128)
            for h2 in range(2):
                for k in range(8):
                    MM(p, psA_half[par][:, h2 * 512:(h2 + 1) * 512], wsv[:, k],
                       x0v[:, k, (par * 0 + 1024 * h2 + 512 * 0):0], start=False)
            si += 1
    p.fence()
    p.build()
    return nc


A_SMALL_PARAMS = (("a_bq", [128, 8]), ("a_bk", [128, 8]), ("a_bv", [128, 8]),
                  ("a_bg", [128, 8]), ("a_bo", [128, 8]), ("a_gns", [128, 8]),
                  ("a_gnb", [128, 8]), ("a_ba", [16, 1]), ("a_bb", [16, 1]),
                  ("a_qc", [128, 32]), ("a_kc", [128, 32]), ("a_vc", [128, 32]),
                  ("a_bgg", [1, 16]))
A_CONSTS = (("c_id16f", [48, 16], F32), ("c_id64b", [128, 64], BF16),
            ("c_ones1f", [1, 128], F32), ("c_ones1b", [1, 128], BF16),
            ("c_ones128b", [128, 1], BF16),
            ("c_maskLsNegB", [128, 2048], BF16), ("c_maskUsNegB", [128, 2048], BF16),
            ("c_maskUiB", [128, 2048], BF16), ("c_selpairB", [16, 1024], BF16),
            ("c_bo2", [128, 2], BF16), ("c_bo2T", [2, 128], BF16))


def hsl(ap, h, w=128):
    return ap[64 * (h % 2):64 * (h % 2) + 64, (h // 2) * w:(h // 2) * w + w]


def wave16(p, emit):
    for h in range(0, 16, 2):
        emit(h)
    t = p.tick("tensor")
    p.wait("tensor", t)
    for h in range(1, 16, 2):
        emit(h)
    return p.tick("tensor")


def build_A(stop=None, nch=NCH, nstage=99):
    nc = bass.Bass("TRN2", target_bir_lowering=False)
    p = Prog(nc)

    x0T = nc.dram_tensor("x0T", [D, S], F32, kind="ExternalInput").ap()
    x1T = nc.dram_tensor("x1T", [D, S], F32, kind="ExternalOutput").ap()
    maskedT = nc.dram_tensor("maskedT", [128, 256], F32, kind="ExternalOutput").ap()
    spill = nc.dram_tensor("spillA", [3, NCH, 128, NP * C], BF16, kind="Internal").ap()

    dram = {}
    for nm in ("a_wq", "a_wk", "a_wv", "a_wg", "a_wo"):
        dram[nm] = nc.dram_tensor(nm, [D, D], F32, kind="ExternalInput").ap()
    for nm in ("a_wa", "a_wb", "a_wgg"):
        dram[nm] = nc.dram_tensor(nm, [D, 16], F32, kind="ExternalInput").ap()
    for nm, shp in A_SMALL_PARAMS:
        dram[nm] = nc.dram_tensor(nm, shp, F32, kind="ExternalInput").ap()
    for nm, shp, dt in A_CONSTS:
        dram[nm] = nc.dram_tensor(nm, shp, dt, kind="ExternalInput").ap()

    AL = nc.alloc_sbuf_tensor
    x0b = AL("x0b", [128, 8 * 2048], BF16).ap()
    chain = AL("chain", [128, 8 * 2048], BF16).ap()
    outs = AL("outs", [128, 8 * 2048], BF16).ap()
    wbuf = AL("wbuf", [128, 4 * 1024], BF16).ap()
    smal = AL("smal", [128, 8 * 1024], BF16).ap()
    smal2 = AL("smal2", [128, 4 * 1024], BF16).ap()
    fsc = AL("fsc", [128, 10240], F32).ap()
    stf = AL("stf", [128, 512], F32).ap()
    stb = AL("stb", [128, 512], BF16).ap()
    wvt = AL("wvt", [128, 1024], BF16).ap()
    gate = AL("gate", [128, 256], F32).ap()
    gate2 = AL("gate2", [128, 256], F32).ap()
    m8t = AL("m8t", [128, 8], F32).ap()
    scr16 = AL("scr16", [128, 16], F32).ap()
    betaBb = AL("betaBb", [16, 2048], BF16).ap()
    normb = AL("normb", [2, 2048], BF16).ap()
    cumTf = AL("cumTf", [128, 16], F32).ap()
    gTnegf = AL("gTnegf", [128, 16], F32).ap()
    gslb = AL("gslb", [16, 128], BF16).ap()
    dslb = AL("dslb", [16, 128], BF16).ap()
    gsl32 = AL("gsl32", [16, 128], F32).ap()
    gccol = AL("gccol", [128, 8], F32).ap()
    statm = AL("statm", [2, 8], F32).ap()
    statq = AL("statq", [2, 8], F32).ap()
    statv = AL("statv", [2, 8], F32).ap()
    statmb = AL("statmb", [2, 8], BF16).ap()
    statrb = AL("statrb", [2, 8], BF16).ap()
    gcol2 = AL("gcol2", [128, 2], BF16).ap()
    gcolA = AL("gcolA", [128, 16], F32).ap()
    sbc = {}
    for nm, shp, dt in A_CONSTS:
        sbc[nm] = AL(nm + "_s", shp, dt).ap()
    sbp = {}
    for nm, shp in A_SMALL_PARAMS:
        sbp[nm] = AL(nm + "_s", shp, F32).ap()
    wab = AL("wab", [128, 8 * 16 * 3], BF16).ap()
    murb = betaBb[0:1, 0:2048]
    rsrb = normb[0:1, 0:2048]

    bggb = AL("bggb", [1, 16], BF16).ap()
    cbias = AL("cbias", [128, 2], F32).ap()

    f_tmpd = fsc[:, 0:2048]
    f_x0f = [fsc[:, 0:2048], fsc[:, 2048:4096]]
    f_p1a = fsc[0:16, 2048:4096]
    f_nrmA = fsc[0:2, 0:2048]
    f_nrmB = fsc[0:2, 2048:4096]
    f_beta = fsc[0:16, 4096:6144]
    f_lnA = fsc[0:16, 6144:8192]
    f_rev = fsc[0:16, 6144:8192]
    f_cum = fsc[0:16, 8192:10240]
    f_cumflat = fsc[0:1, 2048:4096]
    f_murow = fsc[0:1, 4096:6144]
    f_mu2row = fsc[0:1, 6144:8192]
    f_rstdrow = fsc[0:1, 8192:10240]

    cslots = [chain[:, i * 2048:(i + 1) * 2048] for i in range(8)]
    sY1, sZ1, sAt, s3, s4, s5, s6, sTmp = cslots

    pairbuf = [smal[:, 0:2048], smal[:, 2048:4096]]
    sqbuf = [smal[:, 4096:6144], smal[:, 6144:8192]]
    qch = [smal[:, 0:1024], smal[:, 1024:2048]]
    kch = [smal[:, 2048:3072], smal[:, 3072:4096]]
    vch = [smal[:, 4096:5120], smal[:, 5120:6144]]
    kstar = smal[:, 6144:7168]
    qplus = smal[:, 7168:8192]
    muBCb = smal[:, 0:2048]
    rsBCb = smal[:, 2048:4096]
    sqgn = [smal[:, 4096:6144], smal[:, 6144:8192]]
    vTsb = smal2[:, 3072:4096]
    kdec2dm = smal2[:, 1024:2048]
    kdec2tm = smal2[:, 2048:3072]
    gtmp = [smal2[:, 0:2048], smal2[:, 2048:4096]]

    psE = nc.alloc_psum_tensor("psE", [128, 2048], F32).ap()
    psF = nc.alloc_psum_tensor("psF", [128, 2048], F32).ap()
    psB = psF[:, 0:1024]
    psSt = psF[:, 1024:1536]
    psG = psF[:, 1536:2048]

    x0v = x0b.rearrange("p (c t) -> p c t", t=2048)
    wabv = wab.rearrange("p (j c e) -> p j c e", j=3, c=8)
    outsv = outs.rearrange("p (r t) -> p r t", t=2048)
    zbfv = chain.rearrange("p (c t) -> p c t", t=2048)

    for nm, shp, dt in A_CONSTS:
        DMA(p, "sync", sbc[nm][:, :], dram[nm])
    for nm, shp in A_SMALL_PARAMS:
        DMA(p, "sync", sbp[nm][:, :], dram[nm])
    DMA(p, "gpsimd", x0b.rearrange("p (c t) -> p c t", t=2048),
        x0T.rearrange("(c p) t -> p c t", p=128), ch="inig")
    for i, nm in enumerate(("a_wa", "a_wb", "a_wgg")):
        DMA(p, "gpsimd", wabv[:, i],
            dram[nm].rearrange("(c p) e -> p c e", p=128), ch="inig")
    p.fence()
    COPY(p, bggb[:, :], sbp["a_bgg"][:, :])
    MEMSET(p, cbias[:, 0:1], 1e-5)
    MEMSET(p, cbias[:, 1:2], 1.0)
    MEMSET(p, stf[:, :], 0.0)
    MEMSET(p, stb[:, :], 0.0)
    MEMSET(p, outs[:, :], 0.0)
    p.fence()

    def ab_logits(j):
        for cg in range(4):
            for k in range(8):
                MM(p, psE[0:16, cg * 512:(cg + 1) * 512], wabv[:, j, k],
                   x0v[:, k, cg * 512:(cg + 1) * 512], start=(k == 0), stop=(k == 7))

    ab_logits(0)
    t = p.tick("tensor"); p.wait("scalar", t); p.wait("vector", t)
    ACT(p, f_p1a, psE[0:16, :], AF.Abs, bias=sbp["a_ba"][:, 0:1])
    ACT(p, f_lnA, f_p1a, AF.Exp, scale=-1.0)
    ACT(p, f_p1a, f_lnA, AF.Ln, bias=cbias[0:16, 1:2])
    t = p.tick("scalar"); p.wait("vector", t)
    TS(p, f_lnA, psE[0:16, :], sbp["a_ba"][:, 0:1], ALU.add, 0.0, ALU.min)
    TT(p, f_lnA, f_lnA, f_p1a, ALU.subtract)
    t = p.tick("vector"); p.wait("tensor", t)
    ab_logits(1)
    t = p.tick("tensor"); p.wait("scalar", t)
    ACT(p, f_beta, psE[0:16, :], AF.Sigmoid, bias=sbp["a_bb"][:, 0:1])
    t = p.tick("scalar"); p.wait("vector", t)
    COPY(p, betaBb[:, :], f_beta)
    for c in range(NCH):
        sl = slice(c * C, (c + 1) * C)
        SCAN(p, f_cum[:, sl], f_lnA[:, sl], f_lnA[:, sl], 0.0, ALU.add, ALU.bypass)
    t = p.tick("vector"); p.wait("vector", t)
    for c in range(NCH):
        sl = slice(c * C, (c + 1) * C)
        TS(p, f_rev[:, sl], f_cum[:, sl], f_cum[:, c * C + C - 1:c * C + C],
           ALU.subtract, -1.0, ALU.mult)
    p.fence()
    if stop == "P1a":
        _dump_stop(p, nc, {"cum": f_cum, "rev": f_rev, "beta": f_beta,
                           "lnA": f_rev})
        return nc

    wslot = [wbuf[:, i * 1024:(i + 1) * 1024] for i in range(4)]
    wslot_tok = [None] * 4
    spill_tok = [None, None]
    evac_tok = [None, None]
    psP = [psE, psF]
    convw = {"a_wq": "a_qc", "a_wk": "a_kc", "a_wv": "a_vc"}
    si = 0
    for (wnm, bnm, do_norm, jidx) in (("a_wq", "a_bq", True, 0),
                                      ("a_wk", "a_bk", True, 1),
                                      ("a_wv", "a_bv", False, 2)):
        wsrc = dram[wnm].rearrange("(c p) m -> p c m", p=128)
        cw = sbp[convw[wnm]]
        for hp in range(8):
            par = si % 2
            slot = si % 4
            ws = wslot[slot]
            wsv = ws.rearrange("p (c m) -> p c m", m=128)
            if wslot_tok[slot] is not None:
                p.wait("gpsimd", wslot_tok[slot])
            tL = DMA(p, "gpsimd", wsv, wsrc[:, :, hp * 128:(hp + 1) * 128],
                     ch=f"wl{slot}")
            p.wait("tensor", tL)
            if evac_tok[par] is not None:
                p.wait("tensor", evac_tok[par])
            ps = psP[par]
            for cg in range(4):
                for k in range(8):
                    MM(p, ps[:, cg * 512:(cg + 1) * 512], wsv[:, k],
                       x0v[:, k, cg * 512:(cg + 1) * 512], start=(k == 0), stop=(k == 7))
            tMM = p.tick("tensor")
            wslot_tok[slot] = tMM
            p.wait("scalar", tMM)
            if spill_tok[par] is not None:
                p.wait("scalar", spill_tok[par])
            pb = pairbuf[par]
            ACT(p, pb, ps, AF.Identity, bias=sbp[bnm][:, hp:hp + 1])
            tEv = p.tick("scalar")
            evac_tok[par] = tEv
            p.wait("vector", tEv)
            co = gtmp[par]
            TS(p, co, pb, cw[:, hp * 4 + 3:hp * 4 + 4], ALU.mult)
            for j, dlt in ((2, 1), (1, 2), (0, 3)):
                STT(p, co[:, dlt:2048], pb[:, 0:2048 - dlt],
                    cw[:, hp * 4 + j:hp * 4 + j + 1], co[:, dlt:2048],
                    ALU.mult, ALU.add)
            tCv = p.tick("vector")
            p.wait("scalar", tCv)
            ACT(p, pb, co, AF.Silu)
            tSi = p.tick("scalar")
            if do_norm:
                ACT(p, sqbuf[par], pb, AF.Square)
                tSq = p.tick("scalar")
                p.wait("tensor", tSq)
                for cg in range(4):
                    MM(p, ps[0:2, cg * 512:(cg + 1) * 512], sbc["c_bo2"][:, :],
                       sqbuf[par][:, cg * 512:(cg + 1) * 512])
                tQ = p.tick("tensor"); p.wait("scalar", tQ)
                ACT(p, f_nrmA, ps[0:2, :], AF.Sqrt)
                tQ = p.tick("scalar"); p.wait("vector", tQ)
                TS(p, f_nrmA, f_nrmA, 1e-6, ALU.add)
                p.op("vector", lambda v, o=f_nrmB, i=f_nrmA: v.reciprocal(out=o, in_=i))
                COPY(p, normb[:, :], f_nrmB)
                tN = p.tick("vector"); p.wait("tensor", tN)
                for cg in range(4):
                    MM(p, ps[:, cg * 512:(cg + 1) * 512], sbc["c_bo2T"][:, :],
                       normb[:, cg * 512:(cg + 1) * 512])
                tB = p.tick("tensor"); p.wait("vector", tB, tSi)
                TT(p, pb, pb, ps, ALU.mult)
            else:
                p.wait("tensor", tSi)
                for cg in range(4):
                    MM(p, ps[:, cg * 512:(cg + 1) * 512],
                       sbc["c_selpairB"][:, hp * 128:(hp + 1) * 128],
                       betaBb[:, cg * 512:(cg + 1) * 512])
                tB = p.tick("tensor"); p.wait("vector", tB)
                TT(p, pb, pb, ps, ALU.mult)
            tFin = p.tick("vector")
            evac_tok[par] = tFin
            p.wait("sync", tFin)
            spill_tok[par] = DMA(
                p, "sync", spill[jidx, :, :, hp * 128:(hp + 1) * 128]
                .rearrange("c p t -> p c t"),
                pb.rearrange("p (c t) -> p c t", t=128), ch=f"sp{par}")
            si += 1
    p.fence()
    if stop == "P1b":
        qd = nc.dram_tensor("dbg_q", [NCH, 128, NP * C], BF16, kind="ExternalOutput").ap()
        kd = nc.dram_tensor("dbg_k", [NCH, 128, NP * C], BF16, kind="ExternalOutput").ap()
        vd = nc.dram_tensor("dbg_v", [NCH, 128, NP * C], BF16, kind="ExternalOutput").ap()
        DMA(p, "sync", qd, spill[0], ch="dbg")
        DMA(p, "sync", kd, spill[1], ch="dbg")
        DMA(p, "sync", vd, spill[2], ch="dbg")
        p.wait_chan_totals("gpsimd", ["dbg"])
        p.op("gpsimd", lambda g: g.engine_nop())
        p.tick("gpsimd")
        p.build()
        return nc

    ld_tok = [[None, None], [None, None], [None, None]]
    use_tok = [[None, None], [None, None], [None, None]]
    chb = [qch, kch, vch]

    def issue_loads(c):
        par = c % 2
        for j in range(3):
            if use_tok[j][par] is not None:
                p.wait("gpsimd", use_tok[j][par])
            ld_tok[j][par] = DMA(p, "gpsimd", chb[j][par], spill[j, c],
                                 ch=f"ld{j}{par}")

    import os as _os3
    _fs = _os3.environ.get("FENCE_STAGE", "")

    def SF(tag):
        if _fs and (_fs == "all" or tag in _fs.split(",")):
            p.fence("tensor", "vector", "scalar")

    issue_loads(0)
    for c in range(nch):
        par = c % 2
        sl = slice(c * C, (c + 1) * C)
        qcb, kcb, vcb = qch[par], kch[par], vch[par]
        if c + 1 < nch:
            issue_loads(c + 1)
        p.wait("tensor", ld_tok[0][par], ld_tok[1][par], ld_tok[2][par])
        MM(p, psSt[:, 0:16], f_cum[:, sl], sbc["c_id16f"][0:16, :])
        for h in range(16):
            MM(p, psE[0:1, h * 128:(h + 1) * 128], sbc["c_id16f"][0:16, h:h + 1],
               f_cum[:, sl])
        t = p.tick("tensor"); p.wait("vector", t); p.wait("scalar", t)
        COPY(p, cumTf[:, :], psSt[:, 0:16])
        COPY(p, f_cumflat, psE[0:1, :])
        t = p.tick("vector"); p.wait("scalar", t)
        ACT(p, scr16[:, 0:16], cumTf[:, :], AF.Exp)
        ACT(p, gsl32[:, :], f_cum[:, sl], AF.Exp)
        t = p.tick("scalar"); p.wait("vector", t)
        TS(p, gTnegf[:, :], scr16[:, 0:16], -1.0, ALU.mult)
        COPY(p, gslb[:, :], gsl32[:, :])
        t = p.tick("vector"); p.wait("scalar", t)
        ACT(p, gsl32[:, :], f_rev[:, sl], AF.Exp)
        t = p.tick("scalar"); p.wait("vector", t)
        COPY(p, dslb[:, :], gsl32[:, :])
        tV = p.tick("vector")

        SF("decay")
        if nstage <= 1:
            break
        p.wait("tensor", tV)
        for pr in range(8):
            MM(p, psB[:, pr * 128:(pr + 1) * 128],
               sbc["c_selpairB"][:, pr * 128:(pr + 1) * 128], betaBb[:, sl])
        t = p.tick("tensor"); p.wait("vector", t)
        TT(p, kstar, kcb, psB, ALU.mult)
        t = p.tick("vector"); p.wait("tensor", t)
        for pr in range(8):
            MM(p, psB[:, pr * 128:(pr + 1) * 128],
               sbc["c_selpairB"][:, pr * 128:(pr + 1) * 128], gslb[:, :])
        t = p.tick("tensor"); p.wait("vector", t)
        TT(p, qplus, qcb, psB, ALU.mult)
        tQP = p.tick("vector")

        SF("kq")
        if nstage <= 2:
            break
        p.wait("tensor", tQP)
        for hf in range(2):
            for g2 in range(2):
                MM(p, psB[:, g2 * 512:(g2 + 1) * 512], sbc["c_ones1f"][:, :],
                   f_cumflat[0:1, hf * 1024 + g2 * 512: hf * 1024 + (g2 + 1) * 512])
            t = p.tick("tensor"); p.wait("vector", t)
            for h8 in range(8):
                h = hf * 8 + h8
                TS(p, f_tmpd[:, h8 * 128:(h8 + 1) * 128],
                   psB[:, h8 * 128:(h8 + 1) * 128], cumTf[:, h:h + 1],
                   ALU.subtract, 0.0, ALU.min)
            t = p.tick("vector"); p.wait("scalar", t)
            ACT(p, s3[:, hf * 1024:(hf + 1) * 1024], f_tmpd[:, 0:1024], AF.Exp)
            t = p.tick("scalar"); p.wait("vector", t)
            for h8 in range(8):
                h = hf * 8 + h8
                TS(p, f_tmpd[:, 1024 + h8 * 128: 1024 + (h8 + 1) * 128],
                   psB[:, h8 * 128:(h8 + 1) * 128], cumTf[:, h:h + 1],
                   ALU.subtract, 0.0, ALU.max)
            t = p.tick("vector"); p.wait("scalar", t); p.wait("tensor", t)
            ACT(p, s4[:, hf * 1024:(hf + 1) * 1024], f_tmpd[:, 1024:2048],
                AF.Exp, scale=-1.0)
        tLam = p.tick("scalar")

        SF("lam")
        if nstage <= 3:
            break
        p.wait("tensor", tLam)
        t = wave16(p, lambda h: MM(p, psE[:, h * 128:(h + 1) * 128], hsl(kstar, h), hsl(kcb, h)))
        p.wait("vector", t)
        TT(p, sTmp, s4, sbc["c_maskLsNegB"][:, :], ALU.mult)
        TT(p, sY1, sTmp, psE, ALU.mult)
        t = p.tick("vector"); p.wait("tensor", t)
        t = wave16(p, lambda h: MM(p, psE[:, h * 128:(h + 1) * 128], hsl(kcb, h), hsl(kstar, h)))
        p.wait("vector", t)
        TT(p, sTmp, s3, sbc["c_maskUsNegB"][:, :], ALU.mult)
        TT(p, sZ1, sTmp, psE, ALU.mult)
        t = p.tick("vector"); p.wait("tensor", t)
        t = wave16(p, lambda h: MM(p, psE[:, h * 128:(h + 1) * 128], hsl(kcb, h), hsl(qcb, h)))
        p.wait("vector", t)
        TT(p, sTmp, s3, sbc["c_maskUiB"][:, :], ALU.mult)
        TT(p, sAt, sTmp, psE, ALU.mult)
        tBld = p.tick("vector")

        SF("gram")
        if nstage <= 4:
            break
        p.wait("tensor", tBld)
        wave16(p, lambda h: MM(p, psB[:, h * 64:(h + 1) * 64], hsl(kstar, h),
                               hsl(stb, h, 64)))
        t = wave16(p, lambda h: MM(p, psE[:, h * 64:(h + 1) * 64], hsl(vcb, h),
                                   sbc["c_id64b"][64 * (h % 2):64 * (h % 2) + 64, :]))
        p.wait("vector", t)
        COPY(p, vTsb, psE[:, 0:1024])
        t = p.tick("vector"); p.wait("vector", t)
        for h in range(16):
            STT(p, wvt[:, h * 64:(h + 1) * 64], psB[:, h * 64:(h + 1) * 64],
                gTnegf[:, h:h + 1], vTsb[:, h * 64:(h + 1) * 64],
                ALU.mult, ALU.add)
        tR = p.tick("vector")

        SF("R")
        if nstage <= 5:
            break
        lv = [(sY1, sZ1), (s3, s4), (s5, s6), (s3, s4), (s5, s6), (s3, s4)]
        tApply = tR
        _sm = _os3.environ.get("SOLVE_MODE", "")
        for i in range(6):
            sy, sz = lv[i]
            if _sm != "nosapply":
                p.wait("tensor", tApply)
                _nw = int(_os3.environ.get("SOLVE_NW", "16"))
                for h in range(16):
                    if h and h % _nw == 0:
                        tt = p.tick("tensor"); p.wait("tensor", tt)
                    MM(p, psB[:, h * 64:(h + 1) * 64],
                       sz[:, h * 128:(h + 1) * 128], wvt[:, h * 64:(h + 1) * 64])
                t = p.tick("tensor"); p.wait("vector", t)
                TT(p, wvt, wvt, psB, ALU.add)
                tApply = p.tick("vector")
            ny, nz = lv[i + 1] if i < 5 else (None, s5)
            last = (i == 5)
            if _sm == "nosquare":
                continue
            _nw = int(_os3.environ.get("SOLVE_NW", "16"))
            if not last:
                for h in range(16):
                    if h and h % _nw == 0:
                        tt = p.tick("tensor"); p.wait("tensor", tt)
                    MM(p, psE[:, h * 128:(h + 1) * 128],
                       sz[:, h * 128:(h + 1) * 128], sy[:, h * 128:(h + 1) * 128])
                t = p.tick("tensor"); p.wait("vector", t)
                COPY(p, ny, psE)
                t = p.tick("vector"); p.wait("tensor", t)
            for h in range(16):
                if h and h % _nw == 0:
                    tt = p.tick("tensor"); p.wait("tensor", tt)
                MM(p, psE[:, h * 128:(h + 1) * 128],
                   sy[:, h * 128:(h + 1) * 128], sz[:, h * 128:(h + 1) * 128])
            t = p.tick("tensor"); p.wait("vector", t)
            COPY(p, nz, psE)
            t = p.tick("vector"); p.wait("tensor", t)
        p.wait("tensor", tApply)
        for h in range(16):
            MM(p, psB[:, h * 64:(h + 1) * 64],
               s5[:, h * 128:(h + 1) * 128], wvt[:, h * 64:(h + 1) * 64])
        t = p.tick("tensor"); p.wait("vector", t)
        TT(p, wvt, wvt, psB, ALU.add)
        tU = p.tick("vector")

        SF("solve")
        if nstage <= 6:
            break
        p.wait("tensor", tU)

        def emit_O(h):
            po = psE[64 * (h % 2):64 * (h % 2) + 64,
                     (h // 2) * 128:(h // 2) * 128 + 128]
            MM(p, po, wvt[:, h * 64:(h + 1) * 64],
               sAt[:, h * 128:(h + 1) * 128], start=True, stop=False)
            MM(p, po, hsl(stb, h, 64), hsl(qplus, h), start=False, stop=True)
        t = wave16(p, emit_O)
        p.wait("vector", t)
        COPY(p, outsv[:, :, c * 128:(c + 1) * 128],
             psE[:, 0:1024].rearrange("p (r t) -> p r t", t=128))
        tO = p.tick("vector")

        SF("O")
        if nstage <= 7:
            break
        p.wait("tensor", tO)
        for pr in range(8):
            MM(p, psB[:, pr * 128:(pr + 1) * 128],
               sbc["c_selpairB"][:, pr * 128:(pr + 1) * 128], dslb[:, :])
        t = p.tick("tensor"); p.wait("vector", t)
        TT(p, kdec2dm, kcb, psB, ALU.mult)
        t = p.tick("vector"); p.wait("tensor", t)
        t = wave16(p, lambda h: MM(p, psE[:, h * 64:(h + 1) * 64], hsl(kdec2dm, h),
                                   sbc["c_id64b"][64 * (h % 2):64 * (h % 2) + 64, :]))
        p.wait("vector", t)
        COPY(p, kdec2tm[:, :], psE[:, 0:1024])
        t = p.tick("vector"); p.wait("tensor", t)
        wave16(p, lambda h: MM(
            p, psSt[64 * (h % 2):64 * (h % 2) + 64, (h // 2) * 64:(h // 2) * 64 + 64],
            kdec2tm[:, h * 64:(h + 1) * 64], wvt[:, h * 64:(h + 1) * 64]))
        for pr in range(8):
            MM(p, psG[:, pr:pr + 1],
               sbc["c_selpairB"][:, pr * 128:(pr + 1) * 128], gslb[:, 127:128])
        t = p.tick("tensor"); p.wait("vector", t)
        COPY(p, gccol[:, :], psG[:, 0:8])
        t = p.tick("vector"); p.wait("vector", t)
        for pr in range(8):
            STT(p, stf[:, pr * 64:(pr + 1) * 64], stf[:, pr * 64:(pr + 1) * 64],
                gccol[:, pr:pr + 1], psSt[:, pr * 64:(pr + 1) * 64],
                ALU.mult, ALU.add)
        t = p.tick("vector"); p.wait("vector", t)
        COPY(p, stb[:, :], stf[:, :])
        tSt = p.tick("vector")
        p.wait("tensor", tSt)
        import os as _os2
        if _os2.environ.get("FENCE_CHUNK"):
            p.fence("tensor", "vector", "scalar")
        for j in range(3):
            use_tok[j][par] = tSt
    p.fence()
    if stop == "P2":
        _dump_stop(p, nc, {"outs": outs, "stf": stf})
        return nc

    out_toks = []
    for hp in range(8):
        par = hp % 2
        op = outs[:, hp * 2048:(hp + 1) * 2048]
        ACT(p, sqgn[par], op, AF.Square)
        t = p.tick("scalar"); p.wait("tensor", t)
        for cg in range(4):
            MM(p, psF[0:2, cg * 512:(cg + 1) * 512], sbc["c_bo2"][:, :],
               op[:, cg * 512:(cg + 1) * 512])
            MM(p, psE[0:2, cg * 512:(cg + 1) * 512], sbc["c_bo2"][:, :],
               sqgn[par][:, cg * 512:(cg + 1) * 512])
        t = p.tick("tensor"); p.wait("vector", t)
        REDUCE(p, statm[0:2, hp:hp + 1], psF[0:2, :], ALU.add)
        REDUCE(p, statq[0:2, hp:hp + 1], psE[0:2, :], ALU.add)
        t = p.tick("vector"); p.wait("tensor", t); p.wait("scalar", t)
    TS(p, statm, statm, 1.0 / 131072.0, ALU.mult)
    TS(p, statq, statq, 1.0 / 131072.0, ALU.mult)
    TT(p, statv, statm, statm, ALU.mult)
    TT(p, statv, statq, statv, ALU.subtract)
    t = p.tick("vector"); p.wait("scalar", t)
    ACT(p, statv, statv, AF.Ln, bias=cbias[0:2, 0:1])
    ACT(p, statv, statv, AF.Exp, scale=-0.5)
    t = p.tick("scalar"); p.wait("vector", t)
    COPY(p, statmb, statm)
    COPY(p, statrb, statv)
    t = p.tick("vector"); p.wait("tensor", t)
    for hp in range(8):
        MM(p, psG[:, 2 * hp:2 * hp + 1], sbc["c_bo2T"][:, :], statmb[:, hp:hp + 1])
        MM(p, psG[:, 2 * hp + 1:2 * hp + 2], sbc["c_bo2T"][:, :], statrb[:, hp:hp + 1])
    t = p.tick("tensor"); p.wait("vector", t)
    COPY(p, gcolA[:, :], psG[:, 0:16])
    t = p.tick("vector"); p.wait("vector", t); p.wait("tensor", t)
    wsrc_g = dram["a_wg"].rearrange("(c p) m -> p c m", p=128)
    evac_tok = [None, None]
    gtok = [None, None]
    for hp in range(8):
        par = hp % 2
        slot = hp % 4
        op = outs[:, hp * 2048:(hp + 1) * 2048]
        ws = wslot[slot]
        wsv = ws.rearrange("p (c m) -> p c m", m=128)
        if wslot_tok[slot] is not None:
            p.wait("gpsimd", wslot_tok[slot])
        tL = DMA(p, "gpsimd", wsv, wsrc_g[:, :, hp * 128:(hp + 1) * 128],
                 ch=f"wl{slot}")
        TS(p, op, op, gcolA[:, 2 * hp:2 * hp + 1], ALU.subtract,
           gcolA[:, 2 * hp + 1:2 * hp + 2], ALU.mult)
        TS(p, op, op, sbp["a_gns"][:, hp:hp + 1], ALU.mult,
           sbp["a_gnb"][:, hp:hp + 1], ALU.add)
        tGn = p.tick("vector")
        p.wait("tensor", tL)
        if evac_tok[par] is not None:
            p.wait("tensor", evac_tok[par])
        ps = psP[par]
        for cg in range(4):
            for k in range(8):
                MM(p, ps[:, cg * 512:(cg + 1) * 512], wsv[:, k],
                   x0v[:, k, cg * 512:(cg + 1) * 512], start=(k == 0), stop=(k == 7))
        tMM = p.tick("tensor")
        wslot_tok[slot] = tMM
        p.wait("scalar", tMM)
        if gtok[par] is not None:
            p.wait("scalar", gtok[par])
        ACT(p, gtmp[par], ps, AF.Silu, bias=sbp["a_bg"][:, hp:hp + 1])
        t = p.tick("scalar"); p.wait("vector", t, tGn)
        TT(p, op, op, gtmp[par], ALU.mult)
        tMul = p.tick("vector")
        evac_tok[par] = tMul
        gtok[par] = tMul
    p.fence()
    wsrc_o = dram["a_wo"].rearrange("(c p) m -> p c m", p=128)
    x0_tok = [None, None]
    zev_tok = None
    for m in range(8):
        par = m % 2
        slot = m % 4
        ws = wslot[slot]
        wsv = ws.rearrange("p (c m) -> p c m", m=128)
        if wslot_tok[slot] is not None:
            p.wait("gpsimd", wslot_tok[slot])
        tL = DMA(p, "gpsimd", wsv, wsrc_o[:, :, m * 128:(m + 1) * 128],
                 ch=f"wl{slot}")
        if x0_tok[par] is not None:
            p.wait("sync", x0_tok[par])
        tX = DMA(p, "sync", f_x0f[par], x0T[m * 128:(m + 1) * 128, :],
                 ch=f"xf{par}")
        p.wait("tensor", tL)
        if zev_tok is not None:
            p.wait("tensor", zev_tok)
        for cg in range(4):
            for k in range(8):
                MM(p, psE[:, cg * 512:(cg + 1) * 512], wsv[:, k],
                   outsv[:, k, cg * 512:(cg + 1) * 512], start=(k == 0), stop=(k == 7))
        tMM = p.tick("tensor")
        wslot_tok[slot] = tMM
        p.wait("vector", tMM, tX)
        STT(p, zbfv[:, m], psE, sbp["a_bo"][:, m:m + 1], f_x0f[par],
            ALU.add, ALU.add)
        zev_tok = p.tick("vector")
        x0_tok[par] = zev_tok
        p.wait("tensor", zev_tok)
        for cg in range(4):
            MM(p, psF[0:1, cg * 512:(cg + 1) * 512], sbc["c_ones128b"][:, :],
               zbfv[:, m, cg * 512:(cg + 1) * 512], start=(m == 0), stop=(m == 7))
    t = p.tick("tensor"); p.wait("vector", t)
    TS(p, f_murow, psF[0:1, :], 1.0 / 1024.0, ALU.mult)
    t = p.tick("vector"); p.wait("tensor", t); p.wait("scalar", t)
    for m in range(8):
        par = m % 2
        ACT(p, sqgn[par], zbfv[:, m], AF.Square)
        t = p.tick("scalar"); p.wait("tensor", t)
        for cg in range(4):
            MM(p, psF[0:1, cg * 512:(cg + 1) * 512], sbc["c_ones128b"][:, :],
               sqgn[par][:, cg * 512:(cg + 1) * 512], start=(m == 0), stop=(m == 7))
        if m < 7:
            t = p.tick("tensor"); p.wait("scalar", t)
    t = p.tick("tensor"); p.wait("vector", t)
    TS(p, f_mu2row, psF[0:1, :], 1.0 / 1024.0, ALU.mult)
    TT(p, f_rstdrow, f_murow, f_murow, ALU.mult)
    TT(p, f_mu2row, f_mu2row, f_rstdrow, ALU.subtract)
    t = p.tick("vector"); p.wait("scalar", t)
    ACT(p, f_mu2row, f_mu2row, AF.Ln, bias=cbias[0:1, 0:1])
    ACT(p, f_rstdrow, f_mu2row, AF.Exp, scale=-0.5)
    t = p.tick("scalar"); p.wait("vector", t)
    COPY(p, murb, f_murow)
    COPY(p, rsrb, f_rstdrow)
    t = p.tick("vector"); p.wait("tensor", t)
    for cg in range(4):
        MM(p, psE[:, cg * 512:(cg + 1) * 512], sbc["c_ones1b"][:, :],
           murb[0:1, cg * 512:(cg + 1) * 512])
    t = p.tick("tensor"); p.wait("vector", t)
    COPY(p, muBCb, psE)
    t = p.tick("vector"); p.wait("tensor", t)
    for cg in range(4):
        MM(p, psE[:, cg * 512:(cg + 1) * 512], sbc["c_ones1b"][:, :],
           rsrb[0:1, cg * 512:(cg + 1) * 512])
    t = p.tick("tensor"); p.wait("vector", t)
    COPY(p, rsBCb, psE)
    t = p.tick("vector"); p.wait("vector", t)
    xf_tok = [None, None]
    for m in range(8):
        par = m % 2
        if xf_tok[par] is not None:
            p.wait("vector", *xf_tok[par])
        TT(p, f_x0f[par], zbfv[:, m], muBCb, ALU.subtract)
        TT(p, f_x0f[par], f_x0f[par], rsBCb, ALU.mult)
        t = p.tick("vector"); p.wait("sync", t)
        tD = DMA(p, "sync", x1T[m * 128:(m + 1) * 128, :], f_x0f[par],
                 ch=f"xo{par}")
        out_toks.append(tD)
        p.wait("vector", t)
        COPY(p, zbfv[:, m], f_x0f[par])
        xf_tok[par] = (p.tick("vector"), tD)
    p.fence()
    for tb in range(16):
        for k in range(8):
            MM(p, psG[0:128, 0:16], zbfv[:, k, tb * 128:(tb + 1) * 128],
               wabv[:, 2, k], start=(k == 0), stop=False)
        MM(p, psG[0:128, 0:16], sbc["c_ones1b"][:, :], bggb[:, :],
           start=False, stop=True)
        t = p.tick("tensor"); p.wait("scalar", t)
        ACT(p, gate[:, tb * 16:(tb + 1) * 16], psG[0:128, 0:16], AF.Exp)
        t = p.tick("scalar"); p.wait("vector", t)
        gsl_ = gate[:, tb * 16:(tb + 1) * 16]
        REDUCE(p, scr16[:, 0:1], gsl_, ALU.add)
        t = p.tick("vector"); p.wait("vector", t)
        p.op("vector", lambda v, o=scr16[:, 1:2], i=scr16[:, 0:1]:
             v.reciprocal(out=o, in_=i))
        t = p.tick("vector"); p.wait("vector", t)
        TS(p, gsl_, gsl_, scr16[:, 1:2], ALU.mult)
        t = p.tick("vector"); p.wait("vector", t)
        p.op("vector", lambda v, o=m8t[:, :], i=gsl_: v.max(out=o, in_=i))
        t = p.tick("vector"); p.wait("vector", t)
        g2sl = gate2[:, tb * 16:(tb + 1) * 16]
        TS(p, g2sl, gsl_, m8t[:, 1:2], ALU.is_ge)
        t = p.tick("vector"); p.wait("vector", t)
        TT(p, g2sl, g2sl, gsl_, ALU.mult)
        t = p.tick("vector"); p.wait("tensor", t)
    t = p.tick("vector"); p.wait("sync", t)
    out_toks.append(DMA(p, "sync", maskedT, gate2, ch="mo"))
    p.wait_chan_totals("gpsimd", ["xo0", "xo1", "mo"])
    p.op("gpsimd", lambda g: g.engine_nop())
    p.tick("gpsimd")

    p.build()
    return nc



def _run_A(inputs):
    consts = _host_consts_A()
    wA = _layout_A_weights(inputs)
    ncA = build_A()
    x = np.asarray(inputs["L0_attn_x"], dtype=np.float32)
    in_maps = []
    for core in range(8):
        m = {"x0T": np.ascontiguousarray(x[core].T)}
        m.update(wA)
        m.update({k: v for k, v in consts.items() if k in
                  {nm for nm, _, _ in A_CONSTS}})
        in_maps.append(m)
    res = run_bass_kernel_spmd(ncA, in_maps, core_ids=list(range(8)))
    x1 = np.stack([np.ascontiguousarray(res.results[c]["x1T"].T)
                   for c in range(8)])
    masked = np.stack([
        np.ascontiguousarray(
            res.results[c]["maskedT"].reshape(128, 16, 16).transpose(2, 1, 0)
            .reshape(16, 2048, order="F"))
        for c in range(8)])
    masked = np.stack([
        res.results[c]["maskedT"].reshape(128, 16, 16).transpose(2, 1, 0).reshape(16, 16 * 128)
        for c in range(8)])
    return x1, masked


def _relayout_masked(mT):
    return np.ascontiguousarray(
        np.asarray(mT).reshape(128, 16, 16).transpose(2, 1, 0).reshape(16, 2048))


def kernel(**inputs):
    consts_A = _host_consts_A()
    wA = _layout_A_weights(inputs)
    ncA = build_A()
    x = np.asarray(inputs["L0_attn_x"], dtype=np.float32)
    namesA = {nm for nm, _, _ in A_CONSTS}
    in_maps = []
    for core in range(8):
        m = {"x0T": np.ascontiguousarray(x[core].T)}
        m.update(wA)
        m.update({k: v for k, v in consts_A.items() if k in namesA})
        in_maps.append(m)
    resA = run_bass_kernel_spmd(ncA, in_maps, core_ids=list(range(8)))
    masked = [_relayout_masked(resA.results[c]["maskedT"]) for c in range(8)]

    consts_B = _host_consts_B()
    wB = _layout_B_weights(inputs)
    ncB = build_B()
    namesB = {nm for nm, _, _ in B_CONSTS}
    in_mapsB = []
    for core in range(8):
        maskall = np.concatenate(
            [masked[core]] + [masked[j] for j in range(8) if j != core], axis=0)
        m = {"x1T": np.ascontiguousarray(np.asarray(resA.results[core]["x1T"])),
             "maskall": np.ascontiguousarray(maskall)}
        m.update(wB)
        m.update({k: v for k, v in consts_B.items() if k in namesB})
        in_mapsB.append(m)
    resB = run_bass_kernel_spmd(ncB, in_mapsB, core_ids=list(range(8)))
    out = np.stack([np.ascontiguousarray(np.asarray(resB.results[c]["yT"]).T)
                    for c in range(8)])
    return out



B_SMALL = (("b_b1L0", [128, 32]), ("b_gammaL0", [128, 8]), ("b_betaL0", [128, 8]),
           ("b_ln1g", [128, 8]), ("b_ln1b", [128, 8]), ("b_ln2g", [128, 8]),
           ("b_ln2b", [128, 8]), ("b_inpb", [128, 24]), ("b_outpb", [128, 8]),
           ("b_bgrp", [4, 1]), ("b_bexp", [8, 1]), ("b_b1L1", [128, 128]))
B_CONSTS = (("c_ones2d", [128, 128], BF16), ("c_ones2df", [128, 128], F32),
            ("c_sel16f", [128, 16], F32), ("c_esel16", [16, 2048], BF16),
            ("c_P4a", [4, 4], F32),
            ("c_P4b", [4, 4], F32), ("c_L4str", [4, 4], F32),
            ("c_Ppair8", [8, 8], F32), ("c_dup48", [4, 8], F32),
            ("c_even8", [8, 1], F32), ("c_bo2", [128, 2], BF16),
            ("c_bo2T", [2, 128], BF16))


def _host_consts_B():
    c = {}
    c["c_ones2d"] = np.ones((128, 128), BF)
    c["c_ones2df"] = np.ones((128, 128), np.float32)
    sel = np.zeros((128, 16), np.float32)
    for pp in range(128):
        sel[pp, pp % 16] = 1.0
    c["c_sel16f"] = sel
    es = np.zeros((16, 16 * 128), np.float32)
    for e in range(16):
        es[e, e * 128:(e + 1) * 128] = 1.0
    c["c_esel16"] = es.astype(BF)
    P4a = np.zeros((4, 4), np.float32)
    P4b = np.zeros((4, 4), np.float32)
    for r in range(4):
        P4a[r ^ 2, r] = 1.0
        P4b[r ^ 1, r] = 1.0
    c["c_P4a"] = P4a.T.copy()
    c["c_P4b"] = P4b.T.copy()
    L4 = np.zeros((4, 4), np.float32)
    for m in range(4):
        for g in range(m):
            L4[g, m] = 1.0
    c["c_L4str"] = L4
    P8 = np.zeros((8, 8), np.float32)
    for m in range(8):
        P8[m ^ 1, m] = 1.0
    c["c_Ppair8"] = P8
    d48 = np.zeros((4, 8), np.float32)
    for m in range(8):
        d48[m // 2, m] = 1.0
    c["c_dup48"] = d48
    ev = np.zeros((8, 1), np.float32)
    ev[0::2] = 1.0
    c["c_even8"] = ev
    bo = np.zeros((128, 2), np.float32)
    bo[0:64, 0] = 1.0
    bo[64:128, 1] = 1.0
    c["c_bo2"] = bo.astype(BF)
    c["c_bo2T"] = bo.T.astype(BF).copy()
    return c


def _layout_B_weights(w):
    d = {}
    d["b_w1L0"] = np.ascontiguousarray(w["L0_ffn_w1"])
    d["b_w2L0"] = np.ascontiguousarray(w["L0_ffn_w2"])
    d["b_b2L0"] = np.ascontiguousarray(w["L0_ffn_b2"])
    d["b_b1L0"] = np.ascontiguousarray(
        w["L0_ffn_b1"].reshape(16, 2, 128).transpose(2, 0, 1).reshape(128, 32))
    for nm, key in (("b_gammaL0", "L0_ffn_gamma"), ("b_betaL0", "L0_ffn_beta"),
                    ("b_ln1g", "L1_ffn_ln1_g"), ("b_ln1b", "L1_ffn_ln1_b"),
                    ("b_ln2g", "L1_ffn_ln2_g"), ("b_ln2b", "L1_ffn_ln2_b"),
                    ("b_outpb", "L1_ffn_out_proj_b")):
        d[nm] = np.ascontiguousarray(w[key].reshape(8, 128).T)
    d["b_wqkvT"] = np.ascontiguousarray(w["L1_attn_w_qkv"].T)
    d["b_woutT"] = np.ascontiguousarray(w["L1_attn_w_out"].T)
    d["b_winT"] = np.ascontiguousarray(w["L1_ffn_in_proj_w"].T)
    d["b_woutpT"] = np.ascontiguousarray(w["L1_ffn_out_proj_w"].T)
    d["b_inpb"] = np.ascontiguousarray(w["L1_ffn_in_proj_b"].reshape(24, 128).T)
    d["b_inpbrow"] = np.ascontiguousarray(w["L1_ffn_in_proj_b"][2048:3072].reshape(1, 1024))
    d["b_wgrp"] = np.ascontiguousarray(w["L1_ffn_w_group"])
    d["b_bgrp"] = np.ascontiguousarray(w["L1_ffn_b_group"].reshape(4, 1))
    d["b_wexp"] = np.ascontiguousarray(
        w["L1_ffn_w_expert"].transpose(1, 0, 2).reshape(1024, 8))
    d["b_bexp"] = np.ascontiguousarray(w["L1_ffn_b_expert"].reshape(8, 1))
    d["b_w1L1"] = np.ascontiguousarray(w["L1_ffn_w1"])
    d["b_b1L1"] = np.ascontiguousarray(
        w["L1_ffn_b1"].reshape(8, 16, 128).transpose(2, 0, 1).reshape(128, 128))
    d["b_w2L1"] = np.ascontiguousarray(w["L1_ffn_w2"])
    d["b_b2L1"] = np.ascontiguousarray(w["L1_ffn_b2"])
    return d


def build_B(stop=None):
    nc = bass.Bass("TRN2", target_bir_lowering=False)
    p = Prog(nc)

    x1T = nc.dram_tensor("x1T", [D, S], F32, kind="ExternalInput").ap()
    maskall = nc.dram_tensor("maskall", [128, S], F32, kind="ExternalInput").ap()
    yT = nc.dram_tensor("yT", [D, S], F32, kind="ExternalOutput").ap()
    x2sp = nc.dram_tensor("x2sp", [D, S], F32, kind="Internal").ap()
    x3sp = nc.dram_tensor("x3sp", [D, S], F32, kind="Internal").ap()
    x4sp = nc.dram_tensor("x4sp", [D, S], F32, kind="Internal").ap()

    dram = {}
    dram["b_inpbrow"] = nc.dram_tensor("b_inpbrow", [1, 1024], F32,
                                       kind="ExternalInput").ap()
    for nm, shp in (("b_w1L0", [16, 1024, 256]), ("b_w2L0", [16, 256, 1024]),
                    ("b_b2L0", [16, 1024]), ("b_wqkvT", [1024, 3072]),
                    ("b_woutT", [1024, 1024]), ("b_winT", [1024, 3072]),
                    ("b_woutpT", [1024, 1024]), ("b_wgrp", [1024, 4]),
                    ("b_wexp", [1024, 8]), ("b_w1L1", [8, 1024, 2048]),
                    ("b_w2L1", [8, 2048, 1024]), ("b_b2L1", [8, 1024])):
        dram[nm] = nc.dram_tensor(nm, shp, F32, kind="ExternalInput").ap()
    for nm, shp in B_SMALL:
        dram[nm] = nc.dram_tensor(nm, shp, F32, kind="ExternalInput").ap()
    for nm, shp, dt in B_CONSTS:
        dram[nm] = nc.dram_tensor(nm, shp, dt, kind="ExternalInput").ap()

    AL = nc.alloc_sbuf_tensor
    BB0 = AL("BB0", [128, 16384], BF16).ap()
    BB1 = AL("BB1", [128, 16384], BF16).ap()
    BB2 = AL("BB2", [128, 16384], BF16).ap()
    BB3 = AL("BB3", [128, 16384], BF16).ap()
    WS = AL("WS", [128, 8192], BF16).ap()
    FSB = AL("FSB", [128, 10240], F32).ap()
    RB = FSB[0:16, 0:2048]
    RB2 = FSB[0:16, 2048:4096]
    RB3 = FSB[0:16, 4096:6144]
    RB4 = FSB[0:16, 6144:8192]
    R4zone = (FSB[0:4, 2048:4096], FSB[0:4, 4096:6144], FSB[0:4, 6144:8192],
              FSB[0:4, 8192:10240])
    R8 = FSB[0:8, 2048:4096]
    R8b = FSB[0:8, 8192:10240]
    R8d = FSB[0:8, 0:2048]
    RBb = AL("RBb", [16, 2048], BF16).ap()
    muB = AL("muB", [128, 2048], BF16).ap()
    rsB = AL("rsB", [128, 2048], BF16).ap()
    rows1 = FSB[0:1, 6144:10240]
    nrow = AL("nrow", [16, 16], F32).ap()
    b2L0b = AL("b2L0b", [16, 1024], BF16).ap()
    b2L1b = AL("b2L1b", [8, 1024], BF16).ap()
    cbias = AL("cbiasB", [128, 2], F32).ap()
    sbc = {}
    for nm, shp, dt in B_CONSTS:
        sbc[nm] = AL(nm + "_s", shp, dt).ap()
    sbp = {}
    for nm, shp in B_SMALL:
        sbp[nm] = AL(nm + "_s", shp, F32).ap()

    psE = nc.alloc_psum_tensor("psE", [128, 2048], F32).ap()
    psF = nc.alloc_psum_tensor("psF", [128, 2048], F32).ap()

    f_slab = [FSB[:, 0:2048], FSB[:, 2048:4096]]
    f_scr = FSB[:, 4096:6144]
    f_scr2 = FSB[:, 6144:8192]
    Psb = WS[:, 0:4096]
    denb = WS[:, 4096:6144]
    vaug = WS[:, 6144:7184]
    wslot = [WS[:, i * 2048:(i + 1) * 2048] for i in range(4)]
    wtok = [None] * 4

    x1bv = BB0.rearrange("p (c t) -> p c t", t=2048)

    for nm, shp, dt in B_CONSTS:
        DMA(p, "sync", sbc[nm][:, :], dram[nm])
    for nm, shp in B_SMALL:
        DMA(p, "sync", sbp[nm][:, :], dram[nm])
    DMA(p, "gpsimd", BB0.rearrange("p (c t) -> p c t", t=2048),
        x1T.rearrange("(c p) t -> p c t", p=128), ch="inig")
    DMA(p, "gpsimd", b2L0b[:, :], dram["b_b2L0"], ch="inig")
    DMA(p, "gpsimd", b2L1b[:, :], dram["b_b2L1"], ch="inig")
    maskf = FSB[:, 4096:6144]
    DMA(p, "sync", maskf, maskall)
    p.fence()
    MEMSET(p, cbias[:, 0:1], 1e-5)
    MEMSET(p, cbias[:, 1:2], 1.0)
    MEMSET(p, BB3[:, :], 0.0)
    p.fence()

    for cg in range(4):
        MM(p, psE[0:16, cg * 512:(cg + 1) * 512], sbc["c_sel16f"][:, :],
           maskf[:, cg * 512:(cg + 1) * 512])
    t = p.tick("tensor"); p.wait("vector", t)
    TS(p, RB, psE[0:16, :], 1e-6, ALU.add)
    p.op("vector", lambda v, o=RB2, i=RB: v.reciprocal(out=o, in_=i))
    TT(p, RB, maskf[0:16, :], RB2, ALU.mult)
    TS(p, RB, RB, 8.0, ALU.mult)
    TS(p, RB2, RB, 0.0, ALU.is_gt)
    REDUCE(p, nrow[:, 0:1], RB2, ALU.add)
    TS(p, nrow[:, 1:2], nrow[:, 0:1], -1.0, ALU.mult, 1024.0, ALU.add)
    TS(p, RB3, RB2, -1.0, ALU.mult, 1.0, ALU.add)
    SCAN(p, RB4, RB3, RB3, 0.0, ALU.add, ALU.bypass)
    TS(p, RB4, RB4, nrow[:, 1:2], ALU.is_le)
    TT(p, RB4, RB4, RB3, ALU.mult)
    TT(p, RB4, RB4, RB2, ALU.add)
    COPY(p, RBb, RB4)
    p.fence()

    def stats_sum(ps_row, src, m):
        for cg in range(4):
            MM(p, ps_row[:, cg * 512:(cg + 1) * 512],
               sbc["c_ones2d"][:, 0:1], src[:, cg * 512:(cg + 1) * 512],
               start=(m == 0), stop=(m == 7))

    def stats_finalize(ps_row, murow, rstdrow, sq_src_chunks, sqbuf):
        t = p.tick("tensor"); p.wait("vector", t)
        TS(p, murow, ps_row[0:1, :], 1.0 / 1024.0, ALU.mult)
        t = p.tick("vector"); p.wait("scalar", t); p.wait("tensor", t)
        for m in range(8):
            ACT(p, sqbuf, sq_src_chunks(m), AF.Square)
            t = p.tick("scalar"); p.wait("tensor", t)
            stats_sum(ps_row, sqbuf, m)
            if m < 7:
                t = p.tick("tensor"); p.wait("scalar", t)
        t = p.tick("tensor"); p.wait("vector", t)
        TS(p, rstdrow, ps_row[0:1, :], 1.0 / 1024.0, ALU.mult)
        TT(p, f_scr[0:1, 0:2048], murow, murow, ALU.mult)
        TT(p, rstdrow, rstdrow, f_scr[0:1, 0:2048], ALU.subtract)
        t = p.tick("vector"); p.wait("scalar", t)
        ACT(p, rstdrow, rstdrow, AF.Ln, bias=cbias[0:1, 0:1])
        ACT(p, rstdrow, rstdrow, AF.Exp, scale=-0.5)

    def bcast_mu_rs(murow, rstdrow, murb_, rsrb_):
        t = p.tick("scalar"); p.wait("vector", t)
        COPY(p, murb_, murow)
        COPY(p, rsrb_, rstdrow)
        t = p.tick("vector"); p.wait("tensor", t)
        for cg in range(4):
            MM(p, psE[:, cg * 512:(cg + 1) * 512], sbc["c_ones2d"][0:1, :],
               murb_[0:1, cg * 512:(cg + 1) * 512])
        t = p.tick("tensor"); p.wait("vector", t)
        COPY(p, muB, psE)
        t = p.tick("vector"); p.wait("tensor", t)
        for cg in range(4):
            MM(p, psE[:, cg * 512:(cg + 1) * 512], sbc["c_ones2d"][0:1, :],
               rsrb_[0:1, cg * 512:(cg + 1) * 512])
        t = p.tick("tensor"); p.wait("vector", t)
        COPY(p, rsB, psE)
        t = p.tick("vector"); p.wait("tensor", t); p.wait("sync", t)

    f_murow = rows1[0:1, 0:2048]
    f_rsrow = rows1[0:1, 2048:4096]
    mu_b = muB[0:1, :]
    rs_b = rsB[0:1, :]

    accv = BB3.rearrange("p (c t) -> p c t", t=2048)
    h_sb = [BB1[:, 0:2048], BB1[:, 2048:4096]]
    evac_tok = None
    mask_tok = None
    for e in range(16):
        s1, s2 = (2 * e) % 4, (2 * e + 1) % 4
        if wtok[s1] is not None:
            p.wait("gpsimd", wtok[s1])
        tW1 = DMA(p, "gpsimd", wslot[s1].rearrange("p (c f) -> p c f", f=256),
                  dram["b_w1L0"][e].rearrange("(c p) f -> p c f", p=128),
                  ch=f"wl{s1}")
        if wtok[s2] is not None:
            p.wait("gpsimd", wtok[s2])
        tW2 = DMA(p, "gpsimd", wslot[s2].rearrange("p (c d) -> p c d", d=1024),
                  dram["b_w2L0"][e].rearrange("(c p) d -> p c d", p=128),
                  ch=f"wl{s2}")
        w1v = wslot[s1].rearrange("p (c f) -> p c f", f=256)
        w2v = wslot[s2].rearrange("p (c d) -> p c d", d=1024)
        p.wait("tensor", tW1)
        if evac_tok is not None:
            p.wait("tensor", evac_tok)
        for m in range(2):
            ps = psE if m == 0 else psF
            for cg in range(4):
                for k in range(8):
                    MM(p, ps[:, cg * 512:(cg + 1) * 512],
                       w1v[:, k, m * 128:(m + 1) * 128],
                       x1bv[:, k, cg * 512:(cg + 1) * 512],
                       start=(k == 0), stop=(k == 7))
        tMM = p.tick("tensor")
        wtok[s1] = tMM
        p.wait("scalar", tMM)
        if mask_tok is not None:
            p.wait("scalar", mask_tok)
        ACT(p, h_sb[0], psE, AF.Gelu, bias=sbp["b_b1L0"][:, 2 * e:2 * e + 1])
        ACT(p, h_sb[1], psF, AF.Gelu, bias=sbp["b_b1L0"][:, 2 * e + 1:2 * e + 2])
        tEv = p.tick("scalar")
        p.wait("tensor", tEv)
        for cg in range(4):
            MM(p, psE[:, cg * 512:(cg + 1) * 512],
               sbc["c_esel16"][:, e * 128:(e + 1) * 128],
               RBb[:, cg * 512:(cg + 1) * 512])
        t = p.tick("tensor"); p.wait("vector", t)
        TT(p, h_sb[0], h_sb[0], psE, ALU.mult)
        TT(p, h_sb[1], h_sb[1], psE, ALU.mult)
        mask_tok = p.tick("vector")
        p.wait("tensor", mask_tok, tW2)
        for m8 in range(8):
            for cg in range(4):
                pt = psF[:, cg * 512:(cg + 1) * 512]
                MM(p, pt, w2v[:, 0, m8 * 128:(m8 + 1) * 128],
                   h_sb[0][:, cg * 512:(cg + 1) * 512], start=True, stop=False)
                MM(p, pt, w2v[:, 1, m8 * 128:(m8 + 1) * 128],
                   h_sb[1][:, cg * 512:(cg + 1) * 512], start=False, stop=False)
                MM(p, pt, b2L0b[:, m8 * 128:(m8 + 1) * 128],
                   RBb[:, cg * 512:(cg + 1) * 512], start=False, stop=True)
            t = p.tick("tensor"); p.wait("vector", t)
            STT(p, accv[:, m8], psF, 1.0, accv[:, m8], ALU.bypass, ALU.add)
            t = p.tick("vector"); p.wait("tensor", t)
        wtok[s2] = p.last_tok["tensor"]
        evac_tok = p.last_tok["vector"]
    p.fence()
    if stop == "B2":
        _dump_stop(p, nc, {"acc": BB3})
        return nc

    zbv = BB1.rearrange("p (c t) -> p c t", t=2048)
    x2bv = BB2.rearrange("p (c t) -> p c t", t=2048)
    sl_tok = [None, None]
    for m in range(8):
        par = m % 2
        if sl_tok[par] is not None:
            p.wait("sync", sl_tok[par])
        tX = DMA(p, "sync", f_slab[par], x1T[m * 128:(m + 1) * 128, :], ch=f"xf{par}")
        p.wait("vector", tX)
        STT(p, zbv[:, m], f_slab[par], 1.0, accv[:, m], ALU.bypass, ALU.add)
        tZ = p.tick("vector")
        sl_tok[par] = tZ
        p.wait("tensor", tZ)
        stats_sum(psF[0:1], zbv[:, m], m)
    stats_finalize(psF[0:1], f_murow, f_rsrow, lambda m: zbv[:, m], BB2[:, 0:2048])
    bcast_mu_rs(f_murow, f_rsrow, mu_b, rs_b)
    sl_tok = [None, None]
    for m in range(8):
        par = m % 2
        if sl_tok[par] is not None:
            p.wait("sync", sl_tok[par])
        tX = DMA(p, "sync", f_slab[par], x1T[m * 128:(m + 1) * 128, :], ch=f"xf{par}")
        TT(p, f_scr, zbv[:, m], muB, ALU.subtract)
        TT(p, f_scr, f_scr, rsB, ALU.mult)
        TS(p, f_scr, f_scr, sbp["b_gammaL0"][:, m:m + 1], ALU.mult,
           sbp["b_betaL0"][:, m:m + 1], ALU.add)
        p.wait("vector", tX)
        TT(p, zbv[:, m], f_scr, f_slab[par], ALU.add)
        tU = p.tick("vector")
        sl_tok[par] = tU
        p.wait("tensor", tU)
        stats_sum(psF[0:1], zbv[:, m], m)
    stats_finalize(psF[0:1], f_murow, f_rsrow, lambda m: zbv[:, m], BB2[:, 0:2048])
    bcast_mu_rs(f_murow, f_rsrow, mu_b, rs_b)
    xs_tok = [None, None]
    for m in range(8):
        par = m % 2
        if xs_tok[par] is not None:
            p.wait("vector", *xs_tok[par])
        fs = f_slab[par]
        TT(p, fs, zbv[:, m], muB, ALU.subtract)
        TT(p, fs, fs, rsB, ALU.mult)
        t = p.tick("vector"); p.wait("sync", t)
        tD = DMA(p, "sync", x2sp[m * 128:(m + 1) * 128, :], fs, ch=f"xo{par}")
        p.wait("vector", t)
        COPY(p, x2bv[:, m], fs)
        xs_tok[par] = (p.tick("vector"), tD)
    p.fence()
    if stop == "B3":
        _dump_stop(p, nc, {"x2b": BB2})
        return nc

    qv = BB0.rearrange("p (c t) -> p c t", t=2048)
    ktm = BB1.rearrange("p (b f) -> p b f", f=1024)
    vtm = BB3.rearrange("p (b f) -> p b f", f=1024)

    def colblock_load(dst_slot, wsrc, c0, slot):
        if wtok[slot] is not None:
            p.wait("gpsimd", wtok[slot])
        return DMA(p, "gpsimd",
                   wslot[slot][:, 0:1024].rearrange("p (c m) -> p c m", m=128),
                   wsrc.rearrange("(c p) m -> p c m", p=128)[:, :, c0:c0 + 128],
                   ch=f"wl{slot}")

    ev_tok = None
    for m in range(8):
        slot = m % 4
        tL = colblock_load(wslot[slot], dram["b_wqkvT"], m * 128, slot)
        wv_ = wslot[slot].rearrange("p (c m) -> p c m", m=128)
        p.wait("tensor", tL)
        if ev_tok is not None:
            p.wait("tensor", ev_tok)
        ps = psE if m % 2 == 0 else psF
        for cg in range(4):
            for k in range(8):
                MM(p, ps[:, cg * 512:(cg + 1) * 512], wv_[:, k],
                   x2bv[:, k, cg * 512:(cg + 1) * 512], start=(k == 0), stop=(k == 7))
        tMM = p.tick("tensor")
        wtok[slot] = tMM
        p.wait("scalar", tMM)
        ACT(p, f_scr, ps, AF.Relu)
        t = p.tick("scalar"); p.wait("vector", t)
        TS(p, f_scr2, ps, 0.0, ALU.min)
        t = p.tick("vector"); p.wait("scalar", t)
        ACT(p, f_scr2, f_scr2, AF.Exp)
        t = p.tick("scalar"); p.wait("vector", t)
        TT(p, qv[:, m], f_scr, f_scr2, ALU.add)
        ev_tok = p.tick("vector")
    wpair = [WS[:, 0:4096].rearrange("p (c m) -> p c m", m=512),
             WS[:, 4096:8192].rearrange("p (c m) -> p c m", m=512)]
    for j, (base, dst) in enumerate(((1024, ktm), (2048, vtm))):
        ptoks = []
        for fg in range(2):
            for sl_ in ((0, 1) if fg == 0 else (2, 3)):
                if wtok[sl_] is not None:
                    p.wait("gpsimd", wtok[sl_])
            ptoks.append(DMA(p, "gpsimd", wpair[fg],
                             dram["b_wqkvT"].rearrange("(c p) m -> p c m", p=128)
                             [:, :, base + fg * 512:base + (fg + 1) * 512],
                             ch=f"wl{fg}"))
        p.wait("tensor", *ptoks)
        if ev_tok is not None:
            p.wait("tensor", ev_tok)
        par_tok = [None, None]
        for tb in range(16):
            ps = psE if tb % 2 == 0 else psF
            if par_tok[tb % 2] is not None:
                p.wait("tensor", par_tok[tb % 2])
            for fg in range(2):
                wv_ = wpair[fg]
                for k in range(8):
                    MM(p, ps[:, fg * 512:(fg + 1) * 512],
                       x2bv[:, k, tb * 128:(tb + 1) * 128], wv_[:, k, 0:512],
                       start=(k == 0), stop=(k == 7))
            tMM = p.tick("tensor")
            p.wait("scalar", tMM)
            if j == 0:
                if tb > 0:
                    p.wait("scalar", ev_tok)
                ACT(p, f_scr[:, 0:1024], ps[:, 0:1024], AF.Relu)
                t = p.tick("scalar"); p.wait("vector", t)
                TS(p, f_scr2[:, 0:1024], ps[:, 0:1024], 0.0, ALU.min)
                t = p.tick("vector"); p.wait("scalar", t)
                ACT(p, f_scr2[:, 0:1024], f_scr2[:, 0:1024], AF.Exp)
                t = p.tick("scalar"); p.wait("vector", t)
                TT(p, dst[:, tb], f_scr[:, 0:1024], f_scr2[:, 0:1024], ALU.add)
                ev_tok = p.tick("vector")
                par_tok[tb % 2] = ev_tok
            else:
                ACT(p, dst[:, tb], ps[:, 0:1024], AF.Identity)
                ev_tok = p.tick("scalar")
                par_tok[tb % 2] = ev_tok
        for sl_ in range(4):
            wtok[sl_] = p.last_tok["tensor"]
    p.fence()
    kvps = psE[:, 0:512]
    ksps = psE[:, 512:520]
    for par0 in (0, 1):
        for h in range(par0, 16, 2):
            for tb in range(16):
                MM(p, kvps[64 * (h % 2):64 * (h % 2) + 64, (h // 2) * 64:(h // 2) * 64 + 64],
                   ktm[:, tb, h * 64:(h + 1) * 64], vtm[:, tb, h * 64:(h + 1) * 64],
                   start=(tb == 0), stop=(tb == 15))
            for tb in range(16):
                MM(p, ksps[64 * (h % 2):64 * (h % 2) + 64, (h // 2):(h // 2) + 1],
                   ktm[:, tb, h * 64:(h + 1) * 64], sbc["c_ones2d"][:, 0:1],
                   start=(tb == 0), stop=(tb == 15))
        t = p.tick("tensor")
        p.wait("tensor", t)
    t = p.last_tok["tensor"]; p.wait("vector", t)
    kvsb = WS[:, 0:512]
    ksumf = FSB[:, 8200:8208]
    for sl_ in range(4):
        wtok[sl_] = p.last_tok["tensor"]
    COPY(p, kvsb[:, :], kvps)
    COPY(p, ksumf[:, :], ksps)
    t = p.tick("vector"); p.wait("tensor", t)
    out_lin = BB3
    olv = BB3.rearrange("p (c t) -> p c t", t=2048)
    for par0 in (0, 1):
        for h in range(par0, 16, 2):
            for cg in range(4):
                MM(p, psF[64 * (h % 2):64 * (h % 2) + 64, cg * 512:(cg + 1) * 512],
                   kvsb[64 * (h % 2):64 * (h % 2) + 64, (h // 2) * 64:(h // 2) * 64 + 64],
                   qv[64 * (h % 2):64 * (h % 2) + 64, h // 2, cg * 512:(cg + 1) * 512])
            t = p.tick("tensor"); p.wait("vector", t)
            COPY(p, olv[64 * (h % 2):64 * (h % 2) + 64, h // 2, :],
                 psF[64 * (h % 2):64 * (h % 2) + 64, :])
            t = p.tick("vector"); p.wait("tensor", t)
    for pr in range(8):
        TS(p, denb, qv[:, pr, :], ksumf[:, pr:pr + 1], ALU.mult)
        t = p.tick("vector"); p.wait("tensor", t)
        for cg in range(4):
            MM(p, psE[0:2, cg * 512:(cg + 1) * 512], sbc["c_bo2"][:, :],
               denb[:, cg * 512:(cg + 1) * 512])
        t = p.tick("tensor"); p.wait("vector", t)
        TS(p, f_scr2[0:2, 0:2048], psE[0:2, :], 1e-6, ALU.max)
        p.op("vector", lambda v, o=f_scr[0:2, 0:2048], i=f_scr2[0:2, 0:2048]:
             v.reciprocal(out=o, in_=i))
        COPY(p, denb[0:2, 0:2048], f_scr[0:2, 0:2048])
        t = p.tick("vector"); p.wait("tensor", t)
        for cg in range(4):
            MM(p, psE[:, cg * 512:(cg + 1) * 512], sbc["c_bo2T"][:, :],
               denb[0:2, cg * 512:(cg + 1) * 512])
        t = p.tick("tensor"); p.wait("vector", t)
        TT(p, olv[:, pr, :], olv[:, pr, :], psE, ALU.mult)
        t = p.tick("vector"); p.wait("tensor", t)
    p.fence()
    if stop == "B4a":
        _dump_stop(p, nc, {"olin": BB3})
        return nc

    def proj_resid_ln(srcv, wsrcT, c0, resid_sp, zb_t, out_sp, outb_t,
                      bias_col=None, gamma=None, beta=None, sq_scr=None):
        zv = zb_t.rearrange("p (c t) -> p c t", t=2048)
        ov = outb_t.rearrange("p (c t) -> p c t", t=2048)
        ev = None
        stok = [None, None]
        for m in range(8):
            slot = m % 4
            tL = colblock_load(wslot[slot], wsrcT, c0 + m * 128, slot)
            wv_ = wslot[slot].rearrange("p (c m) -> p c m", m=128)
            p.wait("tensor", tL)
            if ev is not None:
                p.wait("tensor", ev)
            ps = psE if m % 2 == 0 else psF[:, 0:2048]
            for cg in range(4):
                for k in range(8):
                    MM(p, ps[:, cg * 512:(cg + 1) * 512], wv_[:, k],
                       srcv[:, k, cg * 512:(cg + 1) * 512],
                       start=(k == 0), stop=(k == 7))
            tMM = p.tick("tensor")
            wtok[slot] = tMM
            par = m % 2
            if stok[par] is not None:
                p.wait("sync", stok[par])
            tX = DMA(p, "sync", f_slab[par], resid_sp[m * 128:(m + 1) * 128, :],
                     ch=f"xf{par}")
            p.wait("vector", tMM, tX)
            if bias_col is not None:
                STT(p, zv[:, m], ps, bias_col[:, m:m + 1], f_slab[par],
                    ALU.add, ALU.add)
            else:
                STT(p, zv[:, m], ps, 0.0, f_slab[par], ALU.add, ALU.add)
            ev = p.tick("vector")
            stok[par] = ev
        p.wait("tensor", ev)
        for m in range(8):
            stats_sum(psF[0:1], zv[:, m], m)
        stats_finalize(psF[0:1], f_murow, f_rsrow, lambda m: zv[:, m], sq_scr)
        bcast_mu_rs(f_murow, f_rsrow, mu_b, rs_b)
        xs = [None, None]
        for m in range(8):
            par = m % 2
            if xs[par] is not None:
                p.wait("vector", *xs[par])
            fs = f_slab[par]
            TT(p, fs, zv[:, m], muB, ALU.subtract)
            TT(p, fs, fs, rsB, ALU.mult)
            if gamma is not None:
                TS(p, fs, fs, gamma[:, m:m + 1], ALU.mult, beta[:, m:m + 1], ALU.add)
            t = p.tick("vector")
            tD = None
            if out_sp is not None:
                p.wait("sync", t)
                tD = DMA(p, "sync", out_sp[m * 128:(m + 1) * 128, :], fs,
                         ch=f"xo{par}")
            p.wait("vector", t)
            COPY(p, ov[:, m], fs)
            xs[par] = (p.tick("vector"),) + ((tD,) if tD else ())
        p.fence()

    proj_resid_ln(olv, dram["b_woutT"], 0, x2sp, BB0, x3sp, BB0,
                  sq_scr=BB1[:, 0:2048])
    if stop == "B4":
        _dump_stop(p, nc, {"x3b": BB0})
        return nc

    x3bv = BB0.rearrange("p (c t) -> p c t", t=2048)
    qmv = BB1.rearrange("p (c t) -> p c t", t=2048)
    kmv = BB2.rearrange("p (c t) -> p c t", t=2048)
    vtm2 = BB3.rearrange("p (b f) -> p b f", f=1024)
    inpbf32 = FSB[0:1, 9216:10240]
    tBV = DMA(p, "sync", inpbf32, dram["b_inpbrow"], ch="xf0")
    p.wait("tensor", tBV)
    for j, dstv in ((0, qmv), (1, kmv)):
        ev = None
        for m in range(8):
            slot = m % 4
            tL = colblock_load(wslot[slot], dram["b_winT"], j * 1024 + m * 128, slot)
            wv_ = wslot[slot].rearrange("p (c m) -> p c m", m=128)
            p.wait("tensor", tL)
            if ev is not None:
                p.wait("tensor", ev)
            ps = psE if m % 2 == 0 else psF[:, 0:2048]
            for cg in range(4):
                for k in range(8):
                    MM(p, ps[:, cg * 512:(cg + 1) * 512], wv_[:, k],
                       x3bv[:, k, cg * 512:(cg + 1) * 512],
                       start=(k == 0), stop=(k == 7))
            tMM = p.tick("tensor")
            wtok[slot] = tMM
            p.wait("scalar", tMM)
            ACT(p, dstv[:, m], ps, AF.Identity,
                bias=sbp["b_inpb"][:, j * 8 + m:j * 8 + m + 1])
            ev = p.tick("scalar")
    ptoks = []
    for fg in range(2):
        for sl_ in ((0, 1) if fg == 0 else (2, 3)):
            if wtok[sl_] is not None:
                p.wait("gpsimd", wtok[sl_])
        ptoks.append(DMA(p, "gpsimd", wpair[fg],
                         dram["b_winT"].rearrange("(c p) m -> p c m", p=128)
                         [:, :, 2048 + fg * 512:2048 + (fg + 1) * 512],
                         ch=f"wl{fg}"))
    p.wait("tensor", *ptoks)
    if ev is not None:
        p.wait("tensor", ev)
    ev = None
    par_tok = [None, None]
    for tb in range(16):
        ps = psE if tb % 2 == 0 else psF[:, 0:2048]
        if par_tok[tb % 2] is not None:
            p.wait("tensor", par_tok[tb % 2])
        for fg in range(2):
            wv_ = wpair[fg]
            for k in range(8):
                MM(p, ps[:, fg * 512:(fg + 1) * 512],
                   x3bv[:, k, tb * 128:(tb + 1) * 128], wv_[:, k, 0:512],
                   start=(k == 0), stop=False)
            MM(p, ps[:, fg * 512:(fg + 1) * 512], sbc["c_ones2df"][0:1, 0:128],
               inpbf32[0:1, fg * 512:(fg + 1) * 512],
               start=False, stop=True)
        tMM = p.tick("tensor")
        p.wait("scalar", tMM)
        ACT(p, vtm2[:, tb], ps[:, 0:1024], AF.Identity)
        par_tok[tb % 2] = p.tick("scalar")
    for sl_ in range(4):
        wtok[sl_] = p.last_tok["tensor"]
    p.fence()
    if stop == "B5a":
        _dump_stop(p, nc, {"qm": BB1, "km": BB2, "vm": BB3})
        return nc
    mhav = BB0.rearrange("p (c t) -> p c t", t=2048)
    Psbv = [Psb[:, 0:2048], Psb[:, 2048:4096]]
    for h in range(16):
        hb = 64 * (h % 2)
        COPY(p, vaug.rearrange("p (b f) -> p b f", f=65)[:, :, 0:64],
             vtm2[:, :, h * 64:(h + 1) * 64])
        MEMSET(p, vaug.rearrange("p (b f) -> p b f", f=65)[:, :, 64:65], 1.0)
        tV = p.tick("vector")
        p.wait("tensor", tV)
        exp_tok = None
        for t2b in range(16):
            for cg in range(4):
                MM(p, psE[:, cg * 512:(cg + 1) * 512],
                   kmv[hb:hb + 64, h // 2, t2b * 128:(t2b + 1) * 128],
                   qmv[hb:hb + 64, h // 2, cg * 512:(cg + 1) * 512])
            tS = p.tick("tensor")
            p.wait("scalar", tS)
            ACT(p, Psbv[t2b % 2], psE, AF.Exp, scale=0.125)
            tE = p.tick("scalar")
            p.wait("tensor", tE)
            for cg in range(4):
                MM(p, psF[0:65, cg * 512:(cg + 1) * 512],
                   vaug[:, t2b * 65:(t2b + 1) * 65],
                   Psbv[t2b % 2][:, cg * 512:(cg + 1) * 512],
                   start=(t2b == 0), stop=(t2b == 15))
        t = p.tick("tensor"); p.wait("scalar", t); p.wait("vector", t)
        ACT(p, mhav[hb:hb + 64, h // 2, :], psF[0:64, :], AF.Identity)
        tO = p.tick("scalar")
        TS(p, f_scr[64:65, 0:2048], psF[64:65, :], 1e-38, ALU.max)
        p.op("vector", lambda v, o=f_scr2[64:65, 0:2048], i=f_scr[64:65, 0:2048]:
             v.reciprocal(out=o, in_=i))
        COPY(p, denb[64:65, 0:2048], f_scr2[64:65, 0:2048])
        t = p.tick("vector"); p.wait("tensor", t, tO)
        for cg in range(4):
            MM(p, psE[hb:hb + 64, cg * 512:(cg + 1) * 512],
               sbc["c_ones2d"][64:65, 0:64],
               denb[64:65, cg * 512:(cg + 1) * 512])
        t = p.tick("tensor"); p.wait("vector", t)
        TT(p, mhav[hb:hb + 64, h // 2, :], mhav[hb:hb + 64, h // 2, :],
           psE[hb:hb + 64, :], ALU.mult)
        t = p.tick("vector"); p.wait("tensor", t); p.wait("scalar", t)
    p.fence()
    for sl_ in range(4):
        wtok[sl_] = p.last_tok["tensor"]
    if stop == "B5b":
        _dump_stop(p, nc, {"mha": BB0})
        return nc
    proj_resid_ln(mhav, dram["b_woutpT"], 0, x3sp, BB1, x4sp, BB1,
                  bias_col=sbp["b_outpb"], gamma=sbp["b_ln1g"], beta=sbp["b_ln1b"],
                  sq_scr=BB2[:, 0:2048])
    if stop == "B5":
        _dump_stop(p, nc, {"x4b": BB1})
        return nc

    x4bv = BB1.rearrange("p (c t) -> p c t", t=2048)
    wgrps = AL("wgrps", [128, 4 * 8 + 8 * 8], BF16).ap()
    DMA(p, "gpsimd", wgrps.rearrange("p (c e) -> p c e", e=12)[:, :, 0:4],
        dram["b_wgrp"].rearrange("(c p) e -> p c e", p=128), ch="inig")
    DMA(p, "gpsimd", wgrps.rearrange("p (c e) -> p c e", e=12)[:, :, 4:12],
        dram["b_wexp"].rearrange("(c p) e -> p c e", p=128), ch="inig")
    p.fence()
    wgv = wgrps.rearrange("p (c e) -> p c e", e=12)
    for cg in range(4):
        for k in range(8):
            MM(p, psE[0:4, cg * 512:(cg + 1) * 512], wgv[:, k, 0:4],
               x4bv[:, k, cg * 512:(cg + 1) * 512], start=(k == 0), stop=(k == 7))
        for k in range(8):
            MM(p, psF[0:8, cg * 512:(cg + 1) * 512], wgv[:, k, 4:12],
               x4bv[:, k, cg * 512:(cg + 1) * 512], start=(k == 0), stop=(k == 7))
    t = p.tick("tensor"); p.wait("scalar", t); p.wait("vector", t)
    R4a, R4b, R4c, R4d = R4zone
    ACT(p, R4a, psE[0:4, :], AF.Exp, bias=sbp["b_bgrp"][:, 0:1])
    t = p.tick("scalar"); p.wait("vector", t); p.wait("tensor", t)
    for st_, (pm, dst, src) in enumerate(((("c_P4a"), R4b, R4a), (("c_P4b"), R4c, R4b))):
        for cg in range(4):
            MM(p, psE[0:4, cg * 512:(cg + 1) * 512], sbc[pm][:, :],
               src[:, cg * 512:(cg + 1) * 512])
        t = p.tick("tensor"); p.wait("vector", t)
        TT(p, dst, src, psE[0:4, :], ALU.add)
        t = p.tick("vector"); p.wait("tensor", t)
    p.op("vector", lambda v, o=R4d, i=R4c: v.reciprocal(out=o, in_=i))
    TT(p, R4a, R4a, R4d, ALU.mult)
    t = p.tick("vector"); p.wait("tensor", t)
    for st_, (pm, dst, src) in enumerate(((("c_P4a"), R4b, R4a), (("c_P4b"), R4c, R4b))):
        for cg in range(4):
            MM(p, psE[0:4, cg * 512:(cg + 1) * 512], sbc[pm][:, :],
               src[:, cg * 512:(cg + 1) * 512])
        t = p.tick("tensor"); p.wait("vector", t)
        TT(p, dst, src, psE[0:4, :], ALU.max)
        t = p.tick("vector"); p.wait("tensor", t)
    TT(p, R4b, R4a, R4c, ALU.is_equal)
    t = p.tick("vector"); p.wait("tensor", t)
    for cg in range(4):
        MM(p, psE[0:4, cg * 512:(cg + 1) * 512], sbc["c_L4str"][:, :],
           R4b[:, cg * 512:(cg + 1) * 512])
    t = p.tick("tensor"); p.wait("vector", t)
    TS(p, R4d, psE[0:4, :], 0.0, ALU.is_equal)
    TT(p, R4b, R4b, R4d, ALU.mult)
    t = p.tick("vector"); p.wait("tensor", t); p.wait("scalar", t)
    ACT(p, R8, psF[0:8, :], AF.Exp, bias=sbp["b_bexp"][:, 0:1])
    t = p.tick("scalar"); p.wait("tensor", t); p.wait("vector", t)
    for cg in range(4):
        MM(p, psE[0:8, cg * 512:(cg + 1) * 512], sbc["c_Ppair8"][:, :],
           R8[:, cg * 512:(cg + 1) * 512])
    t = p.tick("tensor"); p.wait("vector", t)
    TT(p, R8b, R8, psE[0:8, :], ALU.add)
    p.op("vector", lambda v, o=R8d, i=R8b: v.reciprocal(out=o, in_=i))
    TT(p, R8, R8, R8d, ALU.mult)
    t = p.tick("vector"); p.wait("tensor", t)
    for cg in range(4):
        MM(p, psE[0:8, cg * 512:(cg + 1) * 512], sbc["c_dup48"][:, :],
           R4b[:, cg * 512:(cg + 1) * 512])
    t = p.tick("tensor"); p.wait("vector", t)
    TT(p, R8b, R8, psE[0:8, :], ALU.mult)
    t = p.tick("vector"); p.wait("tensor", t)
    for cg in range(4):
        MM(p, psE[0:8, cg * 512:(cg + 1) * 512], sbc["c_Ppair8"][:, :],
           R8b[:, cg * 512:(cg + 1) * 512])
    t = p.tick("tensor"); p.wait("vector", t)
    TT(p, R8d, R8b, psE[0:8, :], ALU.is_gt)
    TT(p, R8, R8b, psE[0:8, :], ALU.is_equal)
    TS(p, R8, R8, sbc["c_even8"][:, 0:1], ALU.mult)
    TT(p, R8d, R8d, R8, ALU.add)
    t = p.tick("vector"); p.wait("tensor", t)
    for cg in range(4):
        MM(p, psF[0:8, cg * 512:(cg + 1) * 512], sbc["c_dup48"][:, :],
           R4b[:, cg * 512:(cg + 1) * 512])
    t = p.tick("tensor"); p.wait("vector", t)
    TT(p, R8d, R8d, psF[0:8, :], ALU.mult)
    TT(p, R8, R8d, R8b, ALU.mult)
    t = p.tick("vector"); p.wait("tensor", t)
    for cg in range(4):
        MM(p, psE[0:1, cg * 512:(cg + 1) * 512], sbc["c_ones2df"][0:8, 0:1],
           R8[:, cg * 512:(cg + 1) * 512])
    t = p.tick("tensor"); p.wait("vector", t)
    TT(p, rows1[0:1, 0:2048], R4c[0:1, :], psE[0:1, :], ALU.mult)
    t = p.tick("vector"); p.wait("tensor", t)
    for cg in range(4):
        MM(p, psE[0:8, cg * 512:(cg + 1) * 512], sbc["c_ones2df"][0:1, 0:8],
           rows1[0:1, cg * 512:(cg + 1) * 512])
    t = p.tick("tensor"); p.wait("vector", t)
    TT(p, R8b, R8d, psE[0:8, :], ALU.mult)
    COPY(p, RBb[0:8, :], R8b)
    p.fence()
    if stop == "B6":
        _dump_stop(p, nc, {"cm": R8b, "hot": R8d})
        return nc

    hLv = [BB2.rearrange("p (c t) -> p c t", t=2048),
           BB3.rearrange("p (c t) -> p c t", t=2048)]
    acc1 = BB0
    acc1v = BB0.rearrange("p (c t) -> p c t", t=2048)
    MEMSET(p, acc1[:, :], 0.0)
    t = p.tick("vector"); p.wait("tensor", t)
    for e in range(8):
        ev = None
        for m in range(16):
            slot = m % 4
            tL = colblock_load(wslot[slot], dram["b_w1L1"][e], m * 128, slot)
            wv_ = wslot[slot].rearrange("p (c m) -> p c m", m=128)
            p.wait("tensor", tL)
            if ev is not None:
                p.wait("tensor", ev)
            ps = psE if m % 2 == 0 else psF
            for cg in range(4):
                for k in range(8):
                    MM(p, ps[:, cg * 512:(cg + 1) * 512], wv_[:, k],
                       x4bv[:, k, cg * 512:(cg + 1) * 512],
                       start=(k == 0), stop=(k == 7))
            tMM = p.tick("tensor")
            wtok[slot] = tMM
            p.wait("scalar", tMM)
            ACT(p, hLv[m // 8][:, m % 8], ps, AF.Relu,
                bias=sbp["b_b1L1"][:, e * 16 + m:e * 16 + m + 1])
            ev = p.tick("scalar")
        p.wait("tensor", ev)
        for cg in range(4):
            MM(p, psE[:, cg * 512:(cg + 1) * 512],
               sbc["c_esel16"][0:8, e * 128:(e + 1) * 128],
               RBb[0:8, cg * 512:(cg + 1) * 512])
        t = p.tick("tensor"); p.wait("vector", t)
        for m in range(16):
            TT(p, hLv[m // 8][:, m % 8], hLv[m // 8][:, m % 8], psE, ALU.mult)
        tMk = p.tick("vector")
        p.wait("tensor", tMk)
        for m8 in range(8):
            slot = m8 % 4
            if wtok[slot] is not None:
                p.wait("gpsimd", wtok[slot])
            tL = DMA(p, "gpsimd",
                     wslot[slot].rearrange("p (c m) -> p c m", m=128),
                     dram["b_w2L1"][e].rearrange("(c p) m -> p c m", p=128)
                     [:, :, m8 * 128:(m8 + 1) * 128], ch=f"wl{slot}")
            wv_ = wslot[slot].rearrange("p (c m) -> p c m", m=128)
            p.wait("tensor", tL)
            for cg in range(4):
                pt = psF[:, cg * 512:(cg + 1) * 512]
                for k in range(16):
                    MM(p, pt, wv_[:, k],
                       hLv[k // 8][:, k % 8, cg * 512:(cg + 1) * 512],
                       start=(k == 0), stop=False)
                MM(p, pt, b2L1b[:, m8 * 128:(m8 + 1) * 128],
                   RBb[0:8, cg * 512:(cg + 1) * 512], start=False, stop=True)
            t = p.tick("tensor")
            wtok[slot] = t
            p.wait("vector", t)
            STT(p, acc1v[:, m8], psF, 1.0, acc1v[:, m8], ALU.bypass, ALU.add)
            t = p.tick("vector"); p.wait("tensor", t)
    p.fence()
    if stop == "B7":
        _dump_stop(p, nc, {"acc1": BB0})
        return nc

    zbv2 = BB2.rearrange("p (c t) -> p c t", t=2048)
    sl_tok = [None, None]
    for m in range(8):
        par = m % 2
        if sl_tok[par] is not None:
            p.wait("sync", sl_tok[par])
        tX = DMA(p, "sync", f_slab[par], x4sp[m * 128:(m + 1) * 128, :], ch=f"xf{par}")
        p.wait("vector", tX)
        STT(p, zbv2[:, m], f_slab[par], 1.0, acc1v[:, m], ALU.bypass, ALU.add)
        tZ = p.tick("vector")
        sl_tok[par] = tZ
        p.wait("tensor", tZ)
        stats_sum(psF[0:1], zbv2[:, m], m)
    stats_finalize(psF[0:1], f_murow, f_rsrow, lambda m: zbv2[:, m], BB3[:, 0:2048])
    bcast_mu_rs(f_murow, f_rsrow, mu_b, rs_b)
    sl_tok = [None, None]
    for m in range(8):
        par = m % 2
        if sl_tok[par] is not None:
            p.wait("sync", sl_tok[par])
        tX = DMA(p, "sync", f_slab[par], x3sp[m * 128:(m + 1) * 128, :], ch=f"xf{par}")
        TT(p, f_scr, zbv2[:, m], muB, ALU.subtract)
        TT(p, f_scr, f_scr, rsB, ALU.mult)
        TS(p, f_scr, f_scr, sbp["b_ln2g"][:, m:m + 1], ALU.mult,
           sbp["b_ln2b"][:, m:m + 1], ALU.add)
        p.wait("vector", tX)
        TT(p, zbv2[:, m], f_scr, f_slab[par], ALU.add)
        tU = p.tick("vector")
        sl_tok[par] = tU
        p.wait("tensor", tU)
        stats_sum(psF[0:1], zbv2[:, m], m)
    stats_finalize(psF[0:1], f_murow, f_rsrow, lambda m: zbv2[:, m], BB3[:, 0:2048])
    bcast_mu_rs(f_murow, f_rsrow, mu_b, rs_b)
    xs_tok = [None, None]
    for m in range(8):
        par = m % 2
        if xs_tok[par] is not None:
            p.wait("vector", *xs_tok[par])
        fs = f_slab[par]
        TT(p, fs, zbv2[:, m], muB, ALU.subtract)
        TT(p, fs, fs, rsB, ALU.mult)
        t = p.tick("vector"); p.wait("sync", t)
        tD = DMA(p, "sync", yT[m * 128:(m + 1) * 128, :], fs, ch=f"yo{par}")
        xs_tok[par] = (p.tick("vector") if False else t, tD)
    p.wait_chan_totals("gpsimd", ["yo0", "yo1"])
    p.op("gpsimd", lambda g: g.engine_nop())
    p.tick("gpsimd")

    p.build()
    return nc


# version 03
# speedup vs baseline: 1.1208x; 1.1208x over previous
import numpy as np
import ml_dtypes
import concourse.bass as bass
import concourse.mybir as mybir
from concourse.bass_utils import run_bass_kernel_spmd

F32 = mybir.dt.float32
BF16 = mybir.dt.bfloat16
I32 = mybir.dt.int32
AF = mybir.ActivationFunctionType
ALU = mybir.AluOpType
AX = mybir.AxisListType

BF = ml_dtypes.bfloat16

S = 2048
D = 1024
H = 16
HD = 64
NP = 8
C = 128
NCH = S // C

ENGS = ("sync", "tensor", "vector", "scalar", "gpsimd")


class Prog:

    def __init__(self, nc):
        self.nc = nc
        self.q = {e: [] for e in ENGS}
        self.cnt = {e: 0 for e in ENGS}
        self.last_tok = {e: (e, 0) for e in ENGS}
        self._dirty = {e: False for e in ENGS}
        self._last_dma = {e: False for e in ENGS}
        self.chans = {}
        self._fence_seen = {}

    SELF_SERIAL = ("vector", "scalar")

    def op(self, eng, fn, dma=False):
        import traceback
        src = traceback.extract_stack(limit=4)
        if eng in self.SELF_SERIAL:
            if self.cnt[eng] > 0:
                self.q[eng].append(["wait", eng, self.cnt[eng]])
            self.q[eng].append(["op", fn, 0, src, eng])
            self._dirty[eng] = True
            self._last_dma[eng] = dma
            self.tick(eng)
            return
        self.q[eng].append(["op", fn, 0, src, eng])
        self._dirty[eng] = True
        self._last_dma[eng] = dma

    def tick(self, eng):
        ent = self.q[eng][-1]
        assert ent[0] == "op"
        inc = 16 if self._last_dma[eng] else 1
        if ent[2] == 0:
            ent[2] = inc
            self.cnt[eng] += inc
        self._dirty[eng] = False
        self.last_tok[eng] = (eng, self.cnt[eng])
        return self.last_tok[eng]

    def dma(self, eng, out, in_, ch):
        if ch not in self.chans:
            self.chans[ch] = 0
        self.chans[ch] += 16
        tok = ("#" + ch, self.chans[ch])
        self.q[eng].append(["op", lambda g, o=out, i=in_:
                            g.dma_start(out=o, in_=i), 16, None, "#" + ch])
        return tok

    def wait(self, eng, *toks):
        for (e2, v) in toks:
            if v > 0:
                self.q[eng].append(["wait", e2, v])

    def wait_chan_totals(self, eng, chans):
        for ch in chans:
            if self.chans.get(ch, 0) > 0:
                self.q[eng].append(["wait", "#" + ch, self.chans[ch]])

    def fence(self, *engs):
        engs = engs or ENGS
        toks = []
        for e in engs:
            if self._dirty[e]:
                toks.append(self.tick(e))
            elif self.cnt[e] > 0:
                toks.append(self.last_tok[e])
        for ch, cnt in self.chans.items():
            if cnt > 0 and self._fence_seen.get(ch) != cnt:
                toks.append(("#" + ch, cnt))
                self._fence_seen[ch] = cnt
        for e in engs:
            self.wait(e, *[t for t in toks if t[0] != e])
        return toks

    def build(self):
        nc = self.nc
        sems = {}
        ctxs = []
        for e in ENGS:
            cm = nc.semaphore(f"sem_{e}")
            sems[e] = cm.__enter__()
            ctxs.append(cm)
        for ch in self.chans:
            cm = nc.semaphore(f"ch_{ch}")
            sems["#" + ch] = cm.__enter__()
            ctxs.append(cm)
        blkcm = nc.Block()
        blk = blkcm.__enter__()
        for ename in ENGS:
            entries = self.q[ename]

            def body(eng, entries=entries, ename=ename):
                for ent in entries:
                    if ent[0] == "wait":
                        eng.wait_ge(sems[ent[1]], ent[2])
                    else:
                        try:
                            inst = ent[1](eng)
                        except Exception:
                            if ent[3] is not None:
                                print("FAILED OP emitted at:")
                                for fr in ent[3]:
                                    print(f"  {fr.filename}:{fr.lineno} {fr.line}")
                            raise
                        if ent[2]:
                            inst.then_inc(sems[ent[4]], ent[2])

            getattr(blk, ename)(body)
        blkcm.__exit__(None, None, None)
        for cm in ctxs:
            cm.__exit__(None, None, None)



def MM(p, out, lhsT, rhs, start=True, stop=True):
    p.op("tensor", lambda t, o=out, l=lhsT, r=rhs, a=start, b=stop:
         t.matmul(o, l, r, start=a, stop=b))


def DMA(p, eng, out, in_, ch="ini"):
    return p.dma(eng, out, in_, ch)


def ACT(p, out, in_, func, bias=0.0, scale=1.0):
    p.op("scalar", lambda s, o=out, i=in_, f=func, b=bias, sc=scale:
         s.activation(out=o, in_=i, func=f, bias=b, scale=sc))


def TT(p, out, in0, in1, alu, eng="vector"):
    p.op(eng, lambda v, o=out, a=in0, b=in1, op=alu:
         v.tensor_tensor(out=o, in0=a, in1=b, op=op))


def TS(p, out, in0, s1, op0, s2=None, op1=None, eng="vector"):
    if s2 is None:
        p.op(eng, lambda v, o=out, a=in0, x=s1, op=op0:
             v.tensor_single_scalar(out=o, in_=a, scalar=x, op=op))
    else:
        p.op(eng, lambda v, o=out, a=in0, x=s1, y=s2, oa=op0, ob=op1:
             v.tensor_scalar(out=o, in0=a, scalar1=x, scalar2=y, op0=oa, op1=ob))


def STT(p, out, in0, scal, in1, op0, op1, eng="vector"):
    p.op(eng, lambda v, o=out, a=in0, s=scal, b=in1, oa=op0, ob=op1:
         v.scalar_tensor_tensor(out=o, in0=a, scalar=s, in1=b, op0=oa, op1=ob))


def COPY(p, out, in_, eng="vector"):
    p.op(eng, lambda v, o=out, i=in_: v.tensor_copy(out=o, in_=i))


def SCAN(p, out, d0, d1, init, op0, op1):
    p.op("vector", lambda v, o=out, a=d0, b=d1, i=init, oa=op0, ob=op1:
         v.tensor_tensor_scan(out=o, data0=a, data1=b, initial=i, op0=oa, op1=ob))


def REDUCE(p, out, in_, alu, eng="vector"):
    p.op(eng, lambda v, o=out, i=in_, op=alu:
         v.tensor_reduce(out=o, in_=i, axis=AX.X, op=op))


def MEMSET(p, ap, val, eng="vector"):
    p.op(eng, lambda v, a=ap, x=val: v.memset(a, x))


def _dump_stop(p, nc, tiles):
    p.fence()
    for nm, ap in tiles.items():
        d = nc.dram_tensor("dbg_" + nm, list(ap.shape), ap.dtype,
                           kind="ExternalOutput").ap()
        DMA(p, "sync", d, ap, ch="dbg")
    p.wait_chan_totals("gpsimd", ["dbg"])
    p.op("gpsimd", lambda g: g.engine_nop())
    p.tick("gpsimd")
    p.build()



def _host_consts_A():
    c = {}
    c["c_id16f"] = np.vstack([np.eye(16)] * 3).astype(np.float32)
    c["c_id64b"] = np.vstack([np.eye(64), np.eye(64)]).astype(BF)
    c["c_id128b"] = np.eye(128, dtype=BF)
    c["c_ones1f"] = np.ones((1, 128), np.float32)
    c["c_ones1b"] = np.ones((1, 128), BF)
    c["c_ones128b"] = np.ones((128, 1), BF)
    t = np.arange(128)
    ls = (t[None, :] < t[:, None]).astype(np.float32)
    us = (t[None, :] > t[:, None]).astype(np.float32)
    ui = (t[None, :] >= t[:, None]).astype(np.float32)
    c["c_maskLsNegB"] = np.tile(-ls, (1, 16)).astype(BF)
    c["c_maskUsNegB"] = np.tile(-us, (1, 16)).astype(BF)
    c["c_maskUiB"] = np.tile(ui, (1, 16)).astype(BF)
    sel = np.zeros((16, 8 * 128), np.float32)
    for hp in range(8):
        sel[2 * hp, hp * 128: hp * 128 + 64] = 1.0
        sel[2 * hp + 1, hp * 128 + 64: hp * 128 + 128] = 1.0
    c["c_selpairB"] = sel.astype(BF)
    bo = np.zeros((128, 2), np.float32)
    bo[0:64, 0] = 1.0
    bo[64:128, 1] = 1.0
    c["c_bo2"] = bo.astype(BF)
    c["c_bo2T"] = bo.T.astype(BF).copy()
    bd = np.zeros((16, 1024), np.float32)
    for g in range(16):
        bd[g, g * 64:(g + 1) * 64] = 1.0
    c["c_BD16f"] = bd
    c["c_ones16_64f"] = np.ones((16, 64), np.float32)
    return c


def _layout_A_weights(w):
    d = {}
    d["a_wq"] = np.ascontiguousarray(w["L0_attn_Wq"])
    d["a_wk"] = np.ascontiguousarray(w["L0_attn_Wk"])
    d["a_wv"] = np.ascontiguousarray(w["L0_attn_Wv"])
    d["a_wg"] = np.ascontiguousarray(w["L0_attn_Wg"])
    d["a_wo"] = np.ascontiguousarray(w["L0_attn_Wo"])
    d["a_wa"] = np.ascontiguousarray(w["L0_attn_Wa"])
    d["a_wb"] = np.ascontiguousarray(w["L0_attn_Wb"])
    for nm, key in (("a_bq", "L0_attn_bq"), ("a_bk", "L0_attn_bk"),
                    ("a_bv", "L0_attn_bv"), ("a_bg", "L0_attn_bg"),
                    ("a_bo", "L0_attn_bo"), ("a_gns", "L0_attn_gn_scale"),
                    ("a_gnb", "L0_attn_gn_bias")):
        d[nm] = np.ascontiguousarray(w[key].reshape(8, 128).T)
    d["a_ba"] = np.ascontiguousarray(w["L0_attn_ba"].reshape(16, 1))
    d["a_bb"] = np.ascontiguousarray(w["L0_attn_bb"].reshape(16, 1))
    for nm, key in (("a_qc", "L0_attn_q_conv"), ("a_kc", "L0_attn_k_conv"),
                    ("a_vc", "L0_attn_v_conv")):
        d[nm] = np.ascontiguousarray(
            w[key].reshape(8, 128, 4).transpose(1, 0, 2).reshape(128, 32))
    d["a_wgg"] = np.ascontiguousarray(w["L0_ffn_wg"])
    d["a_bgg"] = np.ascontiguousarray(w["L0_ffn_bg"].reshape(1, 16))
    return d




def build_A(stop=None, nch=NCH, nstage=99):
    nc = bass.Bass("TRN2", target_bir_lowering=False)
    p = Prog(nc)

    x0T = nc.dram_tensor("x0T", [D, S], F32, kind="ExternalInput").ap()
    x1T = nc.dram_tensor("x1T", [D, S], F32, kind="ExternalOutput").ap()
    maskedT = nc.dram_tensor("maskedT", [128, 256], F32, kind="ExternalOutput").ap()
    spill = nc.dram_tensor("spillA", [3, NCH, 128, NP * C], BF16, kind="Internal").ap()

    dram = {}
    for nm in ("a_wq", "a_wk", "a_wv", "a_wg", "a_wo"):
        dram[nm] = nc.dram_tensor(nm, [D, D], F32, kind="ExternalInput").ap()
    for nm in ("a_wa", "a_wb", "a_wgg"):
        dram[nm] = nc.dram_tensor(nm, [D, 16], F32, kind="ExternalInput").ap()
    small_params = (("a_bq", [128, 8]), ("a_bk", [128, 8]), ("a_bv", [128, 8]),
                    ("a_bg", [128, 8]), ("a_bo", [128, 8]), ("a_gns", [128, 8]),
                    ("a_gnb", [128, 8]), ("a_ba", [16, 1]), ("a_bb", [16, 1]),
                    ("a_qc", [128, 32]), ("a_kc", [128, 32]), ("a_vc", [128, 32]),
                    ("a_bgg", [1, 16]))
    for nm, shp in small_params:
        dram[nm] = nc.dram_tensor(nm, shp, F32, kind="ExternalInput").ap()
    cshapes = (("c_id16f", [48, 16], F32), ("c_id64b", [128, 64], BF16),
               ("c_id128b", [128, 128], BF16),
               ("c_ones1f", [1, 128], F32), ("c_ones1b", [1, 128], BF16),
               ("c_ones128b", [128, 1], BF16),
               ("c_maskLsNegB", [128, 2048], BF16), ("c_maskUsNegB", [128, 2048], BF16),
               ("c_maskUiB", [128, 2048], BF16), ("c_selpairB", [16, 1024], BF16),
               ("c_bo2", [128, 2], BF16), ("c_bo2T", [2, 128], BF16))
    for nm, shp, dt in cshapes:
        dram[nm] = nc.dram_tensor(nm, shp, dt, kind="ExternalInput").ap()

    A = nc.alloc_sbuf_tensor
    x0b = A("x0b", [128, 8 * 2048], BF16).ap()
    chain = A("chain", [128, 8 * 2048], BF16).ap()
    outs = A("outs", [128, 8 * 2048], BF16).ap()
    wbuf = A("wbuf", [128, 8 * 1024], BF16).ap()
    smal = A("smal", [128, 8 * 1024], BF16).ap()
    smal2 = A("smal2", [128, 4 * 1024], BF16).ap()
    fsc = A("fsc", [128, 10240], F32).ap()
    stf = A("stf", [128, 512], F32).ap()
    stb = A("stb", [128, 512], BF16).ap()
    wvt = A("wvt", [128, 1024], BF16).ap()
    gate = A("gate", [128, 256], F32).ap()
    gate2 = A("gate2", [128, 256], F32).ap()
    m8t = A("m8t", [128, 8], F32).ap()
    mk16 = A("mk16", [128, 16], F32).ap()
    betaBb = A("betaBb", [16, 2048], BF16).ap()
    normb = A("normb", [2, 2048], BF16).ap()
    cumTf = A("cumTf", [128, 16], F32).ap()
    gTnegf = A("gTnegf", [128, 16], F32).ap()
    gslb = A("gslb", [16, 128], BF16).ap()
    dslb = A("dslb", [16, 128], BF16).ap()
    gsl32 = A("gsl32", [16, 128], F32).ap()
    gccol = A("gccol", [128, 8], BF16).ap()
    gcb16 = A("gcb16", [16, 1], BF16).ap()
    statm = A("statm", [2, 8], F32).ap()
    statq = A("statq", [2, 8], F32).ap()
    statv = A("statv", [2, 8], F32).ap()
    statmb = A("statmb", [2, 8], BF16).ap()
    statrb = A("statrb", [2, 8], BF16).ap()
    gcol2 = A("gcol2", [128, 2], BF16).ap()
    murb = A("murb", [1, 2048], BF16).ap()
    rsrb = A("rsrb", [1, 2048], BF16).ap()
    sbc = {}
    for nm, shp, dt in cshapes:
        sbc[nm] = A(nm + "_s", shp, dt).ap()
    sbp = {}
    for nm, shp in small_params:
        sbp[nm] = A(nm + "_s", shp, F32).ap()
    wab = A("wab", [128, 8 * 16 * 3], BF16).ap()
    bggb = A("bggb", [1, 16], BF16).ap()

    f_tmpd = fsc[:, 0:2048]
    f_x0fA = fsc[:, 0:2048]
    f_x0fB = fsc[:, 2048:4096]
    f_p1a = fsc[0:16, 2048:4096]
    f_nrmA = fsc[32:34, 2048:4096]
    f_nrmB = fsc[64:66, 2048:4096]
    f_cum = fsc[0:16, 4096:6144]
    f_rev = fsc[32:48, 4096:6144]
    f_beta = fsc[64:80, 4096:6144]
    f_lnA = fsc[0:16, 6144:8192]
    f_murow = fsc[32:33, 6144:8192]
    f_cumflat = fsc[0:1, 6144:8192]
    f_rstdrow = fsc[64:65, 8192:10240]
    f_mu2row = fsc[32:33, 8192:10240]

    cs = [chain[:, i * 2048:(i + 1) * 2048] for i in range(8)]
    sY1, sZ1, sAt, s3, s4, s5, s6, sTmp = cs

    pairbuf = [smal[:, 0:2048], smal[:, 2048:4096]]
    sqbuf = [smal[:, 4096:5120], smal[:, 5120:6144]]
    convsc = [smal[:, 6144:6704], smal[:, 6704:7264]]
    qch = [smal[:, 0:1024], smal[:, 1024:2048]]
    kch = [smal[:, 2048:3072], smal[:, 3072:4096]]
    vch = [smal[:, 4096:5120], smal[:, 5120:6144]]
    kstar = smal[:, 6144:7168]
    qplus = smal[:, 7168:8192]
    muBCb = smal[:, 0:2048]
    rsBCb = smal[:, 2048:4096]
    sqgn = [smal[:, 4096:6144], smal[:, 6144:8192]]
    vTsb = smal2[:, 0:1024]
    vTsb = smal2[:, 3072:4096]
    kdec2dm = smal2[:, 1024:2048]
    kdec2tm = smal2[:, 2048:3072]
    gtmp = [smal2[:, 0:2048], smal2[:, 2048:4096]]

    psA = nc.alloc_psum_tensor("psA", [128, 2048], F32).ap()
    psB = nc.alloc_psum_tensor("psB", [128, 1024], F32).ap()
    psC = nc.alloc_psum_tensor("psC", [128, 512], F32).ap()
    psD = nc.alloc_psum_tensor("psD", [128, 512], F32).ap()

    for nm, shp, dt in cshapes:
        DMA(p, "sync", sbc[nm][:, :], dram[nm])
    for nm, shp in small_params:
        DMA(p, "sync", sbp[nm][:, :], dram[nm])
    DMA(p, "gpsimd", x0b.rearrange("p (c t) -> p c t", t=2048),
        x0T.rearrange("(c p) t -> p c t", p=128), ch="inig")
    for i, nm in enumerate(("a_wa", "a_wb", "a_wgg")):
        DMA(p, "gpsimd", wab.rearrange("p (j c e) -> p j c e", j=3, c=8)[:, i],
            dram[nm].rearrange("(c p) e -> p c e", p=128))
    COPY(p, bggb[:, :], sbp["a_bgg"][:, :])
    MEMSET(p, cbias[:, 0:1], 1e-5)
    MEMSET(p, cbias[:, 1:2], 1.0)
    p.fence()

    wa_v = wab.rearrange("p (j c e) -> p j c e", j=3, c=8)
    x0v = x0b.rearrange("p (c t) -> p c t", t=2048)

    def ab_logits(j):
        for cg in range(4):
            for k in range(8):
                MM(p, psA[0:16, cg * 512:(cg + 1) * 512], wa_v[:, j, k],
                   x0v[:, k, cg * 512:(cg + 1) * 512], start=(k == 0), stop=(k == 7))

    ab_logits(0)
    tA = p.tick("tensor")
    p.wait("scalar", tA)
    ACT(p, f_p1a, psA[0:16, :], AF.Abs, bias=sbp["a_ba"][:, 0:1])
    ACT(p, f_lnA, f_p1a, AF.Exp, scale=-1.0)
    ACT(p, f_p1a, f_lnA, AF.Ln, bias=cbias[0:16, 1:2])
    tS = p.tick("scalar")
    p.wait("vector", tS)
    TS(p, f_lnA, psA[0:16, :], sbp["a_ba"][:, 0:1], ALU.add, 0.0, ALU.min)
    TT(p, f_lnA, f_lnA, f_p1a, ALU.subtract)
    tV = p.tick("vector")
    p.wait("tensor", tV)
    ab_logits(1)
    tA = p.tick("tensor")
    p.wait("scalar", tA)
    ACT(p, f_beta, psA[0:16, :], AF.Sigmoid, bias=sbp["a_bb"][:, 0:1])
    tS = p.tick("scalar")
    p.wait("vector", tS)
    COPY(p, betaBb[:, :], f_beta)
    for c in range(NCH):
        sl = slice(c * C, (c + 1) * C)
        SCAN(p, f_cum[:, sl], f_lnA[:, sl], f_lnA[:, sl], 0.0, ALU.add, ALU.bypass)
        TS(p, f_rev[:, sl], f_cum[:, sl], f_cum[:, c * C + C - 1:c * C + C],
           ALU.subtract, -1.0, ALU.mult)
    p.fence()

    tens_cfg = (("a_wq", "a_bq", "a_qc", True, 0), ("a_wk", "a_bk", "a_kc", True, 1),
                ("a_wv", "a_bv", "a_vc", False, 2))
    wslot = [wbuf[:, i * 1024:(i + 1) * 1024] for i in range(4)]
    wslot_tok = [None] * 4
    pair_tok = [None, None]
    psA_half = [psA[:, 0:1024], psA[:, 1024:2048]]
    psA_tok = [None, None]
    si = 0
    for (wnm, bnm, cnm, do_norm, jidx) in tens_cfg:
        wv_ = dram[wnm].rearrange("(c p) m -> p c m", p=128)
        for hp in range(8):
            par = si % 2
            ws = wslot[si % 8]
            if wslot_tok[si % 8] is not None:
                p.wait("gpsimd", wslot_tok[si % 8])
            DMA(p, "gpsimd", ws.rearrange("p (c m) -> p c m", m=128),
                wv_[:, :, hp * 128:(hp + 1) * 128])
            tL = p.tick("gpsimd")
            p.wait("tensor", tL)
            if psA_tok[par] is not None:
                p.wait("tensor", psA_tok[par])
            if pair_tok[par] is not None:
                p.wait("tensor", pair_tok[par])
            wsv = ws.rearrange("p (c m) -> p c m", m=128)
            for h2 in range(2):
                for k in range(8):
                    MM(p, psA_half[par][:, h2 * 512:(h2 + 1) * 512], wsv[:, k],
                       x0v[:, k, (par * 0 + 1024 * h2 + 512 * 0):0], start=False)
            si += 1
    p.fence()
    p.build()
    return nc


A_SMALL_PARAMS = (("a_bq", [128, 8]), ("a_bk", [128, 8]), ("a_bv", [128, 8]),
                  ("a_bg", [128, 8]), ("a_bo", [128, 8]), ("a_gns", [128, 8]),
                  ("a_gnb", [128, 8]), ("a_ba", [16, 1]), ("a_bb", [16, 1]),
                  ("a_qc", [128, 32]), ("a_kc", [128, 32]), ("a_vc", [128, 32]),
                  ("a_bgg", [1, 16]))
A_CONSTS = (("c_id16f", [48, 16], F32), ("c_id64b", [128, 64], BF16),
            ("c_ones1f", [1, 128], F32), ("c_ones1b", [1, 128], BF16),
            ("c_ones128b", [128, 1], BF16),
            ("c_maskLsNegB", [128, 2048], BF16), ("c_maskUsNegB", [128, 2048], BF16),
            ("c_maskUiB", [128, 2048], BF16), ("c_selpairB", [16, 1024], BF16),
            ("c_bo2", [128, 2], BF16), ("c_bo2T", [2, 128], BF16))


def hsl(ap, h, w=128):
    return ap[64 * (h % 2):64 * (h % 2) + 64, (h // 2) * w:(h // 2) * w + w]


def wave16(p, emit):
    for h in range(0, 16, 2):
        emit(h)
    t = p.tick("tensor")
    p.wait("tensor", t)
    for h in range(1, 16, 2):
        emit(h)
    return p.tick("tensor")


def build_A(stop=None, nch=NCH, nstage=99):
    nc = bass.Bass("TRN2", target_bir_lowering=False)
    p = Prog(nc)

    x0T = nc.dram_tensor("x0T", [D, S], F32, kind="ExternalInput").ap()
    x1T = nc.dram_tensor("x1T", [D, S], F32, kind="ExternalOutput").ap()
    maskedT = nc.dram_tensor("maskedT", [128, 256], F32, kind="ExternalOutput").ap()
    spill = nc.dram_tensor("spillA", [3, NCH, 128, NP * C], BF16, kind="Internal").ap()

    dram = {}
    for nm in ("a_wq", "a_wk", "a_wv", "a_wg", "a_wo"):
        dram[nm] = nc.dram_tensor(nm, [D, D], F32, kind="ExternalInput").ap()
    for nm in ("a_wa", "a_wb", "a_wgg"):
        dram[nm] = nc.dram_tensor(nm, [D, 16], F32, kind="ExternalInput").ap()
    for nm, shp in A_SMALL_PARAMS:
        dram[nm] = nc.dram_tensor(nm, shp, F32, kind="ExternalInput").ap()
    for nm, shp, dt in A_CONSTS:
        dram[nm] = nc.dram_tensor(nm, shp, dt, kind="ExternalInput").ap()

    AL = nc.alloc_sbuf_tensor
    x0b = AL("x0b", [128, 8 * 2048], BF16).ap()
    chain = AL("chain", [128, 8 * 2048], BF16).ap()
    outs = AL("outs", [128, 8 * 2048], BF16).ap()
    wbuf = AL("wbuf", [128, 4 * 1024], BF16).ap()
    smal = AL("smal", [128, 8 * 1024], BF16).ap()
    smal2 = AL("smal2", [128, 4 * 1024], BF16).ap()
    fsc = AL("fsc", [128, 10240], F32).ap()
    stf = AL("stf", [128, 512], F32).ap()
    stb = AL("stb", [128, 512], BF16).ap()
    wvt = AL("wvt", [128, 1024], BF16).ap()
    gate = AL("gate", [128, 256], F32).ap()
    gate2 = AL("gate2", [128, 256], F32).ap()
    m8t = AL("m8t", [128, 8], F32).ap()
    scr16 = AL("scr16", [128, 16], F32).ap()
    betaBb = AL("betaBb", [16, 2048], BF16).ap()
    normb = AL("normb", [2, 2048], BF16).ap()
    cumTf = AL("cumTf", [128, 16], F32).ap()
    gTnegf = AL("gTnegf", [128, 16], F32).ap()
    gslb = AL("gslb", [16, 128], BF16).ap()
    dslb = AL("dslb", [16, 128], BF16).ap()
    gsl32 = AL("gsl32", [16, 128], F32).ap()
    gccol = AL("gccol", [128, 8], F32).ap()
    statm = AL("statm", [2, 8], F32).ap()
    statq = AL("statq", [2, 8], F32).ap()
    statv = AL("statv", [2, 8], F32).ap()
    statmb = AL("statmb", [2, 8], BF16).ap()
    statrb = AL("statrb", [2, 8], BF16).ap()
    gcol2 = AL("gcol2", [128, 2], BF16).ap()
    gcolA = AL("gcolA", [128, 16], F32).ap()
    sbc = {}
    for nm, shp, dt in A_CONSTS:
        sbc[nm] = AL(nm + "_s", shp, dt).ap()
    sbp = {}
    for nm, shp in A_SMALL_PARAMS:
        sbp[nm] = AL(nm + "_s", shp, F32).ap()
    wab = AL("wab", [128, 8 * 16 * 3], BF16).ap()
    murb = betaBb[0:1, 0:2048]
    rsrb = normb[0:1, 0:2048]

    bggb = AL("bggb", [1, 16], BF16).ap()
    cbias = AL("cbias", [128, 2], F32).ap()

    f_tmpd = fsc[:, 0:2048]
    f_x0f = [fsc[:, 0:2048], fsc[:, 2048:4096]]
    f_p1a = fsc[0:16, 2048:4096]
    f_nrmA = fsc[0:2, 0:2048]
    f_nrmB = fsc[0:2, 2048:4096]
    f_beta = fsc[0:16, 4096:6144]
    f_lnA = fsc[0:16, 6144:8192]
    f_rev = fsc[0:16, 6144:8192]
    f_cum = fsc[0:16, 8192:10240]
    f_cumflat = fsc[0:1, 2048:4096]
    f_murow = fsc[0:1, 4096:6144]
    f_mu2row = fsc[0:1, 6144:8192]
    f_rstdrow = fsc[0:1, 8192:10240]

    cslots = [chain[:, i * 2048:(i + 1) * 2048] for i in range(8)]
    sY1, sZ1, sAt, s3, s4, s5, s6, sTmp = cslots

    pairbuf = [smal[:, 0:2048], smal[:, 2048:4096]]
    sqbuf = [smal[:, 4096:6144], smal[:, 6144:8192]]
    qch = [smal[:, 0:1024], smal[:, 1024:2048]]
    kch = [smal[:, 2048:3072], smal[:, 3072:4096]]
    vch = [smal[:, 4096:5120], smal[:, 5120:6144]]
    kstar = smal[:, 6144:7168]
    qplus = smal[:, 7168:8192]
    muBCb = smal[:, 0:2048]
    rsBCb = smal[:, 2048:4096]
    sqgn = [smal[:, 4096:6144], smal[:, 6144:8192]]
    vTsb = smal2[:, 3072:4096]
    kdec2dm = smal2[:, 1024:2048]
    kdec2tm = smal2[:, 2048:3072]
    gtmp = [smal2[:, 0:2048], smal2[:, 2048:4096]]

    psE = nc.alloc_psum_tensor("psE", [128, 2048], F32).ap()
    psF = nc.alloc_psum_tensor("psF", [128, 2048], F32).ap()
    psB = psF[:, 0:1024]
    psSt = psF[:, 1024:1536]
    psG = psF[:, 1536:2048]

    x0v = x0b.rearrange("p (c t) -> p c t", t=2048)
    wabv = wab.rearrange("p (j c e) -> p j c e", j=3, c=8)
    outsv = outs.rearrange("p (r t) -> p r t", t=2048)
    zbfv = chain.rearrange("p (c t) -> p c t", t=2048)

    for nm, shp, dt in A_CONSTS:
        DMA(p, "sync", sbc[nm][:, :], dram[nm])
    for nm, shp in A_SMALL_PARAMS:
        DMA(p, "sync", sbp[nm][:, :], dram[nm])
    DMA(p, "gpsimd", x0b.rearrange("p (c t) -> p c t", t=2048),
        x0T.rearrange("(c p) t -> p c t", p=128), ch="inig")
    for i, nm in enumerate(("a_wa", "a_wb", "a_wgg")):
        DMA(p, "gpsimd", wabv[:, i],
            dram[nm].rearrange("(c p) e -> p c e", p=128), ch="inig")
    p.fence()
    COPY(p, bggb[:, :], sbp["a_bgg"][:, :])
    MEMSET(p, cbias[:, 0:1], 1e-5)
    MEMSET(p, cbias[:, 1:2], 1.0)
    MEMSET(p, stf[:, :], 0.0)
    MEMSET(p, stb[:, :], 0.0)
    MEMSET(p, outs[:, :], 0.0)
    p.fence()

    def ab_logits(j):
        for cg in range(4):
            for k in range(8):
                MM(p, psE[0:16, cg * 512:(cg + 1) * 512], wabv[:, j, k],
                   x0v[:, k, cg * 512:(cg + 1) * 512], start=(k == 0), stop=(k == 7))

    ab_logits(0)
    t = p.tick("tensor"); p.wait("scalar", t); p.wait("vector", t)
    ACT(p, f_p1a, psE[0:16, :], AF.Abs, bias=sbp["a_ba"][:, 0:1])
    ACT(p, f_lnA, f_p1a, AF.Exp, scale=-1.0)
    ACT(p, f_p1a, f_lnA, AF.Ln, bias=cbias[0:16, 1:2])
    t = p.tick("scalar"); p.wait("vector", t)
    TS(p, f_lnA, psE[0:16, :], sbp["a_ba"][:, 0:1], ALU.add, 0.0, ALU.min)
    TT(p, f_lnA, f_lnA, f_p1a, ALU.subtract)
    t = p.tick("vector"); p.wait("tensor", t)
    ab_logits(1)
    t = p.tick("tensor"); p.wait("scalar", t)
    ACT(p, f_beta, psE[0:16, :], AF.Sigmoid, bias=sbp["a_bb"][:, 0:1])
    t = p.tick("scalar"); p.wait("vector", t)
    COPY(p, betaBb[:, :], f_beta)
    for c in range(NCH):
        sl = slice(c * C, (c + 1) * C)
        SCAN(p, f_cum[:, sl], f_lnA[:, sl], f_lnA[:, sl], 0.0, ALU.add, ALU.bypass)
    t = p.tick("vector"); p.wait("vector", t)
    for c in range(NCH):
        sl = slice(c * C, (c + 1) * C)
        TS(p, f_rev[:, sl], f_cum[:, sl], f_cum[:, c * C + C - 1:c * C + C],
           ALU.subtract, -1.0, ALU.mult)
    p.fence()
    if stop == "P1a":
        _dump_stop(p, nc, {"cum": f_cum, "rev": f_rev, "beta": f_beta,
                           "lnA": f_rev})
        return nc

    wslot = [wbuf[:, i * 1024:(i + 1) * 1024] for i in range(4)]
    wslot_tok = [None] * 4
    spill_tok = [None, None]
    evac_tok = [None, None]
    psP = [psE, psF]
    convw = {"a_wq": "a_qc", "a_wk": "a_kc", "a_wv": "a_vc"}
    si = 0
    for (wnm, bnm, do_norm, jidx) in (("a_wq", "a_bq", True, 0),
                                      ("a_wk", "a_bk", True, 1),
                                      ("a_wv", "a_bv", False, 2)):
        wsrc = dram[wnm].rearrange("(c p) m -> p c m", p=128)
        cw = sbp[convw[wnm]]
        for hp in range(8):
            par = si % 2
            slot = si % 4
            ws = wslot[slot]
            wsv = ws.rearrange("p (c m) -> p c m", m=128)
            if wslot_tok[slot] is not None:
                p.wait("gpsimd", wslot_tok[slot])
            tL = DMA(p, "gpsimd", wsv, wsrc[:, :, hp * 128:(hp + 1) * 128],
                     ch=f"wl{slot}")
            p.wait("tensor", tL)
            if evac_tok[par] is not None:
                p.wait("tensor", evac_tok[par])
            ps = psP[par]
            for cg in range(4):
                for k in range(8):
                    MM(p, ps[:, cg * 512:(cg + 1) * 512], wsv[:, k],
                       x0v[:, k, cg * 512:(cg + 1) * 512], start=(k == 0), stop=(k == 7))
            tMM = p.tick("tensor")
            wslot_tok[slot] = tMM
            p.wait("scalar", tMM)
            if spill_tok[par] is not None:
                p.wait("scalar", spill_tok[par])
            pb = pairbuf[par]
            ACT(p, pb, ps, AF.Identity, bias=sbp[bnm][:, hp:hp + 1])
            tEv = p.tick("scalar")
            evac_tok[par] = tEv
            p.wait("vector", tEv)
            co = gtmp[par]
            TS(p, co, pb, cw[:, hp * 4 + 3:hp * 4 + 4], ALU.mult)
            for j, dlt in ((2, 1), (1, 2), (0, 3)):
                STT(p, co[:, dlt:2048], pb[:, 0:2048 - dlt],
                    cw[:, hp * 4 + j:hp * 4 + j + 1], co[:, dlt:2048],
                    ALU.mult, ALU.add)
            tCv = p.tick("vector")
            p.wait("scalar", tCv)
            ACT(p, pb, co, AF.Silu)
            tSi = p.tick("scalar")
            if do_norm:
                ACT(p, sqbuf[par], pb, AF.Square)
                tSq = p.tick("scalar")
                p.wait("tensor", tSq)
                for cg in range(4):
                    MM(p, ps[0:2, cg * 512:(cg + 1) * 512], sbc["c_bo2"][:, :],
                       sqbuf[par][:, cg * 512:(cg + 1) * 512])
                tQ = p.tick("tensor"); p.wait("scalar", tQ)
                ACT(p, f_nrmA, ps[0:2, :], AF.Sqrt)
                tQ = p.tick("scalar"); p.wait("vector", tQ)
                TS(p, f_nrmA, f_nrmA, 1e-6, ALU.add)
                p.op("vector", lambda v, o=f_nrmB, i=f_nrmA: v.reciprocal(out=o, in_=i))
                COPY(p, normb[:, :], f_nrmB)
                tN = p.tick("vector"); p.wait("tensor", tN)
                for cg in range(4):
                    MM(p, ps[:, cg * 512:(cg + 1) * 512], sbc["c_bo2T"][:, :],
                       normb[:, cg * 512:(cg + 1) * 512])
                tB = p.tick("tensor"); p.wait("vector", tB, tSi)
                TT(p, pb, pb, ps, ALU.mult)
            else:
                p.wait("tensor", tSi)
                for cg in range(4):
                    MM(p, ps[:, cg * 512:(cg + 1) * 512],
                       sbc["c_selpairB"][:, hp * 128:(hp + 1) * 128],
                       betaBb[:, cg * 512:(cg + 1) * 512])
                tB = p.tick("tensor"); p.wait("vector", tB)
                TT(p, pb, pb, ps, ALU.mult)
            tFin = p.tick("vector")
            evac_tok[par] = tFin
            p.wait("sync", tFin)
            spill_tok[par] = DMA(
                p, "sync", spill[jidx, :, :, hp * 128:(hp + 1) * 128]
                .rearrange("c p t -> p c t"),
                pb.rearrange("p (c t) -> p c t", t=128), ch=f"sp{par}")
            si += 1
    p.fence()
    if stop == "P1b":
        qd = nc.dram_tensor("dbg_q", [NCH, 128, NP * C], BF16, kind="ExternalOutput").ap()
        kd = nc.dram_tensor("dbg_k", [NCH, 128, NP * C], BF16, kind="ExternalOutput").ap()
        vd = nc.dram_tensor("dbg_v", [NCH, 128, NP * C], BF16, kind="ExternalOutput").ap()
        DMA(p, "sync", qd, spill[0], ch="dbg")
        DMA(p, "sync", kd, spill[1], ch="dbg")
        DMA(p, "sync", vd, spill[2], ch="dbg")
        p.wait_chan_totals("gpsimd", ["dbg"])
        p.op("gpsimd", lambda g: g.engine_nop())
        p.tick("gpsimd")
        p.build()
        return nc

    ld_tok = [[None, None], [None, None], [None, None]]
    use_tok = [[None, None], [None, None], [None, None]]
    chb = [qch, kch, vch]

    def issue_loads(c):
        par = c % 2
        for j in range(3):
            if use_tok[j][par] is not None:
                p.wait("gpsimd", use_tok[j][par])
            ld_tok[j][par] = DMA(p, "gpsimd", chb[j][par], spill[j, c],
                                 ch=f"ld{j}{par}")

    import os as _os3
    _fs = _os3.environ.get("FENCE_STAGE", "")

    def SF(tag):
        if _fs and (_fs == "all" or tag in _fs.split(",")):
            p.fence("tensor", "vector", "scalar")

    issue_loads(0)
    for c in range(nch):
        par = c % 2
        sl = slice(c * C, (c + 1) * C)
        qcb, kcb, vcb = qch[par], kch[par], vch[par]
        if c + 1 < nch:
            issue_loads(c + 1)
        p.wait("tensor", ld_tok[0][par], ld_tok[1][par], ld_tok[2][par])
        MM(p, psSt[:, 0:16], f_cum[:, sl], sbc["c_id16f"][0:16, :])
        for h in range(16):
            MM(p, psE[0:1, h * 128:(h + 1) * 128], sbc["c_id16f"][0:16, h:h + 1],
               f_cum[:, sl])
        t = p.tick("tensor"); p.wait("vector", t); p.wait("scalar", t)
        COPY(p, cumTf[:, :], psSt[:, 0:16])
        COPY(p, f_cumflat, psE[0:1, :])
        t = p.tick("vector"); p.wait("scalar", t)
        ACT(p, scr16[:, 0:16], cumTf[:, :], AF.Exp)
        ACT(p, gsl32[:, :], f_cum[:, sl], AF.Exp)
        t = p.tick("scalar"); p.wait("vector", t)
        TS(p, gTnegf[:, :], scr16[:, 0:16], -1.0, ALU.mult)
        COPY(p, gslb[:, :], gsl32[:, :])
        t = p.tick("vector"); p.wait("scalar", t)
        ACT(p, gsl32[:, :], f_rev[:, sl], AF.Exp)
        t = p.tick("scalar"); p.wait("vector", t)
        COPY(p, dslb[:, :], gsl32[:, :])
        tV = p.tick("vector")

        SF("decay")
        if nstage <= 1:
            break
        p.wait("tensor", tV)
        for pr in range(8):
            MM(p, psB[:, pr * 128:(pr + 1) * 128],
               sbc["c_selpairB"][:, pr * 128:(pr + 1) * 128], betaBb[:, sl])
        t = p.tick("tensor"); p.wait("vector", t)
        TT(p, kstar, kcb, psB, ALU.mult)
        t = p.tick("vector"); p.wait("tensor", t)
        for pr in range(8):
            MM(p, psB[:, pr * 128:(pr + 1) * 128],
               sbc["c_selpairB"][:, pr * 128:(pr + 1) * 128], gslb[:, :])
        t = p.tick("tensor"); p.wait("vector", t)
        TT(p, qplus, qcb, psB, ALU.mult)
        tQP = p.tick("vector")

        SF("kq")
        if nstage <= 2:
            break
        p.wait("tensor", tQP)
        for hf in range(2):
            for g2 in range(2):
                MM(p, psB[:, g2 * 512:(g2 + 1) * 512], sbc["c_ones1f"][:, :],
                   f_cumflat[0:1, hf * 1024 + g2 * 512: hf * 1024 + (g2 + 1) * 512])
            t = p.tick("tensor"); p.wait("vector", t)
            for h8 in range(8):
                h = hf * 8 + h8
                TS(p, f_tmpd[:, h8 * 128:(h8 + 1) * 128],
                   psB[:, h8 * 128:(h8 + 1) * 128], cumTf[:, h:h + 1],
                   ALU.subtract, 0.0, ALU.min)
            t = p.tick("vector"); p.wait("scalar", t)
            ACT(p, s3[:, hf * 1024:(hf + 1) * 1024], f_tmpd[:, 0:1024], AF.Exp)
            t = p.tick("scalar"); p.wait("vector", t)
            for h8 in range(8):
                h = hf * 8 + h8
                TS(p, f_tmpd[:, 1024 + h8 * 128: 1024 + (h8 + 1) * 128],
                   psB[:, h8 * 128:(h8 + 1) * 128], cumTf[:, h:h + 1],
                   ALU.subtract, 0.0, ALU.max)
            t = p.tick("vector"); p.wait("scalar", t); p.wait("tensor", t)
            ACT(p, s4[:, hf * 1024:(hf + 1) * 1024], f_tmpd[:, 1024:2048],
                AF.Exp, scale=-1.0)
        tLam = p.tick("scalar")

        SF("lam")
        if nstage <= 3:
            break
        p.wait("tensor", tLam)
        t = wave16(p, lambda h: MM(p, psE[:, h * 128:(h + 1) * 128], hsl(kstar, h), hsl(kcb, h)))
        p.wait("vector", t)
        TT(p, sTmp, s4, sbc["c_maskLsNegB"][:, :], ALU.mult)
        TT(p, sY1, sTmp, psE, ALU.mult)
        t = p.tick("vector"); p.wait("tensor", t)
        t = wave16(p, lambda h: MM(p, psE[:, h * 128:(h + 1) * 128], hsl(kcb, h), hsl(kstar, h)))
        p.wait("vector", t)
        TT(p, sTmp, s3, sbc["c_maskUsNegB"][:, :], ALU.mult)
        TT(p, sZ1, sTmp, psE, ALU.mult)
        t = p.tick("vector"); p.wait("tensor", t)
        t = wave16(p, lambda h: MM(p, psE[:, h * 128:(h + 1) * 128], hsl(kcb, h), hsl(qcb, h)))
        p.wait("vector", t)
        TT(p, sTmp, s3, sbc["c_maskUiB"][:, :], ALU.mult)
        TT(p, sAt, sTmp, psE, ALU.mult)
        tBld = p.tick("vector")

        SF("gram")
        if nstage <= 4:
            break
        p.wait("tensor", tBld)
        wave16(p, lambda h: MM(p, psB[:, h * 64:(h + 1) * 64], hsl(kstar, h),
                               hsl(stb, h, 64)))
        t = wave16(p, lambda h: MM(p, psE[:, h * 64:(h + 1) * 64], hsl(vcb, h),
                                   sbc["c_id64b"][64 * (h % 2):64 * (h % 2) + 64, :]))
        p.wait("vector", t)
        COPY(p, vTsb, psE[:, 0:1024])
        t = p.tick("vector"); p.wait("vector", t)
        for h in range(16):
            STT(p, wvt[:, h * 64:(h + 1) * 64], psB[:, h * 64:(h + 1) * 64],
                gTnegf[:, h:h + 1], vTsb[:, h * 64:(h + 1) * 64],
                ALU.mult, ALU.add)
        tR = p.tick("vector")

        SF("R")
        if nstage <= 5:
            break
        lv = [(sY1, sZ1), (s3, s4), (s5, s6), (s3, s4), (s5, s6), (s3, s4)]
        tApply = tR
        _sm = _os3.environ.get("SOLVE_MODE", "")
        for i in range(6):
            sy, sz = lv[i]
            if _sm != "nosapply":
                p.wait("tensor", tApply)
                _nw = int(_os3.environ.get("SOLVE_NW", "16"))
                for h in range(16):
                    if h and h % _nw == 0:
                        tt = p.tick("tensor"); p.wait("tensor", tt)
                    MM(p, psB[:, h * 64:(h + 1) * 64],
                       sz[:, h * 128:(h + 1) * 128], wvt[:, h * 64:(h + 1) * 64])
                t = p.tick("tensor"); p.wait("vector", t)
                TT(p, wvt, wvt, psB, ALU.add)
                tApply = p.tick("vector")
            ny, nz = lv[i + 1] if i < 5 else (None, s5)
            last = (i == 5)
            if _sm == "nosquare":
                continue
            _nw = int(_os3.environ.get("SOLVE_NW", "16"))
            if not last:
                for h in range(16):
                    if h and h % _nw == 0:
                        tt = p.tick("tensor"); p.wait("tensor", tt)
                    MM(p, psE[:, h * 128:(h + 1) * 128],
                       sz[:, h * 128:(h + 1) * 128], sy[:, h * 128:(h + 1) * 128])
                t = p.tick("tensor"); p.wait("vector", t)
                COPY(p, ny, psE)
                t = p.tick("vector"); p.wait("tensor", t)
            for h in range(16):
                if h and h % _nw == 0:
                    tt = p.tick("tensor"); p.wait("tensor", tt)
                MM(p, psE[:, h * 128:(h + 1) * 128],
                   sy[:, h * 128:(h + 1) * 128], sz[:, h * 128:(h + 1) * 128])
            t = p.tick("tensor"); p.wait("vector", t)
            COPY(p, nz, psE)
            t = p.tick("vector"); p.wait("tensor", t)
        p.wait("tensor", tApply)
        for h in range(16):
            MM(p, psB[:, h * 64:(h + 1) * 64],
               s5[:, h * 128:(h + 1) * 128], wvt[:, h * 64:(h + 1) * 64])
        t = p.tick("tensor"); p.wait("vector", t)
        TT(p, wvt, wvt, psB, ALU.add)
        tU = p.tick("vector")

        SF("solve")
        if nstage <= 6:
            break
        p.wait("tensor", tU)

        def emit_O(h):
            po = psE[64 * (h % 2):64 * (h % 2) + 64,
                     (h // 2) * 128:(h // 2) * 128 + 128]
            MM(p, po, wvt[:, h * 64:(h + 1) * 64],
               sAt[:, h * 128:(h + 1) * 128], start=True, stop=False)
            MM(p, po, hsl(stb, h, 64), hsl(qplus, h), start=False, stop=True)
        t = wave16(p, emit_O)
        p.wait("vector", t)
        COPY(p, outsv[:, :, c * 128:(c + 1) * 128],
             psE[:, 0:1024].rearrange("p (r t) -> p r t", t=128))
        tO = p.tick("vector")

        SF("O")
        if nstage <= 7:
            break
        p.wait("tensor", tO)
        for pr in range(8):
            MM(p, psB[:, pr * 128:(pr + 1) * 128],
               sbc["c_selpairB"][:, pr * 128:(pr + 1) * 128], dslb[:, :])
        t = p.tick("tensor"); p.wait("vector", t)
        TT(p, kdec2dm, kcb, psB, ALU.mult)
        t = p.tick("vector"); p.wait("tensor", t)
        t = wave16(p, lambda h: MM(p, psE[:, h * 64:(h + 1) * 64], hsl(kdec2dm, h),
                                   sbc["c_id64b"][64 * (h % 2):64 * (h % 2) + 64, :]))
        p.wait("vector", t)
        COPY(p, kdec2tm[:, :], psE[:, 0:1024])
        t = p.tick("vector"); p.wait("tensor", t)
        wave16(p, lambda h: MM(
            p, psSt[64 * (h % 2):64 * (h % 2) + 64, (h // 2) * 64:(h // 2) * 64 + 64],
            kdec2tm[:, h * 64:(h + 1) * 64], wvt[:, h * 64:(h + 1) * 64]))
        for pr in range(8):
            MM(p, psG[:, pr:pr + 1],
               sbc["c_selpairB"][:, pr * 128:(pr + 1) * 128], gslb[:, 127:128])
        t = p.tick("tensor"); p.wait("vector", t)
        COPY(p, gccol[:, :], psG[:, 0:8])
        t = p.tick("vector"); p.wait("vector", t)
        for pr in range(8):
            STT(p, stf[:, pr * 64:(pr + 1) * 64], stf[:, pr * 64:(pr + 1) * 64],
                gccol[:, pr:pr + 1], psSt[:, pr * 64:(pr + 1) * 64],
                ALU.mult, ALU.add)
        t = p.tick("vector"); p.wait("vector", t)
        COPY(p, stb[:, :], stf[:, :])
        tSt = p.tick("vector")
        p.wait("tensor", tSt)
        import os as _os2
        if _os2.environ.get("FENCE_CHUNK"):
            p.fence("tensor", "vector", "scalar")
        for j in range(3):
            use_tok[j][par] = tSt
    p.fence()
    if stop == "P2":
        _dump_stop(p, nc, {"outs": outs, "stf": stf})
        return nc

    out_toks = []
    for hp in range(8):
        par = hp % 2
        op = outs[:, hp * 2048:(hp + 1) * 2048]
        ACT(p, sqgn[par], op, AF.Square)
        t = p.tick("scalar"); p.wait("tensor", t)
        for cg in range(4):
            MM(p, psF[0:2, cg * 512:(cg + 1) * 512], sbc["c_bo2"][:, :],
               op[:, cg * 512:(cg + 1) * 512])
            MM(p, psE[0:2, cg * 512:(cg + 1) * 512], sbc["c_bo2"][:, :],
               sqgn[par][:, cg * 512:(cg + 1) * 512])
        t = p.tick("tensor"); p.wait("vector", t)
        REDUCE(p, statm[0:2, hp:hp + 1], psF[0:2, :], ALU.add)
        REDUCE(p, statq[0:2, hp:hp + 1], psE[0:2, :], ALU.add)
        t = p.tick("vector"); p.wait("tensor", t); p.wait("scalar", t)
    TS(p, statm, statm, 1.0 / 131072.0, ALU.mult)
    TS(p, statq, statq, 1.0 / 131072.0, ALU.mult)
    TT(p, statv, statm, statm, ALU.mult)
    TT(p, statv, statq, statv, ALU.subtract)
    t = p.tick("vector"); p.wait("scalar", t)
    ACT(p, statv, statv, AF.Ln, bias=cbias[0:2, 0:1])
    ACT(p, statv, statv, AF.Exp, scale=-0.5)
    t = p.tick("scalar"); p.wait("vector", t)
    COPY(p, statmb, statm)
    COPY(p, statrb, statv)
    t = p.tick("vector"); p.wait("tensor", t)
    for hp in range(8):
        MM(p, psG[:, 2 * hp:2 * hp + 1], sbc["c_bo2T"][:, :], statmb[:, hp:hp + 1])
        MM(p, psG[:, 2 * hp + 1:2 * hp + 2], sbc["c_bo2T"][:, :], statrb[:, hp:hp + 1])
    t = p.tick("tensor"); p.wait("vector", t)
    COPY(p, gcolA[:, :], psG[:, 0:16])
    t = p.tick("vector"); p.wait("vector", t); p.wait("tensor", t)
    wsrc_g = dram["a_wg"].rearrange("(c p) m -> p c m", p=128)
    evac_tok = [None, None]
    gtok = [None, None]
    for hp in range(8):
        par = hp % 2
        slot = hp % 4
        op = outs[:, hp * 2048:(hp + 1) * 2048]
        ws = wslot[slot]
        wsv = ws.rearrange("p (c m) -> p c m", m=128)
        if wslot_tok[slot] is not None:
            p.wait("gpsimd", wslot_tok[slot])
        tL = DMA(p, "gpsimd", wsv, wsrc_g[:, :, hp * 128:(hp + 1) * 128],
                 ch=f"wl{slot}")
        TS(p, op, op, gcolA[:, 2 * hp:2 * hp + 1], ALU.subtract,
           gcolA[:, 2 * hp + 1:2 * hp + 2], ALU.mult)
        TS(p, op, op, sbp["a_gns"][:, hp:hp + 1], ALU.mult,
           sbp["a_gnb"][:, hp:hp + 1], ALU.add)
        tGn = p.tick("vector")
        p.wait("tensor", tL)
        if evac_tok[par] is not None:
            p.wait("tensor", evac_tok[par])
        ps = psP[par]
        for cg in range(4):
            for k in range(8):
                MM(p, ps[:, cg * 512:(cg + 1) * 512], wsv[:, k],
                   x0v[:, k, cg * 512:(cg + 1) * 512], start=(k == 0), stop=(k == 7))
        tMM = p.tick("tensor")
        wslot_tok[slot] = tMM
        p.wait("scalar", tMM)
        if gtok[par] is not None:
            p.wait("scalar", gtok[par])
        ACT(p, gtmp[par], ps, AF.Silu, bias=sbp["a_bg"][:, hp:hp + 1])
        t = p.tick("scalar"); p.wait("vector", t, tGn)
        TT(p, op, op, gtmp[par], ALU.mult)
        tMul = p.tick("vector")
        evac_tok[par] = tMul
        gtok[par] = tMul
    p.fence()
    wsrc_o = dram["a_wo"].rearrange("(c p) m -> p c m", p=128)
    x0_tok = [None, None]
    zev_tok = None
    for m in range(8):
        par = m % 2
        slot = m % 4
        ws = wslot[slot]
        wsv = ws.rearrange("p (c m) -> p c m", m=128)
        if wslot_tok[slot] is not None:
            p.wait("gpsimd", wslot_tok[slot])
        tL = DMA(p, "gpsimd", wsv, wsrc_o[:, :, m * 128:(m + 1) * 128],
                 ch=f"wl{slot}")
        if x0_tok[par] is not None:
            p.wait("sync", x0_tok[par])
        tX = DMA(p, "sync", f_x0f[par], x0T[m * 128:(m + 1) * 128, :],
                 ch=f"xf{par}")
        p.wait("tensor", tL)
        if zev_tok is not None:
            p.wait("tensor", zev_tok)
        for cg in range(4):
            for k in range(8):
                MM(p, psE[:, cg * 512:(cg + 1) * 512], wsv[:, k],
                   outsv[:, k, cg * 512:(cg + 1) * 512], start=(k == 0), stop=(k == 7))
        tMM = p.tick("tensor")
        wslot_tok[slot] = tMM
        p.wait("vector", tMM, tX)
        STT(p, zbfv[:, m], psE, sbp["a_bo"][:, m:m + 1], f_x0f[par],
            ALU.add, ALU.add)
        zev_tok = p.tick("vector")
        x0_tok[par] = zev_tok
        p.wait("tensor", zev_tok)
        for cg in range(4):
            MM(p, psF[0:1, cg * 512:(cg + 1) * 512], sbc["c_ones128b"][:, :],
               zbfv[:, m, cg * 512:(cg + 1) * 512], start=(m == 0), stop=(m == 7))
    t = p.tick("tensor"); p.wait("vector", t)
    TS(p, f_murow, psF[0:1, :], 1.0 / 1024.0, ALU.mult)
    t = p.tick("vector"); p.wait("tensor", t); p.wait("scalar", t)
    for m in range(8):
        par = m % 2
        ACT(p, sqgn[par], zbfv[:, m], AF.Square)
        t = p.tick("scalar"); p.wait("tensor", t)
        for cg in range(4):
            MM(p, psF[0:1, cg * 512:(cg + 1) * 512], sbc["c_ones128b"][:, :],
               sqgn[par][:, cg * 512:(cg + 1) * 512], start=(m == 0), stop=(m == 7))
        if m < 7:
            t = p.tick("tensor"); p.wait("scalar", t)
    t = p.tick("tensor"); p.wait("vector", t)
    TS(p, f_mu2row, psF[0:1, :], 1.0 / 1024.0, ALU.mult)
    TT(p, f_rstdrow, f_murow, f_murow, ALU.mult)
    TT(p, f_mu2row, f_mu2row, f_rstdrow, ALU.subtract)
    t = p.tick("vector"); p.wait("scalar", t)
    ACT(p, f_mu2row, f_mu2row, AF.Ln, bias=cbias[0:1, 0:1])
    ACT(p, f_rstdrow, f_mu2row, AF.Exp, scale=-0.5)
    t = p.tick("scalar"); p.wait("vector", t)
    COPY(p, murb, f_murow)
    COPY(p, rsrb, f_rstdrow)
    t = p.tick("vector"); p.wait("tensor", t)
    for cg in range(4):
        MM(p, psE[:, cg * 512:(cg + 1) * 512], sbc["c_ones1b"][:, :],
           murb[0:1, cg * 512:(cg + 1) * 512])
    t = p.tick("tensor"); p.wait("vector", t)
    COPY(p, muBCb, psE)
    t = p.tick("vector"); p.wait("tensor", t)
    for cg in range(4):
        MM(p, psE[:, cg * 512:(cg + 1) * 512], sbc["c_ones1b"][:, :],
           rsrb[0:1, cg * 512:(cg + 1) * 512])
    t = p.tick("tensor"); p.wait("vector", t)
    COPY(p, rsBCb, psE)
    t = p.tick("vector"); p.wait("vector", t)
    xf_tok = [None, None]
    for m in range(8):
        par = m % 2
        if xf_tok[par] is not None:
            p.wait("vector", *xf_tok[par])
        TT(p, f_x0f[par], zbfv[:, m], muBCb, ALU.subtract)
        TT(p, f_x0f[par], f_x0f[par], rsBCb, ALU.mult)
        t = p.tick("vector"); p.wait("sync", t)
        tD = DMA(p, "sync", x1T[m * 128:(m + 1) * 128, :], f_x0f[par],
                 ch=f"xo{par}")
        out_toks.append(tD)
        p.wait("vector", t)
        COPY(p, zbfv[:, m], f_x0f[par])
        xf_tok[par] = (p.tick("vector"), tD)
    p.fence()
    for tb in range(16):
        for k in range(8):
            MM(p, psG[0:128, 0:16], zbfv[:, k, tb * 128:(tb + 1) * 128],
               wabv[:, 2, k], start=(k == 0), stop=False)
        MM(p, psG[0:128, 0:16], sbc["c_ones1b"][:, :], bggb[:, :],
           start=False, stop=True)
        t = p.tick("tensor"); p.wait("scalar", t)
        ACT(p, gate[:, tb * 16:(tb + 1) * 16], psG[0:128, 0:16], AF.Exp)
        t = p.tick("scalar"); p.wait("vector", t)
        gsl_ = gate[:, tb * 16:(tb + 1) * 16]
        REDUCE(p, scr16[:, 0:1], gsl_, ALU.add)
        t = p.tick("vector"); p.wait("vector", t)
        p.op("vector", lambda v, o=scr16[:, 1:2], i=scr16[:, 0:1]:
             v.reciprocal(out=o, in_=i))
        t = p.tick("vector"); p.wait("vector", t)
        TS(p, gsl_, gsl_, scr16[:, 1:2], ALU.mult)
        t = p.tick("vector"); p.wait("vector", t)
        p.op("vector", lambda v, o=m8t[:, :], i=gsl_: v.max(out=o, in_=i))
        t = p.tick("vector"); p.wait("vector", t)
        g2sl = gate2[:, tb * 16:(tb + 1) * 16]
        TS(p, g2sl, gsl_, m8t[:, 1:2], ALU.is_ge)
        t = p.tick("vector"); p.wait("vector", t)
        TT(p, g2sl, g2sl, gsl_, ALU.mult)
        t = p.tick("vector"); p.wait("tensor", t)
    t = p.tick("vector"); p.wait("sync", t)
    out_toks.append(DMA(p, "sync", maskedT, gate2, ch="mo"))
    p.wait_chan_totals("gpsimd", ["xo0", "xo1", "mo"])
    p.op("gpsimd", lambda g: g.engine_nop())
    p.tick("gpsimd")

    p.build()
    return nc



def _run_A(inputs):
    consts = _host_consts_A()
    wA = _layout_A_weights(inputs)
    ncA = build_A()
    x = np.asarray(inputs["L0_attn_x"], dtype=np.float32)
    in_maps = []
    for core in range(8):
        m = {"x0T": np.ascontiguousarray(x[core].T)}
        m.update(wA)
        m.update({k: v for k, v in consts.items() if k in
                  {nm for nm, _, _ in A_CONSTS}})
        in_maps.append(m)
    res = run_bass_kernel_spmd(ncA, in_maps, core_ids=list(range(8)))
    x1 = np.stack([np.ascontiguousarray(res.results[c]["x1T"].T)
                   for c in range(8)])
    masked = np.stack([
        np.ascontiguousarray(
            res.results[c]["maskedT"].reshape(128, 16, 16).transpose(2, 1, 0)
            .reshape(16, 2048, order="F"))
        for c in range(8)])
    masked = np.stack([
        res.results[c]["maskedT"].reshape(128, 16, 16).transpose(2, 1, 0).reshape(16, 16 * 128)
        for c in range(8)])
    return x1, masked


def _relayout_masked(mT):
    return np.ascontiguousarray(
        np.asarray(mT).reshape(128, 16, 16).transpose(2, 1, 0).reshape(16, 2048))


def kernel(**inputs):
    consts_A = _host_consts_A()
    wA = _layout_A_weights(inputs)
    ncA = build_A()
    x = np.asarray(inputs["L0_attn_x"], dtype=np.float32)
    namesA = {nm for nm, _, _ in A_CONSTS}
    in_maps = []
    for core in range(8):
        m = {"x0T": np.ascontiguousarray(x[core].T)}
        m.update(wA)
        m.update({k: v for k, v in consts_A.items() if k in namesA})
        in_maps.append(m)
    resA = run_bass_kernel_spmd(ncA, in_maps, core_ids=list(range(8)))
    masked = [_relayout_masked(resA.results[c]["maskedT"]) for c in range(8)]

    consts_B = _host_consts_B()
    wB = _layout_B_weights(inputs)
    ncB = build_B()
    namesB = {nm for nm, _, _ in B_CONSTS}
    in_mapsB = []
    for core in range(8):
        maskall = np.concatenate(
            [masked[core]] + [masked[j] for j in range(8) if j != core], axis=0)
        m = {"x1T": np.ascontiguousarray(np.asarray(resA.results[core]["x1T"])),
             "maskall": np.ascontiguousarray(maskall)}
        m.update(wB)
        m.update({k: v for k, v in consts_B.items() if k in namesB})
        in_mapsB.append(m)
    resB = run_bass_kernel_spmd(ncB, in_mapsB, core_ids=list(range(8)))
    out = np.stack([np.ascontiguousarray(np.asarray(resB.results[c]["yT"]).T)
                    for c in range(8)])
    return out



B_SMALL = (("b_b1L0", [128, 32]), ("b_gammaL0", [128, 8]), ("b_betaL0", [128, 8]),
           ("b_ln1g", [128, 8]), ("b_ln1b", [128, 8]), ("b_ln2g", [128, 8]),
           ("b_ln2b", [128, 8]), ("b_inpb", [128, 24]), ("b_outpb", [128, 8]),
           ("b_bgrp", [4, 1]), ("b_bexp", [8, 1]), ("b_b1L1", [128, 128]))
B_CONSTS = (("c_ones2d", [128, 128], BF16), ("c_ones2df", [128, 128], F32),
            ("c_sel16f", [128, 16], F32), ("c_esel16", [16, 2048], BF16),
            ("c_P4a", [4, 4], F32),
            ("c_P4b", [4, 4], F32), ("c_L4str", [4, 4], F32),
            ("c_Ppair8", [8, 8], F32), ("c_dup48", [4, 8], F32),
            ("c_even8", [8, 1], F32), ("c_bo2", [128, 2], BF16),
            ("c_bo2T", [2, 128], BF16))


def _host_consts_B():
    c = {}
    c["c_ones2d"] = np.ones((128, 128), BF)
    c["c_ones2df"] = np.ones((128, 128), np.float32)
    sel = np.zeros((128, 16), np.float32)
    for pp in range(128):
        sel[pp, pp % 16] = 1.0
    c["c_sel16f"] = sel
    es = np.zeros((16, 16 * 128), np.float32)
    for e in range(16):
        es[e, e * 128:(e + 1) * 128] = 1.0
    c["c_esel16"] = es.astype(BF)
    P4a = np.zeros((4, 4), np.float32)
    P4b = np.zeros((4, 4), np.float32)
    for r in range(4):
        P4a[r ^ 2, r] = 1.0
        P4b[r ^ 1, r] = 1.0
    c["c_P4a"] = P4a.T.copy()
    c["c_P4b"] = P4b.T.copy()
    L4 = np.zeros((4, 4), np.float32)
    for m in range(4):
        for g in range(m):
            L4[g, m] = 1.0
    c["c_L4str"] = L4
    P8 = np.zeros((8, 8), np.float32)
    for m in range(8):
        P8[m ^ 1, m] = 1.0
    c["c_Ppair8"] = P8
    d48 = np.zeros((4, 8), np.float32)
    for m in range(8):
        d48[m // 2, m] = 1.0
    c["c_dup48"] = d48
    ev = np.zeros((8, 1), np.float32)
    ev[0::2] = 1.0
    c["c_even8"] = ev
    bo = np.zeros((128, 2), np.float32)
    bo[0:64, 0] = 1.0
    bo[64:128, 1] = 1.0
    c["c_bo2"] = bo.astype(BF)
    c["c_bo2T"] = bo.T.astype(BF).copy()
    return c


def _layout_B_weights(w):
    d = {}
    d["b_w1L0"] = np.ascontiguousarray(w["L0_ffn_w1"])
    d["b_w2L0"] = np.ascontiguousarray(w["L0_ffn_w2"])
    d["b_b2L0"] = np.ascontiguousarray(w["L0_ffn_b2"])
    d["b_b1L0"] = np.ascontiguousarray(
        w["L0_ffn_b1"].reshape(16, 2, 128).transpose(2, 0, 1).reshape(128, 32))
    for nm, key in (("b_gammaL0", "L0_ffn_gamma"), ("b_betaL0", "L0_ffn_beta"),
                    ("b_ln1g", "L1_ffn_ln1_g"), ("b_ln1b", "L1_ffn_ln1_b"),
                    ("b_ln2g", "L1_ffn_ln2_g"), ("b_ln2b", "L1_ffn_ln2_b"),
                    ("b_outpb", "L1_ffn_out_proj_b")):
        d[nm] = np.ascontiguousarray(w[key].reshape(8, 128).T)
    d["b_wqkvT"] = np.ascontiguousarray(w["L1_attn_w_qkv"].T)
    d["b_woutT"] = np.ascontiguousarray(w["L1_attn_w_out"].T)
    d["b_winT"] = np.ascontiguousarray(w["L1_ffn_in_proj_w"].T)
    d["b_woutpT"] = np.ascontiguousarray(w["L1_ffn_out_proj_w"].T)
    d["b_inpb"] = np.ascontiguousarray(w["L1_ffn_in_proj_b"].reshape(24, 128).T)
    d["b_inpbrow"] = np.ascontiguousarray(w["L1_ffn_in_proj_b"][2048:3072].reshape(1, 1024))
    d["b_wgrp"] = np.ascontiguousarray(w["L1_ffn_w_group"])
    d["b_bgrp"] = np.ascontiguousarray(w["L1_ffn_b_group"].reshape(4, 1))
    d["b_wexp"] = np.ascontiguousarray(
        w["L1_ffn_w_expert"].transpose(1, 0, 2).reshape(1024, 8))
    d["b_bexp"] = np.ascontiguousarray(w["L1_ffn_b_expert"].reshape(8, 1))
    d["b_w1L1"] = np.ascontiguousarray(w["L1_ffn_w1"])
    d["b_b1L1"] = np.ascontiguousarray(
        w["L1_ffn_b1"].reshape(8, 16, 128).transpose(2, 0, 1).reshape(128, 128))
    d["b_w2L1"] = np.ascontiguousarray(w["L1_ffn_w2"])
    d["b_b2L1"] = np.ascontiguousarray(w["L1_ffn_b2"])
    return d


def build_B(stop=None):
    nc = bass.Bass("TRN2", target_bir_lowering=False)
    p = Prog(nc)

    x1T = nc.dram_tensor("x1T", [D, S], F32, kind="ExternalInput").ap()
    maskall = nc.dram_tensor("maskall", [128, S], F32, kind="ExternalInput").ap()
    yT = nc.dram_tensor("yT", [D, S], F32, kind="ExternalOutput").ap()
    x2sp = nc.dram_tensor("x2sp", [D, S], F32, kind="Internal").ap()
    x3sp = nc.dram_tensor("x3sp", [D, S], F32, kind="Internal").ap()
    x4sp = nc.dram_tensor("x4sp", [D, S], F32, kind="Internal").ap()

    dram = {}
    dram["b_inpbrow"] = nc.dram_tensor("b_inpbrow", [1, 1024], F32,
                                       kind="ExternalInput").ap()
    for nm, shp in (("b_w1L0", [16, 1024, 256]), ("b_w2L0", [16, 256, 1024]),
                    ("b_b2L0", [16, 1024]), ("b_wqkvT", [1024, 3072]),
                    ("b_woutT", [1024, 1024]), ("b_winT", [1024, 3072]),
                    ("b_woutpT", [1024, 1024]), ("b_wgrp", [1024, 4]),
                    ("b_wexp", [1024, 8]), ("b_w1L1", [8, 1024, 2048]),
                    ("b_w2L1", [8, 2048, 1024]), ("b_b2L1", [8, 1024])):
        dram[nm] = nc.dram_tensor(nm, shp, F32, kind="ExternalInput").ap()
    for nm, shp in B_SMALL:
        dram[nm] = nc.dram_tensor(nm, shp, F32, kind="ExternalInput").ap()
    for nm, shp, dt in B_CONSTS:
        dram[nm] = nc.dram_tensor(nm, shp, dt, kind="ExternalInput").ap()

    AL = nc.alloc_sbuf_tensor
    BB0 = AL("BB0", [128, 16384], BF16).ap()
    BB1 = AL("BB1", [128, 16384], BF16).ap()
    BB2 = AL("BB2", [128, 16384], BF16).ap()
    BB3 = AL("BB3", [128, 16384], BF16).ap()
    WS = AL("WS", [128, 8192], BF16).ap()
    FSB = AL("FSB", [128, 10240], F32).ap()
    RB = FSB[0:16, 0:2048]
    RB2 = FSB[0:16, 2048:4096]
    RB3 = FSB[0:16, 4096:6144]
    RB4 = FSB[0:16, 6144:8192]
    R4zone = (FSB[0:4, 2048:4096], FSB[0:4, 4096:6144], FSB[0:4, 6144:8192],
              FSB[0:4, 8192:10240])
    R8 = FSB[0:8, 2048:4096]
    R8b = FSB[0:8, 8192:10240]
    R8d = FSB[0:8, 0:2048]
    RBb = AL("RBb", [16, 2048], BF16).ap()
    muB = AL("muB", [128, 2048], BF16).ap()
    rsB = AL("rsB", [128, 2048], BF16).ap()
    rows1 = FSB[0:1, 6144:10240]
    nrow = AL("nrow", [16, 16], F32).ap()
    b2L0b = AL("b2L0b", [16, 1024], BF16).ap()
    b2L1b = AL("b2L1b", [8, 1024], BF16).ap()
    cbias = AL("cbiasB", [128, 2], F32).ap()
    sbc = {}
    for nm, shp, dt in B_CONSTS:
        sbc[nm] = AL(nm + "_s", shp, dt).ap()
    sbp = {}
    for nm, shp in B_SMALL:
        sbp[nm] = AL(nm + "_s", shp, F32).ap()

    psE = nc.alloc_psum_tensor("psE", [128, 2048], F32).ap()
    psF = nc.alloc_psum_tensor("psF", [128, 2048], F32).ap()

    f_slab = [FSB[:, 0:2048], FSB[:, 2048:4096]]
    f_scr = FSB[:, 4096:6144]
    f_scr2 = FSB[:, 6144:8192]
    Psb = WS[:, 0:4096]
    denb = WS[:, 4096:6144]
    vaug = WS[:, 6144:7184]
    wslot = [WS[:, i * 2048:(i + 1) * 2048] for i in range(4)]
    wtok = [None] * 4

    x1bv = BB0.rearrange("p (c t) -> p c t", t=2048)

    for nm, shp, dt in B_CONSTS:
        DMA(p, "sync", sbc[nm][:, :], dram[nm])
    for nm, shp in B_SMALL:
        DMA(p, "sync", sbp[nm][:, :], dram[nm])
    DMA(p, "gpsimd", BB0.rearrange("p (c t) -> p c t", t=2048),
        x1T.rearrange("(c p) t -> p c t", p=128), ch="inig")
    DMA(p, "gpsimd", b2L0b[:, :], dram["b_b2L0"], ch="inig")
    DMA(p, "gpsimd", b2L1b[:, :], dram["b_b2L1"], ch="inig")
    maskf = FSB[:, 4096:6144]
    DMA(p, "sync", maskf, maskall)
    p.fence()
    MEMSET(p, cbias[:, 0:1], 1e-5)
    MEMSET(p, cbias[:, 1:2], 1.0)
    MEMSET(p, BB3[:, :], 0.0)
    p.fence()

    for cg in range(4):
        MM(p, psE[0:16, cg * 512:(cg + 1) * 512], sbc["c_sel16f"][:, :],
           maskf[:, cg * 512:(cg + 1) * 512])
    t = p.tick("tensor"); p.wait("vector", t)
    TS(p, RB, psE[0:16, :], 1e-6, ALU.add)
    p.op("vector", lambda v, o=RB2, i=RB: v.reciprocal(out=o, in_=i))
    TT(p, RB, maskf[0:16, :], RB2, ALU.mult)
    TS(p, RB, RB, 8.0, ALU.mult)
    TS(p, RB2, RB, 0.0, ALU.is_gt)
    REDUCE(p, nrow[:, 0:1], RB2, ALU.add)
    TS(p, nrow[:, 1:2], nrow[:, 0:1], -1.0, ALU.mult, 1024.0, ALU.add)
    TS(p, RB3, RB2, -1.0, ALU.mult, 1.0, ALU.add)
    SCAN(p, RB4, RB3, RB3, 0.0, ALU.add, ALU.bypass)
    TS(p, RB4, RB4, nrow[:, 1:2], ALU.is_le)
    TT(p, RB4, RB4, RB3, ALU.mult)
    TT(p, RB4, RB4, RB2, ALU.add)
    COPY(p, RBb, RB4)
    p.fence()

    def stats_sum(ps_row, src, m):
        for cg in range(4):
            MM(p, ps_row[:, cg * 512:(cg + 1) * 512],
               sbc["c_ones2d"][:, 0:1], src[:, cg * 512:(cg + 1) * 512],
               start=(m == 0), stop=(m == 7))

    def stats_finalize(ps_row, murow, rstdrow, sq_src_chunks, sqbuf):
        t = p.tick("tensor"); p.wait("vector", t)
        TS(p, murow, ps_row[0:1, :], 1.0 / 1024.0, ALU.mult)
        t = p.tick("vector"); p.wait("scalar", t); p.wait("tensor", t)
        for m in range(8):
            ACT(p, sqbuf, sq_src_chunks(m), AF.Square)
            t = p.tick("scalar"); p.wait("tensor", t)
            stats_sum(ps_row, sqbuf, m)
            if m < 7:
                t = p.tick("tensor"); p.wait("scalar", t)
        t = p.tick("tensor"); p.wait("vector", t)
        TS(p, rstdrow, ps_row[0:1, :], 1.0 / 1024.0, ALU.mult)
        TT(p, f_scr[0:1, 0:2048], murow, murow, ALU.mult)
        TT(p, rstdrow, rstdrow, f_scr[0:1, 0:2048], ALU.subtract)
        t = p.tick("vector"); p.wait("scalar", t)
        ACT(p, rstdrow, rstdrow, AF.Ln, bias=cbias[0:1, 0:1])
        ACT(p, rstdrow, rstdrow, AF.Exp, scale=-0.5)

    def bcast_mu_rs(murow, rstdrow, murb_, rsrb_):
        t = p.tick("scalar"); p.wait("vector", t)
        COPY(p, murb_, murow)
        COPY(p, rsrb_, rstdrow)
        t = p.tick("vector"); p.wait("tensor", t)
        for cg in range(4):
            MM(p, psE[:, cg * 512:(cg + 1) * 512], sbc["c_ones2d"][0:1, :],
               murb_[0:1, cg * 512:(cg + 1) * 512])
        t = p.tick("tensor"); p.wait("vector", t)
        COPY(p, muB, psE)
        t = p.tick("vector"); p.wait("tensor", t)
        for cg in range(4):
            MM(p, psE[:, cg * 512:(cg + 1) * 512], sbc["c_ones2d"][0:1, :],
               rsrb_[0:1, cg * 512:(cg + 1) * 512])
        t = p.tick("tensor"); p.wait("vector", t)
        COPY(p, rsB, psE)
        t = p.tick("vector"); p.wait("tensor", t); p.wait("sync", t)

    f_murow = rows1[0:1, 0:2048]
    f_rsrow = rows1[0:1, 2048:4096]
    mu_b = muB[0:1, :]
    rs_b = rsB[0:1, :]

    accv = BB3.rearrange("p (c t) -> p c t", t=2048)
    h_sb = [BB1[:, 0:2048], BB1[:, 2048:4096]]
    evac_tok = None
    mask_tok = None
    for e in range(16):
        s1, s2 = (2 * e) % 4, (2 * e + 1) % 4
        if wtok[s1] is not None:
            p.wait("gpsimd", wtok[s1])
        tW1 = DMA(p, "gpsimd", wslot[s1].rearrange("p (c f) -> p c f", f=256),
                  dram["b_w1L0"][e].rearrange("(c p) f -> p c f", p=128),
                  ch=f"wl{s1}")
        if wtok[s2] is not None:
            p.wait("gpsimd", wtok[s2])
        tW2 = DMA(p, "gpsimd", wslot[s2].rearrange("p (c d) -> p c d", d=1024),
                  dram["b_w2L0"][e].rearrange("(c p) d -> p c d", p=128),
                  ch=f"wl{s2}")
        w1v = wslot[s1].rearrange("p (c f) -> p c f", f=256)
        w2v = wslot[s2].rearrange("p (c d) -> p c d", d=1024)
        p.wait("tensor", tW1)
        if evac_tok is not None:
            p.wait("tensor", evac_tok)
        for m in range(2):
            ps = psE if m == 0 else psF
            for cg in range(4):
                for k in range(8):
                    MM(p, ps[:, cg * 512:(cg + 1) * 512],
                       w1v[:, k, m * 128:(m + 1) * 128],
                       x1bv[:, k, cg * 512:(cg + 1) * 512],
                       start=(k == 0), stop=(k == 7))
        tMM = p.tick("tensor")
        wtok[s1] = tMM
        p.wait("scalar", tMM)
        if mask_tok is not None:
            p.wait("scalar", mask_tok)
        ACT(p, h_sb[0], psE, AF.Gelu, bias=sbp["b_b1L0"][:, 2 * e:2 * e + 1])
        ACT(p, h_sb[1], psF, AF.Gelu, bias=sbp["b_b1L0"][:, 2 * e + 1:2 * e + 2])
        tEv = p.tick("scalar")
        p.wait("tensor", tEv)
        for cg in range(4):
            MM(p, psE[:, cg * 512:(cg + 1) * 512],
               sbc["c_esel16"][:, e * 128:(e + 1) * 128],
               RBb[:, cg * 512:(cg + 1) * 512])
        t = p.tick("tensor"); p.wait("vector", t)
        TT(p, h_sb[0], h_sb[0], psE, ALU.mult)
        TT(p, h_sb[1], h_sb[1], psE, ALU.mult)
        mask_tok = p.tick("vector")
        p.wait("tensor", mask_tok, tW2)
        stt_tok = [None, None]
        for m8 in range(8):
            psw = psF if m8 % 2 == 0 else psE
            if stt_tok[m8 % 2] is not None:
                p.wait("tensor", stt_tok[m8 % 2])
            for cg in range(4):
                pt = psw[:, cg * 512:(cg + 1) * 512]
                MM(p, pt, w2v[:, 0, m8 * 128:(m8 + 1) * 128],
                   h_sb[0][:, cg * 512:(cg + 1) * 512], start=True, stop=False)
                MM(p, pt, w2v[:, 1, m8 * 128:(m8 + 1) * 128],
                   h_sb[1][:, cg * 512:(cg + 1) * 512], start=False, stop=False)
                MM(p, pt, b2L0b[:, m8 * 128:(m8 + 1) * 128],
                   RBb[:, cg * 512:(cg + 1) * 512], start=False, stop=True)
            t = p.tick("tensor"); p.wait("vector", t)
            STT(p, accv[:, m8], psw, 1.0, accv[:, m8], ALU.bypass, ALU.add)
            stt_tok[m8 % 2] = p.tick("vector")
        p.wait("tensor", stt_tok[0], stt_tok[1])
        wtok[s2] = p.last_tok["tensor"]
        evac_tok = p.last_tok["vector"]
    p.fence()
    if stop == "B2":
        _dump_stop(p, nc, {"acc": BB3})
        return nc

    zbv = BB1.rearrange("p (c t) -> p c t", t=2048)
    x2bv = BB2.rearrange("p (c t) -> p c t", t=2048)
    sl_tok = [None, None]
    for m in range(8):
        par = m % 2
        if sl_tok[par] is not None:
            p.wait("sync", sl_tok[par])
        tX = DMA(p, "sync", f_slab[par], x1T[m * 128:(m + 1) * 128, :], ch=f"xf{par}")
        p.wait("vector", tX)
        STT(p, zbv[:, m], f_slab[par], 1.0, accv[:, m], ALU.bypass, ALU.add)
        tZ = p.tick("vector")
        sl_tok[par] = tZ
        p.wait("tensor", tZ)
        stats_sum(psF[0:1], zbv[:, m], m)
    stats_finalize(psF[0:1], f_murow, f_rsrow, lambda m: zbv[:, m], BB2[:, 0:2048])
    bcast_mu_rs(f_murow, f_rsrow, mu_b, rs_b)
    sl_tok = [None, None]
    for m in range(8):
        par = m % 2
        if sl_tok[par] is not None:
            p.wait("sync", sl_tok[par])
        tX = DMA(p, "sync", f_slab[par], x1T[m * 128:(m + 1) * 128, :], ch=f"xf{par}")
        TT(p, f_scr, zbv[:, m], muB, ALU.subtract)
        TT(p, f_scr, f_scr, rsB, ALU.mult)
        TS(p, f_scr, f_scr, sbp["b_gammaL0"][:, m:m + 1], ALU.mult,
           sbp["b_betaL0"][:, m:m + 1], ALU.add)
        p.wait("vector", tX)
        TT(p, zbv[:, m], f_scr, f_slab[par], ALU.add)
        tU = p.tick("vector")
        sl_tok[par] = tU
        p.wait("tensor", tU)
        stats_sum(psF[0:1], zbv[:, m], m)
    stats_finalize(psF[0:1], f_murow, f_rsrow, lambda m: zbv[:, m], BB2[:, 0:2048])
    bcast_mu_rs(f_murow, f_rsrow, mu_b, rs_b)
    xs_tok = [None, None]
    for m in range(8):
        par = m % 2
        if xs_tok[par] is not None:
            p.wait("vector", *xs_tok[par])
        fs = f_slab[par]
        TT(p, fs, zbv[:, m], muB, ALU.subtract)
        TT(p, fs, fs, rsB, ALU.mult)
        t = p.tick("vector"); p.wait("sync", t)
        tD = DMA(p, "sync", x2sp[m * 128:(m + 1) * 128, :], fs, ch=f"xo{par}")
        p.wait("vector", t)
        COPY(p, x2bv[:, m], fs)
        xs_tok[par] = (p.tick("vector"), tD)
    p.fence()
    if stop == "B3":
        _dump_stop(p, nc, {"x2b": BB2})
        return nc

    qv = BB0.rearrange("p (c t) -> p c t", t=2048)
    ktm = BB1.rearrange("p (b f) -> p b f", f=1024)
    vtm = BB3.rearrange("p (b f) -> p b f", f=1024)

    def colblock_load(dst_slot, wsrc, c0, slot):
        if wtok[slot] is not None:
            p.wait("gpsimd", wtok[slot])
        return DMA(p, "gpsimd",
                   wslot[slot][:, 0:1024].rearrange("p (c m) -> p c m", m=128),
                   wsrc.rearrange("(c p) m -> p c m", p=128)[:, :, c0:c0 + 128],
                   ch=f"wl{slot}")

    ev_tok = None
    for m in range(8):
        slot = m % 4
        tL = colblock_load(wslot[slot], dram["b_wqkvT"], m * 128, slot)
        wv_ = wslot[slot].rearrange("p (c m) -> p c m", m=128)
        p.wait("tensor", tL)
        if ev_tok is not None:
            p.wait("tensor", ev_tok)
        ps = psE if m % 2 == 0 else psF
        for cg in range(4):
            for k in range(8):
                MM(p, ps[:, cg * 512:(cg + 1) * 512], wv_[:, k],
                   x2bv[:, k, cg * 512:(cg + 1) * 512], start=(k == 0), stop=(k == 7))
        tMM = p.tick("tensor")
        wtok[slot] = tMM
        p.wait("scalar", tMM)
        ACT(p, f_scr, ps, AF.Relu)
        t = p.tick("scalar"); p.wait("vector", t)
        TS(p, f_scr2, ps, 0.0, ALU.min)
        t = p.tick("vector"); p.wait("scalar", t)
        ACT(p, f_scr2, f_scr2, AF.Exp)
        t = p.tick("scalar"); p.wait("vector", t)
        TT(p, qv[:, m], f_scr, f_scr2, ALU.add)
        ev_tok = p.tick("vector")
    wpair = [WS[:, 0:4096].rearrange("p (c m) -> p c m", m=512),
             WS[:, 4096:8192].rearrange("p (c m) -> p c m", m=512)]
    for j, (base, dst) in enumerate(((1024, ktm), (2048, vtm))):
        ptoks = []
        for fg in range(2):
            for sl_ in ((0, 1) if fg == 0 else (2, 3)):
                if wtok[sl_] is not None:
                    p.wait("gpsimd", wtok[sl_])
            ptoks.append(DMA(p, "gpsimd", wpair[fg],
                             dram["b_wqkvT"].rearrange("(c p) m -> p c m", p=128)
                             [:, :, base + fg * 512:base + (fg + 1) * 512],
                             ch=f"wl{fg}"))
        p.wait("tensor", *ptoks)
        if ev_tok is not None:
            p.wait("tensor", ev_tok)
        par_tok = [None, None]
        scr_tok = [None, None]
        for tb in range(16):
            ps = psE if tb % 2 == 0 else psF
            if par_tok[tb % 2] is not None:
                p.wait("tensor", par_tok[tb % 2])
            for fg in range(2):
                wv_ = wpair[fg]
                for k in range(8):
                    MM(p, ps[:, fg * 512:(fg + 1) * 512],
                       x2bv[:, k, tb * 128:(tb + 1) * 128], wv_[:, k, 0:512],
                       start=(k == 0), stop=(k == 7))
            tMM = p.tick("tensor")
            p.wait("scalar", tMM)
            if j == 0:
                hc = (tb % 2) * 1024
                if scr_tok[tb % 2] is not None:
                    p.wait("scalar", scr_tok[tb % 2])
                ACT(p, f_scr[:, hc:hc + 1024], ps[:, 0:1024], AF.Relu)
                t = p.tick("scalar"); p.wait("vector", t)
                TS(p, f_scr2[:, hc:hc + 1024], ps[:, 0:1024], 0.0, ALU.min)
                t = p.tick("vector"); p.wait("scalar", t)
                ACT(p, f_scr2[:, hc:hc + 1024], f_scr2[:, hc:hc + 1024], AF.Exp)
                t = p.tick("scalar"); p.wait("vector", t)
                TT(p, dst[:, tb], f_scr[:, hc:hc + 1024],
                   f_scr2[:, hc:hc + 1024], ALU.add)
                ev_tok = p.tick("vector")
                par_tok[tb % 2] = ev_tok
                scr_tok[tb % 2] = ev_tok
            else:
                ACT(p, dst[:, tb], ps[:, 0:1024], AF.Identity)
                ev_tok = p.tick("scalar")
                par_tok[tb % 2] = ev_tok
        for sl_ in range(4):
            wtok[sl_] = p.last_tok["tensor"]
    p.fence()
    kvps = psE[:, 0:512]
    ksps = psE[:, 512:520]
    for par0 in (0, 1):
        for h in range(par0, 16, 2):
            for tb in range(16):
                MM(p, kvps[64 * (h % 2):64 * (h % 2) + 64, (h // 2) * 64:(h // 2) * 64 + 64],
                   ktm[:, tb, h * 64:(h + 1) * 64], vtm[:, tb, h * 64:(h + 1) * 64],
                   start=(tb == 0), stop=(tb == 15))
            for tb in range(16):
                MM(p, ksps[64 * (h % 2):64 * (h % 2) + 64, (h // 2):(h // 2) + 1],
                   ktm[:, tb, h * 64:(h + 1) * 64], sbc["c_ones2d"][:, 0:1],
                   start=(tb == 0), stop=(tb == 15))
        t = p.tick("tensor")
        p.wait("tensor", t)
    t = p.last_tok["tensor"]; p.wait("vector", t)
    kvsb = WS[:, 0:512]
    ksumf = FSB[:, 8200:8208]
    for sl_ in range(4):
        wtok[sl_] = p.last_tok["tensor"]
    COPY(p, kvsb[:, :], kvps)
    COPY(p, ksumf[:, :], ksps)
    t = p.tick("vector"); p.wait("tensor", t)
    out_lin = BB3
    olv = BB3.rearrange("p (c t) -> p c t", t=2048)
    for par0 in (0, 1):
        for h in range(par0, 16, 2):
            for cg in range(4):
                MM(p, psF[64 * (h % 2):64 * (h % 2) + 64, cg * 512:(cg + 1) * 512],
                   kvsb[64 * (h % 2):64 * (h % 2) + 64, (h // 2) * 64:(h // 2) * 64 + 64],
                   qv[64 * (h % 2):64 * (h % 2) + 64, h // 2, cg * 512:(cg + 1) * 512])
            t = p.tick("tensor"); p.wait("vector", t)
            COPY(p, olv[64 * (h % 2):64 * (h % 2) + 64, h // 2, :],
                 psF[64 * (h % 2):64 * (h % 2) + 64, :])
            t = p.tick("vector"); p.wait("tensor", t)
    for pr in range(8):
        TS(p, denb, qv[:, pr, :], ksumf[:, pr:pr + 1], ALU.mult)
        t = p.tick("vector"); p.wait("tensor", t)
        for cg in range(4):
            MM(p, psE[0:2, cg * 512:(cg + 1) * 512], sbc["c_bo2"][:, :],
               denb[:, cg * 512:(cg + 1) * 512])
        t = p.tick("tensor"); p.wait("vector", t)
        TS(p, f_scr2[0:2, 0:2048], psE[0:2, :], 1e-6, ALU.max)
        p.op("vector", lambda v, o=f_scr[0:2, 0:2048], i=f_scr2[0:2, 0:2048]:
             v.reciprocal(out=o, in_=i))
        COPY(p, denb[0:2, 0:2048], f_scr[0:2, 0:2048])
        t = p.tick("vector"); p.wait("tensor", t)
        for cg in range(4):
            MM(p, psE[:, cg * 512:(cg + 1) * 512], sbc["c_bo2T"][:, :],
               denb[0:2, cg * 512:(cg + 1) * 512])
        t = p.tick("tensor"); p.wait("vector", t)
        TT(p, olv[:, pr, :], olv[:, pr, :], psE, ALU.mult)
        t = p.tick("vector"); p.wait("tensor", t)
    p.fence()
    if stop == "B4a":
        _dump_stop(p, nc, {"olin": BB3})
        return nc

    def proj_resid_ln(srcv, wsrcT, c0, resid_sp, zb_t, out_sp, outb_t,
                      bias_col=None, gamma=None, beta=None, sq_scr=None):
        zv = zb_t.rearrange("p (c t) -> p c t", t=2048)
        ov = outb_t.rearrange("p (c t) -> p c t", t=2048)
        ev = None
        stok = [None, None]
        for m in range(8):
            slot = m % 4
            par = m % 2
            tL = colblock_load(wslot[slot], wsrcT, c0 + m * 128, slot)
            wv_ = wslot[slot].rearrange("p (c m) -> p c m", m=128)
            p.wait("tensor", tL)
            if stok[par] is not None:
                p.wait("tensor", stok[par])
            ps = psE if m % 2 == 0 else psF[:, 0:2048]
            for cg in range(4):
                for k in range(8):
                    MM(p, ps[:, cg * 512:(cg + 1) * 512], wv_[:, k],
                       srcv[:, k, cg * 512:(cg + 1) * 512],
                       start=(k == 0), stop=(k == 7))
            tMM = p.tick("tensor")
            wtok[slot] = tMM
            if stok[par] is not None:
                p.wait("sync", stok[par])
            tX = DMA(p, "sync", f_slab[par], resid_sp[m * 128:(m + 1) * 128, :],
                     ch=f"xf{par}")
            p.wait("vector", tMM, tX)
            if bias_col is not None:
                STT(p, zv[:, m], ps, bias_col[:, m:m + 1], f_slab[par],
                    ALU.add, ALU.add)
            else:
                STT(p, zv[:, m], ps, 0.0, f_slab[par], ALU.add, ALU.add)
            ev = p.tick("vector")
            stok[par] = ev
        p.wait("tensor", ev)
        for m in range(8):
            stats_sum(psF[0:1], zv[:, m], m)
        stats_finalize(psF[0:1], f_murow, f_rsrow, lambda m: zv[:, m], sq_scr)
        bcast_mu_rs(f_murow, f_rsrow, mu_b, rs_b)
        xs = [None, None]
        for m in range(8):
            par = m % 2
            if xs[par] is not None:
                p.wait("vector", *xs[par])
            fs = f_slab[par]
            TT(p, fs, zv[:, m], muB, ALU.subtract)
            TT(p, fs, fs, rsB, ALU.mult)
            if gamma is not None:
                TS(p, fs, fs, gamma[:, m:m + 1], ALU.mult, beta[:, m:m + 1], ALU.add)
            t = p.tick("vector")
            tD = None
            if out_sp is not None:
                p.wait("sync", t)
                tD = DMA(p, "sync", out_sp[m * 128:(m + 1) * 128, :], fs,
                         ch=f"xo{par}")
            p.wait("vector", t)
            COPY(p, ov[:, m], fs)
            xs[par] = (p.tick("vector"),) + ((tD,) if tD else ())
        p.fence()

    proj_resid_ln(olv, dram["b_woutT"], 0, x2sp, BB0, x3sp, BB0,
                  sq_scr=BB1[:, 0:2048])
    if stop == "B4":
        _dump_stop(p, nc, {"x3b": BB0})
        return nc

    x3bv = BB0.rearrange("p (c t) -> p c t", t=2048)
    qmv = BB1.rearrange("p (c t) -> p c t", t=2048)
    kmv = BB2.rearrange("p (c t) -> p c t", t=2048)
    vtm2 = BB3.rearrange("p (b f) -> p b f", f=1024)
    inpbf32 = FSB[0:1, 9216:10240]
    tBV = DMA(p, "sync", inpbf32, dram["b_inpbrow"], ch="xf0")
    p.wait("tensor", tBV)
    for j, dstv in ((0, qmv), (1, kmv)):
        evp = [None, None] if j == 0 else [evp[0], evp[1]]
        for m in range(8):
            slot = m % 4
            tL = colblock_load(wslot[slot], dram["b_winT"], j * 1024 + m * 128, slot)
            wv_ = wslot[slot].rearrange("p (c m) -> p c m", m=128)
            p.wait("tensor", tL)
            if evp[m % 2] is not None:
                p.wait("tensor", evp[m % 2])
            ps = psE if m % 2 == 0 else psF[:, 0:2048]
            for cg in range(4):
                for k in range(8):
                    MM(p, ps[:, cg * 512:(cg + 1) * 512], wv_[:, k],
                       x3bv[:, k, cg * 512:(cg + 1) * 512],
                       start=(k == 0), stop=(k == 7))
            tMM = p.tick("tensor")
            wtok[slot] = tMM
            p.wait("scalar", tMM)
            ACT(p, dstv[:, m], ps, AF.Identity,
                bias=sbp["b_inpb"][:, j * 8 + m:j * 8 + m + 1])
            ev = p.tick("scalar")
            evp[m % 2] = ev
    ptoks = []
    for fg in range(2):
        for sl_ in ((0, 1) if fg == 0 else (2, 3)):
            if wtok[sl_] is not None:
                p.wait("gpsimd", wtok[sl_])
        ptoks.append(DMA(p, "gpsimd", wpair[fg],
                         dram["b_winT"].rearrange("(c p) m -> p c m", p=128)
                         [:, :, 2048 + fg * 512:2048 + (fg + 1) * 512],
                         ch=f"wl{fg}"))
    p.wait("tensor", *ptoks)
    if ev is not None:
        p.wait("tensor", ev)
    ev = None
    par_tok = [None, None]
    for tb in range(16):
        ps = psE if tb % 2 == 0 else psF[:, 0:2048]
        if par_tok[tb % 2] is not None:
            p.wait("tensor", par_tok[tb % 2])
        for fg in range(2):
            wv_ = wpair[fg]
            for k in range(8):
                MM(p, ps[:, fg * 512:(fg + 1) * 512],
                   x3bv[:, k, tb * 128:(tb + 1) * 128], wv_[:, k, 0:512],
                   start=(k == 0), stop=False)
            MM(p, ps[:, fg * 512:(fg + 1) * 512], sbc["c_ones2df"][0:1, 0:128],
               inpbf32[0:1, fg * 512:(fg + 1) * 512],
               start=False, stop=True)
        tMM = p.tick("tensor")
        p.wait("scalar", tMM)
        ACT(p, vtm2[:, tb], ps[:, 0:1024], AF.Identity)
        par_tok[tb % 2] = p.tick("scalar")
    for sl_ in range(4):
        wtok[sl_] = p.last_tok["tensor"]
    p.fence()
    if stop == "B5a":
        _dump_stop(p, nc, {"qm": BB1, "km": BB2, "vm": BB3})
        return nc
    mhav = BB0.rearrange("p (c t) -> p c t", t=2048)
    Psbv = [Psb[:, 0:2048], Psb[:, 2048:4096]]
    for h in range(16):
        hb = 64 * (h % 2)
        COPY(p, vaug.rearrange("p (b f) -> p b f", f=65)[:, :, 0:64],
             vtm2[:, :, h * 64:(h + 1) * 64])
        MEMSET(p, vaug.rearrange("p (b f) -> p b f", f=65)[:, :, 64:65], 1.0)
        tV = p.tick("vector")
        p.wait("tensor", tV)
        prev_act = [None, None]
        for t2b in range(16):
            tEs = []
            for half in (0, 1):
                if prev_act[half] is not None:
                    p.wait("tensor", prev_act[half])
                for cg in (2 * half, 2 * half + 1):
                    MM(p, psE[:, cg * 512:(cg + 1) * 512],
                       kmv[hb:hb + 64, h // 2, t2b * 128:(t2b + 1) * 128],
                       qmv[hb:hb + 64, h // 2, cg * 512:(cg + 1) * 512])
                tS = p.tick("tensor")
                p.wait("scalar", tS)
                ACT(p, Psbv[t2b % 2][:, half * 1024:(half + 1) * 1024],
                    psE[:, half * 1024:(half + 1) * 1024], AF.Exp, scale=0.125)
                tE = p.tick("scalar")
                prev_act[half] = tE
                tEs.append(tE)
            for half in (0, 1):
                p.wait("tensor", tEs[half])
                for cg in (2 * half, 2 * half + 1):
                    MM(p, psF[0:65, cg * 512:(cg + 1) * 512],
                       vaug[:, t2b * 65:(t2b + 1) * 65],
                       Psbv[t2b % 2][:, cg * 512:(cg + 1) * 512],
                       start=(t2b == 0), stop=(t2b == 15))
        t = p.tick("tensor"); p.wait("scalar", t); p.wait("vector", t)
        ACT(p, mhav[hb:hb + 64, h // 2, :], psF[0:64, :], AF.Identity)
        tO = p.tick("scalar")
        TS(p, f_scr[64:65, 0:2048], psF[64:65, :], 1e-38, ALU.max)
        p.op("vector", lambda v, o=f_scr2[64:65, 0:2048], i=f_scr[64:65, 0:2048]:
             v.reciprocal(out=o, in_=i))
        COPY(p, denb[64:65, 0:2048], f_scr2[64:65, 0:2048])
        t = p.tick("vector"); p.wait("tensor", t, tO)
        for cg in range(4):
            MM(p, psE[hb:hb + 64, cg * 512:(cg + 1) * 512],
               sbc["c_ones2d"][64:65, 0:64],
               denb[64:65, cg * 512:(cg + 1) * 512])
        t = p.tick("tensor"); p.wait("vector", t)
        TT(p, mhav[hb:hb + 64, h // 2, :], mhav[hb:hb + 64, h // 2, :],
           psE[hb:hb + 64, :], ALU.mult)
        t = p.tick("vector"); p.wait("tensor", t); p.wait("scalar", t)
    p.fence()
    for sl_ in range(4):
        wtok[sl_] = p.last_tok["tensor"]
    if stop == "B5b":
        _dump_stop(p, nc, {"mha": BB0})
        return nc
    proj_resid_ln(mhav, dram["b_woutpT"], 0, x3sp, BB1, x4sp, BB1,
                  bias_col=sbp["b_outpb"], gamma=sbp["b_ln1g"], beta=sbp["b_ln1b"],
                  sq_scr=BB2[:, 0:2048])
    if stop == "B5":
        _dump_stop(p, nc, {"x4b": BB1})
        return nc

    x4bv = BB1.rearrange("p (c t) -> p c t", t=2048)
    wgrps = AL("wgrps", [128, 4 * 8 + 8 * 8], BF16).ap()
    DMA(p, "gpsimd", wgrps.rearrange("p (c e) -> p c e", e=12)[:, :, 0:4],
        dram["b_wgrp"].rearrange("(c p) e -> p c e", p=128), ch="inig")
    DMA(p, "gpsimd", wgrps.rearrange("p (c e) -> p c e", e=12)[:, :, 4:12],
        dram["b_wexp"].rearrange("(c p) e -> p c e", p=128), ch="inig")
    p.fence()
    wgv = wgrps.rearrange("p (c e) -> p c e", e=12)
    for cg in range(4):
        for k in range(8):
            MM(p, psE[0:4, cg * 512:(cg + 1) * 512], wgv[:, k, 0:4],
               x4bv[:, k, cg * 512:(cg + 1) * 512], start=(k == 0), stop=(k == 7))
        for k in range(8):
            MM(p, psF[0:8, cg * 512:(cg + 1) * 512], wgv[:, k, 4:12],
               x4bv[:, k, cg * 512:(cg + 1) * 512], start=(k == 0), stop=(k == 7))
    t = p.tick("tensor"); p.wait("scalar", t); p.wait("vector", t)
    R4a, R4b, R4c, R4d = R4zone
    ACT(p, R4a, psE[0:4, :], AF.Exp, bias=sbp["b_bgrp"][:, 0:1])
    t = p.tick("scalar"); p.wait("vector", t); p.wait("tensor", t)
    for st_, (pm, dst, src) in enumerate(((("c_P4a"), R4b, R4a), (("c_P4b"), R4c, R4b))):
        for cg in range(4):
            MM(p, psE[0:4, cg * 512:(cg + 1) * 512], sbc[pm][:, :],
               src[:, cg * 512:(cg + 1) * 512])
        t = p.tick("tensor"); p.wait("vector", t)
        TT(p, dst, src, psE[0:4, :], ALU.add)
        t = p.tick("vector"); p.wait("tensor", t)
    p.op("vector", lambda v, o=R4d, i=R4c: v.reciprocal(out=o, in_=i))
    TT(p, R4a, R4a, R4d, ALU.mult)
    t = p.tick("vector"); p.wait("tensor", t)
    for st_, (pm, dst, src) in enumerate(((("c_P4a"), R4b, R4a), (("c_P4b"), R4c, R4b))):
        for cg in range(4):
            MM(p, psE[0:4, cg * 512:(cg + 1) * 512], sbc[pm][:, :],
               src[:, cg * 512:(cg + 1) * 512])
        t = p.tick("tensor"); p.wait("vector", t)
        TT(p, dst, src, psE[0:4, :], ALU.max)
        t = p.tick("vector"); p.wait("tensor", t)
    TT(p, R4b, R4a, R4c, ALU.is_equal)
    t = p.tick("vector"); p.wait("tensor", t)
    for cg in range(4):
        MM(p, psE[0:4, cg * 512:(cg + 1) * 512], sbc["c_L4str"][:, :],
           R4b[:, cg * 512:(cg + 1) * 512])
    t = p.tick("tensor"); p.wait("vector", t)
    TS(p, R4d, psE[0:4, :], 0.0, ALU.is_equal)
    TT(p, R4b, R4b, R4d, ALU.mult)
    t = p.tick("vector"); p.wait("tensor", t); p.wait("scalar", t)
    ACT(p, R8, psF[0:8, :], AF.Exp, bias=sbp["b_bexp"][:, 0:1])
    t = p.tick("scalar"); p.wait("tensor", t); p.wait("vector", t)
    for cg in range(4):
        MM(p, psE[0:8, cg * 512:(cg + 1) * 512], sbc["c_Ppair8"][:, :],
           R8[:, cg * 512:(cg + 1) * 512])
    t = p.tick("tensor"); p.wait("vector", t)
    TT(p, R8b, R8, psE[0:8, :], ALU.add)
    p.op("vector", lambda v, o=R8d, i=R8b: v.reciprocal(out=o, in_=i))
    TT(p, R8, R8, R8d, ALU.mult)
    t = p.tick("vector"); p.wait("tensor", t)
    for cg in range(4):
        MM(p, psE[0:8, cg * 512:(cg + 1) * 512], sbc["c_dup48"][:, :],
           R4b[:, cg * 512:(cg + 1) * 512])
    t = p.tick("tensor"); p.wait("vector", t)
    TT(p, R8b, R8, psE[0:8, :], ALU.mult)
    t = p.tick("vector"); p.wait("tensor", t)
    for cg in range(4):
        MM(p, psE[0:8, cg * 512:(cg + 1) * 512], sbc["c_Ppair8"][:, :],
           R8b[:, cg * 512:(cg + 1) * 512])
    t = p.tick("tensor"); p.wait("vector", t)
    TT(p, R8d, R8b, psE[0:8, :], ALU.is_gt)
    TT(p, R8, R8b, psE[0:8, :], ALU.is_equal)
    TS(p, R8, R8, sbc["c_even8"][:, 0:1], ALU.mult)
    TT(p, R8d, R8d, R8, ALU.add)
    t = p.tick("vector"); p.wait("tensor", t)
    for cg in range(4):
        MM(p, psF[0:8, cg * 512:(cg + 1) * 512], sbc["c_dup48"][:, :],
           R4b[:, cg * 512:(cg + 1) * 512])
    t = p.tick("tensor"); p.wait("vector", t)
    TT(p, R8d, R8d, psF[0:8, :], ALU.mult)
    TT(p, R8, R8d, R8b, ALU.mult)
    t = p.tick("vector"); p.wait("tensor", t)
    for cg in range(4):
        MM(p, psE[0:1, cg * 512:(cg + 1) * 512], sbc["c_ones2df"][0:8, 0:1],
           R8[:, cg * 512:(cg + 1) * 512])
    t = p.tick("tensor"); p.wait("vector", t)
    TT(p, rows1[0:1, 0:2048], R4c[0:1, :], psE[0:1, :], ALU.mult)
    t = p.tick("vector"); p.wait("tensor", t)
    for cg in range(4):
        MM(p, psE[0:8, cg * 512:(cg + 1) * 512], sbc["c_ones2df"][0:1, 0:8],
           rows1[0:1, cg * 512:(cg + 1) * 512])
    t = p.tick("tensor"); p.wait("vector", t)
    TT(p, R8b, R8d, psE[0:8, :], ALU.mult)
    COPY(p, RBb[0:8, :], R8b)
    p.fence()
    if stop == "B6":
        _dump_stop(p, nc, {"cm": R8b, "hot": R8d})
        return nc

    hLv = [BB2.rearrange("p (c t) -> p c t", t=2048),
           BB3.rearrange("p (c t) -> p c t", t=2048)]
    acc1 = BB0
    acc1v = BB0.rearrange("p (c t) -> p c t", t=2048)
    MEMSET(p, acc1[:, :], 0.0)
    t = p.tick("vector"); p.wait("tensor", t)
    for e in range(8):
        evp = [None, None]
        for m in range(16):
            slot = m % 4
            tL = colblock_load(wslot[slot], dram["b_w1L1"][e], m * 128, slot)
            wv_ = wslot[slot].rearrange("p (c m) -> p c m", m=128)
            p.wait("tensor", tL)
            if evp[m % 2] is not None:
                p.wait("tensor", evp[m % 2])
            ps = psE if m % 2 == 0 else psF
            for cg in range(4):
                for k in range(8):
                    MM(p, ps[:, cg * 512:(cg + 1) * 512], wv_[:, k],
                       x4bv[:, k, cg * 512:(cg + 1) * 512],
                       start=(k == 0), stop=(k == 7))
            tMM = p.tick("tensor")
            wtok[slot] = tMM
            p.wait("scalar", tMM)
            ACT(p, hLv[m // 8][:, m % 8], ps, AF.Relu,
                bias=sbp["b_b1L1"][:, e * 16 + m:e * 16 + m + 1])
            evp[m % 2] = p.tick("scalar")
        p.wait("tensor", evp[1])
        for cg in range(4):
            MM(p, psE[:, cg * 512:(cg + 1) * 512],
               sbc["c_esel16"][0:8, e * 128:(e + 1) * 128],
               RBb[0:8, cg * 512:(cg + 1) * 512])
        t = p.tick("tensor"); p.wait("vector", t)
        for m in range(16):
            TT(p, hLv[m // 8][:, m % 8], hLv[m // 8][:, m % 8], psE, ALU.mult)
        tMk = p.tick("vector")
        p.wait("tensor", tMk)
        stt_tok = [None, None]
        for m8 in range(8):
            slot = m8 % 4
            if wtok[slot] is not None:
                p.wait("gpsimd", wtok[slot])
            tL = DMA(p, "gpsimd",
                     wslot[slot].rearrange("p (c m) -> p c m", m=128),
                     dram["b_w2L1"][e].rearrange("(c p) m -> p c m", p=128)
                     [:, :, m8 * 128:(m8 + 1) * 128], ch=f"wl{slot}")
            wv_ = wslot[slot].rearrange("p (c m) -> p c m", m=128)
            psw = psF if m8 % 2 == 0 else psE
            p.wait("tensor", tL)
            if stt_tok[m8 % 2] is not None:
                p.wait("tensor", stt_tok[m8 % 2])
            for cg in range(4):
                pt = psw[:, cg * 512:(cg + 1) * 512]
                for k in range(16):
                    MM(p, pt, wv_[:, k],
                       hLv[k // 8][:, k % 8, cg * 512:(cg + 1) * 512],
                       start=(k == 0), stop=False)
                MM(p, pt, b2L1b[:, m8 * 128:(m8 + 1) * 128],
                   RBb[0:8, cg * 512:(cg + 1) * 512], start=False, stop=True)
            t = p.tick("tensor")
            wtok[slot] = t
            p.wait("vector", t)
            STT(p, acc1v[:, m8], psw, 1.0, acc1v[:, m8], ALU.bypass, ALU.add)
            stt_tok[m8 % 2] = p.tick("vector")
        p.wait("tensor", stt_tok[0], stt_tok[1])
    p.fence()
    if stop == "B7":
        _dump_stop(p, nc, {"acc1": BB0})
        return nc

    zbv2 = BB2.rearrange("p (c t) -> p c t", t=2048)
    sl_tok = [None, None]
    for m in range(8):
        par = m % 2
        if sl_tok[par] is not None:
            p.wait("sync", sl_tok[par])
        tX = DMA(p, "sync", f_slab[par], x4sp[m * 128:(m + 1) * 128, :], ch=f"xf{par}")
        p.wait("vector", tX)
        STT(p, zbv2[:, m], f_slab[par], 1.0, acc1v[:, m], ALU.bypass, ALU.add)
        tZ = p.tick("vector")
        sl_tok[par] = tZ
        p.wait("tensor", tZ)
        stats_sum(psF[0:1], zbv2[:, m], m)
    stats_finalize(psF[0:1], f_murow, f_rsrow, lambda m: zbv2[:, m], BB3[:, 0:2048])
    bcast_mu_rs(f_murow, f_rsrow, mu_b, rs_b)
    sl_tok = [None, None]
    for m in range(8):
        par = m % 2
        if sl_tok[par] is not None:
            p.wait("sync", sl_tok[par])
        tX = DMA(p, "sync", f_slab[par], x3sp[m * 128:(m + 1) * 128, :], ch=f"xf{par}")
        TT(p, f_scr, zbv2[:, m], muB, ALU.subtract)
        TT(p, f_scr, f_scr, rsB, ALU.mult)
        TS(p, f_scr, f_scr, sbp["b_ln2g"][:, m:m + 1], ALU.mult,
           sbp["b_ln2b"][:, m:m + 1], ALU.add)
        p.wait("vector", tX)
        TT(p, zbv2[:, m], f_scr, f_slab[par], ALU.add)
        tU = p.tick("vector")
        sl_tok[par] = tU
        p.wait("tensor", tU)
        stats_sum(psF[0:1], zbv2[:, m], m)
    stats_finalize(psF[0:1], f_murow, f_rsrow, lambda m: zbv2[:, m], BB3[:, 0:2048])
    bcast_mu_rs(f_murow, f_rsrow, mu_b, rs_b)
    xs_tok = [None, None]
    for m in range(8):
        par = m % 2
        if xs_tok[par] is not None:
            p.wait("vector", *xs_tok[par])
        fs = f_slab[par]
        TT(p, fs, zbv2[:, m], muB, ALU.subtract)
        TT(p, fs, fs, rsB, ALU.mult)
        t = p.tick("vector"); p.wait("sync", t)
        tD = DMA(p, "sync", yT[m * 128:(m + 1) * 128, :], fs, ch=f"yo{par}")
        xs_tok[par] = (p.tick("vector") if False else t, tD)
    p.wait_chan_totals("gpsimd", ["yo0", "yo1"])
    p.op("gpsimd", lambda g: g.engine_nop())
    p.tick("gpsimd")

    p.build()
    return nc


# version 04
# speedup vs baseline: 1.1324x; 1.0104x over previous
import numpy as np
import ml_dtypes
import concourse.bass as bass
import concourse.mybir as mybir
from concourse.bass_utils import run_bass_kernel_spmd

F32 = mybir.dt.float32
BF16 = mybir.dt.bfloat16
I32 = mybir.dt.int32
AF = mybir.ActivationFunctionType
ALU = mybir.AluOpType
AX = mybir.AxisListType

BF = ml_dtypes.bfloat16

S = 2048
D = 1024
H = 16
HD = 64
NP = 8
C = 128
NCH = S // C

ENGS = ("sync", "tensor", "vector", "scalar", "gpsimd")


class Prog:

    def __init__(self, nc):
        self.nc = nc
        self.q = {e: [] for e in ENGS}
        self.cnt = {e: 0 for e in ENGS}
        self.last_tok = {e: (e, 0) for e in ENGS}
        self._dirty = {e: False for e in ENGS}
        self._last_dma = {e: False for e in ENGS}
        self.chans = {}
        self._fence_seen = {}

    SELF_SERIAL = ("vector", "scalar")

    def op(self, eng, fn, dma=False):
        import traceback
        src = traceback.extract_stack(limit=4)
        if eng in self.SELF_SERIAL:
            if self.cnt[eng] > 0:
                self.q[eng].append(["wait", eng, self.cnt[eng]])
            self.q[eng].append(["op", fn, 0, src, eng])
            self._dirty[eng] = True
            self._last_dma[eng] = dma
            self.tick(eng)
            return
        self.q[eng].append(["op", fn, 0, src, eng])
        self._dirty[eng] = True
        self._last_dma[eng] = dma

    def tick(self, eng):
        ent = self.q[eng][-1]
        assert ent[0] == "op"
        inc = 16 if self._last_dma[eng] else 1
        if ent[2] == 0:
            ent[2] = inc
            self.cnt[eng] += inc
        self._dirty[eng] = False
        self.last_tok[eng] = (eng, self.cnt[eng])
        return self.last_tok[eng]

    def dma(self, eng, out, in_, ch):
        if ch not in self.chans:
            self.chans[ch] = 0
        self.chans[ch] += 16
        tok = ("#" + ch, self.chans[ch])
        self.q[eng].append(["op", lambda g, o=out, i=in_:
                            g.dma_start(out=o, in_=i), 16, None, "#" + ch])
        return tok

    def wait(self, eng, *toks):
        for (e2, v) in toks:
            if v > 0:
                self.q[eng].append(["wait", e2, v])

    def wait_chan_totals(self, eng, chans):
        for ch in chans:
            if self.chans.get(ch, 0) > 0:
                self.q[eng].append(["wait", "#" + ch, self.chans[ch]])

    def fence(self, *engs):
        engs = engs or ENGS
        toks = []
        for e in engs:
            if self._dirty[e]:
                toks.append(self.tick(e))
            elif self.cnt[e] > 0:
                toks.append(self.last_tok[e])
        for ch, cnt in self.chans.items():
            if cnt > 0 and self._fence_seen.get(ch) != cnt:
                toks.append(("#" + ch, cnt))
                self._fence_seen[ch] = cnt
        for e in engs:
            self.wait(e, *[t for t in toks if t[0] != e])
        return toks

    def build(self):
        nc = self.nc
        sems = {}
        ctxs = []
        for e in ENGS:
            cm = nc.semaphore(f"sem_{e}")
            sems[e] = cm.__enter__()
            ctxs.append(cm)
        for ch in self.chans:
            cm = nc.semaphore(f"ch_{ch}")
            sems["#" + ch] = cm.__enter__()
            ctxs.append(cm)
        blkcm = nc.Block()
        blk = blkcm.__enter__()
        for ename in ENGS:
            entries = self.q[ename]

            def body(eng, entries=entries, ename=ename):
                for ent in entries:
                    if ent[0] == "wait":
                        eng.wait_ge(sems[ent[1]], ent[2])
                    else:
                        try:
                            inst = ent[1](eng)
                        except Exception:
                            if ent[3] is not None:
                                print("FAILED OP emitted at:")
                                for fr in ent[3]:
                                    print(f"  {fr.filename}:{fr.lineno} {fr.line}")
                            raise
                        if ent[2]:
                            inst.then_inc(sems[ent[4]], ent[2])

            getattr(blk, ename)(body)
        blkcm.__exit__(None, None, None)
        for cm in ctxs:
            cm.__exit__(None, None, None)



def MM(p, out, lhsT, rhs, start=True, stop=True):
    p.op("tensor", lambda t, o=out, l=lhsT, r=rhs, a=start, b=stop:
         t.matmul(o, l, r, start=a, stop=b))


def DMA(p, eng, out, in_, ch="ini"):
    return p.dma(eng, out, in_, ch)


def ACT(p, out, in_, func, bias=0.0, scale=1.0):
    p.op("scalar", lambda s, o=out, i=in_, f=func, b=bias, sc=scale:
         s.activation(out=o, in_=i, func=f, bias=b, scale=sc))


def TT(p, out, in0, in1, alu, eng="vector"):
    p.op(eng, lambda v, o=out, a=in0, b=in1, op=alu:
         v.tensor_tensor(out=o, in0=a, in1=b, op=op))


def TS(p, out, in0, s1, op0, s2=None, op1=None, eng="vector"):
    if s2 is None:
        p.op(eng, lambda v, o=out, a=in0, x=s1, op=op0:
             v.tensor_single_scalar(out=o, in_=a, scalar=x, op=op))
    else:
        p.op(eng, lambda v, o=out, a=in0, x=s1, y=s2, oa=op0, ob=op1:
             v.tensor_scalar(out=o, in0=a, scalar1=x, scalar2=y, op0=oa, op1=ob))


def STT(p, out, in0, scal, in1, op0, op1, eng="vector"):
    p.op(eng, lambda v, o=out, a=in0, s=scal, b=in1, oa=op0, ob=op1:
         v.scalar_tensor_tensor(out=o, in0=a, scalar=s, in1=b, op0=oa, op1=ob))


def COPY(p, out, in_, eng="vector"):
    p.op(eng, lambda v, o=out, i=in_: v.tensor_copy(out=o, in_=i))


def SCAN(p, out, d0, d1, init, op0, op1):
    p.op("vector", lambda v, o=out, a=d0, b=d1, i=init, oa=op0, ob=op1:
         v.tensor_tensor_scan(out=o, data0=a, data1=b, initial=i, op0=oa, op1=ob))


def REDUCE(p, out, in_, alu, eng="vector"):
    p.op(eng, lambda v, o=out, i=in_, op=alu:
         v.tensor_reduce(out=o, in_=i, axis=AX.X, op=op))


def MEMSET(p, ap, val, eng="vector"):
    p.op(eng, lambda v, a=ap, x=val: v.memset(a, x))


def _dump_stop(p, nc, tiles):
    p.fence()
    for nm, ap in tiles.items():
        d = nc.dram_tensor("dbg_" + nm, list(ap.shape), ap.dtype,
                           kind="ExternalOutput").ap()
        DMA(p, "sync", d, ap, ch="dbg")
    p.wait_chan_totals("gpsimd", ["dbg"])
    p.op("gpsimd", lambda g: g.engine_nop())
    p.tick("gpsimd")
    p.build()



def _host_consts_A():
    c = {}
    c["c_id16f"] = np.vstack([np.eye(16)] * 3).astype(np.float32)
    c["c_id64b"] = np.vstack([np.eye(64), np.eye(64)]).astype(BF)
    c["c_id128b"] = np.eye(128, dtype=BF)
    c["c_ones1f"] = np.ones((1, 128), np.float32)
    c["c_ones1b"] = np.ones((1, 128), BF)
    c["c_ones128b"] = np.ones((128, 1), BF)
    t = np.arange(128)
    ls = (t[None, :] < t[:, None]).astype(np.float32)
    us = (t[None, :] > t[:, None]).astype(np.float32)
    ui = (t[None, :] >= t[:, None]).astype(np.float32)
    c["c_maskLsNegB"] = np.tile(-ls, (1, 16)).astype(BF)
    c["c_maskUsNegB"] = np.tile(-us, (1, 16)).astype(BF)
    c["c_maskUiB"] = np.tile(ui, (1, 16)).astype(BF)
    sel = np.zeros((16, 8 * 128), np.float32)
    for hp in range(8):
        sel[2 * hp, hp * 128: hp * 128 + 64] = 1.0
        sel[2 * hp + 1, hp * 128 + 64: hp * 128 + 128] = 1.0
    c["c_selpairB"] = sel.astype(BF)
    bo = np.zeros((128, 2), np.float32)
    bo[0:64, 0] = 1.0
    bo[64:128, 1] = 1.0
    c["c_bo2"] = bo.astype(BF)
    c["c_bo2T"] = bo.T.astype(BF).copy()
    bd = np.zeros((16, 1024), np.float32)
    for g in range(16):
        bd[g, g * 64:(g + 1) * 64] = 1.0
    c["c_BD16f"] = bd
    c["c_ones16_64f"] = np.ones((16, 64), np.float32)
    return c


def _layout_A_weights(w):
    d = {}
    d["a_wq"] = np.ascontiguousarray(w["L0_attn_Wq"])
    d["a_wk"] = np.ascontiguousarray(w["L0_attn_Wk"])
    d["a_wv"] = np.ascontiguousarray(w["L0_attn_Wv"])
    d["a_wg"] = np.ascontiguousarray(w["L0_attn_Wg"])
    d["a_wo"] = np.ascontiguousarray(w["L0_attn_Wo"])
    d["a_wa"] = np.ascontiguousarray(w["L0_attn_Wa"])
    d["a_wb"] = np.ascontiguousarray(w["L0_attn_Wb"])
    for nm, key in (("a_bq", "L0_attn_bq"), ("a_bk", "L0_attn_bk"),
                    ("a_bv", "L0_attn_bv"), ("a_bg", "L0_attn_bg"),
                    ("a_bo", "L0_attn_bo"), ("a_gns", "L0_attn_gn_scale"),
                    ("a_gnb", "L0_attn_gn_bias")):
        d[nm] = np.ascontiguousarray(w[key].reshape(8, 128).T)
    d["a_ba"] = np.ascontiguousarray(w["L0_attn_ba"].reshape(16, 1))
    d["a_bb"] = np.ascontiguousarray(w["L0_attn_bb"].reshape(16, 1))
    for nm, key in (("a_qc", "L0_attn_q_conv"), ("a_kc", "L0_attn_k_conv"),
                    ("a_vc", "L0_attn_v_conv")):
        d[nm] = np.ascontiguousarray(
            w[key].reshape(8, 128, 4).transpose(1, 0, 2).reshape(128, 32))
    d["a_wgg"] = np.ascontiguousarray(w["L0_ffn_wg"])
    d["a_bgg"] = np.ascontiguousarray(w["L0_ffn_bg"].reshape(1, 16))
    return d




def build_A(stop=None, nch=NCH, nstage=99):
    nc = bass.Bass("TRN2", target_bir_lowering=False)
    p = Prog(nc)

    x0T = nc.dram_tensor("x0T", [D, S], F32, kind="ExternalInput").ap()
    x1T = nc.dram_tensor("x1T", [D, S], F32, kind="ExternalOutput").ap()
    maskedT = nc.dram_tensor("maskedT", [128, 256], F32, kind="ExternalOutput").ap()
    spill = nc.dram_tensor("spillA", [3, NCH, 128, NP * C], BF16, kind="Internal").ap()

    dram = {}
    for nm in ("a_wq", "a_wk", "a_wv", "a_wg", "a_wo"):
        dram[nm] = nc.dram_tensor(nm, [D, D], F32, kind="ExternalInput").ap()
    for nm in ("a_wa", "a_wb", "a_wgg"):
        dram[nm] = nc.dram_tensor(nm, [D, 16], F32, kind="ExternalInput").ap()
    small_params = (("a_bq", [128, 8]), ("a_bk", [128, 8]), ("a_bv", [128, 8]),
                    ("a_bg", [128, 8]), ("a_bo", [128, 8]), ("a_gns", [128, 8]),
                    ("a_gnb", [128, 8]), ("a_ba", [16, 1]), ("a_bb", [16, 1]),
                    ("a_qc", [128, 32]), ("a_kc", [128, 32]), ("a_vc", [128, 32]),
                    ("a_bgg", [1, 16]))
    for nm, shp in small_params:
        dram[nm] = nc.dram_tensor(nm, shp, F32, kind="ExternalInput").ap()
    cshapes = (("c_id16f", [48, 16], F32), ("c_id64b", [128, 64], BF16),
               ("c_id128b", [128, 128], BF16),
               ("c_ones1f", [1, 128], F32), ("c_ones1b", [1, 128], BF16),
               ("c_ones128b", [128, 1], BF16),
               ("c_maskLsNegB", [128, 2048], BF16), ("c_maskUsNegB", [128, 2048], BF16),
               ("c_maskUiB", [128, 2048], BF16), ("c_selpairB", [16, 1024], BF16),
               ("c_bo2", [128, 2], BF16), ("c_bo2T", [2, 128], BF16))
    for nm, shp, dt in cshapes:
        dram[nm] = nc.dram_tensor(nm, shp, dt, kind="ExternalInput").ap()

    A = nc.alloc_sbuf_tensor
    x0b = A("x0b", [128, 8 * 2048], BF16).ap()
    chain = A("chain", [128, 8 * 2048], BF16).ap()
    outs = A("outs", [128, 8 * 2048], BF16).ap()
    wbuf = A("wbuf", [128, 8 * 1024], BF16).ap()
    smal = A("smal", [128, 8 * 1024], BF16).ap()
    smal2 = A("smal2", [128, 4 * 1024], BF16).ap()
    fsc = A("fsc", [128, 10240], F32).ap()
    stf = A("stf", [128, 512], F32).ap()
    stb = A("stb", [128, 512], BF16).ap()
    wvt = A("wvt", [128, 1024], BF16).ap()
    gate = A("gate", [128, 256], F32).ap()
    gate2 = A("gate2", [128, 256], F32).ap()
    m8t = A("m8t", [128, 8], F32).ap()
    mk16 = A("mk16", [128, 16], F32).ap()
    betaBb = A("betaBb", [16, 2048], BF16).ap()
    normb = A("normb", [2, 2048], BF16).ap()
    cumTf = A("cumTf", [128, 16], F32).ap()
    gTnegf = A("gTnegf", [128, 16], F32).ap()
    gslb = A("gslb", [16, 128], BF16).ap()
    dslb = A("dslb", [16, 128], BF16).ap()
    gsl32 = A("gsl32", [16, 128], F32).ap()
    gccol = A("gccol", [128, 8], BF16).ap()
    gcb16 = A("gcb16", [16, 1], BF16).ap()
    statm = A("statm", [2, 8], F32).ap()
    statq = A("statq", [2, 8], F32).ap()
    statv = A("statv", [2, 8], F32).ap()
    statmb = A("statmb", [2, 8], BF16).ap()
    statrb = A("statrb", [2, 8], BF16).ap()
    gcol2 = A("gcol2", [128, 2], BF16).ap()
    murb = A("murb", [1, 2048], BF16).ap()
    rsrb = A("rsrb", [1, 2048], BF16).ap()
    sbc = {}
    for nm, shp, dt in cshapes:
        sbc[nm] = A(nm + "_s", shp, dt).ap()
    sbp = {}
    for nm, shp in small_params:
        sbp[nm] = A(nm + "_s", shp, F32).ap()
    wab = A("wab", [128, 8 * 16 * 3], BF16).ap()
    bggb = A("bggb", [1, 16], BF16).ap()

    f_tmpd = fsc[:, 0:2048]
    f_x0fA = fsc[:, 0:2048]
    f_x0fB = fsc[:, 2048:4096]
    f_p1a = fsc[0:16, 2048:4096]
    f_nrmA = fsc[32:34, 2048:4096]
    f_nrmB = fsc[64:66, 2048:4096]
    f_cum = fsc[0:16, 4096:6144]
    f_rev = fsc[32:48, 4096:6144]
    f_beta = fsc[64:80, 4096:6144]
    f_lnA = fsc[0:16, 6144:8192]
    f_murow = fsc[32:33, 6144:8192]
    f_cumflat = fsc[0:1, 6144:8192]
    f_rstdrow = fsc[64:65, 8192:10240]
    f_mu2row = fsc[32:33, 8192:10240]

    cs = [chain[:, i * 2048:(i + 1) * 2048] for i in range(8)]
    sY1, sZ1, sAt, s3, s4, s5, s6, sTmp = cs

    pairbuf = [smal[:, 0:2048], smal[:, 2048:4096]]
    sqbuf = [smal[:, 4096:5120], smal[:, 5120:6144]]
    convsc = [smal[:, 6144:6704], smal[:, 6704:7264]]
    qch = [smal[:, 0:1024], smal[:, 1024:2048]]
    kch = [smal[:, 2048:3072], smal[:, 3072:4096]]
    vch = [smal[:, 4096:5120], smal[:, 5120:6144]]
    kstar = smal[:, 6144:7168]
    qplus = smal[:, 7168:8192]
    muBCb = smal[:, 0:2048]
    rsBCb = smal[:, 2048:4096]
    sqgn = [smal[:, 4096:6144], smal[:, 6144:8192]]
    vTsb = smal2[:, 0:1024]
    vTsb = smal2[:, 3072:4096]
    kdec2dm = smal2[:, 1024:2048]
    kdec2tm = smal2[:, 2048:3072]
    gtmp = [smal2[:, 0:2048], smal2[:, 2048:4096]]

    psA = nc.alloc_psum_tensor("psA", [128, 2048], F32).ap()
    psB = nc.alloc_psum_tensor("psB", [128, 1024], F32).ap()
    psC = nc.alloc_psum_tensor("psC", [128, 512], F32).ap()
    psD = nc.alloc_psum_tensor("psD", [128, 512], F32).ap()

    for nm, shp, dt in cshapes:
        DMA(p, "sync", sbc[nm][:, :], dram[nm])
    for nm, shp in small_params:
        DMA(p, "sync", sbp[nm][:, :], dram[nm])
    DMA(p, "gpsimd", x0b.rearrange("p (c t) -> p c t", t=2048),
        x0T.rearrange("(c p) t -> p c t", p=128), ch="inig")
    for i, nm in enumerate(("a_wa", "a_wb", "a_wgg")):
        DMA(p, "gpsimd", wab.rearrange("p (j c e) -> p j c e", j=3, c=8)[:, i],
            dram[nm].rearrange("(c p) e -> p c e", p=128))
    COPY(p, bggb[:, :], sbp["a_bgg"][:, :])
    MEMSET(p, cbias[:, 0:1], 1e-5)
    MEMSET(p, cbias[:, 1:2], 1.0)
    p.fence()

    wa_v = wab.rearrange("p (j c e) -> p j c e", j=3, c=8)
    x0v = x0b.rearrange("p (c t) -> p c t", t=2048)

    def ab_logits(j):
        for cg in range(4):
            for k in range(8):
                MM(p, psA[0:16, cg * 512:(cg + 1) * 512], wa_v[:, j, k],
                   x0v[:, k, cg * 512:(cg + 1) * 512], start=(k == 0), stop=(k == 7))

    ab_logits(0)
    tA = p.tick("tensor")
    p.wait("scalar", tA)
    ACT(p, f_p1a, psA[0:16, :], AF.Abs, bias=sbp["a_ba"][:, 0:1])
    ACT(p, f_lnA, f_p1a, AF.Exp, scale=-1.0)
    ACT(p, f_p1a, f_lnA, AF.Ln, bias=cbias[0:16, 1:2])
    tS = p.tick("scalar")
    p.wait("vector", tS)
    TS(p, f_lnA, psA[0:16, :], sbp["a_ba"][:, 0:1], ALU.add, 0.0, ALU.min)
    TT(p, f_lnA, f_lnA, f_p1a, ALU.subtract)
    tV = p.tick("vector")
    p.wait("tensor", tV)
    ab_logits(1)
    tA = p.tick("tensor")
    p.wait("scalar", tA)
    ACT(p, f_beta, psA[0:16, :], AF.Sigmoid, bias=sbp["a_bb"][:, 0:1])
    tS = p.tick("scalar")
    p.wait("vector", tS)
    COPY(p, betaBb[:, :], f_beta)
    for c in range(NCH):
        sl = slice(c * C, (c + 1) * C)
        SCAN(p, f_cum[:, sl], f_lnA[:, sl], f_lnA[:, sl], 0.0, ALU.add, ALU.bypass)
        TS(p, f_rev[:, sl], f_cum[:, sl], f_cum[:, c * C + C - 1:c * C + C],
           ALU.subtract, -1.0, ALU.mult)
    p.fence()

    tens_cfg = (("a_wq", "a_bq", "a_qc", True, 0), ("a_wk", "a_bk", "a_kc", True, 1),
                ("a_wv", "a_bv", "a_vc", False, 2))
    wslot = [wbuf[:, i * 1024:(i + 1) * 1024] for i in range(4)]
    wslot_tok = [None] * 4
    pair_tok = [None, None]
    psA_half = [psA[:, 0:1024], psA[:, 1024:2048]]
    psA_tok = [None, None]
    si = 0
    for (wnm, bnm, cnm, do_norm, jidx) in tens_cfg:
        wv_ = dram[wnm].rearrange("(c p) m -> p c m", p=128)
        for hp in range(8):
            par = si % 2
            ws = wslot[si % 8]
            if wslot_tok[si % 8] is not None:
                p.wait("gpsimd", wslot_tok[si % 8])
            DMA(p, "gpsimd", ws.rearrange("p (c m) -> p c m", m=128),
                wv_[:, :, hp * 128:(hp + 1) * 128])
            tL = p.tick("gpsimd")
            p.wait("tensor", tL)
            if psA_tok[par] is not None:
                p.wait("tensor", psA_tok[par])
            if pair_tok[par] is not None:
                p.wait("tensor", pair_tok[par])
            wsv = ws.rearrange("p (c m) -> p c m", m=128)
            for h2 in range(2):
                for k in range(8):
                    MM(p, psA_half[par][:, h2 * 512:(h2 + 1) * 512], wsv[:, k],
                       x0v[:, k, (par * 0 + 1024 * h2 + 512 * 0):0], start=False)
            si += 1
    p.fence()
    p.build()
    return nc


A_SMALL_PARAMS = (("a_bq", [128, 8]), ("a_bk", [128, 8]), ("a_bv", [128, 8]),
                  ("a_bg", [128, 8]), ("a_bo", [128, 8]), ("a_gns", [128, 8]),
                  ("a_gnb", [128, 8]), ("a_ba", [16, 1]), ("a_bb", [16, 1]),
                  ("a_qc", [128, 32]), ("a_kc", [128, 32]), ("a_vc", [128, 32]),
                  ("a_bgg", [1, 16]))
A_CONSTS = (("c_id16f", [48, 16], F32), ("c_id64b", [128, 64], BF16),
            ("c_ones1f", [1, 128], F32), ("c_ones1b", [1, 128], BF16),
            ("c_ones128b", [128, 1], BF16),
            ("c_maskLsNegB", [128, 2048], BF16), ("c_maskUsNegB", [128, 2048], BF16),
            ("c_maskUiB", [128, 2048], BF16), ("c_selpairB", [16, 1024], BF16),
            ("c_bo2", [128, 2], BF16), ("c_bo2T", [2, 128], BF16))


def hsl(ap, h, w=128):
    return ap[64 * (h % 2):64 * (h % 2) + 64, (h // 2) * w:(h // 2) * w + w]


def wave16(p, emit):
    for h in range(0, 16, 2):
        emit(h)
    t = p.tick("tensor")
    p.wait("tensor", t)
    for h in range(1, 16, 2):
        emit(h)
    return p.tick("tensor")


def build_A(stop=None, nch=NCH, nstage=99):
    nc = bass.Bass("TRN2", target_bir_lowering=False)
    p = Prog(nc)

    x0T = nc.dram_tensor("x0T", [D, S], F32, kind="ExternalInput").ap()
    x1T = nc.dram_tensor("x1T", [D, S], F32, kind="ExternalOutput").ap()
    maskedT = nc.dram_tensor("maskedT", [128, 256], F32, kind="ExternalOutput").ap()
    spill = nc.dram_tensor("spillA", [3, NCH, 128, NP * C], BF16, kind="Internal").ap()

    dram = {}
    for nm in ("a_wq", "a_wk", "a_wv", "a_wg", "a_wo"):
        dram[nm] = nc.dram_tensor(nm, [D, D], F32, kind="ExternalInput").ap()
    for nm in ("a_wa", "a_wb", "a_wgg"):
        dram[nm] = nc.dram_tensor(nm, [D, 16], F32, kind="ExternalInput").ap()
    for nm, shp in A_SMALL_PARAMS:
        dram[nm] = nc.dram_tensor(nm, shp, F32, kind="ExternalInput").ap()
    for nm, shp, dt in A_CONSTS:
        dram[nm] = nc.dram_tensor(nm, shp, dt, kind="ExternalInput").ap()

    AL = nc.alloc_sbuf_tensor
    x0b = AL("x0b", [128, 8 * 2048], BF16).ap()
    chain = AL("chain", [128, 8 * 2048], BF16).ap()
    outs = AL("outs", [128, 8 * 2048], BF16).ap()
    wbuf = AL("wbuf", [128, 4 * 1024], BF16).ap()
    smal = AL("smal", [128, 8 * 1024], BF16).ap()
    smal2 = AL("smal2", [128, 4 * 1024], BF16).ap()
    fsc = AL("fsc", [128, 10240], F32).ap()
    stf = AL("stf", [128, 512], F32).ap()
    stb = AL("stb", [128, 512], BF16).ap()
    wvt = AL("wvt", [128, 1024], BF16).ap()
    gate = AL("gate", [128, 256], F32).ap()
    gate2 = AL("gate2", [128, 256], F32).ap()
    m8t = AL("m8t", [128, 8], F32).ap()
    scr16 = AL("scr16", [128, 16], F32).ap()
    betaBb = AL("betaBb", [16, 2048], BF16).ap()
    normb = AL("normb", [2, 2048], BF16).ap()
    cumTf = AL("cumTf", [128, 16], F32).ap()
    gTnegf = AL("gTnegf", [128, 16], F32).ap()
    gslb = AL("gslb", [16, 128], BF16).ap()
    dslb = AL("dslb", [16, 128], BF16).ap()
    gsl32 = AL("gsl32", [16, 128], F32).ap()
    gccol = AL("gccol", [128, 8], F32).ap()
    statm = AL("statm", [2, 8], F32).ap()
    statq = AL("statq", [2, 8], F32).ap()
    statv = AL("statv", [2, 8], F32).ap()
    statmb = AL("statmb", [2, 8], BF16).ap()
    statrb = AL("statrb", [2, 8], BF16).ap()
    gcol2 = AL("gcol2", [128, 2], BF16).ap()
    gcolA = AL("gcolA", [128, 16], F32).ap()
    sbc = {}
    for nm, shp, dt in A_CONSTS:
        sbc[nm] = AL(nm + "_s", shp, dt).ap()
    sbp = {}
    for nm, shp in A_SMALL_PARAMS:
        sbp[nm] = AL(nm + "_s", shp, F32).ap()
    wab = AL("wab", [128, 8 * 16 * 3], BF16).ap()
    murb = betaBb[0:1, 0:2048]
    rsrb = normb[0:1, 0:2048]

    bggb = AL("bggb", [1, 16], BF16).ap()
    cbias = AL("cbias", [128, 2], F32).ap()

    f_tmpd = fsc[:, 0:2048]
    f_x0f = [fsc[:, 0:2048], fsc[:, 2048:4096]]
    f_p1a = fsc[0:16, 2048:4096]
    f_nrmA = fsc[0:2, 0:2048]
    f_nrmB = fsc[0:2, 2048:4096]
    f_beta = fsc[0:16, 4096:6144]
    f_lnA = fsc[0:16, 6144:8192]
    f_rev = fsc[0:16, 6144:8192]
    f_cum = fsc[0:16, 8192:10240]
    f_cumflat = fsc[0:1, 2048:4096]
    f_murow = fsc[0:1, 4096:6144]
    f_mu2row = fsc[0:1, 6144:8192]
    f_rstdrow = fsc[0:1, 8192:10240]

    cslots = [chain[:, i * 2048:(i + 1) * 2048] for i in range(8)]
    sY1, sZ1, sAt, s3, s4, s5, s6, sTmp = cslots

    pairbuf = [smal[:, 0:2048], smal[:, 2048:4096]]
    sqbuf = [smal[:, 4096:6144], smal[:, 6144:8192]]
    qch = [smal[:, 0:1024], smal[:, 1024:2048]]
    kch = [smal[:, 2048:3072], smal[:, 3072:4096]]
    vch = [smal[:, 4096:5120], smal[:, 5120:6144]]
    kstar = smal[:, 6144:7168]
    qplus = smal[:, 7168:8192]
    muBCb = smal[:, 0:2048]
    rsBCb = smal[:, 2048:4096]
    sqgn = [smal[:, 4096:6144], smal[:, 6144:8192]]
    vTsb = smal2[:, 3072:4096]
    kdec2dm = smal2[:, 1024:2048]
    kdec2tm = smal2[:, 2048:3072]
    gtmp = [smal2[:, 0:2048], smal2[:, 2048:4096]]

    psE = nc.alloc_psum_tensor("psE", [128, 2048], F32).ap()
    psF = nc.alloc_psum_tensor("psF", [128, 2048], F32).ap()
    psB = psF[:, 0:1024]
    psSt = psF[:, 1024:1536]
    psG = psF[:, 1536:2048]

    x0v = x0b.rearrange("p (c t) -> p c t", t=2048)
    wabv = wab.rearrange("p (j c e) -> p j c e", j=3, c=8)
    outsv = outs.rearrange("p (r t) -> p r t", t=2048)
    zbfv = chain.rearrange("p (c t) -> p c t", t=2048)

    for nm, shp, dt in A_CONSTS:
        DMA(p, "sync", sbc[nm][:, :], dram[nm])
    for nm, shp in A_SMALL_PARAMS:
        DMA(p, "sync", sbp[nm][:, :], dram[nm])
    DMA(p, "gpsimd", x0b.rearrange("p (c t) -> p c t", t=2048),
        x0T.rearrange("(c p) t -> p c t", p=128), ch="inig")
    for i, nm in enumerate(("a_wa", "a_wb", "a_wgg")):
        DMA(p, "gpsimd", wabv[:, i],
            dram[nm].rearrange("(c p) e -> p c e", p=128), ch="inig")
    p.fence()
    COPY(p, bggb[:, :], sbp["a_bgg"][:, :])
    MEMSET(p, cbias[:, 0:1], 1e-5)
    MEMSET(p, cbias[:, 1:2], 1.0)
    MEMSET(p, stf[:, :], 0.0)
    MEMSET(p, stb[:, :], 0.0)
    MEMSET(p, outs[:, :], 0.0)
    p.fence()

    def ab_logits(j):
        for cg in range(4):
            for k in range(8):
                MM(p, psE[0:16, cg * 512:(cg + 1) * 512], wabv[:, j, k],
                   x0v[:, k, cg * 512:(cg + 1) * 512], start=(k == 0), stop=(k == 7))

    ab_logits(0)
    t = p.tick("tensor"); p.wait("scalar", t); p.wait("vector", t)
    ACT(p, f_p1a, psE[0:16, :], AF.Abs, bias=sbp["a_ba"][:, 0:1])
    ACT(p, f_lnA, f_p1a, AF.Exp, scale=-1.0)
    ACT(p, f_p1a, f_lnA, AF.Ln, bias=cbias[0:16, 1:2])
    t = p.tick("scalar"); p.wait("vector", t)
    TS(p, f_lnA, psE[0:16, :], sbp["a_ba"][:, 0:1], ALU.add, 0.0, ALU.min)
    TT(p, f_lnA, f_lnA, f_p1a, ALU.subtract)
    t = p.tick("vector"); p.wait("tensor", t)
    ab_logits(1)
    t = p.tick("tensor"); p.wait("scalar", t)
    ACT(p, f_beta, psE[0:16, :], AF.Sigmoid, bias=sbp["a_bb"][:, 0:1])
    t = p.tick("scalar"); p.wait("vector", t)
    COPY(p, betaBb[:, :], f_beta)
    for c in range(NCH):
        sl = slice(c * C, (c + 1) * C)
        SCAN(p, f_cum[:, sl], f_lnA[:, sl], f_lnA[:, sl], 0.0, ALU.add, ALU.bypass)
    t = p.tick("vector"); p.wait("vector", t)
    for c in range(NCH):
        sl = slice(c * C, (c + 1) * C)
        TS(p, f_rev[:, sl], f_cum[:, sl], f_cum[:, c * C + C - 1:c * C + C],
           ALU.subtract, -1.0, ALU.mult)
    p.fence()
    if stop == "P1a":
        _dump_stop(p, nc, {"cum": f_cum, "rev": f_rev, "beta": f_beta,
                           "lnA": f_rev})
        return nc

    wslot = [wbuf[:, i * 1024:(i + 1) * 1024] for i in range(4)]
    wslot_tok = [None] * 4
    spill_tok = [None, None]
    evac_tok = [None, None]
    psP = [psE, psF]
    convw = {"a_wq": "a_qc", "a_wk": "a_kc", "a_wv": "a_vc"}
    si = 0
    for (wnm, bnm, do_norm, jidx) in (("a_wq", "a_bq", True, 0),
                                      ("a_wk", "a_bk", True, 1),
                                      ("a_wv", "a_bv", False, 2)):
        wsrc = dram[wnm].rearrange("(c p) m -> p c m", p=128)
        cw = sbp[convw[wnm]]
        for hp in range(8):
            par = si % 2
            slot = si % 4
            ws = wslot[slot]
            wsv = ws.rearrange("p (c m) -> p c m", m=128)
            if wslot_tok[slot] is not None:
                p.wait("gpsimd", wslot_tok[slot])
            tL = DMA(p, "gpsimd", wsv, wsrc[:, :, hp * 128:(hp + 1) * 128],
                     ch=f"wl{slot}")
            p.wait("tensor", tL)
            if evac_tok[par] is not None:
                p.wait("tensor", evac_tok[par])
            ps = psP[par]
            for cg in range(4):
                for k in range(8):
                    MM(p, ps[:, cg * 512:(cg + 1) * 512], wsv[:, k],
                       x0v[:, k, cg * 512:(cg + 1) * 512], start=(k == 0), stop=(k == 7))
            tMM = p.tick("tensor")
            wslot_tok[slot] = tMM
            p.wait("scalar", tMM)
            if spill_tok[par] is not None:
                p.wait("scalar", spill_tok[par])
            pb = pairbuf[par]
            ACT(p, pb, ps, AF.Identity, bias=sbp[bnm][:, hp:hp + 1])
            tEv = p.tick("scalar")
            evac_tok[par] = tEv
            p.wait("vector", tEv)
            co = gtmp[par]
            TS(p, co, pb, cw[:, hp * 4 + 3:hp * 4 + 4], ALU.mult)
            for j, dlt in ((2, 1), (1, 2), (0, 3)):
                STT(p, co[:, dlt:2048], pb[:, 0:2048 - dlt],
                    cw[:, hp * 4 + j:hp * 4 + j + 1], co[:, dlt:2048],
                    ALU.mult, ALU.add)
            tCv = p.tick("vector")
            p.wait("scalar", tCv)
            ACT(p, pb, co, AF.Silu)
            tSi = p.tick("scalar")
            if do_norm:
                ACT(p, sqbuf[par], pb, AF.Square)
                tSq = p.tick("scalar")
                p.wait("tensor", tSq)
                for cg in range(4):
                    MM(p, ps[0:2, cg * 512:(cg + 1) * 512], sbc["c_bo2"][:, :],
                       sqbuf[par][:, cg * 512:(cg + 1) * 512])
                tQ = p.tick("tensor"); p.wait("scalar", tQ)
                ACT(p, f_nrmA, ps[0:2, :], AF.Sqrt)
                tQ = p.tick("scalar"); p.wait("vector", tQ)
                TS(p, f_nrmA, f_nrmA, 1e-6, ALU.add)
                p.op("vector", lambda v, o=f_nrmB, i=f_nrmA: v.reciprocal(out=o, in_=i))
                COPY(p, normb[:, :], f_nrmB)
                tN = p.tick("vector"); p.wait("tensor", tN)
                for cg in range(4):
                    MM(p, ps[:, cg * 512:(cg + 1) * 512], sbc["c_bo2T"][:, :],
                       normb[:, cg * 512:(cg + 1) * 512])
                tB = p.tick("tensor"); p.wait("vector", tB, tSi)
                TT(p, pb, pb, ps, ALU.mult)
            else:
                p.wait("tensor", tSi)
                for cg in range(4):
                    MM(p, ps[:, cg * 512:(cg + 1) * 512],
                       sbc["c_selpairB"][:, hp * 128:(hp + 1) * 128],
                       betaBb[:, cg * 512:(cg + 1) * 512])
                tB = p.tick("tensor"); p.wait("vector", tB)
                TT(p, pb, pb, ps, ALU.mult)
            tFin = p.tick("vector")
            evac_tok[par] = tFin
            p.wait("sync", tFin)
            spill_tok[par] = DMA(
                p, "sync", spill[jidx, :, :, hp * 128:(hp + 1) * 128]
                .rearrange("c p t -> p c t"),
                pb.rearrange("p (c t) -> p c t", t=128), ch=f"sp{par}")
            si += 1
    p.fence()
    if stop == "P1b":
        qd = nc.dram_tensor("dbg_q", [NCH, 128, NP * C], BF16, kind="ExternalOutput").ap()
        kd = nc.dram_tensor("dbg_k", [NCH, 128, NP * C], BF16, kind="ExternalOutput").ap()
        vd = nc.dram_tensor("dbg_v", [NCH, 128, NP * C], BF16, kind="ExternalOutput").ap()
        DMA(p, "sync", qd, spill[0], ch="dbg")
        DMA(p, "sync", kd, spill[1], ch="dbg")
        DMA(p, "sync", vd, spill[2], ch="dbg")
        p.wait_chan_totals("gpsimd", ["dbg"])
        p.op("gpsimd", lambda g: g.engine_nop())
        p.tick("gpsimd")
        p.build()
        return nc

    ld_tok = [[None, None], [None, None], [None, None]]
    use_tok = [[None, None], [None, None], [None, None]]
    chb = [qch, kch, vch]

    def issue_loads(c):
        par = c % 2
        for j in range(3):
            if use_tok[j][par] is not None:
                p.wait("gpsimd", use_tok[j][par])
            ld_tok[j][par] = DMA(p, "gpsimd", chb[j][par], spill[j, c],
                                 ch=f"ld{j}{par}")

    import os as _os3
    _fs = _os3.environ.get("FENCE_STAGE", "")

    def SF(tag):
        if _fs and (_fs == "all" or tag in _fs.split(",")):
            p.fence("tensor", "vector", "scalar")

    issue_loads(0)
    for c in range(nch):
        par = c % 2
        sl = slice(c * C, (c + 1) * C)
        qcb, kcb, vcb = qch[par], kch[par], vch[par]
        if c + 1 < nch:
            issue_loads(c + 1)
        p.wait("tensor", ld_tok[0][par], ld_tok[1][par], ld_tok[2][par])
        MM(p, psSt[:, 0:16], f_cum[:, sl], sbc["c_id16f"][0:16, :])
        for h in range(16):
            MM(p, psE[0:1, h * 128:(h + 1) * 128], sbc["c_id16f"][0:16, h:h + 1],
               f_cum[:, sl])
        t = p.tick("tensor"); p.wait("vector", t); p.wait("scalar", t)
        COPY(p, cumTf[:, :], psSt[:, 0:16])
        COPY(p, f_cumflat, psE[0:1, :])
        t = p.tick("vector"); p.wait("scalar", t)
        ACT(p, scr16[:, 0:16], cumTf[:, :], AF.Exp)
        ACT(p, gsl32[:, :], f_cum[:, sl], AF.Exp)
        t = p.tick("scalar"); p.wait("vector", t)
        TS(p, gTnegf[:, :], scr16[:, 0:16], -1.0, ALU.mult)
        p.tick("vector")
        ACT(p, gslb[:, :], gsl32[:, :], AF.Identity)
        ACT(p, gsl32[:, :], f_rev[:, sl], AF.Exp)
        ACT(p, dslb[:, :], gsl32[:, :], AF.Identity)
        tV = p.tick("scalar")

        SF("decay")
        if nstage <= 1:
            break
        p.wait("tensor", tV)
        for pr in range(8):
            MM(p, psB[:, pr * 128:(pr + 1) * 128],
               sbc["c_selpairB"][:, pr * 128:(pr + 1) * 128], betaBb[:, sl])
        t = p.tick("tensor"); p.wait("vector", t)
        TT(p, kstar, kcb, psB, ALU.mult)
        t = p.tick("vector"); p.wait("tensor", t)
        for pr in range(8):
            MM(p, psB[:, pr * 128:(pr + 1) * 128],
               sbc["c_selpairB"][:, pr * 128:(pr + 1) * 128], gslb[:, :])
        t = p.tick("tensor"); p.wait("vector", t)
        TT(p, qplus, qcb, psB, ALU.mult)
        tQP = p.tick("vector")

        SF("kq")
        if nstage <= 2:
            break
        p.wait("tensor", tQP)
        for hf in range(2):
            for g2 in range(2):
                MM(p, psB[:, g2 * 512:(g2 + 1) * 512], sbc["c_ones1f"][:, :],
                   f_cumflat[0:1, hf * 1024 + g2 * 512: hf * 1024 + (g2 + 1) * 512])
            t = p.tick("tensor"); p.wait("vector", t)
            for h8 in range(8):
                h = hf * 8 + h8
                TS(p, f_tmpd[:, h8 * 128:(h8 + 1) * 128],
                   psB[:, h8 * 128:(h8 + 1) * 128], cumTf[:, h:h + 1],
                   ALU.subtract, 0.0, ALU.min)
            t = p.tick("vector"); p.wait("scalar", t)
            ACT(p, s3[:, hf * 1024:(hf + 1) * 1024], f_tmpd[:, 0:1024], AF.Exp)
            t = p.tick("scalar"); p.wait("vector", t)
            for h8 in range(8):
                h = hf * 8 + h8
                TS(p, f_tmpd[:, 1024 + h8 * 128: 1024 + (h8 + 1) * 128],
                   psB[:, h8 * 128:(h8 + 1) * 128], cumTf[:, h:h + 1],
                   ALU.subtract, 0.0, ALU.max)
            t = p.tick("vector"); p.wait("scalar", t); p.wait("tensor", t)
            ACT(p, s4[:, hf * 1024:(hf + 1) * 1024], f_tmpd[:, 1024:2048],
                AF.Exp, scale=-1.0)
        tLam = p.tick("scalar")

        SF("lam")
        if nstage <= 3:
            break
        p.wait("tensor", tLam)
        t = wave16(p, lambda h: MM(p, psE[:, h * 128:(h + 1) * 128], hsl(kstar, h), hsl(kcb, h)))
        p.wait("vector", t)
        TT(p, sTmp, s4, sbc["c_maskLsNegB"][:, :], ALU.mult)
        TT(p, sY1, sTmp, psE, ALU.mult)
        t = p.tick("vector"); p.wait("tensor", t)
        t = wave16(p, lambda h: MM(p, psE[:, h * 128:(h + 1) * 128], hsl(kcb, h), hsl(kstar, h)))
        p.wait("vector", t)
        TT(p, sTmp, s3, sbc["c_maskUsNegB"][:, :], ALU.mult)
        TT(p, sZ1, sTmp, psE, ALU.mult)
        t = p.tick("vector"); p.wait("tensor", t)
        t = wave16(p, lambda h: MM(p, psE[:, h * 128:(h + 1) * 128], hsl(kcb, h), hsl(qcb, h)))
        p.wait("vector", t)
        TT(p, sTmp, s3, sbc["c_maskUiB"][:, :], ALU.mult)
        TT(p, sAt, sTmp, psE, ALU.mult)
        tBld = p.tick("vector")

        SF("gram")
        if nstage <= 4:
            break
        p.wait("tensor", tBld)
        wave16(p, lambda h: MM(p, psB[:, h * 64:(h + 1) * 64], hsl(kstar, h),
                               hsl(stb, h, 64)))
        t = wave16(p, lambda h: MM(p, psE[:, h * 64:(h + 1) * 64], hsl(vcb, h),
                                   sbc["c_id64b"][64 * (h % 2):64 * (h % 2) + 64, :]))
        p.wait("scalar", t)
        ACT(p, vTsb, psE[:, 0:1024], AF.Identity)
        t = p.tick("scalar"); p.wait("vector", t)
        for h in range(16):
            STT(p, wvt[:, h * 64:(h + 1) * 64], psB[:, h * 64:(h + 1) * 64],
                gTnegf[:, h:h + 1], vTsb[:, h * 64:(h + 1) * 64],
                ALU.mult, ALU.add)
        tR = p.tick("vector")

        SF("R")
        if nstage <= 5:
            break
        lv = [(sY1, sZ1), (s3, s4), (s5, s6), (s3, s4), (s5, s6), (s3, s4)]
        tApply = tR
        _sm = _os3.environ.get("SOLVE_MODE", "")
        for i in range(6):
            sy, sz = lv[i]
            if _sm != "nosapply":
                p.wait("tensor", tApply)
                _nw = int(_os3.environ.get("SOLVE_NW", "16"))
                for h in range(16):
                    if h and h % _nw == 0:
                        tt = p.tick("tensor"); p.wait("tensor", tt)
                    MM(p, psB[:, h * 64:(h + 1) * 64],
                       sz[:, h * 128:(h + 1) * 128], wvt[:, h * 64:(h + 1) * 64])
                t = p.tick("tensor"); p.wait("vector", t)
                TT(p, wvt, wvt, psB, ALU.add)
                tApply = p.tick("vector")
            ny, nz = lv[i + 1] if i < 5 else (None, s5)
            last = (i == 5)
            if _sm == "nosquare":
                continue
            _nw = int(_os3.environ.get("SOLVE_NW", "16"))
            if not last:
                for h in range(16):
                    if h and h % _nw == 0:
                        tt = p.tick("tensor"); p.wait("tensor", tt)
                    MM(p, psE[:, h * 128:(h + 1) * 128],
                       sz[:, h * 128:(h + 1) * 128], sy[:, h * 128:(h + 1) * 128])
                t = p.tick("tensor"); p.wait("scalar", t)
                ACT(p, ny, psE, AF.Identity)
                t = p.tick("scalar"); p.wait("tensor", t)
            for h in range(16):
                if h and h % _nw == 0:
                    tt = p.tick("tensor"); p.wait("tensor", tt)
                MM(p, psE[:, h * 128:(h + 1) * 128],
                   sy[:, h * 128:(h + 1) * 128], sz[:, h * 128:(h + 1) * 128])
            t = p.tick("tensor"); p.wait("scalar", t)
            ACT(p, nz, psE, AF.Identity)
            t = p.tick("scalar"); p.wait("tensor", t)
        p.wait("tensor", tApply)
        for h in range(16):
            MM(p, psB[:, h * 64:(h + 1) * 64],
               s5[:, h * 128:(h + 1) * 128], wvt[:, h * 64:(h + 1) * 64])
        t = p.tick("tensor"); p.wait("vector", t)
        TT(p, wvt, wvt, psB, ALU.add)
        tU = p.tick("vector")

        SF("solve")
        if nstage <= 6:
            break
        p.wait("tensor", tU)

        def emit_O(h):
            po = psE[64 * (h % 2):64 * (h % 2) + 64,
                     (h // 2) * 128:(h // 2) * 128 + 128]
            MM(p, po, wvt[:, h * 64:(h + 1) * 64],
               sAt[:, h * 128:(h + 1) * 128], start=True, stop=False)
            MM(p, po, hsl(stb, h, 64), hsl(qplus, h), start=False, stop=True)
        t = wave16(p, emit_O)
        p.wait("vector", t)
        COPY(p, outsv[:, :, c * 128:(c + 1) * 128],
             psE[:, 0:1024].rearrange("p (r t) -> p r t", t=128))
        tO = p.tick("vector")

        SF("O")
        if nstage <= 7:
            break
        p.wait("tensor", tO)
        for pr in range(8):
            MM(p, psB[:, pr * 128:(pr + 1) * 128],
               sbc["c_selpairB"][:, pr * 128:(pr + 1) * 128], dslb[:, :])
        t = p.tick("tensor"); p.wait("vector", t)
        TT(p, kdec2dm, kcb, psB, ALU.mult)
        t = p.tick("vector"); p.wait("tensor", t)
        t = wave16(p, lambda h: MM(p, psE[:, h * 64:(h + 1) * 64], hsl(kdec2dm, h),
                                   sbc["c_id64b"][64 * (h % 2):64 * (h % 2) + 64, :]))
        p.wait("scalar", t)
        ACT(p, kdec2tm[:, :], psE[:, 0:1024], AF.Identity)
        t = p.tick("scalar"); p.wait("tensor", t)
        wave16(p, lambda h: MM(
            p, psSt[64 * (h % 2):64 * (h % 2) + 64, (h // 2) * 64:(h // 2) * 64 + 64],
            kdec2tm[:, h * 64:(h + 1) * 64], wvt[:, h * 64:(h + 1) * 64]))
        for pr in range(8):
            MM(p, psG[:, pr:pr + 1],
               sbc["c_selpairB"][:, pr * 128:(pr + 1) * 128], gslb[:, 127:128])
        t = p.tick("tensor"); p.wait("scalar", t)
        ACT(p, gccol[:, :], psG[:, 0:8], AF.Identity)
        t = p.tick("scalar"); p.wait("vector", t)
        for pr in range(8):
            STT(p, stf[:, pr * 64:(pr + 1) * 64], stf[:, pr * 64:(pr + 1) * 64],
                gccol[:, pr:pr + 1], psSt[:, pr * 64:(pr + 1) * 64],
                ALU.mult, ALU.add)
        t = p.tick("vector"); p.wait("scalar", t)
        ACT(p, stb[:, :], stf[:, :], AF.Identity)
        tSt = p.tick("scalar")
        p.wait("tensor", tSt)
        import os as _os2
        if _os2.environ.get("FENCE_CHUNK"):
            p.fence("tensor", "vector", "scalar")
        for j in range(3):
            use_tok[j][par] = tSt
    p.fence()
    if stop == "P2":
        _dump_stop(p, nc, {"outs": outs, "stf": stf})
        return nc

    out_toks = []
    for hp in range(8):
        par = hp % 2
        op = outs[:, hp * 2048:(hp + 1) * 2048]
        ACT(p, sqgn[par], op, AF.Square)
        t = p.tick("scalar"); p.wait("tensor", t)
        for cg in range(4):
            MM(p, psF[0:2, cg * 512:(cg + 1) * 512], sbc["c_bo2"][:, :],
               op[:, cg * 512:(cg + 1) * 512])
            MM(p, psE[0:2, cg * 512:(cg + 1) * 512], sbc["c_bo2"][:, :],
               sqgn[par][:, cg * 512:(cg + 1) * 512])
        t = p.tick("tensor"); p.wait("vector", t)
        REDUCE(p, statm[0:2, hp:hp + 1], psF[0:2, :], ALU.add)
        REDUCE(p, statq[0:2, hp:hp + 1], psE[0:2, :], ALU.add)
        t = p.tick("vector"); p.wait("tensor", t); p.wait("scalar", t)
    TS(p, statm, statm, 1.0 / 131072.0, ALU.mult)
    TS(p, statq, statq, 1.0 / 131072.0, ALU.mult)
    TT(p, statv, statm, statm, ALU.mult)
    TT(p, statv, statq, statv, ALU.subtract)
    t = p.tick("vector"); p.wait("scalar", t)
    ACT(p, statv, statv, AF.Ln, bias=cbias[0:2, 0:1])
    ACT(p, statv, statv, AF.Exp, scale=-0.5)
    t = p.tick("scalar"); p.wait("vector", t)
    COPY(p, statmb, statm)
    COPY(p, statrb, statv)
    t = p.tick("vector"); p.wait("tensor", t)
    for hp in range(8):
        MM(p, psG[:, 2 * hp:2 * hp + 1], sbc["c_bo2T"][:, :], statmb[:, hp:hp + 1])
        MM(p, psG[:, 2 * hp + 1:2 * hp + 2], sbc["c_bo2T"][:, :], statrb[:, hp:hp + 1])
    t = p.tick("tensor"); p.wait("vector", t)
    COPY(p, gcolA[:, :], psG[:, 0:16])
    t = p.tick("vector"); p.wait("vector", t); p.wait("tensor", t)
    wsrc_g = dram["a_wg"].rearrange("(c p) m -> p c m", p=128)
    evac_tok = [None, None]
    gtok = [None, None]
    for hp in range(8):
        par = hp % 2
        slot = hp % 4
        op = outs[:, hp * 2048:(hp + 1) * 2048]
        ws = wslot[slot]
        wsv = ws.rearrange("p (c m) -> p c m", m=128)
        if wslot_tok[slot] is not None:
            p.wait("gpsimd", wslot_tok[slot])
        tL = DMA(p, "gpsimd", wsv, wsrc_g[:, :, hp * 128:(hp + 1) * 128],
                 ch=f"wl{slot}")
        TS(p, op, op, gcolA[:, 2 * hp:2 * hp + 1], ALU.subtract,
           gcolA[:, 2 * hp + 1:2 * hp + 2], ALU.mult)
        TS(p, op, op, sbp["a_gns"][:, hp:hp + 1], ALU.mult,
           sbp["a_gnb"][:, hp:hp + 1], ALU.add)
        tGn = p.tick("vector")
        p.wait("tensor", tL)
        if evac_tok[par] is not None:
            p.wait("tensor", evac_tok[par])
        ps = psP[par]
        for cg in range(4):
            for k in range(8):
                MM(p, ps[:, cg * 512:(cg + 1) * 512], wsv[:, k],
                   x0v[:, k, cg * 512:(cg + 1) * 512], start=(k == 0), stop=(k == 7))
        tMM = p.tick("tensor")
        wslot_tok[slot] = tMM
        p.wait("scalar", tMM)
        if gtok[par] is not None:
            p.wait("scalar", gtok[par])
        ACT(p, gtmp[par], ps, AF.Silu, bias=sbp["a_bg"][:, hp:hp + 1])
        t = p.tick("scalar"); p.wait("vector", t, tGn)
        TT(p, op, op, gtmp[par], ALU.mult)
        tMul = p.tick("vector")
        evac_tok[par] = tMul
        gtok[par] = tMul
    p.fence()
    wsrc_o = dram["a_wo"].rearrange("(c p) m -> p c m", p=128)
    x0_tok = [None, None]
    zev_tok = None
    for m in range(8):
        par = m % 2
        slot = m % 4
        ws = wslot[slot]
        wsv = ws.rearrange("p (c m) -> p c m", m=128)
        if wslot_tok[slot] is not None:
            p.wait("gpsimd", wslot_tok[slot])
        tL = DMA(p, "gpsimd", wsv, wsrc_o[:, :, m * 128:(m + 1) * 128],
                 ch=f"wl{slot}")
        if x0_tok[par] is not None:
            p.wait("sync", x0_tok[par])
        tX = DMA(p, "sync", f_x0f[par], x0T[m * 128:(m + 1) * 128, :],
                 ch=f"xf{par}")
        p.wait("tensor", tL)
        if zev_tok is not None:
            p.wait("tensor", zev_tok)
        for cg in range(4):
            for k in range(8):
                MM(p, psE[:, cg * 512:(cg + 1) * 512], wsv[:, k],
                   outsv[:, k, cg * 512:(cg + 1) * 512], start=(k == 0), stop=(k == 7))
        tMM = p.tick("tensor")
        wslot_tok[slot] = tMM
        p.wait("vector", tMM, tX)
        STT(p, zbfv[:, m], psE, sbp["a_bo"][:, m:m + 1], f_x0f[par],
            ALU.add, ALU.add)
        zev_tok = p.tick("vector")
        x0_tok[par] = zev_tok
        p.wait("tensor", zev_tok)
        for cg in range(4):
            MM(p, psF[0:1, cg * 512:(cg + 1) * 512], sbc["c_ones128b"][:, :],
               zbfv[:, m, cg * 512:(cg + 1) * 512], start=(m == 0), stop=(m == 7))
    t = p.tick("tensor"); p.wait("vector", t)
    TS(p, f_murow, psF[0:1, :], 1.0 / 1024.0, ALU.mult)
    t = p.tick("vector"); p.wait("tensor", t); p.wait("scalar", t)
    for m in range(8):
        par = m % 2
        ACT(p, sqgn[par], zbfv[:, m], AF.Square)
        t = p.tick("scalar"); p.wait("tensor", t)
        for cg in range(4):
            MM(p, psF[0:1, cg * 512:(cg + 1) * 512], sbc["c_ones128b"][:, :],
               sqgn[par][:, cg * 512:(cg + 1) * 512], start=(m == 0), stop=(m == 7))
        if m < 7:
            t = p.tick("tensor"); p.wait("scalar", t)
    t = p.tick("tensor"); p.wait("vector", t)
    TS(p, f_mu2row, psF[0:1, :], 1.0 / 1024.0, ALU.mult)
    TT(p, f_rstdrow, f_murow, f_murow, ALU.mult)
    TT(p, f_mu2row, f_mu2row, f_rstdrow, ALU.subtract)
    t = p.tick("vector"); p.wait("scalar", t)
    ACT(p, f_mu2row, f_mu2row, AF.Ln, bias=cbias[0:1, 0:1])
    ACT(p, f_rstdrow, f_mu2row, AF.Exp, scale=-0.5)
    t = p.tick("scalar"); p.wait("vector", t)
    COPY(p, murb, f_murow)
    COPY(p, rsrb, f_rstdrow)
    t = p.tick("vector"); p.wait("tensor", t)
    for cg in range(4):
        MM(p, psE[:, cg * 512:(cg + 1) * 512], sbc["c_ones1b"][:, :],
           murb[0:1, cg * 512:(cg + 1) * 512])
    t = p.tick("tensor"); p.wait("vector", t)
    COPY(p, muBCb, psE)
    t = p.tick("vector"); p.wait("tensor", t)
    for cg in range(4):
        MM(p, psE[:, cg * 512:(cg + 1) * 512], sbc["c_ones1b"][:, :],
           rsrb[0:1, cg * 512:(cg + 1) * 512])
    t = p.tick("tensor"); p.wait("vector", t)
    COPY(p, rsBCb, psE)
    t = p.tick("vector"); p.wait("vector", t)
    xf_tok = [None, None]
    for m in range(8):
        par = m % 2
        if xf_tok[par] is not None:
            p.wait("vector", *xf_tok[par])
        TT(p, f_x0f[par], zbfv[:, m], muBCb, ALU.subtract)
        TT(p, f_x0f[par], f_x0f[par], rsBCb, ALU.mult)
        t = p.tick("vector"); p.wait("sync", t)
        tD = DMA(p, "sync", x1T[m * 128:(m + 1) * 128, :], f_x0f[par],
                 ch=f"xo{par}")
        out_toks.append(tD)
        p.wait("vector", t)
        COPY(p, zbfv[:, m], f_x0f[par])
        xf_tok[par] = (p.tick("vector"), tD)
    p.fence()
    for tb in range(16):
        for k in range(8):
            MM(p, psG[0:128, 0:16], zbfv[:, k, tb * 128:(tb + 1) * 128],
               wabv[:, 2, k], start=(k == 0), stop=False)
        MM(p, psG[0:128, 0:16], sbc["c_ones1b"][:, :], bggb[:, :],
           start=False, stop=True)
        t = p.tick("tensor"); p.wait("scalar", t)
        ACT(p, gate[:, tb * 16:(tb + 1) * 16], psG[0:128, 0:16], AF.Exp)
        t = p.tick("scalar"); p.wait("vector", t)
        gsl_ = gate[:, tb * 16:(tb + 1) * 16]
        REDUCE(p, scr16[:, 0:1], gsl_, ALU.add)
        t = p.tick("vector"); p.wait("vector", t)
        p.op("vector", lambda v, o=scr16[:, 1:2], i=scr16[:, 0:1]:
             v.reciprocal(out=o, in_=i))
        t = p.tick("vector"); p.wait("vector", t)
        TS(p, gsl_, gsl_, scr16[:, 1:2], ALU.mult)
        t = p.tick("vector"); p.wait("vector", t)
        p.op("vector", lambda v, o=m8t[:, :], i=gsl_: v.max(out=o, in_=i))
        t = p.tick("vector"); p.wait("vector", t)
        g2sl = gate2[:, tb * 16:(tb + 1) * 16]
        TS(p, g2sl, gsl_, m8t[:, 1:2], ALU.is_ge)
        t = p.tick("vector"); p.wait("vector", t)
        TT(p, g2sl, g2sl, gsl_, ALU.mult)
        t = p.tick("vector"); p.wait("tensor", t)
    t = p.tick("vector"); p.wait("sync", t)
    out_toks.append(DMA(p, "sync", maskedT, gate2, ch="mo"))
    p.wait_chan_totals("gpsimd", ["xo0", "xo1", "mo"])
    p.op("gpsimd", lambda g: g.engine_nop())
    p.tick("gpsimd")

    p.build()
    return nc



def _run_A(inputs):
    consts = _host_consts_A()
    wA = _layout_A_weights(inputs)
    ncA = build_A()
    x = np.asarray(inputs["L0_attn_x"], dtype=np.float32)
    in_maps = []
    for core in range(8):
        m = {"x0T": np.ascontiguousarray(x[core].T)}
        m.update(wA)
        m.update({k: v for k, v in consts.items() if k in
                  {nm for nm, _, _ in A_CONSTS}})
        in_maps.append(m)
    res = run_bass_kernel_spmd(ncA, in_maps, core_ids=list(range(8)))
    x1 = np.stack([np.ascontiguousarray(res.results[c]["x1T"].T)
                   for c in range(8)])
    masked = np.stack([
        np.ascontiguousarray(
            res.results[c]["maskedT"].reshape(128, 16, 16).transpose(2, 1, 0)
            .reshape(16, 2048, order="F"))
        for c in range(8)])
    masked = np.stack([
        res.results[c]["maskedT"].reshape(128, 16, 16).transpose(2, 1, 0).reshape(16, 16 * 128)
        for c in range(8)])
    return x1, masked


def _relayout_masked(mT):
    return np.ascontiguousarray(
        np.asarray(mT).reshape(128, 16, 16).transpose(2, 1, 0).reshape(16, 2048))


def kernel(**inputs):
    consts_A = _host_consts_A()
    wA = _layout_A_weights(inputs)
    ncA = build_A()
    x = np.asarray(inputs["L0_attn_x"], dtype=np.float32)
    namesA = {nm for nm, _, _ in A_CONSTS}
    in_maps = []
    for core in range(8):
        m = {"x0T": np.ascontiguousarray(x[core].T)}
        m.update(wA)
        m.update({k: v for k, v in consts_A.items() if k in namesA})
        in_maps.append(m)
    resA = run_bass_kernel_spmd(ncA, in_maps, core_ids=list(range(8)))
    masked = [_relayout_masked(resA.results[c]["maskedT"]) for c in range(8)]

    consts_B = _host_consts_B()
    wB = _layout_B_weights(inputs)
    ncB = build_B()
    namesB = {nm for nm, _, _ in B_CONSTS}
    in_mapsB = []
    for core in range(8):
        maskall = np.concatenate(
            [masked[core]] + [masked[j] for j in range(8) if j != core], axis=0)
        m = {"x1T": np.ascontiguousarray(np.asarray(resA.results[core]["x1T"])),
             "maskall": np.ascontiguousarray(maskall)}
        m.update(wB)
        m.update({k: v for k, v in consts_B.items() if k in namesB})
        in_mapsB.append(m)
    resB = run_bass_kernel_spmd(ncB, in_mapsB, core_ids=list(range(8)))
    out = np.stack([np.ascontiguousarray(np.asarray(resB.results[c]["yT"]).T)
                    for c in range(8)])
    return out



B_SMALL = (("b_b1L0", [128, 32]), ("b_gammaL0", [128, 8]), ("b_betaL0", [128, 8]),
           ("b_ln1g", [128, 8]), ("b_ln1b", [128, 8]), ("b_ln2g", [128, 8]),
           ("b_ln2b", [128, 8]), ("b_inpb", [128, 24]), ("b_outpb", [128, 8]),
           ("b_bgrp", [4, 1]), ("b_bexp", [8, 1]), ("b_b1L1", [128, 128]))
B_CONSTS = (("c_ones2d", [128, 128], BF16), ("c_ones2df", [128, 128], F32),
            ("c_sel16f", [128, 16], F32), ("c_esel16", [16, 2048], BF16),
            ("c_P4a", [4, 4], F32),
            ("c_P4b", [4, 4], F32), ("c_L4str", [4, 4], F32),
            ("c_Ppair8", [8, 8], F32), ("c_dup48", [4, 8], F32),
            ("c_even8", [8, 1], F32), ("c_bo2", [128, 2], BF16),
            ("c_bo2T", [2, 128], BF16))


def _host_consts_B():
    c = {}
    c["c_ones2d"] = np.ones((128, 128), BF)
    c["c_ones2df"] = np.ones((128, 128), np.float32)
    sel = np.zeros((128, 16), np.float32)
    for pp in range(128):
        sel[pp, pp % 16] = 1.0
    c["c_sel16f"] = sel
    es = np.zeros((16, 16 * 128), np.float32)
    for e in range(16):
        es[e, e * 128:(e + 1) * 128] = 1.0
    c["c_esel16"] = es.astype(BF)
    P4a = np.zeros((4, 4), np.float32)
    P4b = np.zeros((4, 4), np.float32)
    for r in range(4):
        P4a[r ^ 2, r] = 1.0
        P4b[r ^ 1, r] = 1.0
    c["c_P4a"] = P4a.T.copy()
    c["c_P4b"] = P4b.T.copy()
    L4 = np.zeros((4, 4), np.float32)
    for m in range(4):
        for g in range(m):
            L4[g, m] = 1.0
    c["c_L4str"] = L4
    P8 = np.zeros((8, 8), np.float32)
    for m in range(8):
        P8[m ^ 1, m] = 1.0
    c["c_Ppair8"] = P8
    d48 = np.zeros((4, 8), np.float32)
    for m in range(8):
        d48[m // 2, m] = 1.0
    c["c_dup48"] = d48
    ev = np.zeros((8, 1), np.float32)
    ev[0::2] = 1.0
    c["c_even8"] = ev
    bo = np.zeros((128, 2), np.float32)
    bo[0:64, 0] = 1.0
    bo[64:128, 1] = 1.0
    c["c_bo2"] = bo.astype(BF)
    c["c_bo2T"] = bo.T.astype(BF).copy()
    return c


def _layout_B_weights(w):
    d = {}
    d["b_w1L0"] = np.ascontiguousarray(w["L0_ffn_w1"])
    d["b_w2L0"] = np.ascontiguousarray(w["L0_ffn_w2"])
    d["b_b2L0"] = np.ascontiguousarray(w["L0_ffn_b2"])
    d["b_b1L0"] = np.ascontiguousarray(
        w["L0_ffn_b1"].reshape(16, 2, 128).transpose(2, 0, 1).reshape(128, 32))
    for nm, key in (("b_gammaL0", "L0_ffn_gamma"), ("b_betaL0", "L0_ffn_beta"),
                    ("b_ln1g", "L1_ffn_ln1_g"), ("b_ln1b", "L1_ffn_ln1_b"),
                    ("b_ln2g", "L1_ffn_ln2_g"), ("b_ln2b", "L1_ffn_ln2_b"),
                    ("b_outpb", "L1_ffn_out_proj_b")):
        d[nm] = np.ascontiguousarray(w[key].reshape(8, 128).T)
    d["b_wqkvT"] = np.ascontiguousarray(w["L1_attn_w_qkv"].T)
    d["b_woutT"] = np.ascontiguousarray(w["L1_attn_w_out"].T)
    d["b_winT"] = np.ascontiguousarray(w["L1_ffn_in_proj_w"].T)
    d["b_woutpT"] = np.ascontiguousarray(w["L1_ffn_out_proj_w"].T)
    d["b_inpb"] = np.ascontiguousarray(w["L1_ffn_in_proj_b"].reshape(24, 128).T)
    d["b_inpbrow"] = np.ascontiguousarray(w["L1_ffn_in_proj_b"][2048:3072].reshape(1, 1024))
    d["b_wgrp"] = np.ascontiguousarray(w["L1_ffn_w_group"])
    d["b_bgrp"] = np.ascontiguousarray(w["L1_ffn_b_group"].reshape(4, 1))
    d["b_wexp"] = np.ascontiguousarray(
        w["L1_ffn_w_expert"].transpose(1, 0, 2).reshape(1024, 8))
    d["b_bexp"] = np.ascontiguousarray(w["L1_ffn_b_expert"].reshape(8, 1))
    d["b_w1L1"] = np.ascontiguousarray(w["L1_ffn_w1"])
    d["b_b1L1"] = np.ascontiguousarray(
        w["L1_ffn_b1"].reshape(8, 16, 128).transpose(2, 0, 1).reshape(128, 128))
    d["b_w2L1"] = np.ascontiguousarray(w["L1_ffn_w2"])
    d["b_b2L1"] = np.ascontiguousarray(w["L1_ffn_b2"])
    return d


def build_B(stop=None):
    nc = bass.Bass("TRN2", target_bir_lowering=False)
    p = Prog(nc)

    x1T = nc.dram_tensor("x1T", [D, S], F32, kind="ExternalInput").ap()
    maskall = nc.dram_tensor("maskall", [128, S], F32, kind="ExternalInput").ap()
    yT = nc.dram_tensor("yT", [D, S], F32, kind="ExternalOutput").ap()
    x2sp = nc.dram_tensor("x2sp", [D, S], F32, kind="Internal").ap()
    x3sp = nc.dram_tensor("x3sp", [D, S], F32, kind="Internal").ap()
    x4sp = nc.dram_tensor("x4sp", [D, S], F32, kind="Internal").ap()

    dram = {}
    dram["b_inpbrow"] = nc.dram_tensor("b_inpbrow", [1, 1024], F32,
                                       kind="ExternalInput").ap()
    for nm, shp in (("b_w1L0", [16, 1024, 256]), ("b_w2L0", [16, 256, 1024]),
                    ("b_b2L0", [16, 1024]), ("b_wqkvT", [1024, 3072]),
                    ("b_woutT", [1024, 1024]), ("b_winT", [1024, 3072]),
                    ("b_woutpT", [1024, 1024]), ("b_wgrp", [1024, 4]),
                    ("b_wexp", [1024, 8]), ("b_w1L1", [8, 1024, 2048]),
                    ("b_w2L1", [8, 2048, 1024]), ("b_b2L1", [8, 1024])):
        dram[nm] = nc.dram_tensor(nm, shp, F32, kind="ExternalInput").ap()
    for nm, shp in B_SMALL:
        dram[nm] = nc.dram_tensor(nm, shp, F32, kind="ExternalInput").ap()
    for nm, shp, dt in B_CONSTS:
        dram[nm] = nc.dram_tensor(nm, shp, dt, kind="ExternalInput").ap()

    AL = nc.alloc_sbuf_tensor
    BB0 = AL("BB0", [128, 16384], BF16).ap()
    BB1 = AL("BB1", [128, 16384], BF16).ap()
    BB2 = AL("BB2", [128, 16384], BF16).ap()
    BB3 = AL("BB3", [128, 16384], BF16).ap()
    WS = AL("WS", [128, 8192], BF16).ap()
    FSB = AL("FSB", [128, 10240], F32).ap()
    RB = FSB[0:16, 0:2048]
    RB2 = FSB[0:16, 2048:4096]
    RB3 = FSB[0:16, 4096:6144]
    RB4 = FSB[0:16, 6144:8192]
    R4zone = (FSB[0:4, 2048:4096], FSB[0:4, 4096:6144], FSB[0:4, 6144:8192],
              FSB[0:4, 8192:10240])
    R8 = FSB[0:8, 2048:4096]
    R8b = FSB[0:8, 8192:10240]
    R8d = FSB[0:8, 0:2048]
    RBb = AL("RBb", [16, 2048], BF16).ap()
    muB = AL("muB", [128, 2048], BF16).ap()
    rsB = AL("rsB", [128, 2048], BF16).ap()
    rows1 = FSB[0:1, 6144:10240]
    nrow = AL("nrow", [16, 16], F32).ap()
    b2L0b = AL("b2L0b", [16, 1024], BF16).ap()
    b2L1b = AL("b2L1b", [8, 1024], BF16).ap()
    cbias = AL("cbiasB", [128, 2], F32).ap()
    sbc = {}
    for nm, shp, dt in B_CONSTS:
        sbc[nm] = AL(nm + "_s", shp, dt).ap()
    sbp = {}
    for nm, shp in B_SMALL:
        sbp[nm] = AL(nm + "_s", shp, F32).ap()

    psE = nc.alloc_psum_tensor("psE", [128, 2048], F32).ap()
    psF = nc.alloc_psum_tensor("psF", [128, 2048], F32).ap()

    f_slab = [FSB[:, 0:2048], FSB[:, 2048:4096]]
    f_scr = FSB[:, 4096:6144]
    f_scr2 = FSB[:, 6144:8192]
    Psb = WS[:, 0:4096]
    denb = WS[:, 4096:6144]
    vaug = WS[:, 6144:7184]
    wslot = [WS[:, i * 2048:(i + 1) * 2048] for i in range(4)]
    wtok = [None] * 4

    x1bv = BB0.rearrange("p (c t) -> p c t", t=2048)

    for nm, shp, dt in B_CONSTS:
        DMA(p, "sync", sbc[nm][:, :], dram[nm])
    for nm, shp in B_SMALL:
        DMA(p, "sync", sbp[nm][:, :], dram[nm])
    DMA(p, "gpsimd", BB0.rearrange("p (c t) -> p c t", t=2048),
        x1T.rearrange("(c p) t -> p c t", p=128), ch="inig")
    DMA(p, "gpsimd", b2L0b[:, :], dram["b_b2L0"], ch="inig")
    DMA(p, "gpsimd", b2L1b[:, :], dram["b_b2L1"], ch="inig")
    maskf = FSB[:, 4096:6144]
    DMA(p, "sync", maskf, maskall)
    p.fence()
    MEMSET(p, cbias[:, 0:1], 1e-5)
    MEMSET(p, cbias[:, 1:2], 1.0)
    MEMSET(p, BB3[:, :], 0.0)
    p.fence()

    for cg in range(4):
        MM(p, psE[0:16, cg * 512:(cg + 1) * 512], sbc["c_sel16f"][:, :],
           maskf[:, cg * 512:(cg + 1) * 512])
    t = p.tick("tensor"); p.wait("vector", t)
    TS(p, RB, psE[0:16, :], 1e-6, ALU.add)
    p.op("vector", lambda v, o=RB2, i=RB: v.reciprocal(out=o, in_=i))
    TT(p, RB, maskf[0:16, :], RB2, ALU.mult)
    TS(p, RB, RB, 8.0, ALU.mult)
    TS(p, RB2, RB, 0.0, ALU.is_gt)
    REDUCE(p, nrow[:, 0:1], RB2, ALU.add)
    TS(p, nrow[:, 1:2], nrow[:, 0:1], -1.0, ALU.mult, 1024.0, ALU.add)
    TS(p, RB3, RB2, -1.0, ALU.mult, 1.0, ALU.add)
    SCAN(p, RB4, RB3, RB3, 0.0, ALU.add, ALU.bypass)
    TS(p, RB4, RB4, nrow[:, 1:2], ALU.is_le)
    TT(p, RB4, RB4, RB3, ALU.mult)
    TT(p, RB4, RB4, RB2, ALU.add)
    COPY(p, RBb, RB4)
    p.fence()

    def stats_sum(ps_row, src, m):
        for cg in range(4):
            MM(p, ps_row[:, cg * 512:(cg + 1) * 512],
               sbc["c_ones2d"][:, 0:1], src[:, cg * 512:(cg + 1) * 512],
               start=(m == 0), stop=(m == 7))

    def stats_finalize(ps_row, murow, rstdrow, sq_src_chunks, sqbuf):
        t = p.tick("tensor"); p.wait("vector", t)
        TS(p, murow, ps_row[0:1, :], 1.0 / 1024.0, ALU.mult)
        t = p.tick("vector"); p.wait("scalar", t); p.wait("tensor", t)
        for m in range(8):
            ACT(p, sqbuf, sq_src_chunks(m), AF.Square)
            t = p.tick("scalar"); p.wait("tensor", t)
            stats_sum(ps_row, sqbuf, m)
            if m < 7:
                t = p.tick("tensor"); p.wait("scalar", t)
        t = p.tick("tensor"); p.wait("vector", t)
        TS(p, rstdrow, ps_row[0:1, :], 1.0 / 1024.0, ALU.mult)
        TT(p, f_scr[0:1, 0:2048], murow, murow, ALU.mult)
        TT(p, rstdrow, rstdrow, f_scr[0:1, 0:2048], ALU.subtract)
        t = p.tick("vector"); p.wait("scalar", t)
        ACT(p, rstdrow, rstdrow, AF.Ln, bias=cbias[0:1, 0:1])
        ACT(p, rstdrow, rstdrow, AF.Exp, scale=-0.5)

    def bcast_mu_rs(murow, rstdrow, murb_, rsrb_):
        t = p.tick("scalar"); p.wait("vector", t)
        COPY(p, murb_, murow)
        COPY(p, rsrb_, rstdrow)
        t = p.tick("vector"); p.wait("tensor", t)
        for cg in range(4):
            MM(p, psE[:, cg * 512:(cg + 1) * 512], sbc["c_ones2d"][0:1, :],
               murb_[0:1, cg * 512:(cg + 1) * 512])
        t = p.tick("tensor"); p.wait("vector", t)
        COPY(p, muB, psE)
        t = p.tick("vector"); p.wait("tensor", t)
        for cg in range(4):
            MM(p, psE[:, cg * 512:(cg + 1) * 512], sbc["c_ones2d"][0:1, :],
               rsrb_[0:1, cg * 512:(cg + 1) * 512])
        t = p.tick("tensor"); p.wait("vector", t)
        COPY(p, rsB, psE)
        t = p.tick("vector"); p.wait("tensor", t); p.wait("sync", t)

    f_murow = rows1[0:1, 0:2048]
    f_rsrow = rows1[0:1, 2048:4096]
    mu_b = muB[0:1, :]
    rs_b = rsB[0:1, :]

    accv = BB3.rearrange("p (c t) -> p c t", t=2048)
    h_sb = [BB1[:, 0:2048], BB1[:, 2048:4096]]
    evac_tok = None
    mask_tok = None
    for e in range(16):
        s1, s2 = (2 * e) % 4, (2 * e + 1) % 4
        if wtok[s1] is not None:
            p.wait("gpsimd", wtok[s1])
        tW1 = DMA(p, "gpsimd", wslot[s1].rearrange("p (c f) -> p c f", f=256),
                  dram["b_w1L0"][e].rearrange("(c p) f -> p c f", p=128),
                  ch=f"wl{s1}")
        if wtok[s2] is not None:
            p.wait("gpsimd", wtok[s2])
        tW2 = DMA(p, "gpsimd", wslot[s2].rearrange("p (c d) -> p c d", d=1024),
                  dram["b_w2L0"][e].rearrange("(c p) d -> p c d", p=128),
                  ch=f"wl{s2}")
        w1v = wslot[s1].rearrange("p (c f) -> p c f", f=256)
        w2v = wslot[s2].rearrange("p (c d) -> p c d", d=1024)
        p.wait("tensor", tW1)
        if evac_tok is not None:
            p.wait("tensor", evac_tok)
        for m in range(2):
            ps = psE if m == 0 else psF
            for cg in range(4):
                for k in range(8):
                    MM(p, ps[:, cg * 512:(cg + 1) * 512],
                       w1v[:, k, m * 128:(m + 1) * 128],
                       x1bv[:, k, cg * 512:(cg + 1) * 512],
                       start=(k == 0), stop=(k == 7))
        tMM = p.tick("tensor")
        wtok[s1] = tMM
        p.wait("scalar", tMM)
        if mask_tok is not None:
            p.wait("scalar", mask_tok)
        ACT(p, h_sb[0], psE, AF.Gelu, bias=sbp["b_b1L0"][:, 2 * e:2 * e + 1])
        ACT(p, h_sb[1], psF, AF.Gelu, bias=sbp["b_b1L0"][:, 2 * e + 1:2 * e + 2])
        tEv = p.tick("scalar")
        p.wait("tensor", tEv)
        for cg in range(4):
            MM(p, psE[:, cg * 512:(cg + 1) * 512],
               sbc["c_esel16"][:, e * 128:(e + 1) * 128],
               RBb[:, cg * 512:(cg + 1) * 512])
        t = p.tick("tensor"); p.wait("vector", t)
        TT(p, h_sb[0], h_sb[0], psE, ALU.mult)
        TT(p, h_sb[1], h_sb[1], psE, ALU.mult)
        mask_tok = p.tick("vector")
        p.wait("tensor", mask_tok, tW2)
        stt_tok = [None, None]
        for m8 in range(8):
            psw = psF if m8 % 2 == 0 else psE
            if stt_tok[m8 % 2] is not None:
                p.wait("tensor", stt_tok[m8 % 2])
            for cg in range(4):
                pt = psw[:, cg * 512:(cg + 1) * 512]
                MM(p, pt, w2v[:, 0, m8 * 128:(m8 + 1) * 128],
                   h_sb[0][:, cg * 512:(cg + 1) * 512], start=True, stop=False)
                MM(p, pt, w2v[:, 1, m8 * 128:(m8 + 1) * 128],
                   h_sb[1][:, cg * 512:(cg + 1) * 512], start=False, stop=False)
                MM(p, pt, b2L0b[:, m8 * 128:(m8 + 1) * 128],
                   RBb[:, cg * 512:(cg + 1) * 512], start=False, stop=True)
            t = p.tick("tensor"); p.wait("vector", t)
            STT(p, accv[:, m8], psw, 1.0, accv[:, m8], ALU.bypass, ALU.add)
            stt_tok[m8 % 2] = p.tick("vector")
        p.wait("tensor", stt_tok[0], stt_tok[1])
        wtok[s2] = p.last_tok["tensor"]
        evac_tok = p.last_tok["vector"]
    p.fence()
    if stop == "B2":
        _dump_stop(p, nc, {"acc": BB3})
        return nc

    zbv = BB1.rearrange("p (c t) -> p c t", t=2048)
    x2bv = BB2.rearrange("p (c t) -> p c t", t=2048)
    sl_tok = [None, None]
    for m in range(8):
        par = m % 2
        if sl_tok[par] is not None:
            p.wait("sync", sl_tok[par])
        tX = DMA(p, "sync", f_slab[par], x1T[m * 128:(m + 1) * 128, :], ch=f"xf{par}")
        p.wait("vector", tX)
        STT(p, zbv[:, m], f_slab[par], 1.0, accv[:, m], ALU.bypass, ALU.add)
        tZ = p.tick("vector")
        sl_tok[par] = tZ
        p.wait("tensor", tZ)
        stats_sum(psF[0:1], zbv[:, m], m)
    stats_finalize(psF[0:1], f_murow, f_rsrow, lambda m: zbv[:, m], BB2[:, 0:2048])
    bcast_mu_rs(f_murow, f_rsrow, mu_b, rs_b)
    sl_tok = [None, None]
    for m in range(8):
        par = m % 2
        if sl_tok[par] is not None:
            p.wait("sync", sl_tok[par])
        tX = DMA(p, "sync", f_slab[par], x1T[m * 128:(m + 1) * 128, :], ch=f"xf{par}")
        TT(p, f_scr, zbv[:, m], muB, ALU.subtract)
        TT(p, f_scr, f_scr, rsB, ALU.mult)
        TS(p, f_scr, f_scr, sbp["b_gammaL0"][:, m:m + 1], ALU.mult,
           sbp["b_betaL0"][:, m:m + 1], ALU.add)
        p.wait("vector", tX)
        TT(p, zbv[:, m], f_scr, f_slab[par], ALU.add)
        tU = p.tick("vector")
        sl_tok[par] = tU
        p.wait("tensor", tU)
        stats_sum(psF[0:1], zbv[:, m], m)
    stats_finalize(psF[0:1], f_murow, f_rsrow, lambda m: zbv[:, m], BB2[:, 0:2048])
    bcast_mu_rs(f_murow, f_rsrow, mu_b, rs_b)
    xs_tok = [None, None]
    for m in range(8):
        par = m % 2
        if xs_tok[par] is not None:
            p.wait("vector", *xs_tok[par])
        fs = f_slab[par]
        TT(p, fs, zbv[:, m], muB, ALU.subtract)
        TT(p, fs, fs, rsB, ALU.mult)
        t = p.tick("vector"); p.wait("sync", t)
        tD = DMA(p, "sync", x2sp[m * 128:(m + 1) * 128, :], fs, ch=f"xo{par}")
        p.wait("vector", t)
        COPY(p, x2bv[:, m], fs)
        xs_tok[par] = (p.tick("vector"), tD)
    p.fence()
    if stop == "B3":
        _dump_stop(p, nc, {"x2b": BB2})
        return nc

    qv = BB0.rearrange("p (c t) -> p c t", t=2048)
    ktm = BB1.rearrange("p (b f) -> p b f", f=1024)
    vtm = BB3.rearrange("p (b f) -> p b f", f=1024)

    def colblock_load(dst_slot, wsrc, c0, slot):
        if wtok[slot] is not None:
            p.wait("gpsimd", wtok[slot])
        return DMA(p, "gpsimd",
                   wslot[slot][:, 0:1024].rearrange("p (c m) -> p c m", m=128),
                   wsrc.rearrange("(c p) m -> p c m", p=128)[:, :, c0:c0 + 128],
                   ch=f"wl{slot}")

    ev_tok = None
    for m in range(8):
        slot = m % 4
        tL = colblock_load(wslot[slot], dram["b_wqkvT"], m * 128, slot)
        wv_ = wslot[slot].rearrange("p (c m) -> p c m", m=128)
        p.wait("tensor", tL)
        if ev_tok is not None:
            p.wait("tensor", ev_tok)
        ps = psE if m % 2 == 0 else psF
        for cg in range(4):
            for k in range(8):
                MM(p, ps[:, cg * 512:(cg + 1) * 512], wv_[:, k],
                   x2bv[:, k, cg * 512:(cg + 1) * 512], start=(k == 0), stop=(k == 7))
        tMM = p.tick("tensor")
        wtok[slot] = tMM
        p.wait("scalar", tMM)
        ACT(p, f_scr, ps, AF.Relu)
        t = p.tick("scalar"); p.wait("vector", t)
        TS(p, f_scr2, ps, 0.0, ALU.min)
        t = p.tick("vector"); p.wait("scalar", t)
        ACT(p, f_scr2, f_scr2, AF.Exp)
        t = p.tick("scalar"); p.wait("vector", t)
        TT(p, qv[:, m], f_scr, f_scr2, ALU.add)
        ev_tok = p.tick("vector")
    wpair = [WS[:, 0:4096].rearrange("p (c m) -> p c m", m=512),
             WS[:, 4096:8192].rearrange("p (c m) -> p c m", m=512)]
    for j, (base, dst) in enumerate(((1024, ktm), (2048, vtm))):
        ptoks = []
        for fg in range(2):
            for sl_ in ((0, 1) if fg == 0 else (2, 3)):
                if wtok[sl_] is not None:
                    p.wait("gpsimd", wtok[sl_])
            ptoks.append(DMA(p, "gpsimd", wpair[fg],
                             dram["b_wqkvT"].rearrange("(c p) m -> p c m", p=128)
                             [:, :, base + fg * 512:base + (fg + 1) * 512],
                             ch=f"wl{fg}"))
        p.wait("tensor", *ptoks)
        if ev_tok is not None:
            p.wait("tensor", ev_tok)
        par_tok = [None, None]
        scr_tok = [None, None]
        for tb in range(16):
            ps = psE if tb % 2 == 0 else psF
            if par_tok[tb % 2] is not None:
                p.wait("tensor", par_tok[tb % 2])
            for fg in range(2):
                wv_ = wpair[fg]
                for k in range(8):
                    MM(p, ps[:, fg * 512:(fg + 1) * 512],
                       x2bv[:, k, tb * 128:(tb + 1) * 128], wv_[:, k, 0:512],
                       start=(k == 0), stop=(k == 7))
            tMM = p.tick("tensor")
            p.wait("scalar", tMM)
            if j == 0:
                hc = (tb % 2) * 1024
                if scr_tok[tb % 2] is not None:
                    p.wait("scalar", scr_tok[tb % 2])
                ACT(p, f_scr[:, hc:hc + 1024], ps[:, 0:1024], AF.Relu)
                t = p.tick("scalar"); p.wait("vector", t)
                TS(p, f_scr2[:, hc:hc + 1024], ps[:, 0:1024], 0.0, ALU.min)
                t = p.tick("vector"); p.wait("scalar", t)
                ACT(p, f_scr2[:, hc:hc + 1024], f_scr2[:, hc:hc + 1024], AF.Exp)
                t = p.tick("scalar"); p.wait("vector", t)
                TT(p, dst[:, tb], f_scr[:, hc:hc + 1024],
                   f_scr2[:, hc:hc + 1024], ALU.add)
                ev_tok = p.tick("vector")
                par_tok[tb % 2] = ev_tok
                scr_tok[tb % 2] = ev_tok
            else:
                ACT(p, dst[:, tb], ps[:, 0:1024], AF.Identity)
                ev_tok = p.tick("scalar")
                par_tok[tb % 2] = ev_tok
        for sl_ in range(4):
            wtok[sl_] = p.last_tok["tensor"]
    p.fence()
    kvps = psE[:, 0:512]
    ksps = psE[:, 512:520]
    for par0 in (0, 1):
        for h in range(par0, 16, 2):
            for tb in range(16):
                MM(p, kvps[64 * (h % 2):64 * (h % 2) + 64, (h // 2) * 64:(h // 2) * 64 + 64],
                   ktm[:, tb, h * 64:(h + 1) * 64], vtm[:, tb, h * 64:(h + 1) * 64],
                   start=(tb == 0), stop=(tb == 15))
            for tb in range(16):
                MM(p, ksps[64 * (h % 2):64 * (h % 2) + 64, (h // 2):(h // 2) + 1],
                   ktm[:, tb, h * 64:(h + 1) * 64], sbc["c_ones2d"][:, 0:1],
                   start=(tb == 0), stop=(tb == 15))
        t = p.tick("tensor")
        p.wait("tensor", t)
    t = p.last_tok["tensor"]; p.wait("vector", t)
    kvsb = WS[:, 0:512]
    ksumf = FSB[:, 8200:8208]
    for sl_ in range(4):
        wtok[sl_] = p.last_tok["tensor"]
    COPY(p, kvsb[:, :], kvps)
    COPY(p, ksumf[:, :], ksps)
    t = p.tick("vector"); p.wait("tensor", t)
    out_lin = BB3
    olv = BB3.rearrange("p (c t) -> p c t", t=2048)
    for par0 in (0, 1):
        for h in range(par0, 16, 2):
            for cg in range(4):
                MM(p, psF[64 * (h % 2):64 * (h % 2) + 64, cg * 512:(cg + 1) * 512],
                   kvsb[64 * (h % 2):64 * (h % 2) + 64, (h // 2) * 64:(h // 2) * 64 + 64],
                   qv[64 * (h % 2):64 * (h % 2) + 64, h // 2, cg * 512:(cg + 1) * 512])
            t = p.tick("tensor"); p.wait("vector", t)
            COPY(p, olv[64 * (h % 2):64 * (h % 2) + 64, h // 2, :],
                 psF[64 * (h % 2):64 * (h % 2) + 64, :])
            t = p.tick("vector"); p.wait("tensor", t)
    for pr in range(8):
        TS(p, denb, qv[:, pr, :], ksumf[:, pr:pr + 1], ALU.mult)
        t = p.tick("vector"); p.wait("tensor", t)
        for cg in range(4):
            MM(p, psE[0:2, cg * 512:(cg + 1) * 512], sbc["c_bo2"][:, :],
               denb[:, cg * 512:(cg + 1) * 512])
        t = p.tick("tensor"); p.wait("vector", t)
        TS(p, f_scr2[0:2, 0:2048], psE[0:2, :], 1e-6, ALU.max)
        p.op("vector", lambda v, o=f_scr[0:2, 0:2048], i=f_scr2[0:2, 0:2048]:
             v.reciprocal(out=o, in_=i))
        COPY(p, denb[0:2, 0:2048], f_scr[0:2, 0:2048])
        t = p.tick("vector"); p.wait("tensor", t)
        for cg in range(4):
            MM(p, psE[:, cg * 512:(cg + 1) * 512], sbc["c_bo2T"][:, :],
               denb[0:2, cg * 512:(cg + 1) * 512])
        t = p.tick("tensor"); p.wait("vector", t)
        TT(p, olv[:, pr, :], olv[:, pr, :], psE, ALU.mult)
        t = p.tick("vector"); p.wait("tensor", t)
    p.fence()
    if stop == "B4a":
        _dump_stop(p, nc, {"olin": BB3})
        return nc

    def proj_resid_ln(srcv, wsrcT, c0, resid_sp, zb_t, out_sp, outb_t,
                      bias_col=None, gamma=None, beta=None, sq_scr=None):
        zv = zb_t.rearrange("p (c t) -> p c t", t=2048)
        ov = outb_t.rearrange("p (c t) -> p c t", t=2048)
        ev = None
        stok = [None, None]
        for m in range(8):
            slot = m % 4
            par = m % 2
            tL = colblock_load(wslot[slot], wsrcT, c0 + m * 128, slot)
            wv_ = wslot[slot].rearrange("p (c m) -> p c m", m=128)
            p.wait("tensor", tL)
            if stok[par] is not None:
                p.wait("tensor", stok[par])
            ps = psE if m % 2 == 0 else psF[:, 0:2048]
            for cg in range(4):
                for k in range(8):
                    MM(p, ps[:, cg * 512:(cg + 1) * 512], wv_[:, k],
                       srcv[:, k, cg * 512:(cg + 1) * 512],
                       start=(k == 0), stop=(k == 7))
            tMM = p.tick("tensor")
            wtok[slot] = tMM
            if stok[par] is not None:
                p.wait("sync", stok[par])
            tX = DMA(p, "sync", f_slab[par], resid_sp[m * 128:(m + 1) * 128, :],
                     ch=f"xf{par}")
            p.wait("vector", tMM, tX)
            if bias_col is not None:
                STT(p, zv[:, m], ps, bias_col[:, m:m + 1], f_slab[par],
                    ALU.add, ALU.add)
            else:
                STT(p, zv[:, m], ps, 0.0, f_slab[par], ALU.add, ALU.add)
            ev = p.tick("vector")
            stok[par] = ev
        p.wait("tensor", ev)
        for m in range(8):
            stats_sum(psF[0:1], zv[:, m], m)
        stats_finalize(psF[0:1], f_murow, f_rsrow, lambda m: zv[:, m], sq_scr)
        bcast_mu_rs(f_murow, f_rsrow, mu_b, rs_b)
        xs = [None, None]
        for m in range(8):
            par = m % 2
            if xs[par] is not None:
                p.wait("vector", *xs[par])
            fs = f_slab[par]
            TT(p, fs, zv[:, m], muB, ALU.subtract)
            TT(p, fs, fs, rsB, ALU.mult)
            if gamma is not None:
                TS(p, fs, fs, gamma[:, m:m + 1], ALU.mult, beta[:, m:m + 1], ALU.add)
            t = p.tick("vector")
            tD = None
            if out_sp is not None:
                p.wait("sync", t)
                tD = DMA(p, "sync", out_sp[m * 128:(m + 1) * 128, :], fs,
                         ch=f"xo{par}")
            p.wait("vector", t)
            COPY(p, ov[:, m], fs)
            xs[par] = (p.tick("vector"),) + ((tD,) if tD else ())
        p.fence()

    proj_resid_ln(olv, dram["b_woutT"], 0, x2sp, BB0, x3sp, BB0,
                  sq_scr=BB1[:, 0:2048])
    if stop == "B4":
        _dump_stop(p, nc, {"x3b": BB0})
        return nc

    x3bv = BB0.rearrange("p (c t) -> p c t", t=2048)
    qmv = BB1.rearrange("p (c t) -> p c t", t=2048)
    kmv = BB2.rearrange("p (c t) -> p c t", t=2048)
    vtm2 = BB3.rearrange("p (b f) -> p b f", f=1024)
    inpbf32 = FSB[0:1, 9216:10240]
    tBV = DMA(p, "sync", inpbf32, dram["b_inpbrow"], ch="xf0")
    p.wait("tensor", tBV)
    for j, dstv in ((0, qmv), (1, kmv)):
        evp = [None, None] if j == 0 else [evp[0], evp[1]]
        for m in range(8):
            slot = m % 4
            tL = colblock_load(wslot[slot], dram["b_winT"], j * 1024 + m * 128, slot)
            wv_ = wslot[slot].rearrange("p (c m) -> p c m", m=128)
            p.wait("tensor", tL)
            if evp[m % 2] is not None:
                p.wait("tensor", evp[m % 2])
            ps = psE if m % 2 == 0 else psF[:, 0:2048]
            for cg in range(4):
                for k in range(8):
                    MM(p, ps[:, cg * 512:(cg + 1) * 512], wv_[:, k],
                       x3bv[:, k, cg * 512:(cg + 1) * 512],
                       start=(k == 0), stop=(k == 7))
            tMM = p.tick("tensor")
            wtok[slot] = tMM
            p.wait("scalar", tMM)
            ACT(p, dstv[:, m], ps, AF.Identity,
                bias=sbp["b_inpb"][:, j * 8 + m:j * 8 + m + 1])
            ev = p.tick("scalar")
            evp[m % 2] = ev
    ptoks = []
    for fg in range(2):
        for sl_ in ((0, 1) if fg == 0 else (2, 3)):
            if wtok[sl_] is not None:
                p.wait("gpsimd", wtok[sl_])
        ptoks.append(DMA(p, "gpsimd", wpair[fg],
                         dram["b_winT"].rearrange("(c p) m -> p c m", p=128)
                         [:, :, 2048 + fg * 512:2048 + (fg + 1) * 512],
                         ch=f"wl{fg}"))
    p.wait("tensor", *ptoks)
    if ev is not None:
        p.wait("tensor", ev)
    ev = None
    par_tok = [None, None]
    for tb in range(16):
        ps = psE if tb % 2 == 0 else psF[:, 0:2048]
        if par_tok[tb % 2] is not None:
            p.wait("tensor", par_tok[tb % 2])
        for fg in range(2):
            wv_ = wpair[fg]
            for k in range(8):
                MM(p, ps[:, fg * 512:(fg + 1) * 512],
                   x3bv[:, k, tb * 128:(tb + 1) * 128], wv_[:, k, 0:512],
                   start=(k == 0), stop=False)
            MM(p, ps[:, fg * 512:(fg + 1) * 512], sbc["c_ones2df"][0:1, 0:128],
               inpbf32[0:1, fg * 512:(fg + 1) * 512],
               start=False, stop=True)
        tMM = p.tick("tensor")
        p.wait("scalar", tMM)
        ACT(p, vtm2[:, tb], ps[:, 0:1024], AF.Identity)
        par_tok[tb % 2] = p.tick("scalar")
    for sl_ in range(4):
        wtok[sl_] = p.last_tok["tensor"]
    p.fence()
    if stop == "B5a":
        _dump_stop(p, nc, {"qm": BB1, "km": BB2, "vm": BB3})
        return nc
    mhav = BB0.rearrange("p (c t) -> p c t", t=2048)
    Psbv = [Psb[:, 0:2048], Psb[:, 2048:4096]]
    for h in range(16):
        hb = 64 * (h % 2)
        COPY(p, vaug.rearrange("p (b f) -> p b f", f=65)[:, :, 0:64],
             vtm2[:, :, h * 64:(h + 1) * 64])
        MEMSET(p, vaug.rearrange("p (b f) -> p b f", f=65)[:, :, 64:65], 1.0)
        tV = p.tick("vector")
        p.wait("tensor", tV)
        prev_act = [None, None]
        for t2b in range(16):
            tEs = []
            for half in (0, 1):
                if prev_act[half] is not None:
                    p.wait("tensor", prev_act[half])
                for cg in (2 * half, 2 * half + 1):
                    MM(p, psE[:, cg * 512:(cg + 1) * 512],
                       kmv[hb:hb + 64, h // 2, t2b * 128:(t2b + 1) * 128],
                       qmv[hb:hb + 64, h // 2, cg * 512:(cg + 1) * 512])
                tS = p.tick("tensor")
                p.wait("scalar", tS)
                ACT(p, Psbv[t2b % 2][:, half * 1024:(half + 1) * 1024],
                    psE[:, half * 1024:(half + 1) * 1024], AF.Exp, scale=0.125)
                tE = p.tick("scalar")
                prev_act[half] = tE
                tEs.append(tE)
            for half in (0, 1):
                p.wait("tensor", tEs[half])
                for cg in (2 * half, 2 * half + 1):
                    MM(p, psF[0:65, cg * 512:(cg + 1) * 512],
                       vaug[:, t2b * 65:(t2b + 1) * 65],
                       Psbv[t2b % 2][:, cg * 512:(cg + 1) * 512],
                       start=(t2b == 0), stop=(t2b == 15))
        t = p.tick("tensor"); p.wait("scalar", t); p.wait("vector", t)
        ACT(p, mhav[hb:hb + 64, h // 2, :], psF[0:64, :], AF.Identity)
        tO = p.tick("scalar")
        TS(p, f_scr[64:65, 0:2048], psF[64:65, :], 1e-38, ALU.max)
        p.op("vector", lambda v, o=f_scr2[64:65, 0:2048], i=f_scr[64:65, 0:2048]:
             v.reciprocal(out=o, in_=i))
        COPY(p, denb[64:65, 0:2048], f_scr2[64:65, 0:2048])
        t = p.tick("vector"); p.wait("tensor", t, tO)
        for cg in range(4):
            MM(p, psE[hb:hb + 64, cg * 512:(cg + 1) * 512],
               sbc["c_ones2d"][64:65, 0:64],
               denb[64:65, cg * 512:(cg + 1) * 512])
        t = p.tick("tensor"); p.wait("vector", t)
        TT(p, mhav[hb:hb + 64, h // 2, :], mhav[hb:hb + 64, h // 2, :],
           psE[hb:hb + 64, :], ALU.mult)
        t = p.tick("vector"); p.wait("tensor", t); p.wait("scalar", t)
    p.fence()
    for sl_ in range(4):
        wtok[sl_] = p.last_tok["tensor"]
    if stop == "B5b":
        _dump_stop(p, nc, {"mha": BB0})
        return nc
    proj_resid_ln(mhav, dram["b_woutpT"], 0, x3sp, BB1, x4sp, BB1,
                  bias_col=sbp["b_outpb"], gamma=sbp["b_ln1g"], beta=sbp["b_ln1b"],
                  sq_scr=BB2[:, 0:2048])
    if stop == "B5":
        _dump_stop(p, nc, {"x4b": BB1})
        return nc

    x4bv = BB1.rearrange("p (c t) -> p c t", t=2048)
    wgrps = AL("wgrps", [128, 4 * 8 + 8 * 8], BF16).ap()
    DMA(p, "gpsimd", wgrps.rearrange("p (c e) -> p c e", e=12)[:, :, 0:4],
        dram["b_wgrp"].rearrange("(c p) e -> p c e", p=128), ch="inig")
    DMA(p, "gpsimd", wgrps.rearrange("p (c e) -> p c e", e=12)[:, :, 4:12],
        dram["b_wexp"].rearrange("(c p) e -> p c e", p=128), ch="inig")
    p.fence()
    wgv = wgrps.rearrange("p (c e) -> p c e", e=12)
    for cg in range(4):
        for k in range(8):
            MM(p, psE[0:4, cg * 512:(cg + 1) * 512], wgv[:, k, 0:4],
               x4bv[:, k, cg * 512:(cg + 1) * 512], start=(k == 0), stop=(k == 7))
        for k in range(8):
            MM(p, psF[0:8, cg * 512:(cg + 1) * 512], wgv[:, k, 4:12],
               x4bv[:, k, cg * 512:(cg + 1) * 512], start=(k == 0), stop=(k == 7))
    t = p.tick("tensor"); p.wait("scalar", t); p.wait("vector", t)
    R4a, R4b, R4c, R4d = R4zone
    ACT(p, R4a, psE[0:4, :], AF.Exp, bias=sbp["b_bgrp"][:, 0:1])
    t = p.tick("scalar"); p.wait("vector", t); p.wait("tensor", t)
    for st_, (pm, dst, src) in enumerate(((("c_P4a"), R4b, R4a), (("c_P4b"), R4c, R4b))):
        for cg in range(4):
            MM(p, psE[0:4, cg * 512:(cg + 1) * 512], sbc[pm][:, :],
               src[:, cg * 512:(cg + 1) * 512])
        t = p.tick("tensor"); p.wait("vector", t)
        TT(p, dst, src, psE[0:4, :], ALU.add)
        t = p.tick("vector"); p.wait("tensor", t)
    p.op("vector", lambda v, o=R4d, i=R4c: v.reciprocal(out=o, in_=i))
    TT(p, R4a, R4a, R4d, ALU.mult)
    t = p.tick("vector"); p.wait("tensor", t)
    for st_, (pm, dst, src) in enumerate(((("c_P4a"), R4b, R4a), (("c_P4b"), R4c, R4b))):
        for cg in range(4):
            MM(p, psE[0:4, cg * 512:(cg + 1) * 512], sbc[pm][:, :],
               src[:, cg * 512:(cg + 1) * 512])
        t = p.tick("tensor"); p.wait("vector", t)
        TT(p, dst, src, psE[0:4, :], ALU.max)
        t = p.tick("vector"); p.wait("tensor", t)
    TT(p, R4b, R4a, R4c, ALU.is_equal)
    t = p.tick("vector"); p.wait("tensor", t)
    for cg in range(4):
        MM(p, psE[0:4, cg * 512:(cg + 1) * 512], sbc["c_L4str"][:, :],
           R4b[:, cg * 512:(cg + 1) * 512])
    t = p.tick("tensor"); p.wait("vector", t)
    TS(p, R4d, psE[0:4, :], 0.0, ALU.is_equal)
    TT(p, R4b, R4b, R4d, ALU.mult)
    t = p.tick("vector"); p.wait("tensor", t); p.wait("scalar", t)
    ACT(p, R8, psF[0:8, :], AF.Exp, bias=sbp["b_bexp"][:, 0:1])
    t = p.tick("scalar"); p.wait("tensor", t); p.wait("vector", t)
    for cg in range(4):
        MM(p, psE[0:8, cg * 512:(cg + 1) * 512], sbc["c_Ppair8"][:, :],
           R8[:, cg * 512:(cg + 1) * 512])
    t = p.tick("tensor"); p.wait("vector", t)
    TT(p, R8b, R8, psE[0:8, :], ALU.add)
    p.op("vector", lambda v, o=R8d, i=R8b: v.reciprocal(out=o, in_=i))
    TT(p, R8, R8, R8d, ALU.mult)
    t = p.tick("vector"); p.wait("tensor", t)
    for cg in range(4):
        MM(p, psE[0:8, cg * 512:(cg + 1) * 512], sbc["c_dup48"][:, :],
           R4b[:, cg * 512:(cg + 1) * 512])
    t = p.tick("tensor"); p.wait("vector", t)
    TT(p, R8b, R8, psE[0:8, :], ALU.mult)
    t = p.tick("vector"); p.wait("tensor", t)
    for cg in range(4):
        MM(p, psE[0:8, cg * 512:(cg + 1) * 512], sbc["c_Ppair8"][:, :],
           R8b[:, cg * 512:(cg + 1) * 512])
    t = p.tick("tensor"); p.wait("vector", t)
    TT(p, R8d, R8b, psE[0:8, :], ALU.is_gt)
    TT(p, R8, R8b, psE[0:8, :], ALU.is_equal)
    TS(p, R8, R8, sbc["c_even8"][:, 0:1], ALU.mult)
    TT(p, R8d, R8d, R8, ALU.add)
    t = p.tick("vector"); p.wait("tensor", t)
    for cg in range(4):
        MM(p, psF[0:8, cg * 512:(cg + 1) * 512], sbc["c_dup48"][:, :],
           R4b[:, cg * 512:(cg + 1) * 512])
    t = p.tick("tensor"); p.wait("vector", t)
    TT(p, R8d, R8d, psF[0:8, :], ALU.mult)
    TT(p, R8, R8d, R8b, ALU.mult)
    t = p.tick("vector"); p.wait("tensor", t)
    for cg in range(4):
        MM(p, psE[0:1, cg * 512:(cg + 1) * 512], sbc["c_ones2df"][0:8, 0:1],
           R8[:, cg * 512:(cg + 1) * 512])
    t = p.tick("tensor"); p.wait("vector", t)
    TT(p, rows1[0:1, 0:2048], R4c[0:1, :], psE[0:1, :], ALU.mult)
    t = p.tick("vector"); p.wait("tensor", t)
    for cg in range(4):
        MM(p, psE[0:8, cg * 512:(cg + 1) * 512], sbc["c_ones2df"][0:1, 0:8],
           rows1[0:1, cg * 512:(cg + 1) * 512])
    t = p.tick("tensor"); p.wait("vector", t)
    TT(p, R8b, R8d, psE[0:8, :], ALU.mult)
    COPY(p, RBb[0:8, :], R8b)
    p.fence()
    if stop == "B6":
        _dump_stop(p, nc, {"cm": R8b, "hot": R8d})
        return nc

    hLv = [BB2.rearrange("p (c t) -> p c t", t=2048),
           BB3.rearrange("p (c t) -> p c t", t=2048)]
    acc1 = BB0
    acc1v = BB0.rearrange("p (c t) -> p c t", t=2048)
    MEMSET(p, acc1[:, :], 0.0)
    t = p.tick("vector"); p.wait("tensor", t)
    for e in range(8):
        evp = [None, None]
        for m in range(16):
            slot = m % 4
            tL = colblock_load(wslot[slot], dram["b_w1L1"][e], m * 128, slot)
            wv_ = wslot[slot].rearrange("p (c m) -> p c m", m=128)
            p.wait("tensor", tL)
            if evp[m % 2] is not None:
                p.wait("tensor", evp[m % 2])
            ps = psE if m % 2 == 0 else psF
            for cg in range(4):
                for k in range(8):
                    MM(p, ps[:, cg * 512:(cg + 1) * 512], wv_[:, k],
                       x4bv[:, k, cg * 512:(cg + 1) * 512],
                       start=(k == 0), stop=(k == 7))
            tMM = p.tick("tensor")
            wtok[slot] = tMM
            p.wait("scalar", tMM)
            ACT(p, hLv[m // 8][:, m % 8], ps, AF.Relu,
                bias=sbp["b_b1L1"][:, e * 16 + m:e * 16 + m + 1])
            evp[m % 2] = p.tick("scalar")
        p.wait("tensor", evp[1])
        for cg in range(4):
            MM(p, psE[:, cg * 512:(cg + 1) * 512],
               sbc["c_esel16"][0:8, e * 128:(e + 1) * 128],
               RBb[0:8, cg * 512:(cg + 1) * 512])
        t = p.tick("tensor"); p.wait("vector", t)
        for m in range(16):
            TT(p, hLv[m // 8][:, m % 8], hLv[m // 8][:, m % 8], psE, ALU.mult)
        tMk = p.tick("vector")
        p.wait("tensor", tMk)
        stt_tok = [None, None]
        for m8 in range(8):
            slot = m8 % 4
            if wtok[slot] is not None:
                p.wait("gpsimd", wtok[slot])
            tL = DMA(p, "gpsimd",
                     wslot[slot].rearrange("p (c m) -> p c m", m=128),
                     dram["b_w2L1"][e].rearrange("(c p) m -> p c m", p=128)
                     [:, :, m8 * 128:(m8 + 1) * 128], ch=f"wl{slot}")
            wv_ = wslot[slot].rearrange("p (c m) -> p c m", m=128)
            psw = psF if m8 % 2 == 0 else psE
            p.wait("tensor", tL)
            if stt_tok[m8 % 2] is not None:
                p.wait("tensor", stt_tok[m8 % 2])
            for cg in range(4):
                pt = psw[:, cg * 512:(cg + 1) * 512]
                for k in range(16):
                    MM(p, pt, wv_[:, k],
                       hLv[k // 8][:, k % 8, cg * 512:(cg + 1) * 512],
                       start=(k == 0), stop=False)
                MM(p, pt, b2L1b[:, m8 * 128:(m8 + 1) * 128],
                   RBb[0:8, cg * 512:(cg + 1) * 512], start=False, stop=True)
            t = p.tick("tensor")
            wtok[slot] = t
            p.wait("vector", t)
            STT(p, acc1v[:, m8], psw, 1.0, acc1v[:, m8], ALU.bypass, ALU.add)
            stt_tok[m8 % 2] = p.tick("vector")
        p.wait("tensor", stt_tok[0], stt_tok[1])
    p.fence()
    if stop == "B7":
        _dump_stop(p, nc, {"acc1": BB0})
        return nc

    zbv2 = BB2.rearrange("p (c t) -> p c t", t=2048)
    sl_tok = [None, None]
    for m in range(8):
        par = m % 2
        if sl_tok[par] is not None:
            p.wait("sync", sl_tok[par])
        tX = DMA(p, "sync", f_slab[par], x4sp[m * 128:(m + 1) * 128, :], ch=f"xf{par}")
        p.wait("vector", tX)
        STT(p, zbv2[:, m], f_slab[par], 1.0, acc1v[:, m], ALU.bypass, ALU.add)
        tZ = p.tick("vector")
        sl_tok[par] = tZ
        p.wait("tensor", tZ)
        stats_sum(psF[0:1], zbv2[:, m], m)
    stats_finalize(psF[0:1], f_murow, f_rsrow, lambda m: zbv2[:, m], BB3[:, 0:2048])
    bcast_mu_rs(f_murow, f_rsrow, mu_b, rs_b)
    sl_tok = [None, None]
    for m in range(8):
        par = m % 2
        if sl_tok[par] is not None:
            p.wait("sync", sl_tok[par])
        tX = DMA(p, "sync", f_slab[par], x3sp[m * 128:(m + 1) * 128, :], ch=f"xf{par}")
        TT(p, f_scr, zbv2[:, m], muB, ALU.subtract)
        TT(p, f_scr, f_scr, rsB, ALU.mult)
        TS(p, f_scr, f_scr, sbp["b_ln2g"][:, m:m + 1], ALU.mult,
           sbp["b_ln2b"][:, m:m + 1], ALU.add)
        p.wait("vector", tX)
        TT(p, zbv2[:, m], f_scr, f_slab[par], ALU.add)
        tU = p.tick("vector")
        sl_tok[par] = tU
        p.wait("tensor", tU)
        stats_sum(psF[0:1], zbv2[:, m], m)
    stats_finalize(psF[0:1], f_murow, f_rsrow, lambda m: zbv2[:, m], BB3[:, 0:2048])
    bcast_mu_rs(f_murow, f_rsrow, mu_b, rs_b)
    xs_tok = [None, None]
    for m in range(8):
        par = m % 2
        if xs_tok[par] is not None:
            p.wait("vector", *xs_tok[par])
        fs = f_slab[par]
        TT(p, fs, zbv2[:, m], muB, ALU.subtract)
        TT(p, fs, fs, rsB, ALU.mult)
        t = p.tick("vector"); p.wait("sync", t)
        tD = DMA(p, "sync", yT[m * 128:(m + 1) * 128, :], fs, ch=f"yo{par}")
        xs_tok[par] = (p.tick("vector") if False else t, tD)
    p.wait_chan_totals("gpsimd", ["yo0", "yo1"])
    p.op("gpsimd", lambda g: g.engine_nop())
    p.tick("gpsimd")

    p.build()
    return nc


# version 05
# speedup vs baseline: 1.1489x; 1.0145x over previous
import numpy as np
import ml_dtypes
import concourse.bass as bass
import concourse.mybir as mybir
from concourse.bass_utils import run_bass_kernel_spmd

F32 = mybir.dt.float32
BF16 = mybir.dt.bfloat16
I32 = mybir.dt.int32
AF = mybir.ActivationFunctionType
ALU = mybir.AluOpType
AX = mybir.AxisListType

BF = ml_dtypes.bfloat16

S = 2048
D = 1024
H = 16
HD = 64
NP = 8
C = 128
NCH = S // C

ENGS = ("sync", "tensor", "vector", "scalar", "gpsimd")


class Prog:

    def __init__(self, nc):
        self.nc = nc
        self.q = {e: [] for e in ENGS}
        self.cnt = {e: 0 for e in ENGS}
        self.last_tok = {e: (e, 0) for e in ENGS}
        self._dirty = {e: False for e in ENGS}
        self._last_dma = {e: False for e in ENGS}
        self.chans = {}
        self._fence_seen = {}

    SELF_SERIAL = ("vector", "scalar")

    def op(self, eng, fn, dma=False):
        import traceback
        src = traceback.extract_stack(limit=4)
        if eng in self.SELF_SERIAL:
            if self.cnt[eng] > 0:
                self.q[eng].append(["wait", eng, self.cnt[eng]])
            self.q[eng].append(["op", fn, 0, src, eng])
            self._dirty[eng] = True
            self._last_dma[eng] = dma
            self.tick(eng)
            return
        self.q[eng].append(["op", fn, 0, src, eng])
        self._dirty[eng] = True
        self._last_dma[eng] = dma

    def tick(self, eng):
        ent = self.q[eng][-1]
        assert ent[0] == "op"
        inc = 16 if self._last_dma[eng] else 1
        if ent[2] == 0:
            ent[2] = inc
            self.cnt[eng] += inc
        self._dirty[eng] = False
        self.last_tok[eng] = (eng, self.cnt[eng])
        return self.last_tok[eng]

    def dma(self, eng, out, in_, ch):
        if ch not in self.chans:
            self.chans[ch] = 0
        self.chans[ch] += 16
        tok = ("#" + ch, self.chans[ch])
        self.q[eng].append(["op", lambda g, o=out, i=in_:
                            g.dma_start(out=o, in_=i), 16, None, "#" + ch])
        return tok

    def wait(self, eng, *toks):
        for (e2, v) in toks:
            if v > 0:
                self.q[eng].append(["wait", e2, v])

    def wait_chan_totals(self, eng, chans):
        for ch in chans:
            if self.chans.get(ch, 0) > 0:
                self.q[eng].append(["wait", "#" + ch, self.chans[ch]])

    def fence(self, *engs):
        engs = engs or ENGS
        toks = []
        for e in engs:
            if self._dirty[e]:
                toks.append(self.tick(e))
            elif self.cnt[e] > 0:
                toks.append(self.last_tok[e])
        for ch, cnt in self.chans.items():
            if cnt > 0 and self._fence_seen.get(ch) != cnt:
                toks.append(("#" + ch, cnt))
                self._fence_seen[ch] = cnt
        for e in engs:
            self.wait(e, *[t for t in toks if t[0] != e])
        return toks

    def build(self):
        nc = self.nc
        sems = {}
        ctxs = []
        for e in ENGS:
            cm = nc.semaphore(f"sem_{e}")
            sems[e] = cm.__enter__()
            ctxs.append(cm)
        for ch in self.chans:
            cm = nc.semaphore(f"ch_{ch}")
            sems["#" + ch] = cm.__enter__()
            ctxs.append(cm)
        blkcm = nc.Block()
        blk = blkcm.__enter__()
        for ename in ENGS:
            entries = self.q[ename]

            def body(eng, entries=entries, ename=ename):
                for ent in entries:
                    if ent[0] == "wait":
                        eng.wait_ge(sems[ent[1]], ent[2])
                    else:
                        try:
                            inst = ent[1](eng)
                        except Exception:
                            if ent[3] is not None:
                                print("FAILED OP emitted at:")
                                for fr in ent[3]:
                                    print(f"  {fr.filename}:{fr.lineno} {fr.line}")
                            raise
                        if ent[2]:
                            inst.then_inc(sems[ent[4]], ent[2])

            getattr(blk, ename)(body)
        blkcm.__exit__(None, None, None)
        for cm in ctxs:
            cm.__exit__(None, None, None)



def MM(p, out, lhsT, rhs, start=True, stop=True):
    p.op("tensor", lambda t, o=out, l=lhsT, r=rhs, a=start, b=stop:
         t.matmul(o, l, r, start=a, stop=b))


def DMA(p, eng, out, in_, ch="ini"):
    return p.dma(eng, out, in_, ch)


def ACT(p, out, in_, func, bias=0.0, scale=1.0):
    p.op("scalar", lambda s, o=out, i=in_, f=func, b=bias, sc=scale:
         s.activation(out=o, in_=i, func=f, bias=b, scale=sc))


def TT(p, out, in0, in1, alu, eng="vector"):
    p.op(eng, lambda v, o=out, a=in0, b=in1, op=alu:
         v.tensor_tensor(out=o, in0=a, in1=b, op=op))


def TS(p, out, in0, s1, op0, s2=None, op1=None, eng="vector"):
    if s2 is None:
        p.op(eng, lambda v, o=out, a=in0, x=s1, op=op0:
             v.tensor_single_scalar(out=o, in_=a, scalar=x, op=op))
    else:
        p.op(eng, lambda v, o=out, a=in0, x=s1, y=s2, oa=op0, ob=op1:
             v.tensor_scalar(out=o, in0=a, scalar1=x, scalar2=y, op0=oa, op1=ob))


def STT(p, out, in0, scal, in1, op0, op1, eng="vector"):
    p.op(eng, lambda v, o=out, a=in0, s=scal, b=in1, oa=op0, ob=op1:
         v.scalar_tensor_tensor(out=o, in0=a, scalar=s, in1=b, op0=oa, op1=ob))


def COPY(p, out, in_, eng="vector"):
    p.op(eng, lambda v, o=out, i=in_: v.tensor_copy(out=o, in_=i))


def SCAN(p, out, d0, d1, init, op0, op1):
    p.op("vector", lambda v, o=out, a=d0, b=d1, i=init, oa=op0, ob=op1:
         v.tensor_tensor_scan(out=o, data0=a, data1=b, initial=i, op0=oa, op1=ob))


def REDUCE(p, out, in_, alu, eng="vector"):
    p.op(eng, lambda v, o=out, i=in_, op=alu:
         v.tensor_reduce(out=o, in_=i, axis=AX.X, op=op))


def MEMSET(p, ap, val, eng="vector"):
    p.op(eng, lambda v, a=ap, x=val: v.memset(a, x))


def _dump_stop(p, nc, tiles):
    p.fence()
    for nm, ap in tiles.items():
        d = nc.dram_tensor("dbg_" + nm, list(ap.shape), ap.dtype,
                           kind="ExternalOutput").ap()
        DMA(p, "sync", d, ap, ch="dbg")
    p.wait_chan_totals("gpsimd", ["dbg"])
    p.op("gpsimd", lambda g: g.engine_nop())
    p.tick("gpsimd")
    p.build()



def _host_consts_A():
    c = {}
    c["c_id16f"] = np.vstack([np.eye(16)] * 3).astype(np.float32)
    c["c_id64b"] = np.vstack([np.eye(64), np.eye(64)]).astype(BF)
    c["c_id128b"] = np.eye(128, dtype=BF)
    c["c_ones1f"] = np.ones((1, 128), np.float32)
    c["c_ones1b"] = np.ones((1, 128), BF)
    c["c_ones128b"] = np.ones((128, 1), BF)
    t = np.arange(128)
    ls = (t[None, :] < t[:, None]).astype(np.float32)
    us = (t[None, :] > t[:, None]).astype(np.float32)
    ui = (t[None, :] >= t[:, None]).astype(np.float32)
    c["c_maskLsNegB"] = np.tile(-ls, (1, 16)).astype(BF)
    c["c_maskUsNegB"] = np.tile(-us, (1, 16)).astype(BF)
    c["c_maskUiB"] = np.tile(ui, (1, 16)).astype(BF)
    sel = np.zeros((16, 8 * 128), np.float32)
    for hp in range(8):
        sel[2 * hp, hp * 128: hp * 128 + 64] = 1.0
        sel[2 * hp + 1, hp * 128 + 64: hp * 128 + 128] = 1.0
    c["c_selpairB"] = sel.astype(BF)
    bo = np.zeros((128, 2), np.float32)
    bo[0:64, 0] = 1.0
    bo[64:128, 1] = 1.0
    c["c_bo2"] = bo.astype(BF)
    c["c_bo2T"] = bo.T.astype(BF).copy()
    bd = np.zeros((16, 1024), np.float32)
    for g in range(16):
        bd[g, g * 64:(g + 1) * 64] = 1.0
    c["c_BD16f"] = bd
    c["c_ones16_64f"] = np.ones((16, 64), np.float32)
    return c


def _layout_A_weights(w):
    d = {}
    d["a_wq"] = np.ascontiguousarray(w["L0_attn_Wq"])
    d["a_wk"] = np.ascontiguousarray(w["L0_attn_Wk"])
    d["a_wv"] = np.ascontiguousarray(w["L0_attn_Wv"])
    d["a_wg"] = np.ascontiguousarray(w["L0_attn_Wg"])
    d["a_wo"] = np.ascontiguousarray(w["L0_attn_Wo"])
    d["a_wa"] = np.ascontiguousarray(w["L0_attn_Wa"])
    d["a_wb"] = np.ascontiguousarray(w["L0_attn_Wb"])
    for nm, key in (("a_bq", "L0_attn_bq"), ("a_bk", "L0_attn_bk"),
                    ("a_bv", "L0_attn_bv"), ("a_bg", "L0_attn_bg"),
                    ("a_bo", "L0_attn_bo"), ("a_gns", "L0_attn_gn_scale"),
                    ("a_gnb", "L0_attn_gn_bias")):
        d[nm] = np.ascontiguousarray(w[key].reshape(8, 128).T)
    d["a_ba"] = np.ascontiguousarray(w["L0_attn_ba"].reshape(16, 1))
    d["a_bb"] = np.ascontiguousarray(w["L0_attn_bb"].reshape(16, 1))
    for nm, key in (("a_qc", "L0_attn_q_conv"), ("a_kc", "L0_attn_k_conv"),
                    ("a_vc", "L0_attn_v_conv")):
        d[nm] = np.ascontiguousarray(
            w[key].reshape(8, 128, 4).transpose(1, 0, 2).reshape(128, 32))
    d["a_wgg"] = np.ascontiguousarray(w["L0_ffn_wg"])
    d["a_bgg"] = np.ascontiguousarray(w["L0_ffn_bg"].reshape(1, 16))
    return d




def build_A(stop=None, nch=NCH, nstage=99):
    nc = bass.Bass("TRN2", target_bir_lowering=False)
    p = Prog(nc)

    x0T = nc.dram_tensor("x0T", [D, S], F32, kind="ExternalInput").ap()
    x1T = nc.dram_tensor("x1T", [D, S], F32, kind="ExternalOutput").ap()
    maskedT = nc.dram_tensor("maskedT", [128, 256], F32, kind="ExternalOutput").ap()
    spill = nc.dram_tensor("spillA", [3, NCH, 128, NP * C], BF16, kind="Internal").ap()

    dram = {}
    for nm in ("a_wq", "a_wk", "a_wv", "a_wg", "a_wo"):
        dram[nm] = nc.dram_tensor(nm, [D, D], F32, kind="ExternalInput").ap()
    for nm in ("a_wa", "a_wb", "a_wgg"):
        dram[nm] = nc.dram_tensor(nm, [D, 16], F32, kind="ExternalInput").ap()
    small_params = (("a_bq", [128, 8]), ("a_bk", [128, 8]), ("a_bv", [128, 8]),
                    ("a_bg", [128, 8]), ("a_bo", [128, 8]), ("a_gns", [128, 8]),
                    ("a_gnb", [128, 8]), ("a_ba", [16, 1]), ("a_bb", [16, 1]),
                    ("a_qc", [128, 32]), ("a_kc", [128, 32]), ("a_vc", [128, 32]),
                    ("a_bgg", [1, 16]))
    for nm, shp in small_params:
        dram[nm] = nc.dram_tensor(nm, shp, F32, kind="ExternalInput").ap()
    cshapes = (("c_id16f", [48, 16], F32), ("c_id64b", [128, 64], BF16),
               ("c_id128b", [128, 128], BF16),
               ("c_ones1f", [1, 128], F32), ("c_ones1b", [1, 128], BF16),
               ("c_ones128b", [128, 1], BF16),
               ("c_maskLsNegB", [128, 2048], BF16), ("c_maskUsNegB", [128, 2048], BF16),
               ("c_maskUiB", [128, 2048], BF16), ("c_selpairB", [16, 1024], BF16),
               ("c_bo2", [128, 2], BF16), ("c_bo2T", [2, 128], BF16))
    for nm, shp, dt in cshapes:
        dram[nm] = nc.dram_tensor(nm, shp, dt, kind="ExternalInput").ap()

    A = nc.alloc_sbuf_tensor
    x0b = A("x0b", [128, 8 * 2048], BF16).ap()
    chain = A("chain", [128, 8 * 2048], BF16).ap()
    outs = A("outs", [128, 8 * 2048], BF16).ap()
    wbuf = A("wbuf", [128, 8 * 1024], BF16).ap()
    smal = A("smal", [128, 8 * 1024], BF16).ap()
    smal2 = A("smal2", [128, 4 * 1024], BF16).ap()
    fsc = A("fsc", [128, 10240], F32).ap()
    stf = A("stf", [128, 512], F32).ap()
    stb = A("stb", [128, 512], BF16).ap()
    wvt = A("wvt", [128, 1024], BF16).ap()
    gate = A("gate", [128, 256], F32).ap()
    gate2 = A("gate2", [128, 256], F32).ap()
    m8t = A("m8t", [128, 8], F32).ap()
    mk16 = A("mk16", [128, 16], F32).ap()
    betaBb = A("betaBb", [16, 2048], BF16).ap()
    normb = A("normb", [2, 2048], BF16).ap()
    cumTf = A("cumTf", [128, 16], F32).ap()
    gTnegf = A("gTnegf", [128, 16], F32).ap()
    gslb = A("gslb", [16, 128], BF16).ap()
    dslb = A("dslb", [16, 128], BF16).ap()
    gsl32 = A("gsl32", [16, 128], F32).ap()
    gccol = A("gccol", [128, 8], BF16).ap()
    gcb16 = A("gcb16", [16, 1], BF16).ap()
    statm = A("statm", [2, 8], F32).ap()
    statq = A("statq", [2, 8], F32).ap()
    statv = A("statv", [2, 8], F32).ap()
    statmb = A("statmb", [2, 8], BF16).ap()
    statrb = A("statrb", [2, 8], BF16).ap()
    gcol2 = A("gcol2", [128, 2], BF16).ap()
    murb = A("murb", [1, 2048], BF16).ap()
    rsrb = A("rsrb", [1, 2048], BF16).ap()
    sbc = {}
    for nm, shp, dt in cshapes:
        sbc[nm] = A(nm + "_s", shp, dt).ap()
    sbp = {}
    for nm, shp in small_params:
        sbp[nm] = A(nm + "_s", shp, F32).ap()
    wab = A("wab", [128, 8 * 16 * 3], BF16).ap()
    bggb = A("bggb", [1, 16], BF16).ap()

    f_tmpd = fsc[:, 0:2048]
    f_x0fA = fsc[:, 0:2048]
    f_x0fB = fsc[:, 2048:4096]
    f_p1a = fsc[0:16, 2048:4096]
    f_nrmA = fsc[32:34, 2048:4096]
    f_nrmB = fsc[64:66, 2048:4096]
    f_cum = fsc[0:16, 4096:6144]
    f_rev = fsc[32:48, 4096:6144]
    f_beta = fsc[64:80, 4096:6144]
    f_lnA = fsc[0:16, 6144:8192]
    f_murow = fsc[32:33, 6144:8192]
    f_cumflat = fsc[0:1, 6144:8192]
    f_rstdrow = fsc[64:65, 8192:10240]
    f_mu2row = fsc[32:33, 8192:10240]

    cs = [chain[:, i * 2048:(i + 1) * 2048] for i in range(8)]
    sY1, sZ1, sAt, s3, s4, s5, s6, sTmp = cs

    pairbuf = [smal[:, 0:2048], smal[:, 2048:4096]]
    sqbuf = [smal[:, 4096:5120], smal[:, 5120:6144]]
    convsc = [smal[:, 6144:6704], smal[:, 6704:7264]]
    qch = [smal[:, 0:1024], smal[:, 1024:2048]]
    kch = [smal[:, 2048:3072], smal[:, 3072:4096]]
    vch = [smal[:, 4096:5120], smal[:, 5120:6144]]
    kstar = smal[:, 6144:7168]
    qplus = smal[:, 7168:8192]
    muBCb = smal[:, 0:2048]
    rsBCb = smal[:, 2048:4096]
    sqgn = [smal[:, 4096:6144], smal[:, 6144:8192]]
    vTsb = smal2[:, 0:1024]
    vTsb = smal2[:, 3072:4096]
    kdec2dm = smal2[:, 1024:2048]
    kdec2tm = smal2[:, 2048:3072]
    gtmp = [smal2[:, 0:2048], smal2[:, 2048:4096]]

    psA = nc.alloc_psum_tensor("psA", [128, 2048], F32).ap()
    psB = nc.alloc_psum_tensor("psB", [128, 1024], F32).ap()
    psC = nc.alloc_psum_tensor("psC", [128, 512], F32).ap()
    psD = nc.alloc_psum_tensor("psD", [128, 512], F32).ap()

    for nm, shp, dt in cshapes:
        DMA(p, "sync", sbc[nm][:, :], dram[nm])
    for nm, shp in small_params:
        DMA(p, "sync", sbp[nm][:, :], dram[nm])
    DMA(p, "gpsimd", x0b.rearrange("p (c t) -> p c t", t=2048),
        x0T.rearrange("(c p) t -> p c t", p=128), ch="inig")
    for i, nm in enumerate(("a_wa", "a_wb", "a_wgg")):
        DMA(p, "gpsimd", wab.rearrange("p (j c e) -> p j c e", j=3, c=8)[:, i],
            dram[nm].rearrange("(c p) e -> p c e", p=128))
    COPY(p, bggb[:, :], sbp["a_bgg"][:, :])
    MEMSET(p, cbias[:, 0:1], 1e-5)
    MEMSET(p, cbias[:, 1:2], 1.0)
    p.fence()

    wa_v = wab.rearrange("p (j c e) -> p j c e", j=3, c=8)
    x0v = x0b.rearrange("p (c t) -> p c t", t=2048)

    def ab_logits(j):
        for cg in range(4):
            for k in range(8):
                MM(p, psA[0:16, cg * 512:(cg + 1) * 512], wa_v[:, j, k],
                   x0v[:, k, cg * 512:(cg + 1) * 512], start=(k == 0), stop=(k == 7))

    ab_logits(0)
    tA = p.tick("tensor")
    p.wait("scalar", tA)
    ACT(p, f_p1a, psA[0:16, :], AF.Abs, bias=sbp["a_ba"][:, 0:1])
    ACT(p, f_lnA, f_p1a, AF.Exp, scale=-1.0)
    ACT(p, f_p1a, f_lnA, AF.Ln, bias=cbias[0:16, 1:2])
    tS = p.tick("scalar")
    p.wait("vector", tS)
    TS(p, f_lnA, psA[0:16, :], sbp["a_ba"][:, 0:1], ALU.add, 0.0, ALU.min)
    TT(p, f_lnA, f_lnA, f_p1a, ALU.subtract)
    tV = p.tick("vector")
    p.wait("tensor", tV)
    ab_logits(1)
    tA = p.tick("tensor")
    p.wait("scalar", tA)
    ACT(p, f_beta, psA[0:16, :], AF.Sigmoid, bias=sbp["a_bb"][:, 0:1])
    tS = p.tick("scalar")
    p.wait("vector", tS)
    COPY(p, betaBb[:, :], f_beta)
    for c in range(NCH):
        sl = slice(c * C, (c + 1) * C)
        SCAN(p, f_cum[:, sl], f_lnA[:, sl], f_lnA[:, sl], 0.0, ALU.add, ALU.bypass)
        TS(p, f_rev[:, sl], f_cum[:, sl], f_cum[:, c * C + C - 1:c * C + C],
           ALU.subtract, -1.0, ALU.mult)
    p.fence()

    tens_cfg = (("a_wq", "a_bq", "a_qc", True, 0), ("a_wk", "a_bk", "a_kc", True, 1),
                ("a_wv", "a_bv", "a_vc", False, 2))
    wslot = [wbuf[:, i * 1024:(i + 1) * 1024] for i in range(4)]
    wslot_tok = [None] * 4
    pair_tok = [None, None]
    psA_half = [psA[:, 0:1024], psA[:, 1024:2048]]
    psA_tok = [None, None]
    si = 0
    for (wnm, bnm, cnm, do_norm, jidx) in tens_cfg:
        wv_ = dram[wnm].rearrange("(c p) m -> p c m", p=128)
        for hp in range(8):
            par = si % 2
            ws = wslot[si % 8]
            if wslot_tok[si % 8] is not None:
                p.wait("gpsimd", wslot_tok[si % 8])
            DMA(p, "gpsimd", ws.rearrange("p (c m) -> p c m", m=128),
                wv_[:, :, hp * 128:(hp + 1) * 128])
            tL = p.tick("gpsimd")
            p.wait("tensor", tL)
            if psA_tok[par] is not None:
                p.wait("tensor", psA_tok[par])
            if pair_tok[par] is not None:
                p.wait("tensor", pair_tok[par])
            wsv = ws.rearrange("p (c m) -> p c m", m=128)
            for h2 in range(2):
                for k in range(8):
                    MM(p, psA_half[par][:, h2 * 512:(h2 + 1) * 512], wsv[:, k],
                       x0v[:, k, (par * 0 + 1024 * h2 + 512 * 0):0], start=False)
            si += 1
    p.fence()
    p.build()
    return nc


A_SMALL_PARAMS = (("a_bq", [128, 8]), ("a_bk", [128, 8]), ("a_bv", [128, 8]),
                  ("a_bg", [128, 8]), ("a_bo", [128, 8]), ("a_gns", [128, 8]),
                  ("a_gnb", [128, 8]), ("a_ba", [16, 1]), ("a_bb", [16, 1]),
                  ("a_qc", [128, 32]), ("a_kc", [128, 32]), ("a_vc", [128, 32]),
                  ("a_bgg", [1, 16]))
A_CONSTS = (("c_id16f", [48, 16], F32), ("c_id64b", [128, 64], BF16),
            ("c_ones1f", [1, 128], F32), ("c_ones1b", [1, 128], BF16),
            ("c_ones128b", [128, 1], BF16),
            ("c_maskLsNegB", [128, 2048], BF16), ("c_maskUsNegB", [128, 2048], BF16),
            ("c_maskUiB", [128, 2048], BF16), ("c_selpairB", [16, 1024], BF16),
            ("c_bo2", [128, 2], BF16), ("c_bo2T", [2, 128], BF16))


def hsl(ap, h, w=128):
    return ap[64 * (h % 2):64 * (h % 2) + 64, (h // 2) * w:(h // 2) * w + w]


def wave16(p, emit):
    for h in range(0, 16, 2):
        emit(h)
    t = p.tick("tensor")
    p.wait("tensor", t)
    for h in range(1, 16, 2):
        emit(h)
    return p.tick("tensor")


def build_A(stop=None, nch=NCH, nstage=99):
    nc = bass.Bass("TRN2", target_bir_lowering=False)
    p = Prog(nc)

    x0T = nc.dram_tensor("x0T", [D, S], F32, kind="ExternalInput").ap()
    x1T = nc.dram_tensor("x1T", [D, S], F32, kind="ExternalOutput").ap()
    maskedT = nc.dram_tensor("maskedT", [128, 256], F32, kind="ExternalOutput").ap()
    spill = nc.dram_tensor("spillA", [3, NCH, 128, NP * C], BF16, kind="Internal").ap()

    dram = {}
    for nm in ("a_wq", "a_wk", "a_wv", "a_wg", "a_wo"):
        dram[nm] = nc.dram_tensor(nm, [D, D], F32, kind="ExternalInput").ap()
    for nm in ("a_wa", "a_wb", "a_wgg"):
        dram[nm] = nc.dram_tensor(nm, [D, 16], F32, kind="ExternalInput").ap()
    for nm, shp in A_SMALL_PARAMS:
        dram[nm] = nc.dram_tensor(nm, shp, F32, kind="ExternalInput").ap()
    for nm, shp, dt in A_CONSTS:
        dram[nm] = nc.dram_tensor(nm, shp, dt, kind="ExternalInput").ap()

    AL = nc.alloc_sbuf_tensor
    x0b = AL("x0b", [128, 8 * 2048], BF16).ap()
    chain = AL("chain", [128, 8 * 2048], BF16).ap()
    outs = AL("outs", [128, 8 * 2048], BF16).ap()
    wbuf = AL("wbuf", [128, 4 * 1024], BF16).ap()
    smal = AL("smal", [128, 8 * 1024], BF16).ap()
    smal2 = AL("smal2", [128, 4 * 1024], BF16).ap()
    fsc = AL("fsc", [128, 10240], F32).ap()
    stf = AL("stf", [128, 512], F32).ap()
    stb = AL("stb", [128, 512], BF16).ap()
    wvt = AL("wvt", [128, 1024], BF16).ap()
    gate = AL("gate", [128, 256], F32).ap()
    gate2 = AL("gate2", [128, 256], F32).ap()
    m8t = AL("m8t", [128, 8], F32).ap()
    scr16 = AL("scr16", [128, 16], F32).ap()
    betaBb = AL("betaBb", [16, 2048], BF16).ap()
    normb = AL("normb", [2, 2048], BF16).ap()
    cumTf = AL("cumTf", [128, 16], F32).ap()
    gTnegf = AL("gTnegf", [128, 16], F32).ap()
    gslb = AL("gslb", [16, 128], BF16).ap()
    dslb = AL("dslb", [16, 128], BF16).ap()
    gsl32 = AL("gsl32", [16, 128], F32).ap()
    gccol = AL("gccol", [128, 8], F32).ap()
    statm = AL("statm", [2, 8], F32).ap()
    statq = AL("statq", [2, 8], F32).ap()
    statv = AL("statv", [2, 8], F32).ap()
    statmb = AL("statmb", [2, 8], BF16).ap()
    statrb = AL("statrb", [2, 8], BF16).ap()
    gcol2 = AL("gcol2", [128, 2], BF16).ap()
    gcolA = AL("gcolA", [128, 16], F32).ap()
    sbc = {}
    for nm, shp, dt in A_CONSTS:
        sbc[nm] = AL(nm + "_s", shp, dt).ap()
    sbp = {}
    for nm, shp in A_SMALL_PARAMS:
        sbp[nm] = AL(nm + "_s", shp, F32).ap()
    wab = AL("wab", [128, 8 * 16 * 3], BF16).ap()
    murb = betaBb[0:1, 0:2048]
    rsrb = normb[0:1, 0:2048]

    bggb = AL("bggb", [1, 16], BF16).ap()
    cbias = AL("cbias", [128, 2], F32).ap()

    f_tmpd = fsc[:, 0:2048]
    f_x0f = [fsc[:, 0:2048], fsc[:, 2048:4096]]
    f_p1a = fsc[0:16, 2048:4096]
    f_nrmA = fsc[0:2, 0:2048]
    f_nrmB = fsc[0:2, 2048:4096]
    f_beta = fsc[0:16, 4096:6144]
    f_lnA = fsc[0:16, 6144:8192]
    f_rev = fsc[0:16, 6144:8192]
    f_cum = fsc[0:16, 8192:10240]
    f_cumflat = fsc[0:1, 2048:4096]
    f_murow = fsc[0:1, 4096:6144]
    f_mu2row = fsc[0:1, 6144:8192]
    f_rstdrow = fsc[0:1, 8192:10240]

    cslots = [chain[:, i * 2048:(i + 1) * 2048] for i in range(8)]
    sY1, sZ1, sAt, s3, s4, s5, s6, sTmp = cslots

    pairbuf = [smal[:, 0:2048], smal[:, 2048:4096]]
    sqbuf = [smal[:, 4096:6144], smal[:, 6144:8192]]
    qch = [smal[:, 0:1024], smal[:, 1024:2048]]
    kch = [smal[:, 2048:3072], smal[:, 3072:4096]]
    vch = [smal[:, 4096:5120], smal[:, 5120:6144]]
    kstar = smal[:, 6144:7168]
    qplus = smal[:, 7168:8192]
    muBCb = smal[:, 0:2048]
    rsBCb = smal[:, 2048:4096]
    sqgn = [smal[:, 4096:6144], smal[:, 6144:8192]]
    vTsb = smal2[:, 3072:4096]
    kdec2dm = smal2[:, 1024:2048]
    kdec2tm = smal2[:, 2048:3072]
    gtmp = [smal2[:, 0:2048], smal2[:, 2048:4096]]

    psE = nc.alloc_psum_tensor("psE", [128, 2048], F32).ap()
    psF = nc.alloc_psum_tensor("psF", [128, 2048], F32).ap()
    psB = psF[:, 0:1024]
    psSt = psF[:, 1024:1536]
    psG = psF[:, 1536:2048]

    x0v = x0b.rearrange("p (c t) -> p c t", t=2048)
    wabv = wab.rearrange("p (j c e) -> p j c e", j=3, c=8)
    outsv = outs.rearrange("p (r t) -> p r t", t=2048)
    zbfv = chain.rearrange("p (c t) -> p c t", t=2048)

    for nm, shp, dt in A_CONSTS:
        DMA(p, "sync", sbc[nm][:, :], dram[nm])
    for nm, shp in A_SMALL_PARAMS:
        DMA(p, "sync", sbp[nm][:, :], dram[nm])
    DMA(p, "gpsimd", x0b.rearrange("p (c t) -> p c t", t=2048),
        x0T.rearrange("(c p) t -> p c t", p=128), ch="inig")
    for i, nm in enumerate(("a_wa", "a_wb", "a_wgg")):
        DMA(p, "gpsimd", wabv[:, i],
            dram[nm].rearrange("(c p) e -> p c e", p=128), ch="inig")
    p.fence()
    COPY(p, bggb[:, :], sbp["a_bgg"][:, :])
    MEMSET(p, cbias[:, 0:1], 1e-5)
    MEMSET(p, cbias[:, 1:2], 1.0)
    MEMSET(p, stf[:, :], 0.0)
    MEMSET(p, stb[:, :], 0.0)
    MEMSET(p, outs[:, :], 0.0)
    p.fence()

    def ab_logits(j):
        for cg in range(4):
            for k in range(8):
                MM(p, psE[0:16, cg * 512:(cg + 1) * 512], wabv[:, j, k],
                   x0v[:, k, cg * 512:(cg + 1) * 512], start=(k == 0), stop=(k == 7))

    ab_logits(0)
    t = p.tick("tensor"); p.wait("scalar", t); p.wait("vector", t)
    ACT(p, f_p1a, psE[0:16, :], AF.Abs, bias=sbp["a_ba"][:, 0:1])
    ACT(p, f_lnA, f_p1a, AF.Exp, scale=-1.0)
    ACT(p, f_p1a, f_lnA, AF.Ln, bias=cbias[0:16, 1:2])
    t = p.tick("scalar"); p.wait("vector", t)
    TS(p, f_lnA, psE[0:16, :], sbp["a_ba"][:, 0:1], ALU.add, 0.0, ALU.min)
    TT(p, f_lnA, f_lnA, f_p1a, ALU.subtract)
    t = p.tick("vector"); p.wait("tensor", t)
    ab_logits(1)
    t = p.tick("tensor"); p.wait("scalar", t)
    ACT(p, f_beta, psE[0:16, :], AF.Sigmoid, bias=sbp["a_bb"][:, 0:1])
    t = p.tick("scalar"); p.wait("vector", t)
    COPY(p, betaBb[:, :], f_beta)
    for c in range(NCH):
        sl = slice(c * C, (c + 1) * C)
        SCAN(p, f_cum[:, sl], f_lnA[:, sl], f_lnA[:, sl], 0.0, ALU.add, ALU.bypass)
    t = p.tick("vector"); p.wait("vector", t)
    for c in range(NCH):
        sl = slice(c * C, (c + 1) * C)
        TS(p, f_rev[:, sl], f_cum[:, sl], f_cum[:, c * C + C - 1:c * C + C],
           ALU.subtract, -1.0, ALU.mult)
    p.fence()
    if stop == "P1a":
        _dump_stop(p, nc, {"cum": f_cum, "rev": f_rev, "beta": f_beta,
                           "lnA": f_rev})
        return nc

    wslot = [wbuf[:, i * 1024:(i + 1) * 1024] for i in range(4)]
    wslot_tok = [None] * 4
    spill_tok = [None, None]
    evac_tok = [None, None]
    psP = [psE, psF]
    convw = {"a_wq": "a_qc", "a_wk": "a_kc", "a_wv": "a_vc"}
    si = 0
    for (wnm, bnm, do_norm, jidx) in (("a_wq", "a_bq", True, 0),
                                      ("a_wk", "a_bk", True, 1),
                                      ("a_wv", "a_bv", False, 2)):
        wsrc = dram[wnm].rearrange("(c p) m -> p c m", p=128)
        cw = sbp[convw[wnm]]
        for hp in range(8):
            par = si % 2
            slot = si % 4
            ws = wslot[slot]
            wsv = ws.rearrange("p (c m) -> p c m", m=128)
            if wslot_tok[slot] is not None:
                p.wait("gpsimd", wslot_tok[slot])
            tL = DMA(p, "gpsimd", wsv, wsrc[:, :, hp * 128:(hp + 1) * 128],
                     ch=f"wl{slot}")
            p.wait("tensor", tL)
            if evac_tok[par] is not None:
                p.wait("tensor", evac_tok[par])
            ps = psP[par]
            for cg in range(4):
                for k in range(8):
                    MM(p, ps[:, cg * 512:(cg + 1) * 512], wsv[:, k],
                       x0v[:, k, cg * 512:(cg + 1) * 512], start=(k == 0), stop=(k == 7))
            tMM = p.tick("tensor")
            wslot_tok[slot] = tMM
            p.wait("scalar", tMM)
            if spill_tok[par] is not None:
                p.wait("scalar", spill_tok[par])
            pb = pairbuf[par]
            ACT(p, pb, ps, AF.Identity, bias=sbp[bnm][:, hp:hp + 1])
            tEv = p.tick("scalar")
            evac_tok[par] = tEv
            p.wait("vector", tEv)
            co = gtmp[par]
            TS(p, co, pb, cw[:, hp * 4 + 3:hp * 4 + 4], ALU.mult)
            for j, dlt in ((2, 1), (1, 2), (0, 3)):
                STT(p, co[:, dlt:2048], pb[:, 0:2048 - dlt],
                    cw[:, hp * 4 + j:hp * 4 + j + 1], co[:, dlt:2048],
                    ALU.mult, ALU.add)
            tCv = p.tick("vector")
            p.wait("scalar", tCv)
            ACT(p, pb, co, AF.Silu)
            tSi = p.tick("scalar")
            if do_norm:
                ACT(p, sqbuf[par], pb, AF.Square)
                tSq = p.tick("scalar")
                p.wait("tensor", tSq)
                for cg in range(4):
                    MM(p, ps[0:2, cg * 512:(cg + 1) * 512], sbc["c_bo2"][:, :],
                       sqbuf[par][:, cg * 512:(cg + 1) * 512])
                tQ = p.tick("tensor"); p.wait("scalar", tQ)
                ACT(p, f_nrmA, ps[0:2, :], AF.Sqrt)
                tQ = p.tick("scalar"); p.wait("vector", tQ)
                TS(p, f_nrmA, f_nrmA, 1e-6, ALU.add)
                p.op("vector", lambda v, o=f_nrmB, i=f_nrmA: v.reciprocal(out=o, in_=i))
                COPY(p, normb[:, :], f_nrmB)
                tN = p.tick("vector"); p.wait("tensor", tN)
                for cg in range(4):
                    MM(p, ps[:, cg * 512:(cg + 1) * 512], sbc["c_bo2T"][:, :],
                       normb[:, cg * 512:(cg + 1) * 512])
                tB = p.tick("tensor"); p.wait("vector", tB, tSi)
                TT(p, pb, pb, ps, ALU.mult)
            else:
                p.wait("tensor", tSi)
                for cg in range(4):
                    MM(p, ps[:, cg * 512:(cg + 1) * 512],
                       sbc["c_selpairB"][:, hp * 128:(hp + 1) * 128],
                       betaBb[:, cg * 512:(cg + 1) * 512])
                tB = p.tick("tensor"); p.wait("vector", tB)
                TT(p, pb, pb, ps, ALU.mult)
            tFin = p.tick("vector")
            evac_tok[par] = tFin
            p.wait("sync", tFin)
            spill_tok[par] = DMA(
                p, "sync", spill[jidx, :, :, hp * 128:(hp + 1) * 128]
                .rearrange("c p t -> p c t"),
                pb.rearrange("p (c t) -> p c t", t=128), ch=f"sp{par}")
            si += 1
    p.fence()
    if stop == "P1b":
        qd = nc.dram_tensor("dbg_q", [NCH, 128, NP * C], BF16, kind="ExternalOutput").ap()
        kd = nc.dram_tensor("dbg_k", [NCH, 128, NP * C], BF16, kind="ExternalOutput").ap()
        vd = nc.dram_tensor("dbg_v", [NCH, 128, NP * C], BF16, kind="ExternalOutput").ap()
        DMA(p, "sync", qd, spill[0], ch="dbg")
        DMA(p, "sync", kd, spill[1], ch="dbg")
        DMA(p, "sync", vd, spill[2], ch="dbg")
        p.wait_chan_totals("gpsimd", ["dbg"])
        p.op("gpsimd", lambda g: g.engine_nop())
        p.tick("gpsimd")
        p.build()
        return nc

    ld_tok = [[None, None], [None, None], [None, None]]
    use_tok = [[None, None], [None, None], [None, None]]
    chb = [qch, kch, vch]

    def issue_loads(c):
        par = c % 2
        for j in range(3):
            if use_tok[j][par] is not None:
                p.wait("gpsimd", use_tok[j][par])
            ld_tok[j][par] = DMA(p, "gpsimd", chb[j][par], spill[j, c],
                                 ch=f"ld{j}{par}")

    import os as _os3
    _fs = _os3.environ.get("FENCE_STAGE", "")

    def SF(tag):
        if _fs and (_fs == "all" or tag in _fs.split(",")):
            p.fence("tensor", "vector", "scalar")

    issue_loads(0)
    for c in range(nch):
        par = c % 2
        sl = slice(c * C, (c + 1) * C)
        qcb, kcb, vcb = qch[par], kch[par], vch[par]
        if c + 1 < nch:
            issue_loads(c + 1)
        p.wait("tensor", ld_tok[0][par], ld_tok[1][par], ld_tok[2][par])
        MM(p, psSt[:, 0:16], f_cum[:, sl], sbc["c_id16f"][0:16, :])
        for h in range(16):
            MM(p, psE[0:1, h * 128:(h + 1) * 128], sbc["c_id16f"][0:16, h:h + 1],
               f_cum[:, sl])
        t = p.tick("tensor"); p.wait("vector", t); p.wait("scalar", t)
        COPY(p, cumTf[:, :], psSt[:, 0:16])
        COPY(p, f_cumflat, psE[0:1, :])
        t = p.tick("vector"); p.wait("scalar", t)
        ACT(p, scr16[:, 0:16], cumTf[:, :], AF.Exp)
        ACT(p, gsl32[:, :], f_cum[:, sl], AF.Exp)
        t = p.tick("scalar"); p.wait("vector", t)
        TS(p, gTnegf[:, :], scr16[:, 0:16], -1.0, ALU.mult)
        p.tick("vector")
        ACT(p, gslb[:, :], gsl32[:, :], AF.Identity)
        ACT(p, gsl32[:, :], f_rev[:, sl], AF.Exp)
        ACT(p, dslb[:, :], gsl32[:, :], AF.Identity)
        tV = p.tick("scalar")

        SF("decay")
        if nstage <= 1:
            break
        p.wait("tensor", tV)
        for pr in range(8):
            MM(p, psB[:, pr * 128:(pr + 1) * 128],
               sbc["c_selpairB"][:, pr * 128:(pr + 1) * 128], betaBb[:, sl])
        t = p.tick("tensor"); p.wait("vector", t)
        TT(p, kstar, kcb, psB, ALU.mult)
        t = p.tick("vector"); p.wait("tensor", t)
        for pr in range(8):
            MM(p, psB[:, pr * 128:(pr + 1) * 128],
               sbc["c_selpairB"][:, pr * 128:(pr + 1) * 128], gslb[:, :])
        t = p.tick("tensor"); p.wait("vector", t)
        TT(p, qplus, qcb, psB, ALU.mult)
        tQP = p.tick("vector")

        SF("kq")
        if nstage <= 2:
            break
        p.wait("tensor", tQP)
        for hf in range(2):
            for g2 in range(2):
                MM(p, psB[:, g2 * 512:(g2 + 1) * 512], sbc["c_ones1f"][:, :],
                   f_cumflat[0:1, hf * 1024 + g2 * 512: hf * 1024 + (g2 + 1) * 512])
            t = p.tick("tensor"); p.wait("vector", t)
            for h8 in range(8):
                h = hf * 8 + h8
                TS(p, f_tmpd[:, h8 * 128:(h8 + 1) * 128],
                   psB[:, h8 * 128:(h8 + 1) * 128], cumTf[:, h:h + 1],
                   ALU.subtract, 0.0, ALU.min)
            t = p.tick("vector"); p.wait("scalar", t)
            ACT(p, s3[:, hf * 1024:(hf + 1) * 1024], f_tmpd[:, 0:1024], AF.Exp)
            t = p.tick("scalar"); p.wait("vector", t)
            for h8 in range(8):
                h = hf * 8 + h8
                TS(p, f_tmpd[:, 1024 + h8 * 128: 1024 + (h8 + 1) * 128],
                   psB[:, h8 * 128:(h8 + 1) * 128], cumTf[:, h:h + 1],
                   ALU.subtract, 0.0, ALU.max)
            t = p.tick("vector"); p.wait("scalar", t); p.wait("tensor", t)
            ACT(p, s4[:, hf * 1024:(hf + 1) * 1024], f_tmpd[:, 1024:2048],
                AF.Exp, scale=-1.0)
        tLam = p.tick("scalar")

        SF("lam")
        if nstage <= 3:
            break
        p.wait("tensor", tLam)
        t = wave16(p, lambda h: MM(p, psE[:, h * 128:(h + 1) * 128], hsl(kstar, h), hsl(kcb, h)))
        p.wait("vector", t)
        TT(p, sTmp, s4, sbc["c_maskLsNegB"][:, :], ALU.mult)
        TT(p, sY1, sTmp, psE, ALU.mult)
        t = p.tick("vector"); p.wait("tensor", t)
        t = wave16(p, lambda h: MM(p, psE[:, h * 128:(h + 1) * 128], hsl(kcb, h), hsl(kstar, h)))
        p.wait("vector", t)
        TT(p, sTmp, s3, sbc["c_maskUsNegB"][:, :], ALU.mult)
        TT(p, sZ1, sTmp, psE, ALU.mult)
        t = p.tick("vector"); p.wait("tensor", t)
        t = wave16(p, lambda h: MM(p, psE[:, h * 128:(h + 1) * 128], hsl(kcb, h), hsl(qcb, h)))
        p.wait("vector", t)
        TT(p, sTmp, s3, sbc["c_maskUiB"][:, :], ALU.mult)
        TT(p, sAt, sTmp, psE, ALU.mult)
        tBld = p.tick("vector")

        SF("gram")
        if nstage <= 4:
            break
        p.wait("tensor", tBld)
        wave16(p, lambda h: MM(p, psB[:, h * 64:(h + 1) * 64], hsl(kstar, h),
                               hsl(stb, h, 64)))
        t = wave16(p, lambda h: MM(p, psE[:, h * 64:(h + 1) * 64], hsl(vcb, h),
                                   sbc["c_id64b"][64 * (h % 2):64 * (h % 2) + 64, :]))
        p.wait("scalar", t)
        ACT(p, vTsb, psE[:, 0:1024], AF.Identity)
        t = p.tick("scalar"); p.wait("vector", t)
        for h in range(16):
            STT(p, wvt[:, h * 64:(h + 1) * 64], psB[:, h * 64:(h + 1) * 64],
                gTnegf[:, h:h + 1], vTsb[:, h * 64:(h + 1) * 64],
                ALU.mult, ALU.add)
        tR = p.tick("vector")

        SF("R")
        if nstage <= 5:
            break
        lv = [(sY1, sZ1), (s3, s4), (s5, s6), (s3, s4), (s5, s6), (s3, s4)]
        tApply = tR

        def apply_half(sz):
            p.wait("tensor", tApply)
            tps = []
            for hf in range(2):
                for h in range(hf * 8, hf * 8 + 8):
                    MM(p, psB[:, h * 64:(h + 1) * 64],
                       sz[:, h * 128:(h + 1) * 128], wvt[:, h * 64:(h + 1) * 64])
                tps.append(p.tick("tensor"))
            p.wait("vector", tps[0])
            TT(p, wvt[:, 0:512], wvt[:, 0:512], psB[:, 0:512], ALU.add)
            p.tick("vector")
            p.wait("vector", tps[1])
            TT(p, wvt[:, 512:1024], wvt[:, 512:1024], psB[:, 512:1024], ALU.add)
            return p.tick("vector")

        def square_half(sa, sb, dst, pre):
            tps = []
            for hf in range(2):
                if pre is not None:
                    p.wait("tensor", pre[hf])
                for h in range(hf * 8, hf * 8 + 8):
                    MM(p, psE[:, h * 128:(h + 1) * 128],
                       sa[:, h * 128:(h + 1) * 128], sb[:, h * 128:(h + 1) * 128])
                tps.append(p.tick("tensor"))
            out = []
            for hf in range(2):
                p.wait("scalar", tps[hf])
                ACT(p, dst[:, hf * 1024:(hf + 1) * 1024],
                    psE[:, hf * 1024:(hf + 1) * 1024], AF.Identity)
                out.append(p.tick("scalar"))
            return out

        prev_ts = None
        for i in range(6):
            sy, sz = lv[i]
            if prev_ts is not None:
                p.wait("tensor", *prev_ts)
            tApply = apply_half(sz)
            ny, nz = lv[i + 1] if i < 5 else (None, s5)
            last = (i == 5)
            if not last:
                tsY = square_half(sz, sy, ny, None)
                prev_ts = square_half(sy, sz, nz, tsY)
            else:
                prev_ts = square_half(sy, sz, nz, None)
        p.wait("tensor", *prev_ts)
        tU = apply_half(s5)

        SF("solve")
        if nstage <= 6:
            break
        p.wait("tensor", tU)

        def emit_O(h):
            po = psE[64 * (h % 2):64 * (h % 2) + 64,
                     (h // 2) * 128:(h // 2) * 128 + 128]
            MM(p, po, wvt[:, h * 64:(h + 1) * 64],
               sAt[:, h * 128:(h + 1) * 128], start=True, stop=False)
            MM(p, po, hsl(stb, h, 64), hsl(qplus, h), start=False, stop=True)
        t = wave16(p, emit_O)
        p.wait("vector", t)
        COPY(p, outsv[:, :, c * 128:(c + 1) * 128],
             psE[:, 0:1024].rearrange("p (r t) -> p r t", t=128))
        tO = p.tick("vector")

        SF("O")
        if nstage <= 7:
            break
        p.wait("tensor", tO)
        for pr in range(8):
            MM(p, psB[:, pr * 128:(pr + 1) * 128],
               sbc["c_selpairB"][:, pr * 128:(pr + 1) * 128], dslb[:, :])
        t = p.tick("tensor"); p.wait("vector", t)
        TT(p, kdec2dm, kcb, psB, ALU.mult)
        t = p.tick("vector"); p.wait("tensor", t)
        t = wave16(p, lambda h: MM(p, psE[:, h * 64:(h + 1) * 64], hsl(kdec2dm, h),
                                   sbc["c_id64b"][64 * (h % 2):64 * (h % 2) + 64, :]))
        p.wait("scalar", t)
        ACT(p, kdec2tm[:, :], psE[:, 0:1024], AF.Identity)
        t = p.tick("scalar"); p.wait("tensor", t)
        wave16(p, lambda h: MM(
            p, psSt[64 * (h % 2):64 * (h % 2) + 64, (h // 2) * 64:(h // 2) * 64 + 64],
            kdec2tm[:, h * 64:(h + 1) * 64], wvt[:, h * 64:(h + 1) * 64]))
        for pr in range(8):
            MM(p, psG[:, pr:pr + 1],
               sbc["c_selpairB"][:, pr * 128:(pr + 1) * 128], gslb[:, 127:128])
        t = p.tick("tensor"); p.wait("scalar", t)
        ACT(p, gccol[:, :], psG[:, 0:8], AF.Identity)
        t = p.tick("scalar"); p.wait("vector", t)
        for pr in range(8):
            STT(p, stf[:, pr * 64:(pr + 1) * 64], stf[:, pr * 64:(pr + 1) * 64],
                gccol[:, pr:pr + 1], psSt[:, pr * 64:(pr + 1) * 64],
                ALU.mult, ALU.add)
        t = p.tick("vector"); p.wait("scalar", t)
        ACT(p, stb[:, :], stf[:, :], AF.Identity)
        tSt = p.tick("scalar")
        p.wait("tensor", tSt)
        import os as _os2
        if _os2.environ.get("FENCE_CHUNK"):
            p.fence("tensor", "vector", "scalar")
        for j in range(3):
            use_tok[j][par] = tSt
    p.fence()
    if stop == "P2":
        _dump_stop(p, nc, {"outs": outs, "stf": stf})
        return nc

    out_toks = []
    for hp in range(8):
        par = hp % 2
        op = outs[:, hp * 2048:(hp + 1) * 2048]
        ACT(p, sqgn[par], op, AF.Square)
        t = p.tick("scalar"); p.wait("tensor", t)
        for cg in range(4):
            MM(p, psF[0:2, cg * 512:(cg + 1) * 512], sbc["c_bo2"][:, :],
               op[:, cg * 512:(cg + 1) * 512])
            MM(p, psE[0:2, cg * 512:(cg + 1) * 512], sbc["c_bo2"][:, :],
               sqgn[par][:, cg * 512:(cg + 1) * 512])
        t = p.tick("tensor"); p.wait("vector", t)
        REDUCE(p, statm[0:2, hp:hp + 1], psF[0:2, :], ALU.add)
        REDUCE(p, statq[0:2, hp:hp + 1], psE[0:2, :], ALU.add)
        t = p.tick("vector"); p.wait("tensor", t); p.wait("scalar", t)
    TS(p, statm, statm, 1.0 / 131072.0, ALU.mult)
    TS(p, statq, statq, 1.0 / 131072.0, ALU.mult)
    TT(p, statv, statm, statm, ALU.mult)
    TT(p, statv, statq, statv, ALU.subtract)
    t = p.tick("vector"); p.wait("scalar", t)
    ACT(p, statv, statv, AF.Ln, bias=cbias[0:2, 0:1])
    ACT(p, statv, statv, AF.Exp, scale=-0.5)
    t = p.tick("scalar"); p.wait("vector", t)
    COPY(p, statmb, statm)
    COPY(p, statrb, statv)
    t = p.tick("vector"); p.wait("tensor", t)
    for hp in range(8):
        MM(p, psG[:, 2 * hp:2 * hp + 1], sbc["c_bo2T"][:, :], statmb[:, hp:hp + 1])
        MM(p, psG[:, 2 * hp + 1:2 * hp + 2], sbc["c_bo2T"][:, :], statrb[:, hp:hp + 1])
    t = p.tick("tensor"); p.wait("vector", t)
    COPY(p, gcolA[:, :], psG[:, 0:16])
    t = p.tick("vector"); p.wait("vector", t); p.wait("tensor", t)
    wsrc_g = dram["a_wg"].rearrange("(c p) m -> p c m", p=128)
    evac_tok = [None, None]
    gtok = [None, None]
    for hp in range(8):
        par = hp % 2
        slot = hp % 4
        op = outs[:, hp * 2048:(hp + 1) * 2048]
        ws = wslot[slot]
        wsv = ws.rearrange("p (c m) -> p c m", m=128)
        if wslot_tok[slot] is not None:
            p.wait("gpsimd", wslot_tok[slot])
        tL = DMA(p, "gpsimd", wsv, wsrc_g[:, :, hp * 128:(hp + 1) * 128],
                 ch=f"wl{slot}")
        TS(p, op, op, gcolA[:, 2 * hp:2 * hp + 1], ALU.subtract,
           gcolA[:, 2 * hp + 1:2 * hp + 2], ALU.mult)
        TS(p, op, op, sbp["a_gns"][:, hp:hp + 1], ALU.mult,
           sbp["a_gnb"][:, hp:hp + 1], ALU.add)
        tGn = p.tick("vector")
        p.wait("tensor", tL)
        if evac_tok[par] is not None:
            p.wait("tensor", evac_tok[par])
        ps = psP[par]
        for cg in range(4):
            for k in range(8):
                MM(p, ps[:, cg * 512:(cg + 1) * 512], wsv[:, k],
                   x0v[:, k, cg * 512:(cg + 1) * 512], start=(k == 0), stop=(k == 7))
        tMM = p.tick("tensor")
        wslot_tok[slot] = tMM
        p.wait("scalar", tMM)
        if gtok[par] is not None:
            p.wait("scalar", gtok[par])
        ACT(p, gtmp[par], ps, AF.Silu, bias=sbp["a_bg"][:, hp:hp + 1])
        t = p.tick("scalar"); p.wait("vector", t, tGn)
        TT(p, op, op, gtmp[par], ALU.mult)
        tMul = p.tick("vector")
        evac_tok[par] = tMul
        gtok[par] = tMul
    p.fence()
    wsrc_o = dram["a_wo"].rearrange("(c p) m -> p c m", p=128)
    x0_tok = [None, None]
    zev_tok = None
    for m in range(8):
        par = m % 2
        slot = m % 4
        ws = wslot[slot]
        wsv = ws.rearrange("p (c m) -> p c m", m=128)
        if wslot_tok[slot] is not None:
            p.wait("gpsimd", wslot_tok[slot])
        tL = DMA(p, "gpsimd", wsv, wsrc_o[:, :, m * 128:(m + 1) * 128],
                 ch=f"wl{slot}")
        if x0_tok[par] is not None:
            p.wait("sync", x0_tok[par])
        tX = DMA(p, "sync", f_x0f[par], x0T[m * 128:(m + 1) * 128, :],
                 ch=f"xf{par}")
        p.wait("tensor", tL)
        if zev_tok is not None:
            p.wait("tensor", zev_tok)
        for cg in range(4):
            for k in range(8):
                MM(p, psE[:, cg * 512:(cg + 1) * 512], wsv[:, k],
                   outsv[:, k, cg * 512:(cg + 1) * 512], start=(k == 0), stop=(k == 7))
        tMM = p.tick("tensor")
        wslot_tok[slot] = tMM
        p.wait("vector", tMM, tX)
        STT(p, zbfv[:, m], psE, sbp["a_bo"][:, m:m + 1], f_x0f[par],
            ALU.add, ALU.add)
        zev_tok = p.tick("vector")
        x0_tok[par] = zev_tok
        p.wait("tensor", zev_tok)
        for cg in range(4):
            MM(p, psF[0:1, cg * 512:(cg + 1) * 512], sbc["c_ones128b"][:, :],
               zbfv[:, m, cg * 512:(cg + 1) * 512], start=(m == 0), stop=(m == 7))
    t = p.tick("tensor"); p.wait("vector", t)
    TS(p, f_murow, psF[0:1, :], 1.0 / 1024.0, ALU.mult)
    t = p.tick("vector"); p.wait("tensor", t); p.wait("scalar", t)
    for m in range(8):
        par = m % 2
        ACT(p, sqgn[par], zbfv[:, m], AF.Square)
        t = p.tick("scalar"); p.wait("tensor", t)
        for cg in range(4):
            MM(p, psF[0:1, cg * 512:(cg + 1) * 512], sbc["c_ones128b"][:, :],
               sqgn[par][:, cg * 512:(cg + 1) * 512], start=(m == 0), stop=(m == 7))
        if m < 7:
            t = p.tick("tensor"); p.wait("scalar", t)
    t = p.tick("tensor"); p.wait("vector", t)
    TS(p, f_mu2row, psF[0:1, :], 1.0 / 1024.0, ALU.mult)
    TT(p, f_rstdrow, f_murow, f_murow, ALU.mult)
    TT(p, f_mu2row, f_mu2row, f_rstdrow, ALU.subtract)
    t = p.tick("vector"); p.wait("scalar", t)
    ACT(p, f_mu2row, f_mu2row, AF.Ln, bias=cbias[0:1, 0:1])
    ACT(p, f_rstdrow, f_mu2row, AF.Exp, scale=-0.5)
    t = p.tick("scalar"); p.wait("vector", t)
    COPY(p, murb, f_murow)
    COPY(p, rsrb, f_rstdrow)
    t = p.tick("vector"); p.wait("tensor", t)
    for cg in range(4):
        MM(p, psE[:, cg * 512:(cg + 1) * 512], sbc["c_ones1b"][:, :],
           murb[0:1, cg * 512:(cg + 1) * 512])
    t = p.tick("tensor"); p.wait("vector", t)
    COPY(p, muBCb, psE)
    t = p.tick("vector"); p.wait("tensor", t)
    for cg in range(4):
        MM(p, psE[:, cg * 512:(cg + 1) * 512], sbc["c_ones1b"][:, :],
           rsrb[0:1, cg * 512:(cg + 1) * 512])
    t = p.tick("tensor"); p.wait("vector", t)
    COPY(p, rsBCb, psE)
    t = p.tick("vector"); p.wait("vector", t)
    xf_tok = [None, None]
    for m in range(8):
        par = m % 2
        if xf_tok[par] is not None:
            p.wait("vector", *xf_tok[par])
        TT(p, f_x0f[par], zbfv[:, m], muBCb, ALU.subtract)
        TT(p, f_x0f[par], f_x0f[par], rsBCb, ALU.mult)
        t = p.tick("vector"); p.wait("sync", t)
        tD = DMA(p, "sync", x1T[m * 128:(m + 1) * 128, :], f_x0f[par],
                 ch=f"xo{par}")
        out_toks.append(tD)
        p.wait("vector", t)
        COPY(p, zbfv[:, m], f_x0f[par])
        xf_tok[par] = (p.tick("vector"), tD)
    p.fence()
    for tb in range(16):
        for k in range(8):
            MM(p, psG[0:128, 0:16], zbfv[:, k, tb * 128:(tb + 1) * 128],
               wabv[:, 2, k], start=(k == 0), stop=False)
        MM(p, psG[0:128, 0:16], sbc["c_ones1b"][:, :], bggb[:, :],
           start=False, stop=True)
        t = p.tick("tensor"); p.wait("scalar", t)
        ACT(p, gate[:, tb * 16:(tb + 1) * 16], psG[0:128, 0:16], AF.Exp)
        t = p.tick("scalar"); p.wait("vector", t)
        gsl_ = gate[:, tb * 16:(tb + 1) * 16]
        REDUCE(p, scr16[:, 0:1], gsl_, ALU.add)
        t = p.tick("vector"); p.wait("vector", t)
        p.op("vector", lambda v, o=scr16[:, 1:2], i=scr16[:, 0:1]:
             v.reciprocal(out=o, in_=i))
        t = p.tick("vector"); p.wait("vector", t)
        TS(p, gsl_, gsl_, scr16[:, 1:2], ALU.mult)
        t = p.tick("vector"); p.wait("vector", t)
        p.op("vector", lambda v, o=m8t[:, :], i=gsl_: v.max(out=o, in_=i))
        t = p.tick("vector"); p.wait("vector", t)
        g2sl = gate2[:, tb * 16:(tb + 1) * 16]
        TS(p, g2sl, gsl_, m8t[:, 1:2], ALU.is_ge)
        t = p.tick("vector"); p.wait("vector", t)
        TT(p, g2sl, g2sl, gsl_, ALU.mult)
        t = p.tick("vector"); p.wait("tensor", t)
    t = p.tick("vector"); p.wait("sync", t)
    out_toks.append(DMA(p, "sync", maskedT, gate2, ch="mo"))
    p.wait_chan_totals("gpsimd", ["xo0", "xo1", "mo"])
    p.op("gpsimd", lambda g: g.engine_nop())
    p.tick("gpsimd")

    p.build()
    return nc



def _run_A(inputs):
    consts = _host_consts_A()
    wA = _layout_A_weights(inputs)
    ncA = build_A()
    x = np.asarray(inputs["L0_attn_x"], dtype=np.float32)
    in_maps = []
    for core in range(8):
        m = {"x0T": np.ascontiguousarray(x[core].T)}
        m.update(wA)
        m.update({k: v for k, v in consts.items() if k in
                  {nm for nm, _, _ in A_CONSTS}})
        in_maps.append(m)
    res = run_bass_kernel_spmd(ncA, in_maps, core_ids=list(range(8)))
    x1 = np.stack([np.ascontiguousarray(res.results[c]["x1T"].T)
                   for c in range(8)])
    masked = np.stack([
        np.ascontiguousarray(
            res.results[c]["maskedT"].reshape(128, 16, 16).transpose(2, 1, 0)
            .reshape(16, 2048, order="F"))
        for c in range(8)])
    masked = np.stack([
        res.results[c]["maskedT"].reshape(128, 16, 16).transpose(2, 1, 0).reshape(16, 16 * 128)
        for c in range(8)])
    return x1, masked


def _relayout_masked(mT):
    return np.ascontiguousarray(
        np.asarray(mT).reshape(128, 16, 16).transpose(2, 1, 0).reshape(16, 2048))


def kernel(**inputs):
    consts_A = _host_consts_A()
    wA = _layout_A_weights(inputs)
    ncA = build_A()
    x = np.asarray(inputs["L0_attn_x"], dtype=np.float32)
    namesA = {nm for nm, _, _ in A_CONSTS}
    in_maps = []
    for core in range(8):
        m = {"x0T": np.ascontiguousarray(x[core].T)}
        m.update(wA)
        m.update({k: v for k, v in consts_A.items() if k in namesA})
        in_maps.append(m)
    resA = run_bass_kernel_spmd(ncA, in_maps, core_ids=list(range(8)))
    masked = [_relayout_masked(resA.results[c]["maskedT"]) for c in range(8)]

    consts_B = _host_consts_B()
    wB = _layout_B_weights(inputs)
    ncB = build_B()
    namesB = {nm for nm, _, _ in B_CONSTS}
    in_mapsB = []
    for core in range(8):
        maskall = np.concatenate(
            [masked[core]] + [masked[j] for j in range(8) if j != core], axis=0)
        m = {"x1T": np.ascontiguousarray(np.asarray(resA.results[core]["x1T"])),
             "maskall": np.ascontiguousarray(maskall)}
        m.update(wB)
        m.update({k: v for k, v in consts_B.items() if k in namesB})
        in_mapsB.append(m)
    resB = run_bass_kernel_spmd(ncB, in_mapsB, core_ids=list(range(8)))
    out = np.stack([np.ascontiguousarray(np.asarray(resB.results[c]["yT"]).T)
                    for c in range(8)])
    return out



B_SMALL = (("b_b1L0", [128, 32]), ("b_gammaL0", [128, 8]), ("b_betaL0", [128, 8]),
           ("b_ln1g", [128, 8]), ("b_ln1b", [128, 8]), ("b_ln2g", [128, 8]),
           ("b_ln2b", [128, 8]), ("b_inpb", [128, 24]), ("b_outpb", [128, 8]),
           ("b_bgrp", [4, 1]), ("b_bexp", [8, 1]), ("b_b1L1", [128, 128]))
B_CONSTS = (("c_ones2d", [128, 128], BF16), ("c_ones2df", [128, 128], F32),
            ("c_sel16f", [128, 16], F32), ("c_esel16", [16, 2048], BF16),
            ("c_P4a", [4, 4], F32),
            ("c_P4b", [4, 4], F32), ("c_L4str", [4, 4], F32),
            ("c_Ppair8", [8, 8], F32), ("c_dup48", [4, 8], F32),
            ("c_even8", [8, 1], F32), ("c_bo2", [128, 2], BF16),
            ("c_bo2T", [2, 128], BF16))


def _host_consts_B():
    c = {}
    c["c_ones2d"] = np.ones((128, 128), BF)
    c["c_ones2df"] = np.ones((128, 128), np.float32)
    sel = np.zeros((128, 16), np.float32)
    for pp in range(128):
        sel[pp, pp % 16] = 1.0
    c["c_sel16f"] = sel
    es = np.zeros((16, 16 * 128), np.float32)
    for e in range(16):
        es[e, e * 128:(e + 1) * 128] = 1.0
    c["c_esel16"] = es.astype(BF)
    P4a = np.zeros((4, 4), np.float32)
    P4b = np.zeros((4, 4), np.float32)
    for r in range(4):
        P4a[r ^ 2, r] = 1.0
        P4b[r ^ 1, r] = 1.0
    c["c_P4a"] = P4a.T.copy()
    c["c_P4b"] = P4b.T.copy()
    L4 = np.zeros((4, 4), np.float32)
    for m in range(4):
        for g in range(m):
            L4[g, m] = 1.0
    c["c_L4str"] = L4
    P8 = np.zeros((8, 8), np.float32)
    for m in range(8):
        P8[m ^ 1, m] = 1.0
    c["c_Ppair8"] = P8
    d48 = np.zeros((4, 8), np.float32)
    for m in range(8):
        d48[m // 2, m] = 1.0
    c["c_dup48"] = d48
    ev = np.zeros((8, 1), np.float32)
    ev[0::2] = 1.0
    c["c_even8"] = ev
    bo = np.zeros((128, 2), np.float32)
    bo[0:64, 0] = 1.0
    bo[64:128, 1] = 1.0
    c["c_bo2"] = bo.astype(BF)
    c["c_bo2T"] = bo.T.astype(BF).copy()
    return c


def _layout_B_weights(w):
    d = {}
    d["b_w1L0"] = np.ascontiguousarray(w["L0_ffn_w1"])
    d["b_w2L0"] = np.ascontiguousarray(w["L0_ffn_w2"])
    d["b_b2L0"] = np.ascontiguousarray(w["L0_ffn_b2"])
    d["b_b1L0"] = np.ascontiguousarray(
        w["L0_ffn_b1"].reshape(16, 2, 128).transpose(2, 0, 1).reshape(128, 32))
    for nm, key in (("b_gammaL0", "L0_ffn_gamma"), ("b_betaL0", "L0_ffn_beta"),
                    ("b_ln1g", "L1_ffn_ln1_g"), ("b_ln1b", "L1_ffn_ln1_b"),
                    ("b_ln2g", "L1_ffn_ln2_g"), ("b_ln2b", "L1_ffn_ln2_b"),
                    ("b_outpb", "L1_ffn_out_proj_b")):
        d[nm] = np.ascontiguousarray(w[key].reshape(8, 128).T)
    d["b_wqkvT"] = np.ascontiguousarray(w["L1_attn_w_qkv"].T)
    d["b_woutT"] = np.ascontiguousarray(w["L1_attn_w_out"].T)
    d["b_winT"] = np.ascontiguousarray(w["L1_ffn_in_proj_w"].T)
    d["b_woutpT"] = np.ascontiguousarray(w["L1_ffn_out_proj_w"].T)
    d["b_inpb"] = np.ascontiguousarray(w["L1_ffn_in_proj_b"].reshape(24, 128).T)
    d["b_inpbrow"] = np.ascontiguousarray(w["L1_ffn_in_proj_b"][2048:3072].reshape(1, 1024))
    d["b_wgrp"] = np.ascontiguousarray(w["L1_ffn_w_group"])
    d["b_bgrp"] = np.ascontiguousarray(w["L1_ffn_b_group"].reshape(4, 1))
    d["b_wexp"] = np.ascontiguousarray(
        w["L1_ffn_w_expert"].transpose(1, 0, 2).reshape(1024, 8))
    d["b_bexp"] = np.ascontiguousarray(w["L1_ffn_b_expert"].reshape(8, 1))
    d["b_w1L1"] = np.ascontiguousarray(w["L1_ffn_w1"])
    d["b_b1L1"] = np.ascontiguousarray(
        w["L1_ffn_b1"].reshape(8, 16, 128).transpose(2, 0, 1).reshape(128, 128))
    d["b_w2L1"] = np.ascontiguousarray(w["L1_ffn_w2"])
    d["b_b2L1"] = np.ascontiguousarray(w["L1_ffn_b2"])
    return d


def build_B(stop=None):
    nc = bass.Bass("TRN2", target_bir_lowering=False)
    p = Prog(nc)

    x1T = nc.dram_tensor("x1T", [D, S], F32, kind="ExternalInput").ap()
    maskall = nc.dram_tensor("maskall", [128, S], F32, kind="ExternalInput").ap()
    yT = nc.dram_tensor("yT", [D, S], F32, kind="ExternalOutput").ap()
    x2sp = nc.dram_tensor("x2sp", [D, S], F32, kind="Internal").ap()
    x3sp = nc.dram_tensor("x3sp", [D, S], F32, kind="Internal").ap()
    x4sp = nc.dram_tensor("x4sp", [D, S], F32, kind="Internal").ap()

    dram = {}
    dram["b_inpbrow"] = nc.dram_tensor("b_inpbrow", [1, 1024], F32,
                                       kind="ExternalInput").ap()
    for nm, shp in (("b_w1L0", [16, 1024, 256]), ("b_w2L0", [16, 256, 1024]),
                    ("b_b2L0", [16, 1024]), ("b_wqkvT", [1024, 3072]),
                    ("b_woutT", [1024, 1024]), ("b_winT", [1024, 3072]),
                    ("b_woutpT", [1024, 1024]), ("b_wgrp", [1024, 4]),
                    ("b_wexp", [1024, 8]), ("b_w1L1", [8, 1024, 2048]),
                    ("b_w2L1", [8, 2048, 1024]), ("b_b2L1", [8, 1024])):
        dram[nm] = nc.dram_tensor(nm, shp, F32, kind="ExternalInput").ap()
    for nm, shp in B_SMALL:
        dram[nm] = nc.dram_tensor(nm, shp, F32, kind="ExternalInput").ap()
    for nm, shp, dt in B_CONSTS:
        dram[nm] = nc.dram_tensor(nm, shp, dt, kind="ExternalInput").ap()

    AL = nc.alloc_sbuf_tensor
    BB0 = AL("BB0", [128, 16384], BF16).ap()
    BB1 = AL("BB1", [128, 16384], BF16).ap()
    BB2 = AL("BB2", [128, 16384], BF16).ap()
    BB3 = AL("BB3", [128, 16384], BF16).ap()
    WS = AL("WS", [128, 8192], BF16).ap()
    FSB = AL("FSB", [128, 10240], F32).ap()
    RB = FSB[0:16, 0:2048]
    RB2 = FSB[0:16, 2048:4096]
    RB3 = FSB[0:16, 4096:6144]
    RB4 = FSB[0:16, 6144:8192]
    R4zone = (FSB[0:4, 2048:4096], FSB[0:4, 4096:6144], FSB[0:4, 6144:8192],
              FSB[0:4, 8192:10240])
    R8 = FSB[0:8, 2048:4096]
    R8b = FSB[0:8, 8192:10240]
    R8d = FSB[0:8, 0:2048]
    RBb = AL("RBb", [16, 2048], BF16).ap()
    muB = AL("muB", [128, 2048], BF16).ap()
    rsB = AL("rsB", [128, 2048], BF16).ap()
    rows1 = FSB[0:1, 6144:10240]
    nrow = AL("nrow", [16, 16], F32).ap()
    b2L0b = AL("b2L0b", [16, 1024], BF16).ap()
    b2L1b = AL("b2L1b", [8, 1024], BF16).ap()
    cbias = AL("cbiasB", [128, 2], F32).ap()
    sbc = {}
    for nm, shp, dt in B_CONSTS:
        sbc[nm] = AL(nm + "_s", shp, dt).ap()
    sbp = {}
    for nm, shp in B_SMALL:
        sbp[nm] = AL(nm + "_s", shp, F32).ap()

    psE = nc.alloc_psum_tensor("psE", [128, 2048], F32).ap()
    psF = nc.alloc_psum_tensor("psF", [128, 2048], F32).ap()

    f_slab = [FSB[:, 0:2048], FSB[:, 2048:4096]]
    f_scr = FSB[:, 4096:6144]
    f_scr2 = FSB[:, 6144:8192]
    Psb = WS[:, 0:4096]
    denb = WS[:, 4096:6144]
    vaug = WS[:, 6144:7184]
    wslot = [WS[:, i * 2048:(i + 1) * 2048] for i in range(4)]
    wtok = [None] * 4

    x1bv = BB0.rearrange("p (c t) -> p c t", t=2048)

    for nm, shp, dt in B_CONSTS:
        DMA(p, "sync", sbc[nm][:, :], dram[nm])
    for nm, shp in B_SMALL:
        DMA(p, "sync", sbp[nm][:, :], dram[nm])
    DMA(p, "gpsimd", BB0.rearrange("p (c t) -> p c t", t=2048),
        x1T.rearrange("(c p) t -> p c t", p=128), ch="inig")
    DMA(p, "gpsimd", b2L0b[:, :], dram["b_b2L0"], ch="inig")
    DMA(p, "gpsimd", b2L1b[:, :], dram["b_b2L1"], ch="inig")
    maskf = FSB[:, 4096:6144]
    DMA(p, "sync", maskf, maskall)
    p.fence()
    MEMSET(p, cbias[:, 0:1], 1e-5)
    MEMSET(p, cbias[:, 1:2], 1.0)
    MEMSET(p, BB3[:, :], 0.0)
    p.fence()

    for cg in range(4):
        MM(p, psE[0:16, cg * 512:(cg + 1) * 512], sbc["c_sel16f"][:, :],
           maskf[:, cg * 512:(cg + 1) * 512])
    t = p.tick("tensor"); p.wait("vector", t)
    TS(p, RB, psE[0:16, :], 1e-6, ALU.add)
    p.op("vector", lambda v, o=RB2, i=RB: v.reciprocal(out=o, in_=i))
    TT(p, RB, maskf[0:16, :], RB2, ALU.mult)
    TS(p, RB, RB, 8.0, ALU.mult)
    TS(p, RB2, RB, 0.0, ALU.is_gt)
    REDUCE(p, nrow[:, 0:1], RB2, ALU.add)
    TS(p, nrow[:, 1:2], nrow[:, 0:1], -1.0, ALU.mult, 1024.0, ALU.add)
    TS(p, RB3, RB2, -1.0, ALU.mult, 1.0, ALU.add)
    SCAN(p, RB4, RB3, RB3, 0.0, ALU.add, ALU.bypass)
    TS(p, RB4, RB4, nrow[:, 1:2], ALU.is_le)
    TT(p, RB4, RB4, RB3, ALU.mult)
    TT(p, RB4, RB4, RB2, ALU.add)
    COPY(p, RBb, RB4)
    p.fence()

    def stats_sum(ps_row, src, m):
        for cg in range(4):
            MM(p, ps_row[:, cg * 512:(cg + 1) * 512],
               sbc["c_ones2d"][:, 0:1], src[:, cg * 512:(cg + 1) * 512],
               start=(m == 0), stop=(m == 7))

    def stats_finalize(ps_row, murow, rstdrow, sq_src_chunks, sqbuf):
        t = p.tick("tensor"); p.wait("vector", t)
        TS(p, murow, ps_row[0:1, :], 1.0 / 1024.0, ALU.mult)
        t = p.tick("vector"); p.wait("scalar", t); p.wait("tensor", t)
        for m in range(8):
            ACT(p, sqbuf, sq_src_chunks(m), AF.Square)
            t = p.tick("scalar"); p.wait("tensor", t)
            stats_sum(ps_row, sqbuf, m)
            if m < 7:
                t = p.tick("tensor"); p.wait("scalar", t)
        t = p.tick("tensor"); p.wait("vector", t)
        TS(p, rstdrow, ps_row[0:1, :], 1.0 / 1024.0, ALU.mult)
        TT(p, f_scr[0:1, 0:2048], murow, murow, ALU.mult)
        TT(p, rstdrow, rstdrow, f_scr[0:1, 0:2048], ALU.subtract)
        t = p.tick("vector"); p.wait("scalar", t)
        ACT(p, rstdrow, rstdrow, AF.Ln, bias=cbias[0:1, 0:1])
        ACT(p, rstdrow, rstdrow, AF.Exp, scale=-0.5)

    def bcast_mu_rs(murow, rstdrow, murb_, rsrb_):
        t = p.tick("scalar"); p.wait("vector", t)
        COPY(p, murb_, murow)
        COPY(p, rsrb_, rstdrow)
        t = p.tick("vector"); p.wait("tensor", t)
        for cg in range(4):
            MM(p, psE[:, cg * 512:(cg + 1) * 512], sbc["c_ones2d"][0:1, :],
               murb_[0:1, cg * 512:(cg + 1) * 512])
        t = p.tick("tensor"); p.wait("vector", t)
        COPY(p, muB, psE)
        t = p.tick("vector"); p.wait("tensor", t)
        for cg in range(4):
            MM(p, psE[:, cg * 512:(cg + 1) * 512], sbc["c_ones2d"][0:1, :],
               rsrb_[0:1, cg * 512:(cg + 1) * 512])
        t = p.tick("tensor"); p.wait("vector", t)
        COPY(p, rsB, psE)
        t = p.tick("vector"); p.wait("tensor", t); p.wait("sync", t)

    f_murow = rows1[0:1, 0:2048]
    f_rsrow = rows1[0:1, 2048:4096]
    mu_b = muB[0:1, :]
    rs_b = rsB[0:1, :]

    accv = BB3.rearrange("p (c t) -> p c t", t=2048)
    h_sb = [BB1[:, 0:2048], BB1[:, 2048:4096]]
    evac_tok = None
    mask_tok = None
    for e in range(16):
        s1, s2 = (2 * e) % 4, (2 * e + 1) % 4
        if wtok[s1] is not None:
            p.wait("gpsimd", wtok[s1])
        tW1 = DMA(p, "gpsimd", wslot[s1].rearrange("p (c f) -> p c f", f=256),
                  dram["b_w1L0"][e].rearrange("(c p) f -> p c f", p=128),
                  ch=f"wl{s1}")
        if wtok[s2] is not None:
            p.wait("gpsimd", wtok[s2])
        tW2 = DMA(p, "gpsimd", wslot[s2].rearrange("p (c d) -> p c d", d=1024),
                  dram["b_w2L0"][e].rearrange("(c p) d -> p c d", p=128),
                  ch=f"wl{s2}")
        w1v = wslot[s1].rearrange("p (c f) -> p c f", f=256)
        w2v = wslot[s2].rearrange("p (c d) -> p c d", d=1024)
        p.wait("tensor", tW1)
        if evac_tok is not None:
            p.wait("tensor", evac_tok)
        for m in range(2):
            ps = psE if m == 0 else psF
            for cg in range(4):
                for k in range(8):
                    MM(p, ps[:, cg * 512:(cg + 1) * 512],
                       w1v[:, k, m * 128:(m + 1) * 128],
                       x1bv[:, k, cg * 512:(cg + 1) * 512],
                       start=(k == 0), stop=(k == 7))
        tMM = p.tick("tensor")
        wtok[s1] = tMM
        p.wait("scalar", tMM)
        if mask_tok is not None:
            p.wait("scalar", mask_tok)
        ACT(p, h_sb[0], psE, AF.Gelu, bias=sbp["b_b1L0"][:, 2 * e:2 * e + 1])
        ACT(p, h_sb[1], psF, AF.Gelu, bias=sbp["b_b1L0"][:, 2 * e + 1:2 * e + 2])
        tEv = p.tick("scalar")
        p.wait("tensor", tEv)
        for cg in range(4):
            MM(p, psE[:, cg * 512:(cg + 1) * 512],
               sbc["c_esel16"][:, e * 128:(e + 1) * 128],
               RBb[:, cg * 512:(cg + 1) * 512])
        t = p.tick("tensor"); p.wait("vector", t)
        TT(p, h_sb[0], h_sb[0], psE, ALU.mult)
        TT(p, h_sb[1], h_sb[1], psE, ALU.mult)
        mask_tok = p.tick("vector")
        p.wait("tensor", mask_tok, tW2)
        stt_tok = [None, None]
        for m8 in range(8):
            psw = psF if m8 % 2 == 0 else psE
            if stt_tok[m8 % 2] is not None:
                p.wait("tensor", stt_tok[m8 % 2])
            for cg in range(4):
                pt = psw[:, cg * 512:(cg + 1) * 512]
                MM(p, pt, w2v[:, 0, m8 * 128:(m8 + 1) * 128],
                   h_sb[0][:, cg * 512:(cg + 1) * 512], start=True, stop=False)
                MM(p, pt, w2v[:, 1, m8 * 128:(m8 + 1) * 128],
                   h_sb[1][:, cg * 512:(cg + 1) * 512], start=False, stop=False)
                MM(p, pt, b2L0b[:, m8 * 128:(m8 + 1) * 128],
                   RBb[:, cg * 512:(cg + 1) * 512], start=False, stop=True)
            t = p.tick("tensor"); p.wait("vector", t)
            STT(p, accv[:, m8], psw, 1.0, accv[:, m8], ALU.bypass, ALU.add)
            stt_tok[m8 % 2] = p.tick("vector")
        p.wait("tensor", stt_tok[0], stt_tok[1])
        wtok[s2] = p.last_tok["tensor"]
        evac_tok = p.last_tok["vector"]
    p.fence()
    if stop == "B2":
        _dump_stop(p, nc, {"acc": BB3})
        return nc

    zbv = BB1.rearrange("p (c t) -> p c t", t=2048)
    x2bv = BB2.rearrange("p (c t) -> p c t", t=2048)
    sl_tok = [None, None]
    for m in range(8):
        par = m % 2
        if sl_tok[par] is not None:
            p.wait("sync", sl_tok[par])
        tX = DMA(p, "sync", f_slab[par], x1T[m * 128:(m + 1) * 128, :], ch=f"xf{par}")
        p.wait("vector", tX)
        STT(p, zbv[:, m], f_slab[par], 1.0, accv[:, m], ALU.bypass, ALU.add)
        tZ = p.tick("vector")
        sl_tok[par] = tZ
        p.wait("tensor", tZ)
        stats_sum(psF[0:1], zbv[:, m], m)
    stats_finalize(psF[0:1], f_murow, f_rsrow, lambda m: zbv[:, m], BB2[:, 0:2048])
    bcast_mu_rs(f_murow, f_rsrow, mu_b, rs_b)
    sl_tok = [None, None]
    for m in range(8):
        par = m % 2
        if sl_tok[par] is not None:
            p.wait("sync", sl_tok[par])
        tX = DMA(p, "sync", f_slab[par], x1T[m * 128:(m + 1) * 128, :], ch=f"xf{par}")
        TT(p, f_scr, zbv[:, m], muB, ALU.subtract)
        TT(p, f_scr, f_scr, rsB, ALU.mult)
        TS(p, f_scr, f_scr, sbp["b_gammaL0"][:, m:m + 1], ALU.mult,
           sbp["b_betaL0"][:, m:m + 1], ALU.add)
        p.wait("vector", tX)
        TT(p, zbv[:, m], f_scr, f_slab[par], ALU.add)
        tU = p.tick("vector")
        sl_tok[par] = tU
        p.wait("tensor", tU)
        stats_sum(psF[0:1], zbv[:, m], m)
    stats_finalize(psF[0:1], f_murow, f_rsrow, lambda m: zbv[:, m], BB2[:, 0:2048])
    bcast_mu_rs(f_murow, f_rsrow, mu_b, rs_b)
    xs_tok = [None, None]
    for m in range(8):
        par = m % 2
        if xs_tok[par] is not None:
            p.wait("vector", *xs_tok[par])
        fs = f_slab[par]
        TT(p, fs, zbv[:, m], muB, ALU.subtract)
        TT(p, fs, fs, rsB, ALU.mult)
        t = p.tick("vector"); p.wait("sync", t)
        tD = DMA(p, "sync", x2sp[m * 128:(m + 1) * 128, :], fs, ch=f"xo{par}")
        p.wait("vector", t)
        COPY(p, x2bv[:, m], fs)
        xs_tok[par] = (p.tick("vector"), tD)
    p.fence()
    if stop == "B3":
        _dump_stop(p, nc, {"x2b": BB2})
        return nc

    qv = BB0.rearrange("p (c t) -> p c t", t=2048)
    ktm = BB1.rearrange("p (b f) -> p b f", f=1024)
    vtm = BB3.rearrange("p (b f) -> p b f", f=1024)

    def colblock_load(dst_slot, wsrc, c0, slot):
        if wtok[slot] is not None:
            p.wait("gpsimd", wtok[slot])
        return DMA(p, "gpsimd",
                   wslot[slot][:, 0:1024].rearrange("p (c m) -> p c m", m=128),
                   wsrc.rearrange("(c p) m -> p c m", p=128)[:, :, c0:c0 + 128],
                   ch=f"wl{slot}")

    ev_tok = None
    for m in range(8):
        slot = m % 4
        tL = colblock_load(wslot[slot], dram["b_wqkvT"], m * 128, slot)
        wv_ = wslot[slot].rearrange("p (c m) -> p c m", m=128)
        p.wait("tensor", tL)
        if ev_tok is not None:
            p.wait("tensor", ev_tok)
        ps = psE if m % 2 == 0 else psF
        for cg in range(4):
            for k in range(8):
                MM(p, ps[:, cg * 512:(cg + 1) * 512], wv_[:, k],
                   x2bv[:, k, cg * 512:(cg + 1) * 512], start=(k == 0), stop=(k == 7))
        tMM = p.tick("tensor")
        wtok[slot] = tMM
        p.wait("scalar", tMM)
        ACT(p, f_scr, ps, AF.Relu)
        t = p.tick("scalar"); p.wait("vector", t)
        TS(p, f_scr2, ps, 0.0, ALU.min)
        t = p.tick("vector"); p.wait("scalar", t)
        ACT(p, f_scr2, f_scr2, AF.Exp)
        t = p.tick("scalar"); p.wait("vector", t)
        TT(p, qv[:, m], f_scr, f_scr2, ALU.add)
        ev_tok = p.tick("vector")
    wpair = [WS[:, 0:4096].rearrange("p (c m) -> p c m", m=512),
             WS[:, 4096:8192].rearrange("p (c m) -> p c m", m=512)]
    for j, (base, dst) in enumerate(((1024, ktm), (2048, vtm))):
        ptoks = []
        for fg in range(2):
            for sl_ in ((0, 1) if fg == 0 else (2, 3)):
                if wtok[sl_] is not None:
                    p.wait("gpsimd", wtok[sl_])
            ptoks.append(DMA(p, "gpsimd", wpair[fg],
                             dram["b_wqkvT"].rearrange("(c p) m -> p c m", p=128)
                             [:, :, base + fg * 512:base + (fg + 1) * 512],
                             ch=f"wl{fg}"))
        p.wait("tensor", *ptoks)
        if ev_tok is not None:
            p.wait("tensor", ev_tok)
        par_tok = [None, None]
        scr_tok = [None, None]
        for tb in range(16):
            ps = psE if tb % 2 == 0 else psF
            if par_tok[tb % 2] is not None:
                p.wait("tensor", par_tok[tb % 2])
            for fg in range(2):
                wv_ = wpair[fg]
                for k in range(8):
                    MM(p, ps[:, fg * 512:(fg + 1) * 512],
                       x2bv[:, k, tb * 128:(tb + 1) * 128], wv_[:, k, 0:512],
                       start=(k == 0), stop=(k == 7))
            tMM = p.tick("tensor")
            p.wait("scalar", tMM)
            if j == 0:
                hc = (tb % 2) * 1024
                if scr_tok[tb % 2] is not None:
                    p.wait("scalar", scr_tok[tb % 2])
                ACT(p, f_scr[:, hc:hc + 1024], ps[:, 0:1024], AF.Relu)
                t = p.tick("scalar"); p.wait("vector", t)
                TS(p, f_scr2[:, hc:hc + 1024], ps[:, 0:1024], 0.0, ALU.min)
                t = p.tick("vector"); p.wait("scalar", t)
                ACT(p, f_scr2[:, hc:hc + 1024], f_scr2[:, hc:hc + 1024], AF.Exp)
                t = p.tick("scalar"); p.wait("vector", t)
                TT(p, dst[:, tb], f_scr[:, hc:hc + 1024],
                   f_scr2[:, hc:hc + 1024], ALU.add)
                ev_tok = p.tick("vector")
                par_tok[tb % 2] = ev_tok
                scr_tok[tb % 2] = ev_tok
            else:
                ACT(p, dst[:, tb], ps[:, 0:1024], AF.Identity)
                ev_tok = p.tick("scalar")
                par_tok[tb % 2] = ev_tok
        for sl_ in range(4):
            wtok[sl_] = p.last_tok["tensor"]
    p.fence()
    kvps = psE[:, 0:512]
    ksps = psE[:, 512:520]
    for par0 in (0, 1):
        for h in range(par0, 16, 2):
            for tb in range(16):
                MM(p, kvps[64 * (h % 2):64 * (h % 2) + 64, (h // 2) * 64:(h // 2) * 64 + 64],
                   ktm[:, tb, h * 64:(h + 1) * 64], vtm[:, tb, h * 64:(h + 1) * 64],
                   start=(tb == 0), stop=(tb == 15))
            for tb in range(16):
                MM(p, ksps[64 * (h % 2):64 * (h % 2) + 64, (h // 2):(h // 2) + 1],
                   ktm[:, tb, h * 64:(h + 1) * 64], sbc["c_ones2d"][:, 0:1],
                   start=(tb == 0), stop=(tb == 15))
        t = p.tick("tensor")
        p.wait("tensor", t)
    t = p.last_tok["tensor"]; p.wait("vector", t)
    kvsb = WS[:, 0:512]
    ksumf = FSB[:, 8200:8208]
    for sl_ in range(4):
        wtok[sl_] = p.last_tok["tensor"]
    COPY(p, kvsb[:, :], kvps)
    COPY(p, ksumf[:, :], ksps)
    t = p.tick("vector"); p.wait("tensor", t)
    out_lin = BB3
    olv = BB3.rearrange("p (c t) -> p c t", t=2048)
    for par0 in (0, 1):
        for h in range(par0, 16, 2):
            for cg in range(4):
                MM(p, psF[64 * (h % 2):64 * (h % 2) + 64, cg * 512:(cg + 1) * 512],
                   kvsb[64 * (h % 2):64 * (h % 2) + 64, (h // 2) * 64:(h // 2) * 64 + 64],
                   qv[64 * (h % 2):64 * (h % 2) + 64, h // 2, cg * 512:(cg + 1) * 512])
            t = p.tick("tensor"); p.wait("vector", t)
            COPY(p, olv[64 * (h % 2):64 * (h % 2) + 64, h // 2, :],
                 psF[64 * (h % 2):64 * (h % 2) + 64, :])
            t = p.tick("vector"); p.wait("tensor", t)
    for pr in range(8):
        TS(p, denb, qv[:, pr, :], ksumf[:, pr:pr + 1], ALU.mult)
        t = p.tick("vector"); p.wait("tensor", t)
        for cg in range(4):
            MM(p, psE[0:2, cg * 512:(cg + 1) * 512], sbc["c_bo2"][:, :],
               denb[:, cg * 512:(cg + 1) * 512])
        t = p.tick("tensor"); p.wait("vector", t)
        TS(p, f_scr2[0:2, 0:2048], psE[0:2, :], 1e-6, ALU.max)
        p.op("vector", lambda v, o=f_scr[0:2, 0:2048], i=f_scr2[0:2, 0:2048]:
             v.reciprocal(out=o, in_=i))
        COPY(p, denb[0:2, 0:2048], f_scr[0:2, 0:2048])
        t = p.tick("vector"); p.wait("tensor", t)
        for cg in range(4):
            MM(p, psE[:, cg * 512:(cg + 1) * 512], sbc["c_bo2T"][:, :],
               denb[0:2, cg * 512:(cg + 1) * 512])
        t = p.tick("tensor"); p.wait("vector", t)
        TT(p, olv[:, pr, :], olv[:, pr, :], psE, ALU.mult)
        t = p.tick("vector"); p.wait("tensor", t)
    p.fence()
    if stop == "B4a":
        _dump_stop(p, nc, {"olin": BB3})
        return nc

    def proj_resid_ln(srcv, wsrcT, c0, resid_sp, zb_t, out_sp, outb_t,
                      bias_col=None, gamma=None, beta=None, sq_scr=None):
        zv = zb_t.rearrange("p (c t) -> p c t", t=2048)
        ov = outb_t.rearrange("p (c t) -> p c t", t=2048)
        ev = None
        stok = [None, None]
        for m in range(8):
            slot = m % 4
            par = m % 2
            tL = colblock_load(wslot[slot], wsrcT, c0 + m * 128, slot)
            wv_ = wslot[slot].rearrange("p (c m) -> p c m", m=128)
            p.wait("tensor", tL)
            if stok[par] is not None:
                p.wait("tensor", stok[par])
            ps = psE if m % 2 == 0 else psF[:, 0:2048]
            for cg in range(4):
                for k in range(8):
                    MM(p, ps[:, cg * 512:(cg + 1) * 512], wv_[:, k],
                       srcv[:, k, cg * 512:(cg + 1) * 512],
                       start=(k == 0), stop=(k == 7))
            tMM = p.tick("tensor")
            wtok[slot] = tMM
            if stok[par] is not None:
                p.wait("sync", stok[par])
            tX = DMA(p, "sync", f_slab[par], resid_sp[m * 128:(m + 1) * 128, :],
                     ch=f"xf{par}")
            p.wait("vector", tMM, tX)
            if bias_col is not None:
                STT(p, zv[:, m], ps, bias_col[:, m:m + 1], f_slab[par],
                    ALU.add, ALU.add)
            else:
                STT(p, zv[:, m], ps, 0.0, f_slab[par], ALU.add, ALU.add)
            ev = p.tick("vector")
            stok[par] = ev
        p.wait("tensor", ev)
        for m in range(8):
            stats_sum(psF[0:1], zv[:, m], m)
        stats_finalize(psF[0:1], f_murow, f_rsrow, lambda m: zv[:, m], sq_scr)
        bcast_mu_rs(f_murow, f_rsrow, mu_b, rs_b)
        xs = [None, None]
        for m in range(8):
            par = m % 2
            if xs[par] is not None:
                p.wait("vector", *xs[par])
            fs = f_slab[par]
            TT(p, fs, zv[:, m], muB, ALU.subtract)
            TT(p, fs, fs, rsB, ALU.mult)
            if gamma is not None:
                TS(p, fs, fs, gamma[:, m:m + 1], ALU.mult, beta[:, m:m + 1], ALU.add)
            t = p.tick("vector")
            tD = None
            if out_sp is not None:
                p.wait("sync", t)
                tD = DMA(p, "sync", out_sp[m * 128:(m + 1) * 128, :], fs,
                         ch=f"xo{par}")
            p.wait("vector", t)
            COPY(p, ov[:, m], fs)
            xs[par] = (p.tick("vector"),) + ((tD,) if tD else ())
        p.fence()

    proj_resid_ln(olv, dram["b_woutT"], 0, x2sp, BB0, x3sp, BB0,
                  sq_scr=BB1[:, 0:2048])
    if stop == "B4":
        _dump_stop(p, nc, {"x3b": BB0})
        return nc

    x3bv = BB0.rearrange("p (c t) -> p c t", t=2048)
    qmv = BB1.rearrange("p (c t) -> p c t", t=2048)
    kmv = BB2.rearrange("p (c t) -> p c t", t=2048)
    vtm2 = BB3.rearrange("p (b f) -> p b f", f=1024)
    inpbf32 = FSB[0:1, 9216:10240]
    tBV = DMA(p, "sync", inpbf32, dram["b_inpbrow"], ch="xf0")
    p.wait("tensor", tBV)
    for j, dstv in ((0, qmv), (1, kmv)):
        evp = [None, None] if j == 0 else [evp[0], evp[1]]
        for m in range(8):
            slot = m % 4
            tL = colblock_load(wslot[slot], dram["b_winT"], j * 1024 + m * 128, slot)
            wv_ = wslot[slot].rearrange("p (c m) -> p c m", m=128)
            p.wait("tensor", tL)
            if evp[m % 2] is not None:
                p.wait("tensor", evp[m % 2])
            ps = psE if m % 2 == 0 else psF[:, 0:2048]
            for cg in range(4):
                for k in range(8):
                    MM(p, ps[:, cg * 512:(cg + 1) * 512], wv_[:, k],
                       x3bv[:, k, cg * 512:(cg + 1) * 512],
                       start=(k == 0), stop=(k == 7))
            tMM = p.tick("tensor")
            wtok[slot] = tMM
            p.wait("scalar", tMM)
            ACT(p, dstv[:, m], ps, AF.Identity,
                bias=sbp["b_inpb"][:, j * 8 + m:j * 8 + m + 1])
            ev = p.tick("scalar")
            evp[m % 2] = ev
    ptoks = []
    for fg in range(2):
        for sl_ in ((0, 1) if fg == 0 else (2, 3)):
            if wtok[sl_] is not None:
                p.wait("gpsimd", wtok[sl_])
        ptoks.append(DMA(p, "gpsimd", wpair[fg],
                         dram["b_winT"].rearrange("(c p) m -> p c m", p=128)
                         [:, :, 2048 + fg * 512:2048 + (fg + 1) * 512],
                         ch=f"wl{fg}"))
    p.wait("tensor", *ptoks)
    if ev is not None:
        p.wait("tensor", ev)
    ev = None
    par_tok = [None, None]
    for tb in range(16):
        ps = psE if tb % 2 == 0 else psF[:, 0:2048]
        if par_tok[tb % 2] is not None:
            p.wait("tensor", par_tok[tb % 2])
        for fg in range(2):
            wv_ = wpair[fg]
            for k in range(8):
                MM(p, ps[:, fg * 512:(fg + 1) * 512],
                   x3bv[:, k, tb * 128:(tb + 1) * 128], wv_[:, k, 0:512],
                   start=(k == 0), stop=False)
            MM(p, ps[:, fg * 512:(fg + 1) * 512], sbc["c_ones2df"][0:1, 0:128],
               inpbf32[0:1, fg * 512:(fg + 1) * 512],
               start=False, stop=True)
        tMM = p.tick("tensor")
        p.wait("scalar", tMM)
        ACT(p, vtm2[:, tb], ps[:, 0:1024], AF.Identity)
        par_tok[tb % 2] = p.tick("scalar")
    for sl_ in range(4):
        wtok[sl_] = p.last_tok["tensor"]
    p.fence()
    if stop == "B5a":
        _dump_stop(p, nc, {"qm": BB1, "km": BB2, "vm": BB3})
        return nc
    mhav = BB0.rearrange("p (c t) -> p c t", t=2048)
    Psbv = [Psb[:, 0:2048], Psb[:, 2048:4096]]
    for h in range(16):
        hb = 64 * (h % 2)
        COPY(p, vaug.rearrange("p (b f) -> p b f", f=65)[:, :, 0:64],
             vtm2[:, :, h * 64:(h + 1) * 64])
        MEMSET(p, vaug.rearrange("p (b f) -> p b f", f=65)[:, :, 64:65], 1.0)
        tV = p.tick("vector")
        p.wait("tensor", tV)
        prev_act = [None, None]
        for t2b in range(16):
            tEs = []
            for half in (0, 1):
                if prev_act[half] is not None:
                    p.wait("tensor", prev_act[half])
                for cg in (2 * half, 2 * half + 1):
                    MM(p, psE[:, cg * 512:(cg + 1) * 512],
                       kmv[hb:hb + 64, h // 2, t2b * 128:(t2b + 1) * 128],
                       qmv[hb:hb + 64, h // 2, cg * 512:(cg + 1) * 512])
                tS = p.tick("tensor")
                p.wait("scalar", tS)
                ACT(p, Psbv[t2b % 2][:, half * 1024:(half + 1) * 1024],
                    psE[:, half * 1024:(half + 1) * 1024], AF.Exp, scale=0.125)
                tE = p.tick("scalar")
                prev_act[half] = tE
                tEs.append(tE)
            for half in (0, 1):
                p.wait("tensor", tEs[half])
                for cg in (2 * half, 2 * half + 1):
                    MM(p, psF[0:65, cg * 512:(cg + 1) * 512],
                       vaug[:, t2b * 65:(t2b + 1) * 65],
                       Psbv[t2b % 2][:, cg * 512:(cg + 1) * 512],
                       start=(t2b == 0), stop=(t2b == 15))
        t = p.tick("tensor"); p.wait("scalar", t); p.wait("vector", t)
        ACT(p, mhav[hb:hb + 64, h // 2, :], psF[0:64, :], AF.Identity)
        tO = p.tick("scalar")
        TS(p, f_scr[64:65, 0:2048], psF[64:65, :], 1e-38, ALU.max)
        p.op("vector", lambda v, o=f_scr2[64:65, 0:2048], i=f_scr[64:65, 0:2048]:
             v.reciprocal(out=o, in_=i))
        COPY(p, denb[64:65, 0:2048], f_scr2[64:65, 0:2048])
        t = p.tick("vector"); p.wait("tensor", t, tO)
        for cg in range(4):
            MM(p, psE[hb:hb + 64, cg * 512:(cg + 1) * 512],
               sbc["c_ones2d"][64:65, 0:64],
               denb[64:65, cg * 512:(cg + 1) * 512])
        t = p.tick("tensor"); p.wait("vector", t)
        TT(p, mhav[hb:hb + 64, h // 2, :], mhav[hb:hb + 64, h // 2, :],
           psE[hb:hb + 64, :], ALU.mult)
        t = p.tick("vector"); p.wait("tensor", t); p.wait("scalar", t)
    p.fence()
    for sl_ in range(4):
        wtok[sl_] = p.last_tok["tensor"]
    if stop == "B5b":
        _dump_stop(p, nc, {"mha": BB0})
        return nc
    proj_resid_ln(mhav, dram["b_woutpT"], 0, x3sp, BB1, x4sp, BB1,
                  bias_col=sbp["b_outpb"], gamma=sbp["b_ln1g"], beta=sbp["b_ln1b"],
                  sq_scr=BB2[:, 0:2048])
    if stop == "B5":
        _dump_stop(p, nc, {"x4b": BB1})
        return nc

    x4bv = BB1.rearrange("p (c t) -> p c t", t=2048)
    wgrps = AL("wgrps", [128, 4 * 8 + 8 * 8], BF16).ap()
    DMA(p, "gpsimd", wgrps.rearrange("p (c e) -> p c e", e=12)[:, :, 0:4],
        dram["b_wgrp"].rearrange("(c p) e -> p c e", p=128), ch="inig")
    DMA(p, "gpsimd", wgrps.rearrange("p (c e) -> p c e", e=12)[:, :, 4:12],
        dram["b_wexp"].rearrange("(c p) e -> p c e", p=128), ch="inig")
    p.fence()
    wgv = wgrps.rearrange("p (c e) -> p c e", e=12)
    for cg in range(4):
        for k in range(8):
            MM(p, psE[0:4, cg * 512:(cg + 1) * 512], wgv[:, k, 0:4],
               x4bv[:, k, cg * 512:(cg + 1) * 512], start=(k == 0), stop=(k == 7))
        for k in range(8):
            MM(p, psF[0:8, cg * 512:(cg + 1) * 512], wgv[:, k, 4:12],
               x4bv[:, k, cg * 512:(cg + 1) * 512], start=(k == 0), stop=(k == 7))
    t = p.tick("tensor"); p.wait("scalar", t); p.wait("vector", t)
    R4a, R4b, R4c, R4d = R4zone
    ACT(p, R4a, psE[0:4, :], AF.Exp, bias=sbp["b_bgrp"][:, 0:1])
    t = p.tick("scalar"); p.wait("vector", t); p.wait("tensor", t)
    for st_, (pm, dst, src) in enumerate(((("c_P4a"), R4b, R4a), (("c_P4b"), R4c, R4b))):
        for cg in range(4):
            MM(p, psE[0:4, cg * 512:(cg + 1) * 512], sbc[pm][:, :],
               src[:, cg * 512:(cg + 1) * 512])
        t = p.tick("tensor"); p.wait("vector", t)
        TT(p, dst, src, psE[0:4, :], ALU.add)
        t = p.tick("vector"); p.wait("tensor", t)
    p.op("vector", lambda v, o=R4d, i=R4c: v.reciprocal(out=o, in_=i))
    TT(p, R4a, R4a, R4d, ALU.mult)
    t = p.tick("vector"); p.wait("tensor", t)
    for st_, (pm, dst, src) in enumerate(((("c_P4a"), R4b, R4a), (("c_P4b"), R4c, R4b))):
        for cg in range(4):
            MM(p, psE[0:4, cg * 512:(cg + 1) * 512], sbc[pm][:, :],
               src[:, cg * 512:(cg + 1) * 512])
        t = p.tick("tensor"); p.wait("vector", t)
        TT(p, dst, src, psE[0:4, :], ALU.max)
        t = p.tick("vector"); p.wait("tensor", t)
    TT(p, R4b, R4a, R4c, ALU.is_equal)
    t = p.tick("vector"); p.wait("tensor", t)
    for cg in range(4):
        MM(p, psE[0:4, cg * 512:(cg + 1) * 512], sbc["c_L4str"][:, :],
           R4b[:, cg * 512:(cg + 1) * 512])
    t = p.tick("tensor"); p.wait("vector", t)
    TS(p, R4d, psE[0:4, :], 0.0, ALU.is_equal)
    TT(p, R4b, R4b, R4d, ALU.mult)
    t = p.tick("vector"); p.wait("tensor", t); p.wait("scalar", t)
    ACT(p, R8, psF[0:8, :], AF.Exp, bias=sbp["b_bexp"][:, 0:1])
    t = p.tick("scalar"); p.wait("tensor", t); p.wait("vector", t)
    for cg in range(4):
        MM(p, psE[0:8, cg * 512:(cg + 1) * 512], sbc["c_Ppair8"][:, :],
           R8[:, cg * 512:(cg + 1) * 512])
    t = p.tick("tensor"); p.wait("vector", t)
    TT(p, R8b, R8, psE[0:8, :], ALU.add)
    p.op("vector", lambda v, o=R8d, i=R8b: v.reciprocal(out=o, in_=i))
    TT(p, R8, R8, R8d, ALU.mult)
    t = p.tick("vector"); p.wait("tensor", t)
    for cg in range(4):
        MM(p, psE[0:8, cg * 512:(cg + 1) * 512], sbc["c_dup48"][:, :],
           R4b[:, cg * 512:(cg + 1) * 512])
    t = p.tick("tensor"); p.wait("vector", t)
    TT(p, R8b, R8, psE[0:8, :], ALU.mult)
    t = p.tick("vector"); p.wait("tensor", t)
    for cg in range(4):
        MM(p, psE[0:8, cg * 512:(cg + 1) * 512], sbc["c_Ppair8"][:, :],
           R8b[:, cg * 512:(cg + 1) * 512])
    t = p.tick("tensor"); p.wait("vector", t)
    TT(p, R8d, R8b, psE[0:8, :], ALU.is_gt)
    TT(p, R8, R8b, psE[0:8, :], ALU.is_equal)
    TS(p, R8, R8, sbc["c_even8"][:, 0:1], ALU.mult)
    TT(p, R8d, R8d, R8, ALU.add)
    t = p.tick("vector"); p.wait("tensor", t)
    for cg in range(4):
        MM(p, psF[0:8, cg * 512:(cg + 1) * 512], sbc["c_dup48"][:, :],
           R4b[:, cg * 512:(cg + 1) * 512])
    t = p.tick("tensor"); p.wait("vector", t)
    TT(p, R8d, R8d, psF[0:8, :], ALU.mult)
    TT(p, R8, R8d, R8b, ALU.mult)
    t = p.tick("vector"); p.wait("tensor", t)
    for cg in range(4):
        MM(p, psE[0:1, cg * 512:(cg + 1) * 512], sbc["c_ones2df"][0:8, 0:1],
           R8[:, cg * 512:(cg + 1) * 512])
    t = p.tick("tensor"); p.wait("vector", t)
    TT(p, rows1[0:1, 0:2048], R4c[0:1, :], psE[0:1, :], ALU.mult)
    t = p.tick("vector"); p.wait("tensor", t)
    for cg in range(4):
        MM(p, psE[0:8, cg * 512:(cg + 1) * 512], sbc["c_ones2df"][0:1, 0:8],
           rows1[0:1, cg * 512:(cg + 1) * 512])
    t = p.tick("tensor"); p.wait("vector", t)
    TT(p, R8b, R8d, psE[0:8, :], ALU.mult)
    COPY(p, RBb[0:8, :], R8b)
    p.fence()
    if stop == "B6":
        _dump_stop(p, nc, {"cm": R8b, "hot": R8d})
        return nc

    hLv = [BB2.rearrange("p (c t) -> p c t", t=2048),
           BB3.rearrange("p (c t) -> p c t", t=2048)]
    acc1 = BB0
    acc1v = BB0.rearrange("p (c t) -> p c t", t=2048)
    MEMSET(p, acc1[:, :], 0.0)
    t = p.tick("vector"); p.wait("tensor", t)
    for e in range(8):
        evp = [None, None]
        for m in range(16):
            slot = m % 4
            tL = colblock_load(wslot[slot], dram["b_w1L1"][e], m * 128, slot)
            wv_ = wslot[slot].rearrange("p (c m) -> p c m", m=128)
            p.wait("tensor", tL)
            if evp[m % 2] is not None:
                p.wait("tensor", evp[m % 2])
            ps = psE if m % 2 == 0 else psF
            for cg in range(4):
                for k in range(8):
                    MM(p, ps[:, cg * 512:(cg + 1) * 512], wv_[:, k],
                       x4bv[:, k, cg * 512:(cg + 1) * 512],
                       start=(k == 0), stop=(k == 7))
            tMM = p.tick("tensor")
            wtok[slot] = tMM
            p.wait("scalar", tMM)
            ACT(p, hLv[m // 8][:, m % 8], ps, AF.Relu,
                bias=sbp["b_b1L1"][:, e * 16 + m:e * 16 + m + 1])
            evp[m % 2] = p.tick("scalar")
        p.wait("tensor", evp[1])
        for cg in range(4):
            MM(p, psE[:, cg * 512:(cg + 1) * 512],
               sbc["c_esel16"][0:8, e * 128:(e + 1) * 128],
               RBb[0:8, cg * 512:(cg + 1) * 512])
        t = p.tick("tensor"); p.wait("vector", t)
        for m in range(16):
            TT(p, hLv[m // 8][:, m % 8], hLv[m // 8][:, m % 8], psE, ALU.mult)
        tMk = p.tick("vector")
        p.wait("tensor", tMk)
        stt_tok = [None, None]
        for m8 in range(8):
            slot = m8 % 4
            if wtok[slot] is not None:
                p.wait("gpsimd", wtok[slot])
            tL = DMA(p, "gpsimd",
                     wslot[slot].rearrange("p (c m) -> p c m", m=128),
                     dram["b_w2L1"][e].rearrange("(c p) m -> p c m", p=128)
                     [:, :, m8 * 128:(m8 + 1) * 128], ch=f"wl{slot}")
            wv_ = wslot[slot].rearrange("p (c m) -> p c m", m=128)
            psw = psF if m8 % 2 == 0 else psE
            p.wait("tensor", tL)
            if stt_tok[m8 % 2] is not None:
                p.wait("tensor", stt_tok[m8 % 2])
            for cg in range(4):
                pt = psw[:, cg * 512:(cg + 1) * 512]
                for k in range(16):
                    MM(p, pt, wv_[:, k],
                       hLv[k // 8][:, k % 8, cg * 512:(cg + 1) * 512],
                       start=(k == 0), stop=False)
                MM(p, pt, b2L1b[:, m8 * 128:(m8 + 1) * 128],
                   RBb[0:8, cg * 512:(cg + 1) * 512], start=False, stop=True)
            t = p.tick("tensor")
            wtok[slot] = t
            p.wait("vector", t)
            STT(p, acc1v[:, m8], psw, 1.0, acc1v[:, m8], ALU.bypass, ALU.add)
            stt_tok[m8 % 2] = p.tick("vector")
        p.wait("tensor", stt_tok[0], stt_tok[1])
    p.fence()
    if stop == "B7":
        _dump_stop(p, nc, {"acc1": BB0})
        return nc

    zbv2 = BB2.rearrange("p (c t) -> p c t", t=2048)
    sl_tok = [None, None]
    for m in range(8):
        par = m % 2
        if sl_tok[par] is not None:
            p.wait("sync", sl_tok[par])
        tX = DMA(p, "sync", f_slab[par], x4sp[m * 128:(m + 1) * 128, :], ch=f"xf{par}")
        p.wait("vector", tX)
        STT(p, zbv2[:, m], f_slab[par], 1.0, acc1v[:, m], ALU.bypass, ALU.add)
        tZ = p.tick("vector")
        sl_tok[par] = tZ
        p.wait("tensor", tZ)
        stats_sum(psF[0:1], zbv2[:, m], m)
    stats_finalize(psF[0:1], f_murow, f_rsrow, lambda m: zbv2[:, m], BB3[:, 0:2048])
    bcast_mu_rs(f_murow, f_rsrow, mu_b, rs_b)
    sl_tok = [None, None]
    for m in range(8):
        par = m % 2
        if sl_tok[par] is not None:
            p.wait("sync", sl_tok[par])
        tX = DMA(p, "sync", f_slab[par], x3sp[m * 128:(m + 1) * 128, :], ch=f"xf{par}")
        TT(p, f_scr, zbv2[:, m], muB, ALU.subtract)
        TT(p, f_scr, f_scr, rsB, ALU.mult)
        TS(p, f_scr, f_scr, sbp["b_ln2g"][:, m:m + 1], ALU.mult,
           sbp["b_ln2b"][:, m:m + 1], ALU.add)
        p.wait("vector", tX)
        TT(p, zbv2[:, m], f_scr, f_slab[par], ALU.add)
        tU = p.tick("vector")
        sl_tok[par] = tU
        p.wait("tensor", tU)
        stats_sum(psF[0:1], zbv2[:, m], m)
    stats_finalize(psF[0:1], f_murow, f_rsrow, lambda m: zbv2[:, m], BB3[:, 0:2048])
    bcast_mu_rs(f_murow, f_rsrow, mu_b, rs_b)
    xs_tok = [None, None]
    for m in range(8):
        par = m % 2
        if xs_tok[par] is not None:
            p.wait("vector", *xs_tok[par])
        fs = f_slab[par]
        TT(p, fs, zbv2[:, m], muB, ALU.subtract)
        TT(p, fs, fs, rsB, ALU.mult)
        t = p.tick("vector"); p.wait("sync", t)
        tD = DMA(p, "sync", yT[m * 128:(m + 1) * 128, :], fs, ch=f"yo{par}")
        xs_tok[par] = (p.tick("vector") if False else t, tD)
    p.wait_chan_totals("gpsimd", ["yo0", "yo1"])
    p.op("gpsimd", lambda g: g.engine_nop())
    p.tick("gpsimd")

    p.build()
    return nc
